# version 03
# baseline (speedup 1.0000x reference)
import math
import numpy as np
import ml_dtypes
import concourse.bass as bass
import concourse.mybir as mybir
from concourse.bass_utils import run_bass_kernel_spmd

F32 = mybir.dt.float32
BF16 = mybir.dt.bfloat16
AF = mybir.ActivationFunctionType
ALU = mybir.AluOpType
AX = mybir.AxisListType

B, S, D = 4, 4096, 2048
H, HD, RD = 16, 128, 64
DKV = 512
DFF = 8192
TOK = S // 2
SCALE0 = 1.0 / math.sqrt(HD + RD)

BF = ml_dtypes.bfloat16



class _Op:
    __slots__ = ("stream", "emit", "deps", "need_inc", "is_dma", "dma_use",
                 "dma_sem", "count", "tag")

    def __init__(self, stream, emit, is_dma, tag):
        self.stream = stream
        self.emit = emit
        self.deps = []
        self.need_inc = False
        self.is_dma = is_dma
        self.dma_use = 0
        self.dma_sem = None
        self.count = 0
        self.tag = tag


_ENGS = ("tensor", "vector", "scalar", "gpsimd", "sync")
_DMA_POOL = 16


class MK:
    def __init__(self, nc):
        self.nc = nc
        self.streams = {e: [] for e in _ENGS}
        self.comp_sem = {e: nc.alloc_semaphore(f"cs_{e}") for e in _ENGS}
        self.dma_sems = {}
        self.dma_ops = {}
        self.tok_w = {}
        self.tok_r = {}
        self.pending = []
        base = (nc.sbuf_base + 63) & ~63
        self.sb_free = [[base, nc.sbuf_top]]
        self.sb_allocs = {}
        self.ps = [nc.alloc_psum_tensor(f"ps{i}", [128, 512], F32).ap()
                   for i in range(8)]
        self._uid = 0

    def sbuf(self, name, shape, dtype):
        per_part = int(np.prod(shape[1:])) * (2 if dtype == BF16 else 4)
        size = (per_part + 63) & ~63
        for seg in self.sb_free:
            if seg[1] - seg[0] >= size:
                off = seg[0]
                seg[0] += size
                break
        else:
            raise MemoryError(f"sbuf full allocating {name} ({size}B/part)")
        self._uid += 1
        h = self.nc.alloc_sbuf_tensor_at(f"{name}_{self._uid}", shape, dtype,
                                         offset=off)
        ap = h.ap()
        self.sb_allocs[name] = [off, size, ap, list(self.pending)]
        return ap

    def sfree(self, *names):
        pend = list(self.pending)
        for name in names:
            off, size, _, _ = self.sb_allocs.pop(name)
            self.sb_free.append([off, off + size])
            pref = name + "."
            for t in list(self.tok_w):
                if t == name or t.startswith(pref):
                    pend.append(self.tok_w.pop(t))
            for t in list(self.tok_r):
                if t == name or t.startswith(pref):
                    pend.extend(self.tok_r.pop(t))
        self.sb_free.sort()
        merged = []
        for seg in self.sb_free:
            if merged and merged[-1][1] >= seg[0]:
                merged[-1][1] = max(merged[-1][1], seg[1])
            elif seg[1] > seg[0]:
                merged.append(seg)
        self.sb_free = merged
        self.pending = pend

    def barrier(self):
        self.pending = [s[-1] for s in self.streams.values() if s]

    def _record(self, op, reads, writes):
        deps = []
        for t in reads:
            w = self.tok_w.get(t)
            if w is not None:
                deps.append(w)
        for t in writes:
            w = self.tok_w.get(t)
            if w is not None:
                deps.append(w)
            deps.extend(self.tok_r.get(t, ()))
            al = self.sb_allocs.get(t.split(".", 1)[0])
            if al is not None and al[3]:
                deps.extend(al[3])
        for d in deps:
            if d is op:
                continue
            if (d.stream == op.stream and not d.is_dma and not op.is_dma
                    and op.stream == "tensor"):
                continue
            op.deps.append(d)
            d.need_inc = True
        for t in reads:
            self.tok_r.setdefault(t, []).append(op)
        for t in writes:
            self.tok_w[t] = op
            self.tok_r[t] = []
        self.streams[op.stream].append(op)
        return op

    def op(self, stream, emit, reads=(), writes=(), tag=""):
        return self._record(_Op(stream, emit, False, tag), reads, writes)

    def dma(self, queue, out_ap, in_ap, reads=(), writes=(), transpose=False,
            tag=""):
        if transpose:
            emit = lambda e: e.dma_start_transpose(out_ap, in_ap)
        else:
            emit = lambda e: e.dma_start(out=out_ap, in_=in_ap)
        op = _Op(queue, emit, True, tag)
        ops = self.dma_ops.setdefault(queue, [])
        j = len(ops)
        if queue not in self.dma_sems:
            self.dma_sems[queue] = [
                self.nc.alloc_semaphore(f"dq_{queue}{i}")
                for i in range(_DMA_POOL)]
        op.dma_sem = self.dma_sems[queue][j % _DMA_POOL]
        op.dma_use = j // _DMA_POOL
        op.need_inc = True
        if j >= _DMA_POOL:
            op.deps.append(ops[j - _DMA_POOL])
        ops.append(op)
        return self._record(op, reads, writes)

    def _completion(self, op):
        if op.is_dma:
            return op.dma_sem, 16 * (op.dma_use + 1)
        return self.comp_sem[op.stream], op.count

    def finalize(self):
        for e in _ENGS:
            c = 0
            for op in self.streams[e]:
                if not op.is_dma and op.need_inc:
                    c += 1
                    op.count = c
        known = {e: {} for e in _ENGS}

        def section(eng_name):
            def run(eng):
                kn = known[eng_name]
                for op in self.streams[eng_name]:
                    best = {}
                    for d in op.deps:
                        sem, val = self._completion(d)
                        if val > best.get(id(sem), (None, -1))[1]:
                            best[id(sem)] = (sem, val)
                    for sem, val in best.values():
                        if kn.get(id(sem), -1) >= val:
                            continue
                        eng.wait_ge(sem, val)
                        kn[id(sem)] = val
                    inst = op.emit(eng)
                    if op.need_inc:
                        if op.is_dma:
                            inst.then_inc(op.dma_sem, 16)
                        else:
                            inst.then_inc(self.comp_sem[op.stream], 1)
                if eng_name == "sync":
                    for q, ops in self.dma_ops.items():
                        fin = {}
                        for op in ops:
                            fin[id(op.dma_sem)] = (op.dma_sem,
                                                   16 * (op.dma_use + 1))
                        for sem, val in fin.values():
                            if kn.get(id(sem), -1) < val:
                                eng.wait_ge(sem, val)
                                kn[id(sem)] = val
                    for e2 in _ENGS:
                        if e2 == "sync":
                            continue
                        cnt = [o.count for o in self.streams[e2]
                               if not o.is_dma and o.need_inc]
                        if cnt and kn.get(id(self.comp_sem[e2]), -1) < cnt[-1]:
                            eng.wait_ge(self.comp_sem[e2], cnt[-1])
            return run

        with self.nc.Block() as block:
            if self.streams["tensor"]:
                block.tensor(section("tensor"))
            if self.streams["vector"]:
                block.vector(section("vector"))
            if self.streams["scalar"]:
                block.scalar(section("scalar"))
            if self.streams["gpsimd"]:
                block.gpsimd(section("gpsimd"))
            block.sync(section("sync"))



def _host_tables():
    inv = 1.0 / (10000.0 ** (np.arange(0, RD, 2, dtype=np.float64) / RD))
    pos = np.arange(S, dtype=np.float64)
    fr = inv[:, None] * pos[None, :]
    c64 = np.concatenate([np.cos(fr), np.cos(fr)], 0)
    s64 = np.concatenate([-np.sin(fr), np.sin(fr)], 0)
    ropeC = np.concatenate([c64, c64], 0).astype(np.float32)
    ropeS = np.concatenate([s64, s64], 0).astype(np.float32)
    qi = np.arange(512)[None, None, :]
    p = np.arange(128)[None, :, None]
    m = np.arange(4)[:, None, None]
    masks = (qi >= 128 * m + p).astype(np.float32).astype(BF)
    return ropeC, ropeS, masks



def _alloc_ln_scratch(mk, pref):
    sc = {}
    for n in ("sum", "s2", "nmean", "ex2", "nvar", "lnv", "rstd", "b"):
        sc[n] = mk.sbuf(f"{pref}{n}", [128, 1], F32)
    sc["eps"] = mk.sbuf(f"{pref}eps", [128, 1], F32)
    sc["_p"] = pref
    mk.op("vector", lambda e: e.memset(sc["eps"][:], 1e-5),
          writes=[f"{pref}eps"])
    return sc


def _ln_chain(mk, xt, xt_toks, n_elems, junk, out_bf, out_tok, sc):
    p = sc["_p"]
    inv = 1.0 / n_elems
    mk.op("vector", lambda e: e.reduce_sum(sc["sum"][:], xt[:], axis=AX.X),
          reads=xt_toks, writes=[f"{p}sum"])
    mk.op("scalar", lambda e: e.activation(out=junk[:], in_=xt[:],
          func=AF.Square, accum_out=sc["s2"][:, 0:1]),
          reads=xt_toks, writes=[f"{p}s2", f"{p}junk"])
    mk.op("vector", lambda e: e.tensor_scalar_mul(sc["nmean"][:], sc["sum"][:],
          -inv), reads=[f"{p}sum"], writes=[f"{p}nmean"])
    mk.op("vector", lambda e: e.tensor_scalar_mul(sc["ex2"][:], sc["s2"][:],
          inv), reads=[f"{p}s2"], writes=[f"{p}ex2"])
    mk.op("vector", lambda e: e.tensor_scalar(sc["nvar"][:], sc["nmean"][:],
          sc["nmean"][:, 0:1], sc["ex2"][:, 0:1], ALU.mult, ALU.subtract),
          reads=[f"{p}nmean", f"{p}ex2"], writes=[f"{p}nvar"])
    mk.op("scalar", lambda e: e.activation(out=sc["lnv"][:], in_=sc["nvar"][:],
          func=AF.Ln, bias=sc["eps"][:, 0:1], scale=-1.0),
          reads=[f"{p}nvar", f"{p}eps"], writes=[f"{p}lnv"])
    mk.op("scalar", lambda e: e.activation(out=sc["rstd"][:], in_=sc["lnv"][:],
          func=AF.Exp, scale=-0.5),
          reads=[f"{p}lnv"], writes=[f"{p}rstd"])
    mk.op("vector", lambda e: e.tensor_scalar_mul(sc["b"][:], sc["nmean"][:],
          sc["rstd"][:, 0:1]),
          reads=[f"{p}nmean", f"{p}rstd"], writes=[f"{p}b"])
    mk.op("vector", lambda e: e.tensor_scalar(out_bf[:], xt[:],
          sc["rstd"][:, 0:1], sc["b"][:, 0:1], ALU.mult, ALU.add),
          reads=list(xt_toks) + [f"{p}rstd", f"{p}b"], writes=[out_tok])


def _ph_ln(mk, T, src, dst, nrows, pref):
    sc = _alloc_ln_scratch(mk, pref)
    xt = [mk.sbuf(f"{pref}x{i}", [128, D], F32) for i in range(2)]
    junk = mk.sbuf(f"{pref}junk", [128, D], BF16)
    ob = [mk.sbuf(f"{pref}o{i}", [128, D], BF16) for i in range(2)]
    for i in range(nrows // 128):
        sl = i % 2
        mk.dma("sync", xt[sl][:], T[src][i * 128:(i + 1) * 128, :],
               reads=[src], writes=[f"{pref}x{sl}"])
        _ln_chain(mk, xt[sl], [f"{pref}x{sl}"], D, junk, ob[sl],
                  f"{pref}o{sl}", sc)
        mk.dma("sync", T[dst][i * 128:(i + 1) * 128, :], ob[sl][:],
               reads=[f"{pref}o{sl}"], writes=[f"{dst}.{i}"])
    mk.sfree(*[f"{pref}{n}" for n in
               ("sum", "s2", "nmean", "ex2", "nvar", "lnv", "rstd", "b",
                "eps", "junk", "x0", "x1", "o0", "o1")])


def _tm_tok(dst, i0, n):
    return [f"{dst}.{i}" for i in range(i0 // 128, (i0 + n + 127) // 128)]



def _ph_l0_proj(mk, T):
    nc = mk.nc
    wq = mk.sbuf("wq", [128, 16 * 1024], BF16)
    wqr = mk.sbuf("wqr", [128, 16 * 512], BF16)
    wkvd = mk.sbuf("wkvd", [128, 16 * 512], BF16)
    wkr = mk.sbuf("wkr", [128, 16 * 64], BF16)
    for dc in range(16):
        mk.dma("gpsimd", wq[:, dc * 1024:(dc + 1) * 1024],
               T["Wq_h"][dc * 128:(dc + 1) * 128, :],
               reads=["Wq_h"], writes=[f"wq.{dc}"])
        mk.dma("gpsimd", wqr[:, dc * 512:(dc + 1) * 512],
               T["Wqr_h"][dc * 128:(dc + 1) * 128, :],
               reads=["Wqr_h"], writes=[f"wqr.{dc}"])
        mk.dma("gpsimd", wkvd[:, dc * 512:(dc + 1) * 512],
               T["Wkvd"][dc * 128:(dc + 1) * 128, :],
               reads=["Wkvd"], writes=[f"wkvd.{dc}"])
        mk.dma("gpsimd", wkr[:, dc * 64:(dc + 1) * 64],
               T["Wkr"][dc * 128:(dc + 1) * 128, :],
               reads=["Wkr"], writes=[f"wkr.{dc}"])
    ckvT = mk.sbuf("ckvT", [128, 4 * S], BF16)
    xnT = [mk.sbuf(f"pxnT{i}", [128, 16 * 512], BF16) for i in range(2)]
    ev = [mk.sbuf(f"pev{i}", [128, 512], BF16) for i in range(4)]
    evf = [mk.sbuf(f"pevf{i}", [128, 512], F32) for i in range(2)]
    pi = [0]

    def mm_chain(n_k, lhsT_of, rhs_of, rtoks, mrows=128):
        bank = pi[0] % 2
        pi[0] += 1
        ptok = f"ps.{bank}"
        for k in range(n_k):
            mk.op("tensor",
                  (lambda e, k=k, bank=bank: e.matmul(
                      mk.ps[bank][0:mrows, :], lhsT_of(k), rhs_of(k),
                      start=(k == 0), stop=(k == n_k - 1))),
                  reads=rtoks(k), writes=[ptok])
        return mk.ps[bank], ptok

    for tb in range(8):
        t0 = tb * 512
        sl = tb % 2
        for dc in range(16):
            mk.dma("sync", xnT[sl][:, dc * 512:(dc + 1) * 512],
                   T["xn0_tm"][t0:t0 + 512, dc * 128:(dc + 1) * 128],
                   reads=_tm_tok("xn0_tm", t0, 512),
                   writes=[f"pxnT{sl}.{dc}"], transpose=True)
        for oc in range(4):
            p, ptok = mm_chain(
                16,
                lambda k, oc=oc: wkvd[:, k * 512 + oc * 128:
                                      k * 512 + (oc + 1) * 128],
                lambda k, sl=sl: xnT[sl][:, k * 512:(k + 1) * 512],
                lambda k, sl=sl: [f"wkvd.{k}", f"pxnT{sl}.{k}"])
            mk.op("vector",
                  (lambda e, p=p, oc=oc, t0=t0: e.tensor_copy(
                      ckvT[:, oc * S + t0:oc * S + t0 + 512], p[:])),
                  reads=[ptok], writes=[f"ckvT.{oc}.{tb}"])
        p, ptok = mm_chain(
            16,
            lambda k: wkr[:, k * 64:(k + 1) * 64],
            lambda k, sl=sl: xnT[sl][:, k * 512:(k + 1) * 512],
            lambda k, sl=sl: [f"wkr.{k}", f"pxnT{sl}.{k}"], mrows=64)
        fs = tb % 2
        mk.op("vector",
              (lambda e, p=p, fs=fs: e.tensor_copy(
                  evf[fs][0:64, :], p[0:64, :])),
              reads=[ptok], writes=[f"pevf{fs}"])
        mk.dma("sync", T["krraw"][0:64, t0:t0 + 512], evf[fs][0:64, :],
               reads=[f"pevf{fs}"], writes=[f"krraw.{tb}"])
        for oc in range(8):
            p, ptok = mm_chain(
                16,
                lambda k, oc=oc: wq[:, k * 1024 + oc * 128:
                                    k * 1024 + (oc + 1) * 128],
                lambda k, sl=sl: xnT[sl][:, k * 512:(k + 1) * 512],
                lambda k, sl=sl: [f"wq.{k}", f"pxnT{sl}.{k}"])
            es = oc % 4
            mk.op("vector",
                  (lambda e, p=p, es=es: e.tensor_copy(ev[es][:], p[:])),
                  reads=[ptok], writes=[f"pev{es}"])
            mk.dma("sync", T["qcT"][oc * 128:(oc + 1) * 128, t0:t0 + 512],
                   ev[es][:], reads=[f"pev{es}"], writes=[f"qcT.{oc}.{tb}"])
        for oc in range(4):
            p, ptok = mm_chain(
                16,
                lambda k, oc=oc: wqr[:, k * 512 + oc * 128:
                                     k * 512 + (oc + 1) * 128],
                lambda k, sl=sl: xnT[sl][:, k * 512:(k + 1) * 512],
                lambda k, sl=sl: [f"wqr.{k}", f"pxnT{sl}.{k}"])
            fs = oc % 2
            mk.op("vector",
                  (lambda e, p=p, fs=fs: e.tensor_copy(evf[fs][:], p[:])),
                  reads=[ptok], writes=[f"pevf{fs}"])
            mk.dma("sync", T["qrraw"][oc * 128:(oc + 1) * 128, t0:t0 + 512],
                   evf[fs][:], reads=[f"pevf{fs}"],
                   writes=[f"qrraw.{oc}.{tb}"])
    mk.sfree("wq", "wqr", "wkvd", "wkr")
    wkvu = mk.sbuf("wkvu", [128, 4 * 2048], BF16)
    for cc in range(4):
        mk.dma("gpsimd", wkvu[:, cc * 2048:(cc + 1) * 2048],
               T["Wkvup_h"][cc * 128:(cc + 1) * 128, :],
               reads=["Wkvup_h"], writes=[f"wkvu.{cc}"])
    for tb in range(8):
        t0 = tb * 512
        for oc in range(8):
            p, ptok = mm_chain(
                4,
                lambda k, oc=oc: wkvu[:, k * 2048 + oc * 128:
                                      k * 2048 + (oc + 1) * 128],
                lambda k, t0=t0: ckvT[:, k * S + t0:k * S + t0 + 512],
                lambda k, tb=tb: [f"wkvu.{k}", f"ckvT.{k}.{tb}"])
            es = oc % 4
            mk.op("vector",
                  (lambda e, p=p, es=es: e.tensor_copy(ev[es][:], p[:])),
                  reads=[ptok], writes=[f"pev{es}"])
            mk.dma("sync", T["kcT"][oc * 128:(oc + 1) * 128, t0:t0 + 512],
                   ev[es][:], reads=[f"pev{es}"], writes=[f"kcT.{oc}.{tb}"])
    for tc in range(S // 128):
        for dvb in range(2):
            p, ptok = mm_chain(
                4,
                lambda k, tc=tc: ckvT[:, k * S + tc * 128:k * S + (tc+1)*128],
                lambda k, dvb=dvb: wkvu[:, k * 2048 + 1024 + dvb * 512:
                                        k * 2048 + 1024 + (dvb + 1) * 512],
                lambda k, tc=tc: [f"wkvu.{k}", f"ckvT.{k}.{tc // 4}"])
            es = dvb % 4
            mk.op("vector",
                  (lambda e, p=p, es=es: e.tensor_copy(ev[es][:], p[:])),
                  reads=[ptok], writes=[f"pev{es}"])
            mk.dma("sync", T["v_tm"][tc * 128:(tc + 1) * 128,
                                     dvb * 512:(dvb + 1) * 512],
                   ev[es][:], reads=[f"pev{es}"], writes=[f"v_tm.{tc}.{dvb}"])
    mk.sfree("wkvu", "ckvT", "pxnT0", "pxnT1", "pev0", "pev1", "pev2",
             "pev3", "pevf0", "pevf1")
    ropeC = mk.sbuf("ropeC", [128, S], F32)
    ropeS = mk.sbuf("ropeS", [128, S], F32)
    mk.dma("sync", ropeC[:], T["ropeC"][:, :], reads=["ropeC_in"],
           writes=["ropeC"])
    mk.dma("sync", ropeS[:], T["ropeS"][:, :], reads=["ropeS_in"],
           writes=["ropeS"])
    rx = [mk.sbuf(f"rx{i}", [128, 512], F32) for i in range(2)]
    rr = [mk.sbuf(f"rr{i}", [128, 512], F32) for i in range(2)]
    rt = [mk.sbuf(f"rt{i}", [128, 512], F32) for i in range(2)]
    ro = [mk.sbuf(f"ro{i}", [128, 512], BF16) for i in range(2)]
    krT = mk.sbuf("krT", [64, S], BF16)
    T["_krT"] = krT

    def rope_block(src, stok, r0, nrow, t0, dst_ap, dtok, sl):
        xa, ra, ta, oa = rx[sl], rr[sl], rt[sl], ro[sl]
        rts = [f"rr{sl}.{g}{ab}" for g in range(nrow // 64) for ab in "ab"]
        mk.dma("sync", xa[0:nrow, :], src[r0:r0 + nrow, t0:t0 + 512],
               reads=[stok], writes=[f"rx{sl}"])
        for g in range(nrow // 64):
            mk.dma("sync", ra[g * 64:g * 64 + 32, :],
                   src[r0 + g * 64 + 32:r0 + g * 64 + 64, t0:t0 + 512],
                   reads=[stok], writes=[f"rr{sl}.{g}a"])
            mk.dma("sync", ra[g * 64 + 32:g * 64 + 64, :],
                   src[r0 + g * 64:r0 + g * 64 + 32, t0:t0 + 512],
                   reads=[stok], writes=[f"rr{sl}.{g}b"])
        mk.op("vector", lambda e: e.tensor_mul(ta[0:nrow, :], xa[0:nrow, :],
              ropeC[0:nrow, t0:t0 + 512]),
              reads=[f"rx{sl}", "ropeC"], writes=[f"rt{sl}"])
        mk.op("vector", lambda e: e.tensor_mul(ra[0:nrow, :], ra[0:nrow, :],
              ropeS[0:nrow, t0:t0 + 512]),
              reads=rts + ["ropeS"], writes=rts)
        mk.op("vector", lambda e: e.tensor_add(oa[0:nrow, :], ta[0:nrow, :],
              ra[0:nrow, :]),
              reads=[f"rt{sl}"] + rts, writes=[f"ro{sl}"])
        mk.dma("sync", dst_ap, oa[0:nrow, :],
               reads=[f"ro{sl}"], writes=[dtok])

    for blk in range(8):
        t0 = blk * 512
        for r in range(4):
            sl = (blk * 4 + r) % 2
            rope_block(T["qrraw"], f"qrraw.{r}.{blk}", r * 128, 128, t0,
                       T["qrT"][r * 128:(r + 1) * 128, t0:t0 + 512],
                       f"qrT.{r}.{blk}", sl)
        sl = blk % 2
        rope_block(T["krraw"], f"krraw.{blk}", 0, 64, t0,
                   krT[0:64, t0:t0 + 512], f"krT.{blk}", sl)
    mk.sfree("ropeC", "ropeS", "rx0", "rx1", "rr0", "rr1", "rt0", "rt1",
             "ro0", "ro1")


def _ph_l0_attn(mk, T, filler=None):
    krT = T["_krT"]
    mask = mk.sbuf("mask", [128, 4 * 512], BF16)
    for m in range(4):
        mk.dma("sync", mask[:, m * 512:(m + 1) * 512], T["masks"][m, :, :],
               reads=["masks_in"], writes=[f"mask.{m}"])
    sc192 = mk.sbuf("sc192", [128, 1], F32)
    mk.op("vector", lambda e: e.memset(sc192[:], SCALE0), writes=["sc192"])
    ones_b = mk.sbuf("ones_b", [128, 1], BF16)
    mk.op("vector", lambda e: e.memset(ones_b[:], 1.0), writes=["ones_b"])
    ones_f = mk.sbuf("ones_f", [1, 128], F32)
    mk.op("vector", lambda e: e.memset(ones_f[:], 1.0), writes=["ones_f"])

    qc = [mk.sbuf(f"aqc{i}", [128, 512], BF16) for i in range(2)]
    qr = [mk.sbuf(f"aqr{i}", [64, 512], BF16) for i in range(2)]
    kb = [mk.sbuf(f"akb{i}", [128, S], BF16) for i in range(2)]
    vb = [mk.sbuf(f"avb{i}", [128, S], BF16) for i in range(2)]
    pt = [mk.sbuf(f"apt{i}", [128, 512], BF16) for i in range(2)]
    dens = [mk.sbuf(f"adens{i}", [1, 512], F32) for i in range(2)]
    denr = [mk.sbuf(f"adenr{i}", [1, 512], F32) for i in range(2)]
    bcs = mk.sbuf("abcs", [128, 512], F32)
    onb = mk.sbuf("aonb", [128, 512], BF16)
    psS = [mk.ps[0], mk.ps[1]]
    psO = [mk.ps[2], mk.ps[3]]
    psD = [mk.ps[4][0:1, :], mk.ps[5][0:1, :]]
    psB = mk.ps[6]
    pend = []

    def epilogue():
        if not pend:
            return
        h, qb, par = pend.pop(0)
        q0 = qb * 512
        mk.op("vector", (lambda e, par=par: e.tensor_copy(
                  dens[par][:], psD[par][:])),
              reads=[f"ps.{4 + par}"], writes=[f"adens{par}"])
        mk.op("vector", (lambda e, par=par: e.reciprocal(
                  denr[par][:], dens[par][:])),
              reads=[f"adens{par}"], writes=[f"adenr{par}"])
        mk.op("tensor", (lambda e, par=par: e.matmul(
                  psB[:], ones_f[:], denr[par][:], start=True, stop=True)),
              reads=["ones_f", f"adenr{par}"], writes=["ps.6"])
        mk.op("vector", lambda e: e.tensor_copy(bcs[:], psB[:]),
              reads=["ps.6"], writes=["abcs"])
        mk.op("vector", (lambda e, par=par: e.tensor_mul(
                  onb[:], psO[par][:], bcs[:])),
              reads=[f"ps.{2 + par}", "abcs"], writes=["aonb"])
        mk.dma("sync", T["onormT"][h * 128:(h + 1) * 128, q0:q0 + 512],
               onb[:], reads=["aonb"], writes=[f"onormT.{h}.{qb}"])

    for h in range(8):
        for qb in range(8):
            q0 = qb * 512
            nk = 4 * qb + 4
            par = qb % 2
            sq = (h * 8 + qb) % 2
            mk.dma("scalar", qc[sq][:],
                   T["qcT"][h * 128:(h + 1) * 128, q0:q0 + 512],
                   reads=[f"qcT.{h}.{qb}"], writes=[f"aqc{sq}"])
            mk.dma("scalar", qr[sq][:],
                   T["qrT"][h * 64:(h + 1) * 64, q0:q0 + 512],
                   reads=[f"qrT.{h // 2}.{qb}"], writes=[f"aqr{sq}"])
            mk.dma("sync", kb[sq][:, 0:nk * 128],
                   T["kcT"][h * 128:(h + 1) * 128, 0:nk * 128],
                   reads=[f"kcT.{h}.{g}" for g in range((nk + 3) // 4)],
                   writes=[f"akb{sq}"])
            for g4 in range((nk + 3) // 4):
                n4 = min(4, nk - g4 * 4)
                vsrc = T["v_tm"][g4 * 512:g4 * 512 + n4 * 128,
                                 h * 128:(h + 1) * 128]
                vsrc = vsrc.rearrange("(g r) c -> r g c", g=n4)
                mk.dma("sync", vb[sq][:, g4 * 512:g4 * 512 + n4 * 128],
                       vsrc,
                       reads=[f"v_tm.{g4 * 4 + i}.{h // 4}"
                              for i in range(n4)],
                       writes=[f"avb{sq}.{g4}"])
            for k in range(nk):
                sl = k % 2
                mk.op("tensor", (lambda e, sl=sl, sq=sq, k=k: e.matmul(
                          psS[sl][:], kb[sq][:, k * 128:(k + 1) * 128],
                          qc[sq][:], start=True, stop=False)),
                      reads=[f"akb{sq}", f"aqc{sq}"], writes=[f"ps.{sl}"])
                mk.op("tensor", (lambda e, sl=sl, sq=sq, k=k: e.matmul(
                          psS[sl][:], krT[0:64, k * 128:(k + 1) * 128],
                          qr[sq][:], start=False, stop=True)),
                      reads=[f"krT.{k // 4}", f"aqr{sq}"], writes=[f"ps.{sl}"])
                mk.op("scalar", (lambda e, sl=sl: e.activation(
                          out=pt[sl][:], in_=psS[sl][:], func=AF.Exp,
                          scale=sc192[:, 0:1])),
                      reads=[f"ps.{sl}", "sc192"], writes=[f"apt{sl}"])
                m = k - 4 * qb
                if m >= 0:
                    mk.op("vector", (lambda e, sl=sl, m=m: e.tensor_mul(
                              pt[sl][:], pt[sl][:],
                              mask[:, m * 512:(m + 1) * 512])),
                          reads=[f"apt{sl}", f"mask.{m}"],
                          writes=[f"apt{sl}"])
                mk.op("tensor", (lambda e, sl=sl, sq=sq, k=k, nk=nk, par=par:
                      e.matmul(psO[par][:],
                               vb[sq][:, k * 128:(k + 1) * 128], pt[sl][:],
                               start=(k == 0), stop=(k == nk - 1))),
                      reads=[f"avb{sq}.{k // 4}", f"apt{sl}"],
                      writes=[f"ps.{2 + par}"])
                mk.op("tensor", (lambda e, sl=sl, k=k, nk=nk, par=par:
                      e.matmul(psD[par][:], ones_b[:], pt[sl][:],
                               start=(k == 0), stop=(k == nk - 1))),
                      reads=["ones_b", f"apt{sl}"], writes=[f"ps.{4 + par}"])
            pend.append((h, qb, par))
            if len(pend) > 1:
                epilogue()
        if filler is not None:
            filler(h)
    while pend:
        epilogue()
    mk.sfree("mask", "sc192", "ones_b", "ones_f", "abcs", "aonb",
             "aqc0", "aqc1", "aqr0", "aqr1", "akb0", "akb1", "avb0", "avb1",
             "adens0", "adens1", "adenr0", "adenr1", "apt0", "apt1")


def _ph_l0_wo(mk, T, cc=False):
    woh = mk.sbuf("woh", [128, 8 * 2048], BF16)
    for r in range(8):
        mk.dma("gpsimd", woh[:, r * 2048:(r + 1) * 2048],
               T["Wo_h"][r * 128:(r + 1) * 128, :],
               reads=["Wo_h"], writes=[f"woh.{r}"])
    onb = [mk.sbuf(f"won{i}", [128, 8 * 512], BF16) for i in range(2)]
    ev = [mk.sbuf(f"wev{i}", [128, 512], BF16) for i in range(2)]
    for tb in range(8):
        t0 = tb * 512
        sl = tb % 2
        for r in range(8):
            mk.dma("sync", onb[sl][:, r * 512:(r + 1) * 512],
                   T["onormT"][r * 128:(r + 1) * 128, t0:t0 + 512],
                   reads=[f"onormT.{r}.{tb}"], writes=[f"won{sl}.{r}"])
        for do in range(16):
            bank = do % 2
            for r in range(8):
                mk.op("tensor", (lambda e, bank=bank, r=r, do=do, sl=sl:
                      e.matmul(mk.ps[bank][:],
                               woh[:, r * 2048 + do * 128:
                                   r * 2048 + (do + 1) * 128],
                               onb[sl][:, r * 512:(r + 1) * 512],
                               start=(r == 0), stop=(r == 7))),
                      reads=[f"woh.{r}", f"won{sl}.{r}"],
                      writes=[f"ps.{bank}"])
            mk.op("vector", (lambda e, bank=bank: e.tensor_copy(
                      ev[bank][:], mk.ps[bank][:])),
                  reads=[f"ps.{bank}"], writes=[f"wev{bank}"])
            if cc:
                jblk, tq = tb // 4, tb % 4
                dst = T["ccA_in"][jblk * D + do * 128:jblk * D + (do + 1) * 128,
                                  tq * 512:(tq + 1) * 512]
                dtok = f"ccA_in.{do}.{tb}"
            else:
                dst = T["a0pT"][do * 128:(do + 1) * 128, t0:t0 + 512]
                dtok = f"a0pT.{do}.{tb}"
            mk.dma("sync", dst, ev[bank][:], reads=[f"wev{bank}"],
                   writes=[dtok])
    mk.sfree("woh", "won0", "won1", "wev0", "wev1")



def _ph_quant_p1(mk, T, wname, nt, ncols, nb=2):
    p = f"q{wname}"
    wt = [mk.sbuf(f"{p}w{i}", [128, ncols], F32) for i in range(nb)]
    junk = mk.sbuf(f"{p}junk", [128, ncols], BF16)
    acc = mk.sbuf(f"{p}acc", [128, 64], F32)
    accv = mk.sbuf(f"{p}accv", [128, 1], F32)
    ones128 = mk.sbuf(f"{p}o128", [128, 1], F32)
    ones1 = mk.sbuf(f"{p}o1", [1, 128], F32)
    cpos = mk.sbuf(f"{p}cp", [1, 1], F32)
    cneg = mk.sbuf(f"{p}cn", [1, 1], F32)
    trow = mk.sbuf(f"{p}tr", [1, 2], F32)
    tpos = mk.sbuf(f"{p}tp", [128, 1], F32)
    tneg = mk.sbuf(f"{p}tn", [128, 1], F32)
    numel = nt * 128 * ncols
    mk.op("vector", lambda e: e.memset(ones128[:], 1.0), writes=[f"{p}o128"])
    mk.op("vector", lambda e: e.memset(ones1[:], 1.0), writes=[f"{p}o1"])
    mk.op("vector", lambda e: e.memset(cpos[:], 0.7 / numel),
          writes=[f"{p}cp"])
    mk.op("vector", lambda e: e.memset(cneg[:], -0.7 / numel),
          writes=[f"{p}cn"])
    for i in range(nt):
        sl = i % nb
        mk.dma("gpsimd", wt[sl][:], T[wname][i * 128:(i + 1) * 128, :],
               reads=[wname], writes=[f"{p}w{sl}"])
        mk.op("scalar", (lambda e, sl=sl, i=i: e.activation(
                  out=junk[:], in_=wt[sl][:], func=AF.Abs,
                  accum_out=acc[:, i:i + 1])),
              reads=[f"{p}w{sl}"], writes=[f"{p}acc.{i}", f"{p}junk"])
    mk.op("vector", lambda e: e.reduce_sum(accv[:], acc[:, 0:nt], axis=AX.X),
          reads=[f"{p}acc.{i}" for i in range(nt)], writes=[f"{p}accv"])
    psT = mk.ps[7][0:1, 0:1]
    mk.op("tensor", lambda e: e.matmul(psT, ones128[:], accv[:],
          start=True, stop=True),
          reads=[f"{p}o128", f"{p}accv"], writes=["ps.7"])
    mk.op("vector", lambda e: e.tensor_scalar_mul(trow[0:1, 0:1], psT,
          cpos[0:1, 0:1]),
          reads=["ps.7", f"{p}cp"], writes=[f"{p}tr"])
    mk.op("vector", lambda e: e.tensor_scalar_mul(trow[0:1, 1:2], psT,
          cneg[0:1, 0:1]),
          reads=["ps.7", f"{p}cn", f"{p}tr"], writes=[f"{p}tr"])
    psBC = mk.ps[7][:, 0:2]
    mk.op("tensor", lambda e: e.matmul(psBC, ones1[:], trow[0:1, :],
          start=True, stop=True),
          reads=[f"{p}o1", f"{p}tr"], writes=["ps.7"])
    mk.op("vector", lambda e: e.tensor_copy(tpos[:], mk.ps[7][:, 0:1]),
          reads=["ps.7"], writes=[f"{p}tp"])
    mk.op("vector", lambda e: e.tensor_copy(tneg[:], mk.ps[7][:, 1:2]),
          reads=["ps.7"], writes=[f"{p}tn"])
    mk.sfree(f"{p}junk", f"{p}acc", f"{p}accv", f"{p}o128", f"{p}o1",
             f"{p}cp", f"{p}cn", f"{p}tr")
    T[f"_{p}tp"], T[f"_{p}tn"] = tpos, tneg
    s1 = [mk.sbuf(f"{p}s1{i}", [128, ncols], BF16) for i in range(nb)]
    s2 = [mk.sbuf(f"{p}s2{i}", [128, ncols], BF16) for i in range(nb)]
    wq = [mk.sbuf(f"{p}wq{i}", [128, ncols], BF16) for i in range(nb)]
    T[f"_{p}bufs"] = (wt, s1, s2, wq)


def _quant_p2_tile(mk, T, wname, qname, i):
    p = f"q{wname}"
    wt, s1, s2, wq = T[f"_{p}bufs"]
    tpos, tneg = T[f"_{p}tp"], T[f"_{p}tn"]
    nb = len(wt)
    sl = i % nb
    mk.dma("gpsimd", wt[sl][:], T[wname][i * 128:(i + 1) * 128, :],
           reads=[wname], writes=[f"{p}w{sl}"])
    mk.op("vector", (lambda e, sl=sl: e.tensor_single_scalar(
              s1[sl][:], wt[sl][:], tpos[:, 0:1], ALU.is_ge)),
          reads=[f"{p}w{sl}", f"{p}tp"], writes=[f"{p}s1{sl}"])
    mk.op("gpsimd", (lambda e, sl=sl: e.tensor_single_scalar(
              s2[sl][:], wt[sl][:], tneg[:, 0:1], ALU.is_le)),
          reads=[f"{p}w{sl}", f"{p}tn"], writes=[f"{p}s2{sl}"])
    mk.op("vector", (lambda e, sl=sl: e.tensor_sub(
              wq[sl][:], s1[sl][:], s2[sl][:])),
          reads=[f"{p}s1{sl}", f"{p}s2{sl}"], writes=[f"{p}wq{sl}"])
    mk.dma("gpsimd", T[qname][i * 128:(i + 1) * 128, :], wq[sl][:],
           reads=[f"{p}wq{sl}"], writes=[f"{qname}.{i}"])


def _ph_quant_free(mk, T, wname):
    p = f"q{wname}"
    nb = len(T[f"_{p}bufs"][0])
    names = []
    for i in range(nb):
        names += [f"{p}w{i}", f"{p}s1{i}", f"{p}s2{i}", f"{p}wq{i}"]
    mk.sfree(f"{p}tp", f"{p}tn", *names)


def _ph_quant(mk, T, wname, qname, nt, ncols):
    _ph_quant_p1(mk, T, wname, nt, ncols)
    for i in range(nt):
        _quant_p2_tile(mk, T, wname, qname, i)
    _ph_quant_free(mk, T, wname)


def _ph_l0_ffn(mk, T):
    b1t = mk.sbuf("b1t", [128, 64], F32)
    b2t = mk.sbuf("b2t", [128, 16], F32)
    mk.dma("sync", b1t[:], T["b1t"][:, :], reads=["b1t_in"], writes=["b1t"])
    mk.dma("sync", b2t[:], T["b2t"][:, :], reads=["b2t_in"], writes=["b2t"])
    for th in range(2):
        tt0 = th * 1024
        xnT = mk.sbuf("fxnT", [128, 16 * 1024], BF16)
        hbuf = mk.sbuf("fh", [128, 64 * 1024], BF16)
        w1t = mk.sbuf("fw1t", [128, 16 * 512], BF16)
        w2t = [mk.sbuf(f"fw2t{i}", [128, 512], BF16) for i in range(2)]
        fev = [mk.sbuf(f"fev{i}", [128, 512], BF16) for i in range(2)]
        for dc in range(16):
            for tq in range(2):
                mk.dma("sync", xnT[:, dc * 1024 + tq * 512:
                                   dc * 1024 + (tq + 1) * 512],
                       T["xn0_own"][tt0 + tq * 512:tt0 + (tq + 1) * 512,
                                    dc * 128:(dc + 1) * 128],
                       reads=_tm_tok("xn0_own", tt0 + tq * 512, 512), writes=[f"fxnT.{dc}.{tq}"],
                       transpose=True)
        for fg in range(16):
            for dc in range(16):
                mk.dma("sync", w1t[:, dc * 512:(dc + 1) * 512],
                       T["w1q"][fg * 512:(fg + 1) * 512,
                                dc * 128:(dc + 1) * 128],
                       reads=[f"w1q.{fg * 4 + i}" for i in range(4)],
                       writes=[f"fw1t.{dc}"], transpose=True)
            for half in range(2):
                for dc in range(16):
                    for fl in range(2):
                        for tq in range(2):
                            pb = half * 4 + fl * 2 + tq
                            fc = fg * 4 + half * 2 + fl
                            mk.op("tensor",
                                  (lambda e, pb=pb, dc=dc, fl=fl, tq=tq,
                                   half=half: e.matmul(
                                      mk.ps[pb][:],
                                      w1t[:, dc * 512 + (half * 2 + fl) * 128:
                                          dc * 512 + (half * 2 + fl + 1) * 128],
                                      xnT[:, dc * 1024 + tq * 512:
                                          dc * 1024 + (tq + 1) * 512],
                                      start=(dc == 0), stop=(dc == 15))),
                                  reads=[f"fw1t.{dc}", f"fxnT.{dc}.{tq}"],
                                  writes=[f"ps.{pb}"])
                for fl in range(2):
                    for tq in range(2):
                        pb = half * 4 + fl * 2 + tq
                        fc = fg * 4 + half * 2 + fl
                        mk.op("scalar",
                              (lambda e, pb=pb, fc=fc, tq=tq: e.activation(
                                  out=hbuf[:, fc * 1024 + tq * 512:
                                           fc * 1024 + (tq + 1) * 512],
                                  in_=mk.ps[pb][:], func=AF.Gelu,
                                  bias=b1t[:, fc:fc + 1])),
                              reads=[f"ps.{pb}", "b1t"],
                              writes=[f"fh.{fc}.{tq}"])
        for dg in range(4):
            for fc in range(64):
                sl = fc % 2
                mk.dma("sync", w2t[sl][:],
                       T["w2q"][dg * 512:(dg + 1) * 512,
                                fc * 128:(fc + 1) * 128],
                       reads=[f"w2q.{dg * 4 + i}" for i in range(4)],
                       writes=[f"fw2t{sl}"], transpose=True)
                for do4 in range(4):
                    for tq in range(2):
                        pb = do4 * 2 + tq
                        mk.op("tensor",
                              (lambda e, pb=pb, sl=sl, do4=do4, tq=tq, fc=fc:
                               e.matmul(
                                  mk.ps[pb][:],
                                  w2t[sl][:, do4 * 128:(do4 + 1) * 128],
                                  hbuf[:, fc * 1024 + tq * 512:
                                       fc * 1024 + (tq + 1) * 512],
                                  start=(fc == 0), stop=(fc == 63))),
                              reads=[f"fw2t{sl}", f"fh.{fc}.{tq}"],
                              writes=[f"ps.{pb}"])
            for do4 in range(4):
                for tq in range(2):
                    pb = do4 * 2 + tq
                    do = dg * 4 + do4
                    es = pb % 2
                    mk.op("vector",
                          (lambda e, pb=pb, do=do, es=es:
                           e.tensor_scalar_add(
                              fev[es][:], mk.ps[pb][:],
                              b2t[:, do:do + 1])),
                          reads=[f"ps.{pb}", "b2t"], writes=[f"fev{es}"])
                    mk.dma("sync",
                           T["f0T"][do * 128:(do + 1) * 128,
                                    tt0 + tq * 512:tt0 + (tq + 1) * 512],
                           fev[es][:], reads=[f"fev{es}"],
                           writes=[f"f0T.{do}.{th * 2 + tq}"])
        mk.sfree("fxnT", "fh", "fw1t", "fw2t0", "fw2t1", "fev0", "fev1")
    mk.sfree("b1t", "b2t")


def _ph_x1_ln1(mk, T, fused=False):
    sc = _alloc_ln_scratch(mk, "l1n")
    x0t = mk.sbuf("x0t", [128, D], F32)
    ab = mk.sbuf("x1a", [128, D], BF16)
    bb = mk.sbuf("x1b", [128, D], BF16)
    fb = mk.sbuf("x1f", [128, D], BF16)
    x1t = mk.sbuf("x1t", [128, D], F32)
    junk = mk.sbuf("x1junk", [128, D], BF16)
    ob = mk.sbuf("x1o", [128, D], BF16)
    for i in range(TOK // 128):
        i0 = i * 128
        mk.dma("sync", x0t[:], T["x0_own"][i0:i0 + 128, :],
               reads=["x0_own"], writes=["x0t"])
        for blk in range(4):
            if fused:
                mk.dma("sync", ab[:, blk * 512:(blk + 1) * 512],
                       T["ccA_out"][blk * 512:(blk + 1) * 512, i0:i0 + 128],
                       reads=["ccA_out"], writes=[f"x1a.{blk}"],
                       transpose=True)
            else:
                mk.dma("sync", ab[:, blk * 512:(blk + 1) * 512],
                       T["a0s"][blk * 512:(blk + 1) * 512, i0:i0 + 128],
                       reads=["a0s"], writes=[f"x1a.{blk}"], transpose=True)
                mk.dma("sync", bb[:, blk * 512:(blk + 1) * 512],
                       T["a0p"][blk * 512:(blk + 1) * 512, i0:i0 + 128],
                       reads=["a0p"], writes=[f"x1b.{blk}"], transpose=True)
            mk.dma("sync", fb[:, blk * 512:(blk + 1) * 512],
                   T["f0T"][blk * 512:(blk + 1) * 512, i0:i0 + 128],
                   reads=[f"f0T.{blk * 4 + q}.{i // 4}" for q in range(4)],
                   writes=[f"x1f.{blk}"], transpose=True)
        atoks = [f"x1a.{b}" for b in range(4)]
        btoks = [f"x1b.{b}" for b in range(4)]
        ftoks = [f"x1f.{b}" for b in range(4)]
        if fused:
            mk.op("vector", lambda e: e.tensor_add(x1t[:], ab[:], fb[:]),
                  reads=atoks + ftoks, writes=["x1t"])
        else:
            mk.op("vector", lambda e: e.tensor_add(x1t[:], ab[:], bb[:]),
                  reads=atoks + btoks, writes=["x1t"])
            mk.op("vector", lambda e: e.tensor_add(x1t[:], x1t[:], fb[:]),
                  reads=["x1t"] + ftoks, writes=["x1t"])
        mk.op("vector", lambda e: e.tensor_add(x1t[:], x1t[:], x0t[:]),
              reads=["x1t", "x0t"], writes=["x1t"])
        mk.dma("sync", T["x1_tm"][i0:i0 + 128, :], x1t[:],
               reads=["x1t"], writes=[f"x1_tm.{i}"])
        _ln_chain(mk, x1t, ["x1t"], D, junk, ob, "x1o", sc)
        mk.dma("sync", T["xn1_tm"][i0:i0 + 128, :], ob[:],
               reads=["x1o"], writes=[f"xn1_tm.{i}"])
    mk.sfree("x0t", "x1a", "x1b", "x1f", "x1t", "x1junk", "x1o",
             *[f"l1n{n}" for n in ("sum", "s2", "nmean", "ex2", "nvar",
                                   "lnv", "rstd", "b", "eps")])



def _ph_l1_qkv(mk, T, sp_bf16=False):
    xnT = mk.sbuf("c_xnT", [128, 16 * 2048], BF16)
    for dc in range(16):
        for tb in range(4):
            mk.dma("sync", xnT[:, dc * 2048 + tb * 512:
                               dc * 2048 + (tb + 1) * 512],
                   T["xn1_tm"][tb * 512:(tb + 1) * 512,
                               dc * 128:(dc + 1) * 128],
                   reads=_tm_tok("xn1_tm", tb * 512, 512),
                   writes=[f"c_xnT.{dc}.{tb}"], transpose=True)
    wv = mk.sbuf("c_wv", [128, 16 * 2048], BF16)
    ev = [mk.sbuf(f"c_ev{i}", [128, 512], BF16) for i in range(2)]
    for dc in range(16):
        mk.dma("gpsimd", wv[:, dc * 2048:(dc + 1) * 2048],
               T["Wv1"][dc * 128:(dc + 1) * 128, :],
               reads=["Wv1"], writes=[f"c_wv.{dc}"])
    for dvc in range(16):
        for tb in range(4):
            bank = (dvc * 4 + tb) % 2
            for dc in range(16):
                mk.op("tensor", (lambda e, bank=bank, dc=dc, dvc=dvc, tb=tb:
                      e.matmul(mk.ps[bank][:],
                               wv[:, dc * 2048 + dvc * 128:
                                  dc * 2048 + (dvc + 1) * 128],
                               xnT[:, dc * 2048 + tb * 512:
                                   dc * 2048 + (tb + 1) * 512],
                               start=(dc == 0), stop=(dc == 15))),
                      reads=[f"c_wv.{dc}", f"c_xnT.{dc}.{tb}"],
                      writes=[f"ps.{bank}"])
            mk.op("vector", (lambda e, bank=bank: e.tensor_copy(
                      ev[bank][:], mk.ps[bank][:])),
                  reads=[f"ps.{bank}"], writes=[f"c_ev{bank}"])
            mk.dma("sync", T["vT1"][dvc * 128:(dvc + 1) * 128,
                                    tb * 512:(tb + 1) * 512],
                   ev[bank][:], reads=[f"c_ev{bank}"],
                   writes=[f"vT1.{dvc}.{tb}"])
    mk.sfree("c_wv")
    wp = mk.sbuf("c_wp", [128, 16 * 512], BF16)
    for wname, dst in (("Wq1", "q1"), ("Wk1", "k1")):
        for dg in range(4):
            for dc in range(16):
                mk.dma("gpsimd", wp[:, dc * 512:(dc + 1) * 512],
                       T[wname][dc * 128:(dc + 1) * 128,
                                dg * 512:(dg + 1) * 512],
                       reads=[wname], writes=[f"c_wp.{dc}"])
            for tc in range(16):
                bank = tc % 2
                for dc in range(16):
                    mk.op("tensor", (lambda e, bank=bank, dc=dc, tc=tc:
                          e.matmul(mk.ps[bank][:],
                                   xnT[:, dc * 2048 + tc * 128:
                                       dc * 2048 + (tc + 1) * 128],
                                   wp[:, dc * 512:(dc + 1) * 512],
                                   start=(dc == 0), stop=(dc == 15))),
                          reads=[f"c_wp.{dc}", f"c_xnT.{dc}.{tc // 4}"],
                          writes=[f"ps.{bank}"])
                mk.op("vector", (lambda e, bank=bank: e.tensor_copy(
                          ev[bank][:], mk.ps[bank][:])),
                      reads=[f"ps.{bank}"], writes=[f"c_ev{bank}"])
                mk.dma("sync", T[dst][tc * 128:(tc + 1) * 128,
                                      dg * 512:(dg + 1) * 512],
                       ev[bank][:], reads=[f"c_ev{bank}"],
                       writes=[f"{dst}.{tc}.{dg}"])
    mk.sfree("c_xnT", "c_wp")
    sp_dt = BF16 if sp_bf16 else F32
    qbuf = mk.sbuf("c_qb", [128, 16 * 2048], BF16)
    for sc_ in range(16):
        mk.dma("sync", qbuf[:, sc_ * 2048:(sc_ + 1) * 2048],
               T["q1"][sc_ * 128:(sc_ + 1) * 128, :],
               reads=[f"q1.{sc_}.{g}" for g in range(4)],
               writes=[f"c_qb.{sc_}"])
    kbuf = mk.sbuf("c_kb", [128, 16 * 512], BF16)
    evf = [mk.sbuf(f"c_evf{i}", [128, 512], sp_dt) for i in range(2)]
    for fb in range(4):
        for sc_ in range(16):
            mk.dma("sync", kbuf[:, sc_ * 512:(sc_ + 1) * 512],
                   T["k1"][sc_ * 128:(sc_ + 1) * 128,
                           fb * 512:(fb + 1) * 512],
                   reads=[f"k1.{sc_}.{fb}"], writes=[f"c_kb.{sc_}"])
        for ec in range(16):
            bank = ec % 2
            for sc_ in range(16):
                mk.op("tensor", (lambda e, bank=bank, sc_=sc_, ec=ec:
                      e.matmul(mk.ps[bank][:],
                               qbuf[:, sc_ * 2048 + ec * 128:
                                    sc_ * 2048 + (ec + 1) * 128],
                               kbuf[:, sc_ * 512:(sc_ + 1) * 512],
                               start=(sc_ == 0), stop=(sc_ == 15))),
                      reads=[f"c_qb.{sc_}", f"c_kb.{sc_}"],
                      writes=[f"ps.{bank}"])
            mk.op("vector", (lambda e, bank=bank: e.tensor_copy(
                      evf[bank][:], mk.ps[bank][:])),
                  reads=[f"ps.{bank}"], writes=[f"c_evf{bank}"])
            mk.dma("sync", T["sp"][ec * 128:(ec + 1) * 128,
                                   fb * 512:(fb + 1) * 512],
                   evf[bank][:], reads=[f"c_evf{bank}"],
                   writes=[f"sp.{ec}.{fb}"])
    mk.sfree("c_qb", "c_kb", "c_ev0", "c_ev1", "c_evf0", "c_evf1")
    wdt = [mk.sbuf(f"c_wd{i}", [128, 2048], BF16) for i in range(2)]
    for ic in range(64):
        sl = ic % 2
        mk.dma("gpsimd", wdt[sl][:], T["wd"][ic * 128:(ic + 1) * 128, :],
               reads=["wd"], writes=[f"c_wd{sl}"])
        mk.dma("sync", T["wd_bf"][ic * 128:(ic + 1) * 128, :], wdt[sl][:],
               reads=[f"c_wd{sl}"], writes=[f"wd_bf.{ic}"])
    mk.sfree("c_wd0", "c_wd1")



def _ph_l1_softmax(mk, T, fused=False):
    sc = {}
    for n in ("mx", "nmx", "rs", "rr"):
        sc[n] = mk.sbuf(f"sm{n}", [128, 1], F32)
    spa = mk.sbuf("smspa", [128, D], BF16 if fused else F32)
    spb = mk.sbuf("smspb", [128, D], F32)
    Sb = mk.sbuf("smS", [128, D], F32)
    Pb = mk.sbuf("smP", [128, D], F32)
    Wb = mk.sbuf("smW", [128, D], BF16)
    for ec in range(16):
        e0 = ec * 128
        if fused:
            mk.dma("sync", spa[:], T["ccS_out"][e0:e0 + 128, :],
                   reads=["ccS_out"], writes=["smspa"])
            mk.op("vector", lambda e: e.tensor_copy(Sb[:], spa[:]),
                  reads=["smspa"], writes=["smS"])
        else:
            mk.dma("sync", spa[:], T["sp_s"][e0:e0 + 128, :],
                   reads=["sp_s"], writes=["smspa"])
            mk.dma("sync", spb[:], T["sp_p"][e0:e0 + 128, :],
                   reads=["sp_p"], writes=["smspb"])
            mk.op("vector", lambda e: e.tensor_add(Sb[:], spa[:], spb[:]),
                  reads=["smspa", "smspb"], writes=["smS"])
        mk.op("vector", lambda e: e.reduce_max(sc["mx"][:], Sb[:], axis=AX.X),
              reads=["smS"], writes=["smmx"])
        mk.op("vector", lambda e: e.tensor_scalar_mul(sc["nmx"][:],
              sc["mx"][:], -0.125), reads=["smmx"], writes=["smnmx"])
        mk.op("scalar", lambda e: e.activation(out=Pb[:], in_=Sb[:],
              func=AF.Exp, bias=sc["nmx"][:, 0:1], scale=0.125,
              accum_out=sc["rs"][:, 0:1]),
              reads=["smS", "smnmx"], writes=["smP", "smrs"])
        mk.op("vector", lambda e: e.reciprocal(sc["rr"][:], sc["rs"][:]),
              reads=["smrs"], writes=["smrr"])
        mk.op("vector", lambda e: e.tensor_scalar_mul(Wb[:], Pb[:],
              sc["rr"][:, 0:1]), reads=["smP", "smrr"], writes=["smW"])
        mk.dma("sync", T["wbuf"][e0:e0 + 128, :], Wb[:],
               reads=["smW"], writes=[f"wbuf.{ec}"])
    mk.sfree("smmx", "smnmx", "smrs", "smrr", "smspa", "smspb", "smS",
             "smP", "smW")


def _ph_l1_av_wo(mk, T):
    wT = mk.sbuf("d_wT", [128, 16 * 2048], BF16)
    for fc in range(16):
        for eb in range(4):
            mk.dma("sync", wT[:, fc * 2048 + eb * 512:
                              fc * 2048 + (eb + 1) * 512],
                   T["wbuf"][eb * 512:(eb + 1) * 512,
                             fc * 128:(fc + 1) * 128],
                   reads=[f"wbuf.{eb * 4 + i}" for i in range(4)],
                   writes=[f"d_wT.{fc}.{eb}"], transpose=True)
    wo = mk.sbuf("d_wo", [128, 16 * 2048], BF16)
    for ec in range(16):
        mk.dma("gpsimd", wo[:, ec * 2048:(ec + 1) * 2048],
               T["Wo1"][ec * 128:(ec + 1) * 128, :],
               reads=["Wo1"], writes=[f"d_wo.{ec}"])
    vb = mk.sbuf("d_vb", [128, 16 * 512], BF16)
    ao = mk.sbuf("d_ao", [128, 16 * 512], BF16)
    ev = [mk.sbuf(f"d_ev{i}", [128, 512], BF16) for i in range(2)]
    for sb in range(4):
        s0 = sb * 512
        for fc in range(16):
            mk.dma("sync", vb[:, fc * 512:(fc + 1) * 512],
                   T["vT1"][fc * 128:(fc + 1) * 128, s0:s0 + 512],
                   reads=[f"vT1.{fc}.{sb}"], writes=[f"d_vb.{fc}"])
        for ec in range(16):
            bank = ec % 2
            for fc in range(16):
                mk.op("tensor", (lambda e, bank=bank, fc=fc, ec=ec:
                      e.matmul(mk.ps[bank][:],
                               wT[:, fc * 2048 + ec * 128:
                                  fc * 2048 + (ec + 1) * 128],
                               vb[:, fc * 512:(fc + 1) * 512],
                               start=(fc == 0), stop=(fc == 15))),
                      reads=[f"d_wT.{fc}.{ec // 4}", f"d_vb.{fc}"],
                      writes=[f"ps.{bank}"])
            mk.op("vector", (lambda e, bank=bank, ec=ec: e.tensor_copy(
                      ao[:, ec * 512:(ec + 1) * 512], mk.ps[bank][:])),
                  reads=[f"ps.{bank}"], writes=[f"d_ao.{ec}"])
        for do in range(16):
            bank = 2 + do % 2
            for ec in range(16):
                mk.op("tensor", (lambda e, bank=bank, ec=ec, do=do:
                      e.matmul(mk.ps[bank][:],
                               wo[:, ec * 2048 + do * 128:
                                  ec * 2048 + (do + 1) * 128],
                               ao[:, ec * 512:(ec + 1) * 512],
                               start=(ec == 0), stop=(ec == 15))),
                      reads=[f"d_wo.{ec}", f"d_ao.{ec}"],
                      writes=[f"ps.{bank}"])
            es = do % 2
            mk.op("vector", (lambda e, bank=bank, es=es: e.tensor_copy(
                      ev[es][:], mk.ps[bank][:])),
                  reads=[f"ps.{bank}"], writes=[f"d_ev{es}"])
            mk.dma("sync", T["a1T"][do * 128:(do + 1) * 128, s0:s0 + 512],
                   ev[es][:], reads=[f"d_ev{es}"],
                   writes=[f"a1T.{do}.{sb}"])
    mk.sfree("d_wT", "d_wo", "d_vb", "d_ao", "d_ev0", "d_ev1")


def _ph_l1_gateup(mk, T):
    xnT = mk.sbuf("g_xnT", [128, 16 * 2048], BF16)
    for dc in range(16):
        for tb in range(4):
            mk.dma("sync", xnT[:, dc * 2048 + tb * 512:
                               dc * 2048 + (tb + 1) * 512],
                   T["xn1_tm"][tb * 512:(tb + 1) * 512,
                               dc * 128:(dc + 1) * 128],
                   reads=_tm_tok("xn1_tm", tb * 512, 512), writes=[f"g_xnT.{dc}.{tb}"],
                   transpose=True)
    wg = mk.sbuf("g_wg", [128, 16 * 512], BF16)
    wu = mk.sbuf("g_wu", [128, 16 * 512], BF16)
    sg = mk.sbuf("g_sg", [128, 512], F32)
    hb = mk.sbuf("g_h", [128, 512], BF16)
    habs = mk.sbuf("g_habs", [128, 512], F32)
    q4 = [mk.sbuf(f"g_q{i}", [128, 128], F32) for i in range(5)]
    m4 = [mk.sbuf(f"g_m{i}", [128, 128], BF16) for i in range(4)]
    hs = [mk.sbuf(f"g_hs{i}", [128, 512], BF16) for i in range(2)]
    for ig in range(16):
        i0 = ig * 512
        for dc in range(16):
            mk.dma("gpsimd", wg[:, dc * 512:(dc + 1) * 512],
                   T["wg"][dc * 128:(dc + 1) * 128, i0:i0 + 512],
                   reads=["wg"], writes=[f"g_wg.{dc}"])
            mk.dma("gpsimd", wu[:, dc * 512:(dc + 1) * 512],
                   T["wu"][dc * 128:(dc + 1) * 128, i0:i0 + 512],
                   reads=["wu"], writes=[f"g_wu.{dc}"])
        for tc in range(16):
            for dc in range(16):
                mk.op("tensor", (lambda e, dc=dc, tc=tc: e.matmul(
                          mk.ps[0][:],
                          xnT[:, dc * 2048 + tc * 128:
                              dc * 2048 + (tc + 1) * 128],
                          wg[:, dc * 512:(dc + 1) * 512],
                          start=(dc == 0), stop=(dc == 15))),
                      reads=[f"g_xnT.{dc}.{tc // 4}", f"g_wg.{dc}"],
                      writes=["ps.0"])
                mk.op("tensor", (lambda e, dc=dc, tc=tc: e.matmul(
                          mk.ps[1][:],
                          xnT[:, dc * 2048 + tc * 128:
                              dc * 2048 + (tc + 1) * 128],
                          wu[:, dc * 512:(dc + 1) * 512],
                          start=(dc == 0), stop=(dc == 15))),
                      reads=[f"g_xnT.{dc}.{tc // 4}", f"g_wu.{dc}"],
                      writes=["ps.1"])
            mk.op("scalar", lambda e: e.activation(out=sg[:], in_=mk.ps[0][:],
                  func=AF.Silu), reads=["ps.0"], writes=["g_sg"])
            mk.op("vector", lambda e: e.tensor_mul(hb[:], sg[:],
                  mk.ps[1][:]), reads=["g_sg", "ps.1"], writes=["g_h"])
            mk.op("scalar", lambda e: e.activation(out=habs[:], in_=hb[:],
                  func=AF.Abs), reads=["g_h"], writes=["g_habs"])
            mk.op("vector", lambda e: e.tensor_max(q4[0][:], habs[:, 0::4],
                  habs[:, 1::4]), reads=["g_habs"], writes=["g_q0"])
            mk.op("vector", lambda e: e.tensor_tensor(q4[1][:],
                  habs[:, 0::4], habs[:, 1::4], ALU.min),
                  reads=["g_habs"], writes=["g_q1"])
            mk.op("vector", lambda e: e.tensor_max(q4[2][:], habs[:, 2::4],
                  habs[:, 3::4]), reads=["g_habs"], writes=["g_q2"])
            mk.op("vector", lambda e: e.tensor_tensor(q4[3][:],
                  habs[:, 2::4], habs[:, 3::4], ALU.min),
                  reads=["g_habs"], writes=["g_q3"])
            mk.op("vector", lambda e: e.tensor_tensor(q4[0][:], q4[0][:],
                  q4[2][:], ALU.min), reads=["g_q0", "g_q2"],
                  writes=["g_q0"])
            mk.op("vector", lambda e: e.tensor_max(q4[1][:], q4[1][:],
                  q4[3][:]), reads=["g_q1", "g_q3"], writes=["g_q1"])
            mk.op("vector", lambda e: e.tensor_max(q4[4][:], q4[0][:],
                  q4[1][:]), reads=["g_q0", "g_q1"], writes=["g_q4"])
            sl = tc % 2
            for jj in range(4):
                mk.op("vector", (lambda e, jj=jj: e.tensor_tensor(
                          m4[jj][:], habs[:, jj::4], q4[4][:], ALU.is_ge)),
                      reads=["g_habs", "g_q4"], writes=[f"g_m{jj}"])
            for jj in range(4):
                mk.op("gpsimd", (lambda e, jj=jj, sl=sl: e.tensor_mul(
                          hs[sl][:, jj::4], hb[:, jj::4], m4[jj][:])),
                      reads=["g_h", f"g_m{jj}"], writes=[f"g_hs{sl}"])
            mk.dma("sync", T["hsp"][tc * 128:(tc + 1) * 128, i0:i0 + 512],
                   hs[sl][:], reads=[f"g_hs{sl}"],
                   writes=[f"hsp.{tc}.{ig}"])
    mk.sfree("g_xnT", "g_wg", "g_wu", "g_sg", "g_h", "g_habs",
             "g_q0", "g_q1", "g_q2", "g_q3", "g_q4", "g_m0", "g_m1", "g_m2", "g_m3", "g_hs0", "g_hs1")


def _ph_l1_down_out(mk, T):
    for th in range(2):
        tt0 = th * 1024
        hT = mk.sbuf("o_hT", [128, 64 * 1024], BF16)
        for ic in range(64):
            for tq in range(2):
                mk.dma("sync", hT[:, ic * 1024 + tq * 512:
                                  ic * 1024 + (tq + 1) * 512],
                       T["hsp"][tt0 + tq * 512:tt0 + (tq + 1) * 512,
                                ic * 128:(ic + 1) * 128],
                       reads=[f"hsp.{(tt0 + tq * 512) // 128 + q}.{ic // 4}"
                              for q in range(4)],
                       writes=[f"o_hT.{ic}.{tq}"], transpose=True)
        wdc = mk.sbuf("o_wdc", [128, 64 * 128], BF16)
        a1b = [mk.sbuf(f"o_a1{i}", [128, 512], BF16) for i in range(2)]
        yev = [mk.sbuf(f"o_y{i}", [128, 512], BF16) for i in range(2)]
        for dc in range(16):
            for g8 in range(8):
                src = T["wd_bf"][g8 * 1024:(g8 + 1) * 1024,
                                 dc * 128:(dc + 1) * 128]
                src = src.rearrange("(g r) c -> r g c", g=8)
                mk.dma("sync", wdc[:, g8 * 1024:(g8 + 1) * 1024], src,
                       reads=[f"wd_bf.{g8 * 8 + i}" for i in range(8)],
                       writes=[f"o_wdc.{g8}"])
            for tq in range(2):
                bank = tq
                for ic in range(64):
                    mk.op("tensor", (lambda e, bank=bank, ic=ic, tq=tq:
                          e.matmul(mk.ps[bank][:],
                                   wdc[:, ic * 128:(ic + 1) * 128],
                                   hT[:, ic * 1024 + tq * 512:
                                      ic * 1024 + (tq + 1) * 512],
                                   start=(ic == 0), stop=(ic == 63))),
                          reads=[f"o_wdc.{ic // 8}", f"o_hT.{ic}.{tq}"],
                          writes=[f"ps.{bank}"])
                mk.dma("sync", a1b[tq][:],
                       T["a1T"][dc * 128:(dc + 1) * 128,
                                tt0 + tq * 512:tt0 + (tq + 1) * 512],
                       reads=[f"a1T.{dc}.{th * 2 + tq}"],
                       writes=[f"o_a1{tq}"])
                mk.op("vector", (lambda e, bank=bank, tq=tq: e.tensor_add(
                          yev[tq][:], mk.ps[bank][:], a1b[tq][:])),
                      reads=[f"ps.{bank}", f"o_a1{tq}"], writes=[f"o_y{tq}"])
                mk.dma("sync",
                       T["y1T"][dc * 128:(dc + 1) * 128,
                                tt0 + tq * 512:tt0 + (tq + 1) * 512],
                       yev[tq][:], reads=[f"o_y{tq}"],
                       writes=[f"y1T.{dc}.{th * 2 + tq}"])
        mk.sfree("o_hT", "o_wdc", "o_a10", "o_a11", "o_y0", "o_y1")
    x1t = mk.sbuf("o_x1", [128, D], F32)
    y1b = mk.sbuf("o_y1b", [128, D], BF16)
    outt = mk.sbuf("o_out", [128, D], F32)
    for i in range(TOK // 128):
        i0 = i * 128
        mk.dma("sync", x1t[:], T["x1_tm"][i0:i0 + 128, :],
               reads=[f"x1_tm.{i}"], writes=["o_x1"])
        for blk in range(4):
            mk.dma("sync", y1b[:, blk * 512:(blk + 1) * 512],
                   T["y1T"][blk * 512:(blk + 1) * 512, i0:i0 + 128],
                   reads=[f"y1T.{blk * 4 + q}.{i // 4}" for q in range(4)],
                   writes=[f"o_y1b.{blk}"], transpose=True)
        mk.op("vector", lambda e: e.tensor_add(outt[:], x1t[:], y1b[:]),
              reads=["o_x1"] + [f"o_y1b.{b}" for b in range(4)],
              writes=["o_out"])
        mk.dma("sync", T["OUT"][i0:i0 + 128, :], outt[:],
               reads=["o_out"], writes=[f"OUT.{i}"])
    mk.sfree("o_x1", "o_y1b", "o_out")



def _dram(nc, specs):
    T = {}
    for name, shape, dt, kind in specs:
        T[name] = nc.dram_tensor(name, shape, dt, kind=kind).ap()
    return T


def _build_A():
    nc = bass.Bass("TRN2", target_bir_lowering=False)
    T = _dram(nc, [
        ("x0", [S, D], F32, "ExternalInput"),
        ("Wq_h", [D, 1024], F32, "ExternalInput"),
        ("Wqr_h", [D, 512], F32, "ExternalInput"),
        ("Wkvd", [D, DKV], F32, "ExternalInput"),
        ("Wkvup_h", [DKV, 2048], F32, "ExternalInput"),
        ("Wkr", [D, RD], F32, "ExternalInput"),
        ("Wo_h", [1024, D], F32, "ExternalInput"),
        ("ropeC", [128, S], F32, "ExternalInput"),
        ("ropeS", [128, S], F32, "ExternalInput"),
        ("masks", [4, 128, 512], BF16, "ExternalInput"),
        ("xn0_tm", [S, D], BF16, "ExternalOutput"),
        ("a0pT", [D, S], BF16, "ExternalOutput"),
        ("qcT", [1024, S], BF16, "Internal"),
        ("qrT", [512, S], BF16, "Internal"),
        ("qrraw", [512, S], F32, "Internal"),
        ("krraw", [64, S], F32, "Internal"),
        ("kcT", [1024, S], BF16, "Internal"),
        ("v_tm", [S, 1024], BF16, "Internal"),
        ("onormT", [1024, S], BF16, "Internal"),
    ])
    mk = MK(nc)
    _ph_ln(mk, T, "x0", "xn0_tm", S, "ln0")
    _ph_l0_proj(mk, T)
    _ph_l0_attn(mk, T)
    _ph_l0_wo(mk, T)
    mk.finalize()
    return nc


def _build_B():
    nc = bass.Bass("TRN2", target_bir_lowering=False)
    T = _dram(nc, [
        ("x0_own", [TOK, D], F32, "ExternalInput"),
        ("xn0_own", [TOK, D], BF16, "ExternalInput"),
        ("w1", [DFF, D], F32, "ExternalInput"),
        ("w2", [D, DFF], F32, "ExternalInput"),
        ("b1t", [128, 64], F32, "ExternalInput"),
        ("b2t", [128, 16], F32, "ExternalInput"),
        ("a0s", [D, TOK], BF16, "ExternalInput"),
        ("a0p", [D, TOK], BF16, "ExternalInput"),
        ("x1_tm", [TOK, D], F32, "ExternalOutput"),
        ("xn1_tm", [TOK, D], BF16, "ExternalOutput"),
        ("w1q", [DFF, D], BF16, "Internal"),
        ("w2q", [D, DFF], BF16, "Internal"),
        ("f0T", [D, TOK], BF16, "Internal"),
    ])
    mk = MK(nc)
    _ph_quant(mk, T, "w1", "w1q", 64, 2048)
    _ph_quant(mk, T, "w2", "w2q", 16, 8192)
    _ph_l0_ffn(mk, T)
    _ph_x1_ln1(mk, T)
    mk.finalize()
    return nc


def _build_C():
    nc = bass.Bass("TRN2", target_bir_lowering=False)
    T = _dram(nc, [
        ("xn1_tm", [TOK, D], BF16, "ExternalInput"),
        ("Wq1", [D, D], F32, "ExternalInput"),
        ("Wk1", [D, D], F32, "ExternalInput"),
        ("Wv1", [D, D], F32, "ExternalInput"),
        ("wd", [DFF, D], F32, "ExternalInput"),
        ("sp", [D, D], F32, "ExternalOutput"),
        ("vT1", [D, TOK], BF16, "ExternalOutput"),
        ("wd_bf", [DFF, D], BF16, "ExternalOutput"),
        ("q1", [TOK, D], BF16, "Internal"),
        ("k1", [TOK, D], BF16, "Internal"),
    ])
    mk = MK(nc)
    _ph_l1_qkv(mk, T)
    mk.finalize()
    return nc


def _build_D():
    nc = bass.Bass("TRN2", target_bir_lowering=False)
    T = _dram(nc, [
        ("sp_s", [D, D], F32, "ExternalInput"),
        ("sp_p", [D, D], F32, "ExternalInput"),
        ("vT1", [D, TOK], BF16, "ExternalInput"),
        ("Wo1", [D, D], F32, "ExternalInput"),
        ("xn1_tm", [TOK, D], BF16, "ExternalInput"),
        ("wg", [D, DFF], F32, "ExternalInput"),
        ("wu", [D, DFF], F32, "ExternalInput"),
        ("wd_bf", [DFF, D], BF16, "ExternalInput"),
        ("x1_tm", [TOK, D], F32, "ExternalInput"),
        ("OUT", [TOK, D], F32, "ExternalOutput"),
        ("wbuf", [D, D], BF16, "Internal"),
        ("a1T", [D, TOK], BF16, "Internal"),
        ("hsp", [TOK, DFF], BF16, "Internal"),
        ("y1T", [D, TOK], BF16, "Internal"),
    ])
    mk = MK(nc)
    _ph_l1_softmax(mk, T)
    _ph_l1_av_wo(mk, T)
    _ph_l1_gateup(mk, T)
    _ph_l1_down_out(mk, T)
    mk.finalize()
    return nc




_RG = [[0, 1], [2, 3], [4, 5], [6, 7]]


def _build_fused():
    nc = bass.Bass("TRN2", target_bir_lowering=False)
    T = _dram(nc, [
        ("x0", [S, D], F32, "ExternalInput"),
        ("x0_own", [TOK, D], F32, "ExternalInput"),
        ("Wq_h", [D, 1024], F32, "ExternalInput"),
        ("Wqr_h", [D, 512], F32, "ExternalInput"),
        ("Wkvd", [D, DKV], F32, "ExternalInput"),
        ("Wkvup_h", [DKV, 2048], F32, "ExternalInput"),
        ("Wkr", [D, RD], F32, "ExternalInput"),
        ("Wo_h", [1024, D], F32, "ExternalInput"),
        ("ropeC", [128, S], F32, "ExternalInput"),
        ("ropeS", [128, S], F32, "ExternalInput"),
        ("masks", [4, 128, 512], BF16, "ExternalInput"),
        ("w1", [DFF, D], F32, "ExternalInput"),
        ("w2", [D, DFF], F32, "ExternalInput"),
        ("b1t", [128, 64], F32, "ExternalInput"),
        ("b2t", [128, 16], F32, "ExternalInput"),
        ("Wq1", [D, D], F32, "ExternalInput"),
        ("Wk1", [D, D], F32, "ExternalInput"),
        ("Wv1", [D, D], F32, "ExternalInput"),
        ("Wo1", [D, D], F32, "ExternalInput"),
        ("wg", [D, DFF], F32, "ExternalInput"),
        ("wu", [D, DFF], F32, "ExternalInput"),
        ("wd", [DFF, D], F32, "ExternalInput"),
        ("OUT", [TOK, D], F32, "ExternalOutput"),
        ("xn0_tm", [S, D], BF16, "Internal"),
        ("xn0_own", [TOK, D], BF16, "Internal"),
        ("qcT", [1024, S], BF16, "Internal"),
        ("qrT", [512, S], BF16, "Internal"),
        ("qrraw", [512, S], F32, "Internal"),
        ("krraw", [64, S], F32, "Internal"),
        ("kcT", [1024, S], BF16, "Internal"),
        ("v_tm", [S, 1024], BF16, "Internal"),
        ("onormT", [1024, S], BF16, "Internal"),
        ("ccA_in", [2 * D, TOK], BF16, "Internal"),
        ("ccA_out", [D, TOK], BF16, "Internal"),
        ("w1q", [DFF, D], BF16, "Internal"),
        ("w2q", [D, DFF], BF16, "Internal"),
        ("f0T", [D, TOK], BF16, "Internal"),
        ("x1_tm", [TOK, D], F32, "Internal"),
        ("xn1_tm", [TOK, D], BF16, "Internal"),
        ("q1", [TOK, D], BF16, "Internal"),
        ("k1", [TOK, D], BF16, "Internal"),
        ("vT1", [D, TOK], BF16, "Internal"),
        ("sp", [D, D], BF16, "Internal"),
        ("ccS_out", [D, D], BF16, "Internal"),
        ("wd_bf", [DFF, D], BF16, "Internal"),
        ("wbuf", [D, D], BF16, "Internal"),
        ("a1T", [D, TOK], BF16, "Internal"),
        ("hsp", [TOK, DFF], BF16, "Internal"),
        ("y1T", [D, TOK], BF16, "Internal"),
    ])
    mk = MK(nc)
    _ph_ln(mk, T, "x0", "xn0_tm", S, "ln0")
    _ph_ln(mk, T, "x0_own", "xn0_own", TOK, "lnb")
    _ph_l0_proj(mk, T)
    _ph_quant_p1(mk, T, "w1", 64, 2048)
    _ph_quant_p1(mk, T, "w2", 16, 8192, nb=1)
    work = [("w1", "w1q", i) for i in range(64)] + \
           [("w2", "w2q", i) for i in range(16)]

    def filler(h):
        for _ in range(10):
            if work:
                wn, qn, i = work.pop(0)
                _quant_p2_tile(mk, T, wn, qn, i)

    _ph_l0_attn(mk, T, filler=filler)
    while work:
        wn, qn, i = work.pop(0)
        _quant_p2_tile(mk, T, wn, qn, i)
    _ph_quant_free(mk, T, "w1")
    _ph_quant_free(mk, T, "w2")
    _ph_l0_wo(mk, T, cc=True)
    mk.op("gpsimd", lambda e: e.collective_compute(
              "ReduceScatter", ALU.add, replica_groups=_RG,
              ins=[T["ccA_in"][:, :]], outs=[T["ccA_out"][:, :]]),
          reads=[f"ccA_in.{do}.{tb}" for do in range(16) for tb in range(8)],
          writes=["ccA_out"])
    _ph_l0_ffn(mk, T)
    _ph_x1_ln1(mk, T, fused=True)
    _ph_l1_qkv(mk, T, sp_bf16=True)
    mk.op("gpsimd", lambda e: e.collective_compute(
              "AllReduce", ALU.add, replica_groups=_RG,
              ins=[T["sp"][:, :]], outs=[T["ccS_out"][:, :]]),
          reads=[f"sp.{ec}.{fb}" for ec in range(16) for fb in range(4)],
          writes=["ccS_out"])
    _ph_l1_gateup(mk, T)
    _ph_l1_softmax(mk, T, fused=True)
    _ph_l1_av_wo(mk, T)
    _ph_l1_down_out(mk, T)
    mk.finalize()
    return nc



_FUSED = True

def kernel(**inp):
    if _FUSED:
        return _kernel_fused(**inp)
    return _kernel_unfused(**inp)


def _common_host(inp):
    f32 = np.float32

    def np32(name):
        return np.ascontiguousarray(np.asarray(inp[name], dtype=f32))

    W = {k: np32(k) for k in inp}
    b1 = W["L0_ffn_b1"]
    b2 = W["L0_ffn_b2"]
    W["b1t"] = np.ascontiguousarray(b1.reshape(64, 128).T)
    W["b2t"] = np.ascontiguousarray(b2.reshape(16, 128).T)
    return W


def _kernel_fused(**inp):
    ropeC, ropeS, masks = _host_tables()
    W = _common_host(inp)
    x0 = W["L0_attn_x"]
    cores = list(range(8))
    nc = _build_fused()
    in_maps = []
    for c in cores:
        b, j = c // 2, c % 2
        own = slice(j * TOK, (j + 1) * TOK)
        Wkvu = W["L0_attn_Wkv_up"]
        in_maps.append({
            "x0": x0[b],
            "x0_own": np.ascontiguousarray(x0[b][own]),
            "Wq_h": np.ascontiguousarray(
                W["L0_attn_Wq"][:, j * 1024:(j + 1) * 1024]),
            "Wqr_h": np.ascontiguousarray(
                W["L0_attn_Wq_rope"][:, j * 512:(j + 1) * 512]),
            "Wkvd": W["L0_attn_Wkv_down"],
            "Wkvup_h": np.ascontiguousarray(np.concatenate(
                [Wkvu[:, j * 1024:(j + 1) * 1024],
                 Wkvu[:, 2048 + j * 1024:2048 + (j + 1) * 1024]], axis=1)),
            "Wkr": W["L0_attn_Wk_rope"],
            "Wo_h": np.ascontiguousarray(
                W["L0_attn_Wo"][j * 1024:(j + 1) * 1024, :]),
            "ropeC": ropeC, "ropeS": ropeS, "masks": masks,
            "w1": W["L0_ffn_w1"], "w2": W["L0_ffn_w2"],
            "b1t": W["b1t"], "b2t": W["b2t"],
            "Wq1": W["L1_attn_Wq"], "Wk1": W["L1_attn_Wk"],
            "Wv1": W["L1_attn_Wv"], "Wo1": W["L1_attn_Wo"],
            "wg": W["L1_ffn_w_gate"], "wu": W["L1_ffn_w_up"],
            "wd": W["L1_ffn_w_down"],
        })
    res = run_bass_kernel_spmd(nc, in_maps, core_ids=cores).results
    out = np.empty((B, S, D), np.float32)
    for c in cores:
        b, j = c // 2, c % 2
        out[b, j * TOK:(j + 1) * TOK, :] = res[c]["OUT"]
    return out


def _kernel_unfused(**inp):
    f32 = np.float32
    ropeC, ropeS, masks = _host_tables()
    cores = list(range(8))

    def np32(name):
        return np.ascontiguousarray(np.asarray(inp[name], dtype=f32))

    x0 = np32("L0_attn_x")
    Wq, Wqr = np32("L0_attn_Wq"), np32("L0_attn_Wq_rope")
    Wkvd, Wkvu = np32("L0_attn_Wkv_down"), np32("L0_attn_Wkv_up")
    Wkr, Wo = np32("L0_attn_Wk_rope"), np32("L0_attn_Wo")
    w1, b1 = np32("L0_ffn_w1"), np32("L0_ffn_b1")
    w2, b2 = np32("L0_ffn_w2"), np32("L0_ffn_b2")
    Wq1, Wk1 = np32("L1_attn_Wq"), np32("L1_attn_Wk")
    Wv1, Wo1 = np32("L1_attn_Wv"), np32("L1_attn_Wo")
    wg, wu, wd = np32("L1_ffn_w_gate"), np32("L1_ffn_w_up"), \
        np32("L1_ffn_w_down")
    b1t = np.ascontiguousarray(b1.reshape(64, 128).T)
    b2t = np.ascontiguousarray(b2.reshape(16, 128).T)

    ncA = _build_A()
    inA = []
    for c in cores:
        b, j = c // 2, c % 2
        inA.append({
            "x0": x0[b],
            "Wq_h": np.ascontiguousarray(Wq[:, j * 1024:(j + 1) * 1024]),
            "Wqr_h": np.ascontiguousarray(Wqr[:, j * 512:(j + 1) * 512]),
            "Wkvd": Wkvd,
            "Wkvup_h": np.ascontiguousarray(np.concatenate(
                [Wkvu[:, j * 1024:(j + 1) * 1024],
                 Wkvu[:, 2048 + j * 1024:2048 + (j + 1) * 1024]], axis=1)),
            "Wkr": Wkr,
            "Wo_h": np.ascontiguousarray(Wo[j * 1024:(j + 1) * 1024, :]),
            "ropeC": ropeC, "ropeS": ropeS, "masks": masks,
        })
    rA = run_bass_kernel_spmd(ncA, inA, core_ids=cores).results

    ncB = _build_B()
    inB = []
    for c in cores:
        b, j = c // 2, c % 2
        own = slice(j * TOK, (j + 1) * TOK)
        inB.append({
            "x0_own": np.ascontiguousarray(x0[b][own]),
            "xn0_own": np.ascontiguousarray(rA[c]["xn0_tm"][own]),
            "w1": w1, "w2": w2, "b1t": b1t, "b2t": b2t,
            "a0s": np.ascontiguousarray(rA[c]["a0pT"][:, own]),
            "a0p": np.ascontiguousarray(rA[c ^ 1]["a0pT"][:, own]),
        })
    rB = run_bass_kernel_spmd(ncB, inB, core_ids=cores).results

    ncC = _build_C()
    inC = [{"xn1_tm": rB[c]["xn1_tm"], "Wq1": Wq1, "Wk1": Wk1, "Wv1": Wv1,
            "wd": wd} for c in cores]
    rC = run_bass_kernel_spmd(ncC, inC, core_ids=cores).results

    ncD = _build_D()
    inD = [{"sp_s": rC[c]["sp"], "sp_p": rC[c ^ 1]["sp"],
            "vT1": rC[c]["vT1"], "Wo1": Wo1, "xn1_tm": rB[c]["xn1_tm"],
            "wg": wg, "wu": wu, "wd_bf": rC[c]["wd_bf"],
            "x1_tm": rB[c]["x1_tm"]} for c in cores]
    rD = run_bass_kernel_spmd(ncD, inD, core_ids=cores).results

    out = np.empty((B, S, D), np.float32)
    for c in cores:
        b, j = c // 2, c % 2
        out[b, j * TOK:(j + 1) * TOK, :] = rD[c]["OUT"]
    return out


# version 05
# speedup vs baseline: 1.2161x; 1.2161x over previous
import math
import numpy as np
import ml_dtypes
import concourse.bass as bass
import concourse.mybir as mybir
from concourse.bass_utils import run_bass_kernel_spmd

F32 = mybir.dt.float32
BF16 = mybir.dt.bfloat16
AF = mybir.ActivationFunctionType
ALU = mybir.AluOpType
AX = mybir.AxisListType

B, S, D = 4, 4096, 2048
H, HD, RD = 16, 128, 64
DKV = 512
DFF = 8192
TOK = S // 2
SCALE0 = 1.0 / math.sqrt(HD + RD)

BF = ml_dtypes.bfloat16



class _Op:
    __slots__ = ("stream", "emit", "deps", "need_inc", "is_dma", "dma_use",
                 "dma_sem", "count", "tag")

    def __init__(self, stream, emit, is_dma, tag):
        self.stream = stream
        self.emit = emit
        self.deps = []
        self.need_inc = False
        self.is_dma = is_dma
        self.dma_use = 0
        self.dma_sem = None
        self.count = 0
        self.tag = tag


_ENGS = ("tensor", "vector", "scalar", "gpsimd", "sync")
_DMA_POOL = 16


class MK:
    def __init__(self, nc):
        self.nc = nc
        self.streams = {e: [] for e in _ENGS}
        self.comp_sem = {e: nc.alloc_semaphore(f"cs_{e}") for e in _ENGS}
        self.dma_sems = {}
        self.dma_ops = {}
        self.tok_w = {}
        self.tok_r = {}
        self.pending = []
        base = (nc.sbuf_base + 63) & ~63
        self.sb_free = [[base, nc.sbuf_top]]
        self.sb_allocs = {}
        self.ps = [nc.alloc_psum_tensor(f"ps{i}", [128, 512], F32).ap()
                   for i in range(8)]
        self._uid = 0

    def sbuf(self, name, shape, dtype):
        per_part = int(np.prod(shape[1:])) * (2 if dtype == BF16 else 4)
        size = (per_part + 63) & ~63
        for seg in self.sb_free:
            if seg[1] - seg[0] >= size:
                off = seg[0]
                seg[0] += size
                break
        else:
            raise MemoryError(f"sbuf full allocating {name} ({size}B/part)")
        self._uid += 1
        h = self.nc.alloc_sbuf_tensor_at(f"{name}_{self._uid}", shape, dtype,
                                         offset=off)
        ap = h.ap()
        self.sb_allocs[name] = [off, size, ap, list(self.pending)]
        return ap

    def sfree(self, *names):
        pend = list(self.pending)
        for name in names:
            off, size, _, _ = self.sb_allocs.pop(name)
            self.sb_free.append([off, off + size])
            pref = name + "."
            for t in list(self.tok_w):
                if t == name or t.startswith(pref):
                    pend.append(self.tok_w.pop(t))
            for t in list(self.tok_r):
                if t == name or t.startswith(pref):
                    pend.extend(self.tok_r.pop(t))
        self.sb_free.sort()
        merged = []
        for seg in self.sb_free:
            if merged and merged[-1][1] >= seg[0]:
                merged[-1][1] = max(merged[-1][1], seg[1])
            elif seg[1] > seg[0]:
                merged.append(seg)
        self.sb_free = merged
        self.pending = pend

    def barrier(self):
        self.pending = [s[-1] for s in self.streams.values() if s]

    def _record(self, op, reads, writes):
        deps = []
        for t in reads:
            w = self.tok_w.get(t)
            if w is not None:
                deps.append(w)
        for t in writes:
            w = self.tok_w.get(t)
            if w is not None:
                deps.append(w)
            deps.extend(self.tok_r.get(t, ()))
            al = self.sb_allocs.get(t.split(".", 1)[0])
            if al is not None and al[3]:
                deps.extend(al[3])
        for d in deps:
            if d is op:
                continue
            if (d.stream == op.stream and not d.is_dma and not op.is_dma
                    and op.stream == "tensor"):
                continue
            op.deps.append(d)
            d.need_inc = True
        for t in reads:
            self.tok_r.setdefault(t, []).append(op)
        for t in writes:
            self.tok_w[t] = op
            self.tok_r[t] = []
        self.streams[op.stream].append(op)
        return op

    def op(self, stream, emit, reads=(), writes=(), tag=""):
        return self._record(_Op(stream, emit, False, tag), reads, writes)

    def dma(self, queue, out_ap, in_ap, reads=(), writes=(), transpose=False,
            tag=""):
        if transpose:
            emit = lambda e: e.dma_start_transpose(out_ap, in_ap)
        else:
            emit = lambda e: e.dma_start(out=out_ap, in_=in_ap)
        op = _Op(queue, emit, True, tag)
        ops = self.dma_ops.setdefault(queue, [])
        j = len(ops)
        if queue not in self.dma_sems:
            self.dma_sems[queue] = [
                self.nc.alloc_semaphore(f"dq_{queue}{i}")
                for i in range(_DMA_POOL)]
        op.dma_sem = self.dma_sems[queue][j % _DMA_POOL]
        op.dma_use = j // _DMA_POOL
        op.need_inc = True
        if j >= _DMA_POOL:
            op.deps.append(ops[j - _DMA_POOL])
        ops.append(op)
        return self._record(op, reads, writes)

    def _completion(self, op):
        if op.is_dma:
            return op.dma_sem, 16 * (op.dma_use + 1)
        return self.comp_sem[op.stream], op.count

    def finalize(self):
        for e in _ENGS:
            c = 0
            for op in self.streams[e]:
                if not op.is_dma and op.need_inc:
                    c += 1
                    op.count = c
        known = {e: {} for e in _ENGS}

        def section(eng_name):
            def run(eng):
                kn = known[eng_name]
                for op in self.streams[eng_name]:
                    best = {}
                    for d in op.deps:
                        sem, val = self._completion(d)
                        if val > best.get(id(sem), (None, -1))[1]:
                            best[id(sem)] = (sem, val)
                    for sem, val in best.values():
                        if kn.get(id(sem), -1) >= val:
                            continue
                        eng.wait_ge(sem, val)
                        kn[id(sem)] = val
                    inst = op.emit(eng)
                    if op.need_inc:
                        if op.is_dma:
                            inst.then_inc(op.dma_sem, 16)
                        else:
                            inst.then_inc(self.comp_sem[op.stream], 1)
                if eng_name == "sync":
                    for q, ops in self.dma_ops.items():
                        fin = {}
                        for op in ops:
                            fin[id(op.dma_sem)] = (op.dma_sem,
                                                   16 * (op.dma_use + 1))
                        for sem, val in fin.values():
                            if kn.get(id(sem), -1) < val:
                                eng.wait_ge(sem, val)
                                kn[id(sem)] = val
                    for e2 in _ENGS:
                        if e2 == "sync":
                            continue
                        cnt = [o.count for o in self.streams[e2]
                               if not o.is_dma and o.need_inc]
                        if cnt and kn.get(id(self.comp_sem[e2]), -1) < cnt[-1]:
                            eng.wait_ge(self.comp_sem[e2], cnt[-1])
            return run

        with self.nc.Block() as block:
            if self.streams["tensor"]:
                block.tensor(section("tensor"))
            if self.streams["vector"]:
                block.vector(section("vector"))
            if self.streams["scalar"]:
                block.scalar(section("scalar"))
            if self.streams["gpsimd"]:
                block.gpsimd(section("gpsimd"))
            block.sync(section("sync"))



def _host_tables():
    inv = 1.0 / (10000.0 ** (np.arange(0, RD, 2, dtype=np.float64) / RD))
    pos = np.arange(S, dtype=np.float64)
    fr = inv[:, None] * pos[None, :]
    c64 = np.concatenate([np.cos(fr), np.cos(fr)], 0)
    s64 = np.concatenate([-np.sin(fr), np.sin(fr)], 0)
    ropeC = np.concatenate([c64, c64], 0).astype(np.float32)
    ropeS = np.concatenate([s64, s64], 0).astype(np.float32)
    qi = np.arange(512)[None, None, :]
    p = np.arange(128)[None, :, None]
    m = np.arange(4)[:, None, None]
    masks = (qi >= 128 * m + p).astype(np.float32).astype(BF)
    return ropeC, ropeS, masks



def _alloc_ln_scratch(mk, pref):
    sc = {}
    for n in ("sum", "s2", "nmean", "ex2", "nvar", "lnv", "rstd", "b"):
        sc[n] = mk.sbuf(f"{pref}{n}", [128, 1], F32)
    sc["eps"] = mk.sbuf(f"{pref}eps", [128, 1], F32)
    sc["_p"] = pref
    mk.op("vector", lambda e: e.memset(sc["eps"][:], 1e-5),
          writes=[f"{pref}eps"])
    return sc


def _ln_chain(mk, xt, xt_toks, n_elems, junk, out_bf, out_tok, sc):
    p = sc["_p"]
    inv = 1.0 / n_elems
    mk.op("vector", lambda e: e.reduce_sum(sc["sum"][:], xt[:], axis=AX.X),
          reads=xt_toks, writes=[f"{p}sum"])
    mk.op("scalar", lambda e: e.activation(out=junk[:], in_=xt[:],
          func=AF.Square, accum_out=sc["s2"][:, 0:1]),
          reads=xt_toks, writes=[f"{p}s2", f"{p}junk"])
    mk.op("vector", lambda e: e.tensor_scalar_mul(sc["nmean"][:], sc["sum"][:],
          -inv), reads=[f"{p}sum"], writes=[f"{p}nmean"])
    mk.op("vector", lambda e: e.tensor_scalar_mul(sc["ex2"][:], sc["s2"][:],
          inv), reads=[f"{p}s2"], writes=[f"{p}ex2"])
    mk.op("vector", lambda e: e.tensor_scalar(sc["nvar"][:], sc["nmean"][:],
          sc["nmean"][:, 0:1], sc["ex2"][:, 0:1], ALU.mult, ALU.subtract),
          reads=[f"{p}nmean", f"{p}ex2"], writes=[f"{p}nvar"])
    mk.op("scalar", lambda e: e.activation(out=sc["lnv"][:], in_=sc["nvar"][:],
          func=AF.Ln, bias=sc["eps"][:, 0:1], scale=-1.0),
          reads=[f"{p}nvar", f"{p}eps"], writes=[f"{p}lnv"])
    mk.op("scalar", lambda e: e.activation(out=sc["rstd"][:], in_=sc["lnv"][:],
          func=AF.Exp, scale=-0.5),
          reads=[f"{p}lnv"], writes=[f"{p}rstd"])
    mk.op("vector", lambda e: e.tensor_scalar_mul(sc["b"][:], sc["nmean"][:],
          sc["rstd"][:, 0:1]),
          reads=[f"{p}nmean", f"{p}rstd"], writes=[f"{p}b"])
    mk.op("vector", lambda e: e.tensor_scalar(out_bf[:], xt[:],
          sc["rstd"][:, 0:1], sc["b"][:, 0:1], ALU.mult, ALU.add),
          reads=list(xt_toks) + [f"{p}rstd", f"{p}b"], writes=[out_tok])


def _ph_ln(mk, T, src, dst, nrows, pref):
    sc = _alloc_ln_scratch(mk, pref)
    xt = [mk.sbuf(f"{pref}x{i}", [128, D], F32) for i in range(2)]
    junk = mk.sbuf(f"{pref}junk", [128, D], BF16)
    ob = [mk.sbuf(f"{pref}o{i}", [128, D], BF16) for i in range(2)]
    for i in range(nrows // 128):
        sl = i % 2
        mk.dma("sync", xt[sl][:], T[src][i * 128:(i + 1) * 128, :],
               reads=[src], writes=[f"{pref}x{sl}"])
        _ln_chain(mk, xt[sl], [f"{pref}x{sl}"], D, junk, ob[sl],
                  f"{pref}o{sl}", sc)
        mk.dma("sync", T[dst][i * 128:(i + 1) * 128, :], ob[sl][:],
               reads=[f"{pref}o{sl}"], writes=[f"{dst}.{i}"])
    mk.sfree(*[f"{pref}{n}" for n in
               ("sum", "s2", "nmean", "ex2", "nvar", "lnv", "rstd", "b",
                "eps", "junk", "x0", "x1", "o0", "o1")])


def _tm_tok(dst, i0, n):
    return [f"{dst}.{i}" for i in range(i0 // 128, (i0 + n + 127) // 128)]



def _ph_l0_proj(mk, T):
    nc = mk.nc
    wq = mk.sbuf("wq", [128, 16 * 1024], BF16)
    wqr = mk.sbuf("wqr", [128, 16 * 512], BF16)
    wkvd = mk.sbuf("wkvd", [128, 16 * 512], BF16)
    wkr = mk.sbuf("wkr", [128, 16 * 64], BF16)
    for dc in range(16):
        mk.dma("gpsimd", wq[:, dc * 1024:(dc + 1) * 1024],
               T["Wq_h"][dc * 128:(dc + 1) * 128, :],
               reads=["Wq_h"], writes=[f"wq.{dc}"])
        mk.dma("gpsimd", wqr[:, dc * 512:(dc + 1) * 512],
               T["Wqr_h"][dc * 128:(dc + 1) * 128, :],
               reads=["Wqr_h"], writes=[f"wqr.{dc}"])
        mk.dma("gpsimd", wkvd[:, dc * 512:(dc + 1) * 512],
               T["Wkvd"][dc * 128:(dc + 1) * 128, :],
               reads=["Wkvd"], writes=[f"wkvd.{dc}"])
        mk.dma("gpsimd", wkr[:, dc * 64:(dc + 1) * 64],
               T["Wkr"][dc * 128:(dc + 1) * 128, :],
               reads=["Wkr"], writes=[f"wkr.{dc}"])
    ckvT = mk.sbuf("ckvT", [128, 4 * S], BF16)
    xnT = [mk.sbuf(f"pxnT{i}", [128, 16 * 512], BF16) for i in range(2)]
    ev = [mk.sbuf(f"pev{i}", [128, 512], BF16) for i in range(4)]
    evf = [mk.sbuf(f"pevf{i}", [128, 512], F32) for i in range(2)]
    pi = [0]

    def mm_chain(n_k, lhsT_of, rhs_of, rtoks, mrows=128):
        bank = pi[0] % 2
        pi[0] += 1
        ptok = f"ps.{bank}"
        for k in range(n_k):
            mk.op("tensor",
                  (lambda e, k=k, bank=bank: e.matmul(
                      mk.ps[bank][0:mrows, :], lhsT_of(k), rhs_of(k),
                      start=(k == 0), stop=(k == n_k - 1))),
                  reads=rtoks(k), writes=[ptok])
        return mk.ps[bank], ptok

    for tb in range(8):
        t0 = tb * 512
        sl = tb % 2
        for dc in range(16):
            mk.dma("sync", xnT[sl][:, dc * 512:(dc + 1) * 512],
                   T["xn0_tm"][t0:t0 + 512, dc * 128:(dc + 1) * 128],
                   reads=_tm_tok("xn0_tm", t0, 512),
                   writes=[f"pxnT{sl}.{dc}"], transpose=True)
        for oc in range(4):
            p, ptok = mm_chain(
                16,
                lambda k, oc=oc: wkvd[:, k * 512 + oc * 128:
                                      k * 512 + (oc + 1) * 128],
                lambda k, sl=sl: xnT[sl][:, k * 512:(k + 1) * 512],
                lambda k, sl=sl: [f"wkvd.{k}", f"pxnT{sl}.{k}"])
            mk.op("vector",
                  (lambda e, p=p, oc=oc, t0=t0: e.tensor_copy(
                      ckvT[:, oc * S + t0:oc * S + t0 + 512], p[:])),
                  reads=[ptok], writes=[f"ckvT.{oc}.{tb}"])
        p, ptok = mm_chain(
            16,
            lambda k: wkr[:, k * 64:(k + 1) * 64],
            lambda k, sl=sl: xnT[sl][:, k * 512:(k + 1) * 512],
            lambda k, sl=sl: [f"wkr.{k}", f"pxnT{sl}.{k}"], mrows=64)
        fs = tb % 2
        mk.op("vector",
              (lambda e, p=p, fs=fs: e.tensor_copy(
                  evf[fs][0:64, :], p[0:64, :])),
              reads=[ptok], writes=[f"pevf{fs}"])
        mk.dma("sync", T["krraw"][0:64, t0:t0 + 512], evf[fs][0:64, :],
               reads=[f"pevf{fs}"], writes=[f"krraw.{tb}"])
        for oc in range(8):
            p, ptok = mm_chain(
                16,
                lambda k, oc=oc: wq[:, k * 1024 + oc * 128:
                                    k * 1024 + (oc + 1) * 128],
                lambda k, sl=sl: xnT[sl][:, k * 512:(k + 1) * 512],
                lambda k, sl=sl: [f"wq.{k}", f"pxnT{sl}.{k}"])
            es = oc % 4
            mk.op("vector",
                  (lambda e, p=p, es=es: e.tensor_copy(ev[es][:], p[:])),
                  reads=[ptok], writes=[f"pev{es}"])
            mk.dma("sync", T["qcT"][oc * 128:(oc + 1) * 128, t0:t0 + 512],
                   ev[es][:], reads=[f"pev{es}"], writes=[f"qcT.{oc}.{tb}"])
        for oc in range(4):
            p, ptok = mm_chain(
                16,
                lambda k, oc=oc: wqr[:, k * 512 + oc * 128:
                                     k * 512 + (oc + 1) * 128],
                lambda k, sl=sl: xnT[sl][:, k * 512:(k + 1) * 512],
                lambda k, sl=sl: [f"wqr.{k}", f"pxnT{sl}.{k}"])
            fs = oc % 2
            mk.op("vector",
                  (lambda e, p=p, fs=fs: e.tensor_copy(evf[fs][:], p[:])),
                  reads=[ptok], writes=[f"pevf{fs}"])
            mk.dma("sync", T["qrraw"][oc * 128:(oc + 1) * 128, t0:t0 + 512],
                   evf[fs][:], reads=[f"pevf{fs}"],
                   writes=[f"qrraw.{oc}.{tb}"])
    mk.sfree("wq", "wqr", "wkvd", "wkr")
    wkvu = mk.sbuf("wkvu", [128, 4 * 2048], BF16)
    for cc in range(4):
        mk.dma("gpsimd", wkvu[:, cc * 2048:(cc + 1) * 2048],
               T["Wkvup_h"][cc * 128:(cc + 1) * 128, :],
               reads=["Wkvup_h"], writes=[f"wkvu.{cc}"])
    for tb in range(8):
        t0 = tb * 512
        for oc in range(8):
            p, ptok = mm_chain(
                4,
                lambda k, oc=oc: wkvu[:, k * 2048 + oc * 128:
                                      k * 2048 + (oc + 1) * 128],
                lambda k, t0=t0: ckvT[:, k * S + t0:k * S + t0 + 512],
                lambda k, tb=tb: [f"wkvu.{k}", f"ckvT.{k}.{tb}"])
            es = oc % 4
            mk.op("vector",
                  (lambda e, p=p, es=es: e.tensor_copy(ev[es][:], p[:])),
                  reads=[ptok], writes=[f"pev{es}"])
            mk.dma("sync", T["kcT"][oc * 128:(oc + 1) * 128, t0:t0 + 512],
                   ev[es][:], reads=[f"pev{es}"], writes=[f"kcT.{oc}.{tb}"])
    for tc in range(S // 128):
        for dvb in range(2):
            p, ptok = mm_chain(
                4,
                lambda k, tc=tc: ckvT[:, k * S + tc * 128:k * S + (tc+1)*128],
                lambda k, dvb=dvb: wkvu[:, k * 2048 + 1024 + dvb * 512:
                                        k * 2048 + 1024 + (dvb + 1) * 512],
                lambda k, tc=tc: [f"wkvu.{k}", f"ckvT.{k}.{tc // 4}"])
            es = dvb % 4
            mk.op("vector",
                  (lambda e, p=p, es=es: e.tensor_copy(ev[es][:], p[:])),
                  reads=[ptok], writes=[f"pev{es}"])
            mk.dma("sync", T["v_tm"][tc * 128:(tc + 1) * 128,
                                     dvb * 512:(dvb + 1) * 512],
                   ev[es][:], reads=[f"pev{es}"], writes=[f"v_tm.{tc}.{dvb}"])
    mk.sfree("wkvu", "ckvT", "pxnT0", "pxnT1", "pev0", "pev1", "pev2",
             "pev3", "pevf0", "pevf1")
    ropeC = mk.sbuf("ropeC", [128, S], F32)
    ropeS = mk.sbuf("ropeS", [128, S], F32)
    mk.dma("sync", ropeC[:], T["ropeC"][:, :], reads=["ropeC_in"],
           writes=["ropeC"])
    mk.dma("sync", ropeS[:], T["ropeS"][:, :], reads=["ropeS_in"],
           writes=["ropeS"])
    rx = [mk.sbuf(f"rx{i}", [128, 512], F32) for i in range(2)]
    rr = [mk.sbuf(f"rr{i}", [128, 512], F32) for i in range(2)]
    rt = [mk.sbuf(f"rt{i}", [128, 512], F32) for i in range(2)]
    ro = [mk.sbuf(f"ro{i}", [128, 512], BF16) for i in range(2)]
    krT = mk.sbuf("krT", [64, S], BF16)
    T["_krT"] = krT

    def rope_block(src, stok, r0, nrow, t0, dst_ap, dtok, sl):
        xa, ra, ta, oa = rx[sl], rr[sl], rt[sl], ro[sl]
        rts = [f"rr{sl}.{g}{ab}" for g in range(nrow // 64) for ab in "ab"]
        mk.dma("sync", xa[0:nrow, :], src[r0:r0 + nrow, t0:t0 + 512],
               reads=[stok], writes=[f"rx{sl}"])
        for g in range(nrow // 64):
            mk.dma("sync", ra[g * 64:g * 64 + 32, :],
                   src[r0 + g * 64 + 32:r0 + g * 64 + 64, t0:t0 + 512],
                   reads=[stok], writes=[f"rr{sl}.{g}a"])
            mk.dma("sync", ra[g * 64 + 32:g * 64 + 64, :],
                   src[r0 + g * 64:r0 + g * 64 + 32, t0:t0 + 512],
                   reads=[stok], writes=[f"rr{sl}.{g}b"])
        mk.op("vector", lambda e: e.tensor_mul(ta[0:nrow, :], xa[0:nrow, :],
              ropeC[0:nrow, t0:t0 + 512]),
              reads=[f"rx{sl}", "ropeC"], writes=[f"rt{sl}"])
        mk.op("vector", lambda e: e.tensor_mul(ra[0:nrow, :], ra[0:nrow, :],
              ropeS[0:nrow, t0:t0 + 512]),
              reads=rts + ["ropeS"], writes=rts)
        mk.op("vector", lambda e: e.tensor_add(oa[0:nrow, :], ta[0:nrow, :],
              ra[0:nrow, :]),
              reads=[f"rt{sl}"] + rts, writes=[f"ro{sl}"])
        mk.dma("sync", dst_ap, oa[0:nrow, :],
               reads=[f"ro{sl}"], writes=[dtok])

    for blk in range(8):
        t0 = blk * 512
        for r in range(4):
            sl = (blk * 4 + r) % 2
            rope_block(T["qrraw"], f"qrraw.{r}.{blk}", r * 128, 128, t0,
                       T["qrT"][r * 128:(r + 1) * 128, t0:t0 + 512],
                       f"qrT.{r}.{blk}", sl)
        sl = blk % 2
        rope_block(T["krraw"], f"krraw.{blk}", 0, 64, t0,
                   krT[0:64, t0:t0 + 512], f"krT.{blk}", sl)
    mk.sfree("ropeC", "ropeS", "rx0", "rx1", "rr0", "rr1", "rt0", "rt1",
             "ro0", "ro1")


def _ph_l0_attn(mk, T, filler=None):
    krT = T["_krT"]
    mask = mk.sbuf("mask", [128, 4 * 512], BF16)
    for m in range(4):
        mk.dma("sync", mask[:, m * 512:(m + 1) * 512], T["masks"][m, :, :],
               reads=["masks_in"], writes=[f"mask.{m}"])
    sc192 = mk.sbuf("sc192", [128, 1], F32)
    mk.op("vector", lambda e: e.memset(sc192[:], SCALE0), writes=["sc192"])
    ones_b = mk.sbuf("ones_b", [128, 1], BF16)
    mk.op("vector", lambda e: e.memset(ones_b[:], 1.0), writes=["ones_b"])
    ones_f = mk.sbuf("ones_f", [1, 128], F32)
    mk.op("vector", lambda e: e.memset(ones_f[:], 1.0), writes=["ones_f"])

    qc = [mk.sbuf(f"aqc{i}", [128, 512], BF16) for i in range(2)]
    qr = [mk.sbuf(f"aqr{i}", [64, 512], BF16) for i in range(2)]
    kb = [mk.sbuf(f"akb{i}", [128, S], BF16) for i in range(2)]
    vb = [mk.sbuf(f"avb{i}", [128, S], BF16) for i in range(2)]
    pt = [mk.sbuf(f"apt{i}", [128, 512], BF16) for i in range(3)]
    dens = [mk.sbuf(f"adens{i}", [1, 512], F32) for i in range(2)]
    denr = [mk.sbuf(f"adenr{i}", [1, 512], F32) for i in range(2)]
    bcs = mk.sbuf("abcs", [128, 512], F32)
    onb = mk.sbuf("aonb", [128, 512], BF16)
    psS = [mk.ps[0], mk.ps[1], mk.ps[7]]
    psSt = ["ps.0", "ps.1", "ps.7"]
    psO = [mk.ps[2], mk.ps[3]]
    psD = [mk.ps[4][0:1, :], mk.ps[5][0:1, :]]
    psB = mk.ps[6]
    pend = []

    def epilogue():
        if not pend:
            return
        h, qb, par = pend.pop(0)
        q0 = qb * 512
        mk.op("vector", (lambda e, par=par: e.tensor_copy(
                  dens[par][:], psD[par][:])),
              reads=[f"ps.{4 + par}"], writes=[f"adens{par}"])
        mk.op("vector", (lambda e, par=par: e.reciprocal(
                  denr[par][:], dens[par][:])),
              reads=[f"adens{par}"], writes=[f"adenr{par}"])
        mk.op("tensor", (lambda e, par=par: e.matmul(
                  psB[:], ones_f[:], denr[par][:], start=True, stop=True)),
              reads=["ones_f", f"adenr{par}"], writes=["ps.6"])
        mk.op("vector", lambda e: e.tensor_copy(bcs[:], psB[:]),
              reads=["ps.6"], writes=["abcs"])
        mk.op("vector", (lambda e, par=par: e.tensor_mul(
                  onb[:], psO[par][:], bcs[:])),
              reads=[f"ps.{2 + par}", "abcs"], writes=["aonb"])
        mk.dma("sync", T["onormT"][h * 128:(h + 1) * 128, q0:q0 + 512],
               onb[:], reads=["aonb"], writes=[f"onormT.{h}.{qb}"])

    for h in range(8):
        for qb in range(8):
            q0 = qb * 512
            nk = 4 * qb + 4
            par = qb % 2
            sq = (h * 8 + qb) % 2
            mk.dma("scalar", qc[sq][:],
                   T["qcT"][h * 128:(h + 1) * 128, q0:q0 + 512],
                   reads=[f"qcT.{h}.{qb}"], writes=[f"aqc{sq}"])
            mk.dma("scalar", qr[sq][:],
                   T["qrT"][h * 64:(h + 1) * 64, q0:q0 + 512],
                   reads=[f"qrT.{h // 2}.{qb}"], writes=[f"aqr{sq}"])
            mk.dma("sync", kb[sq][:, 0:nk * 128],
                   T["kcT"][h * 128:(h + 1) * 128, 0:nk * 128],
                   reads=[f"kcT.{h}.{g}" for g in range((nk + 3) // 4)],
                   writes=[f"akb{sq}"])
            for g4 in range((nk + 3) // 4):
                n4 = min(4, nk - g4 * 4)
                vsrc = T["v_tm"][g4 * 512:g4 * 512 + n4 * 128,
                                 h * 128:(h + 1) * 128]
                vsrc = vsrc.rearrange("(g r) c -> r g c", g=n4)
                mk.dma("sync", vb[sq][:, g4 * 512:g4 * 512 + n4 * 128],
                       vsrc,
                       reads=[f"v_tm.{g4 * 4 + i}.{h // 4}"
                              for i in range(n4)],
                       writes=[f"avb{sq}.{g4}"])
            for k in range(nk):
                sl = k % 3
                mk.op("tensor", (lambda e, sl=sl, sq=sq, k=k: e.matmul(
                          psS[sl][:], kb[sq][:, k * 128:(k + 1) * 128],
                          qc[sq][:], start=True, stop=False)),
                      reads=[f"akb{sq}", f"aqc{sq}"], writes=[psSt[sl]])
                mk.op("tensor", (lambda e, sl=sl, sq=sq, k=k: e.matmul(
                          psS[sl][:], krT[0:64, k * 128:(k + 1) * 128],
                          qr[sq][:], start=False, stop=True)),
                      reads=[f"krT.{k // 4}", f"aqr{sq}"], writes=[psSt[sl]])
                mk.op("scalar", (lambda e, sl=sl: e.activation(
                          out=pt[sl][:], in_=psS[sl][:], func=AF.Exp,
                          scale=sc192[:, 0:1])),
                      reads=[psSt[sl], "sc192"], writes=[f"apt{sl}"])
                m = k - 4 * qb
                if m >= 0:
                    mk.op("vector", (lambda e, sl=sl, m=m: e.tensor_mul(
                              pt[sl][:], pt[sl][:],
                              mask[:, m * 512:(m + 1) * 512])),
                          reads=[f"apt{sl}", f"mask.{m}"],
                          writes=[f"apt{sl}"])
                mk.op("tensor", (lambda e, sl=sl, sq=sq, k=k, nk=nk, par=par:
                      e.matmul(psO[par][:],
                               vb[sq][:, k * 128:(k + 1) * 128], pt[sl][:],
                               start=(k == 0), stop=(k == nk - 1))),
                      reads=[f"avb{sq}.{k // 4}", f"apt{sl}"],
                      writes=[f"ps.{2 + par}"])
                mk.op("tensor", (lambda e, sl=sl, k=k, nk=nk, par=par:
                      e.matmul(psD[par][:], ones_b[:], pt[sl][:],
                               start=(k == 0), stop=(k == nk - 1))),
                      reads=["ones_b", f"apt{sl}"], writes=[f"ps.{4 + par}"])
            pend.append((h, qb, par))
            if len(pend) > 1:
                epilogue()
            if filler is not None:
                filler(h * 8 + qb)
    while pend:
        epilogue()
    mk.sfree("mask", "sc192", "ones_b", "ones_f", "abcs", "aonb",
             "aqc0", "aqc1", "aqr0", "aqr1", "akb0", "akb1", "avb0", "avb1",
             "adens0", "adens1", "adenr0", "adenr1", "apt0", "apt1", "apt2",
             "krT")


def _ph_l0_wo(mk, T, cc=False):
    woh = mk.sbuf("woh", [128, 8 * 2048], BF16)
    for r in range(8):
        mk.dma("gpsimd", woh[:, r * 2048:(r + 1) * 2048],
               T["Wo_h"][r * 128:(r + 1) * 128, :],
               reads=["Wo_h"], writes=[f"woh.{r}"])
    onb = [mk.sbuf(f"won{i}", [128, 8 * 512], BF16) for i in range(2)]
    ev = [mk.sbuf(f"wev{i}", [128, 512], BF16) for i in range(2)]
    for tb in range(8):
        t0 = tb * 512
        sl = tb % 2
        for r in range(8):
            mk.dma("sync", onb[sl][:, r * 512:(r + 1) * 512],
                   T["onormT"][r * 128:(r + 1) * 128, t0:t0 + 512],
                   reads=[f"onormT.{r}.{tb}"], writes=[f"won{sl}.{r}"])
        for do in range(16):
            bank = do % 2
            for r in range(8):
                mk.op("tensor", (lambda e, bank=bank, r=r, do=do, sl=sl:
                      e.matmul(mk.ps[bank][:],
                               woh[:, r * 2048 + do * 128:
                                   r * 2048 + (do + 1) * 128],
                               onb[sl][:, r * 512:(r + 1) * 512],
                               start=(r == 0), stop=(r == 7))),
                      reads=[f"woh.{r}", f"won{sl}.{r}"],
                      writes=[f"ps.{bank}"])
            mk.op("vector", (lambda e, bank=bank: e.tensor_copy(
                      ev[bank][:], mk.ps[bank][:])),
                  reads=[f"ps.{bank}"], writes=[f"wev{bank}"])
            if cc:
                jblk, tq = tb // 4, tb % 4
                dst = T["ccA_in"][jblk * D + do * 128:jblk * D + (do + 1) * 128,
                                  tq * 512:(tq + 1) * 512]
                dtok = f"ccA_in.{do}.{tb}"
            else:
                dst = T["a0pT"][do * 128:(do + 1) * 128, t0:t0 + 512]
                dtok = f"a0pT.{do}.{tb}"
            mk.dma("sync", dst, ev[bank][:], reads=[f"wev{bank}"],
                   writes=[dtok])
    mk.sfree("woh", "won0", "won1", "wev0", "wev1")



def _ph_quant_p1(mk, T, wname, nt, ncols, nb=2):
    p = f"q{wname}"
    wt = [mk.sbuf(f"{p}w{i}", [128, ncols], F32) for i in range(nb)]
    junk = mk.sbuf(f"{p}junk", [128, ncols], BF16)
    acc = mk.sbuf(f"{p}acc", [128, 64], F32)
    accv = mk.sbuf(f"{p}accv", [128, 1], F32)
    ones128 = mk.sbuf(f"{p}o128", [128, 1], F32)
    ones1 = mk.sbuf(f"{p}o1", [1, 128], F32)
    cpos = mk.sbuf(f"{p}cp", [1, 1], F32)
    cneg = mk.sbuf(f"{p}cn", [1, 1], F32)
    trow = mk.sbuf(f"{p}tr", [1, 2], F32)
    tpos = mk.sbuf(f"{p}tp", [128, 1], F32)
    tneg = mk.sbuf(f"{p}tn", [128, 1], F32)
    numel = nt * 128 * ncols
    mk.op("vector", lambda e: e.memset(ones128[:], 1.0), writes=[f"{p}o128"])
    mk.op("vector", lambda e: e.memset(ones1[:], 1.0), writes=[f"{p}o1"])
    mk.op("vector", lambda e: e.memset(cpos[:], 0.7 / numel),
          writes=[f"{p}cp"])
    mk.op("vector", lambda e: e.memset(cneg[:], -0.7 / numel),
          writes=[f"{p}cn"])
    for i in range(nt):
        sl = i % nb
        mk.dma("gpsimd", wt[sl][:], T[wname][i * 128:(i + 1) * 128, :],
               reads=[wname], writes=[f"{p}w{sl}"])
        mk.op("scalar", (lambda e, sl=sl, i=i: e.activation(
                  out=junk[:], in_=wt[sl][:], func=AF.Abs,
                  accum_out=acc[:, i:i + 1])),
              reads=[f"{p}w{sl}"], writes=[f"{p}acc.{i}", f"{p}junk"])
    mk.op("vector", lambda e: e.reduce_sum(accv[:], acc[:, 0:nt], axis=AX.X),
          reads=[f"{p}acc.{i}" for i in range(nt)], writes=[f"{p}accv"])
    psT = mk.ps[7][0:1, 0:1]
    mk.op("tensor", lambda e: e.matmul(psT, ones128[:], accv[:],
          start=True, stop=True),
          reads=[f"{p}o128", f"{p}accv"], writes=["ps.7"])
    mk.op("vector", lambda e: e.tensor_scalar_mul(trow[0:1, 0:1], psT,
          cpos[0:1, 0:1]),
          reads=["ps.7", f"{p}cp"], writes=[f"{p}tr"])
    mk.op("vector", lambda e: e.tensor_scalar_mul(trow[0:1, 1:2], psT,
          cneg[0:1, 0:1]),
          reads=["ps.7", f"{p}cn", f"{p}tr"], writes=[f"{p}tr"])
    psBC = mk.ps[7][:, 0:2]
    mk.op("tensor", lambda e: e.matmul(psBC, ones1[:], trow[0:1, :],
          start=True, stop=True),
          reads=[f"{p}o1", f"{p}tr"], writes=["ps.7"])
    mk.op("vector", lambda e: e.tensor_copy(tpos[:], mk.ps[7][:, 0:1]),
          reads=["ps.7"], writes=[f"{p}tp"])
    mk.op("vector", lambda e: e.tensor_copy(tneg[:], mk.ps[7][:, 1:2]),
          reads=["ps.7"], writes=[f"{p}tn"])
    mk.sfree(f"{p}junk", f"{p}acc", f"{p}accv", f"{p}o128", f"{p}o1",
             f"{p}cp", f"{p}cn", f"{p}tr")
    T[f"_{p}tp"], T[f"_{p}tn"] = tpos, tneg
    s1 = [mk.sbuf(f"{p}s1{i}", [128, ncols], BF16) for i in range(nb)]
    s2 = [mk.sbuf(f"{p}s2{i}", [128, ncols], BF16) for i in range(nb)]
    wq = [mk.sbuf(f"{p}wq{i}", [128, ncols], BF16) for i in range(nb)]
    T[f"_{p}bufs"] = (wt, s1, s2, wq)


def _quant_p2_tile(mk, T, wname, qname, i):
    p = f"q{wname}"
    wt, s1, s2, wq = T[f"_{p}bufs"]
    tpos, tneg = T[f"_{p}tp"], T[f"_{p}tn"]
    nb = len(wt)
    sl = i % nb
    mk.dma("gpsimd", wt[sl][:], T[wname][i * 128:(i + 1) * 128, :],
           reads=[wname], writes=[f"{p}w{sl}"])
    mk.op("scalar", (lambda e, sl=sl: e.activation(
              out=s1[sl][:], in_=wt[sl][:], func=AF.Sign,
              bias=tneg[:, 0:1])),
          reads=[f"{p}w{sl}", f"{p}tn"], writes=[f"{p}s1{sl}"])
    mk.op("scalar", (lambda e, sl=sl: e.activation(
              out=s2[sl][:], in_=wt[sl][:], func=AF.Sign,
              bias=tpos[:, 0:1])),
          reads=[f"{p}w{sl}", f"{p}tp"], writes=[f"{p}s2{sl}"])
    mk.op("vector", (lambda e, sl=sl: e.tensor_add(
              wq[sl][:], s1[sl][:], s2[sl][:])),
          reads=[f"{p}s1{sl}", f"{p}s2{sl}"], writes=[f"{p}wq{sl}"])
    mk.dma("gpsimd", T[qname][i * 128:(i + 1) * 128, :], wq[sl][:],
           reads=[f"{p}wq{sl}"], writes=[f"{qname}.{i}"])


def _ph_quant_free(mk, T, wname):
    p = f"q{wname}"
    nb = len(T[f"_{p}bufs"][0])
    names = []
    for i in range(nb):
        names += [f"{p}w{i}", f"{p}s1{i}", f"{p}s2{i}", f"{p}wq{i}"]
    mk.sfree(f"{p}tp", f"{p}tn", *names)


def _ph_quant(mk, T, wname, qname, nt, ncols):
    _ph_quant_p1(mk, T, wname, nt, ncols)
    for i in range(nt):
        _quant_p2_tile(mk, T, wname, qname, i)
    _ph_quant_free(mk, T, wname)


def _ph_l0_ffn(mk, T):
    b1t = mk.sbuf("b1t", [128, 64], F32)
    b2t = mk.sbuf("b2t", [128, 16], F32)
    mk.dma("sync", b1t[:], T["b1t"][:, :], reads=["b1t_in"], writes=["b1t"])
    mk.dma("sync", b2t[:], T["b2t"][:, :], reads=["b2t_in"], writes=["b2t"])
    for th in range(2):
        tt0 = th * 1024
        xnT = mk.sbuf("fxnT", [128, 16 * 1024], BF16)
        hbuf = mk.sbuf("fh", [128, 64 * 1024], BF16)
        w1t = mk.sbuf("fw1t", [128, 16 * 512], BF16)
        w2t = [mk.sbuf(f"fw2t{i}", [128, 512], BF16) for i in range(2)]
        fev = [mk.sbuf(f"fev{i}", [128, 512], BF16) for i in range(2)]
        for dc in range(16):
            for tq in range(2):
                mk.dma("sync", xnT[:, dc * 1024 + tq * 512:
                                   dc * 1024 + (tq + 1) * 512],
                       T["xn0_own"][tt0 + tq * 512:tt0 + (tq + 1) * 512,
                                    dc * 128:(dc + 1) * 128],
                       reads=_tm_tok("xn0_own", tt0 + tq * 512, 512), writes=[f"fxnT.{dc}.{tq}"],
                       transpose=True)
        for fg in range(16):
            for dc in range(16):
                mk.dma("sync", w1t[:, dc * 512:(dc + 1) * 512],
                       T["w1q"][fg * 512:(fg + 1) * 512,
                                dc * 128:(dc + 1) * 128],
                       reads=[f"w1q.{fg * 4 + i}" for i in range(4)],
                       writes=[f"fw1t.{dc}"], transpose=True)
            for half in range(2):
                for dc in range(16):
                    for fl in range(2):
                        for tq in range(2):
                            pb = half * 4 + fl * 2 + tq
                            fc = fg * 4 + half * 2 + fl
                            mk.op("tensor",
                                  (lambda e, pb=pb, dc=dc, fl=fl, tq=tq,
                                   half=half: e.matmul(
                                      mk.ps[pb][:],
                                      w1t[:, dc * 512 + (half * 2 + fl) * 128:
                                          dc * 512 + (half * 2 + fl + 1) * 128],
                                      xnT[:, dc * 1024 + tq * 512:
                                          dc * 1024 + (tq + 1) * 512],
                                      start=(dc == 0), stop=(dc == 15))),
                                  reads=[f"fw1t.{dc}", f"fxnT.{dc}.{tq}"],
                                  writes=[f"ps.{pb}"])
                for fl in range(2):
                    for tq in range(2):
                        pb = half * 4 + fl * 2 + tq
                        fc = fg * 4 + half * 2 + fl
                        mk.op("scalar",
                              (lambda e, pb=pb, fc=fc, tq=tq: e.activation(
                                  out=hbuf[:, fc * 1024 + tq * 512:
                                           fc * 1024 + (tq + 1) * 512],
                                  in_=mk.ps[pb][:], func=AF.Gelu,
                                  bias=b1t[:, fc:fc + 1], scale=0.5)),
                              reads=[f"ps.{pb}", "b1t"],
                              writes=[f"fh.{fc}.{tq}"])
        for dg in range(4):
            for fc in range(64):
                sl = fc % 2
                mk.dma("sync", w2t[sl][:],
                       T["w2q"][dg * 512:(dg + 1) * 512,
                                fc * 128:(fc + 1) * 128],
                       reads=[f"w2q.{dg * 4 + i}" for i in range(4)],
                       writes=[f"fw2t{sl}"], transpose=True)
                for do4 in range(4):
                    for tq in range(2):
                        pb = do4 * 2 + tq
                        mk.op("tensor",
                              (lambda e, pb=pb, sl=sl, do4=do4, tq=tq, fc=fc:
                               e.matmul(
                                  mk.ps[pb][:],
                                  w2t[sl][:, do4 * 128:(do4 + 1) * 128],
                                  hbuf[:, fc * 1024 + tq * 512:
                                       fc * 1024 + (tq + 1) * 512],
                                  start=(fc == 0), stop=(fc == 63))),
                              reads=[f"fw2t{sl}", f"fh.{fc}.{tq}"],
                              writes=[f"ps.{pb}"])
            for do4 in range(4):
                for tq in range(2):
                    pb = do4 * 2 + tq
                    do = dg * 4 + do4
                    es = pb % 2
                    mk.op("vector",
                          (lambda e, pb=pb, do=do, es=es: e.tensor_scalar(
                              fev[es][:], mk.ps[pb][:], 0.5,
                              b2t[:, do:do + 1], ALU.mult, ALU.add)),
                          reads=[f"ps.{pb}", "b2t"], writes=[f"fev{es}"])
                    mk.dma("sync",
                           T["f0T"][do * 128:(do + 1) * 128,
                                    tt0 + tq * 512:tt0 + (tq + 1) * 512],
                           fev[es][:], reads=[f"fev{es}"],
                           writes=[f"f0T.{do}.{th * 2 + tq}"])
        mk.sfree("fxnT", "fh", "fw1t", "fw2t0", "fw2t1", "fev0", "fev1")
    mk.sfree("b1t", "b2t")


def _ph_x1_ln1(mk, T, fused=False):
    sc = [_alloc_ln_scratch(mk, f"l1n{s}") for s in range(2)]
    x0t = [mk.sbuf(f"x0t{s}", [128, D], F32) for s in range(2)]
    ab = [mk.sbuf(f"x1a{s}", [128, D], BF16) for s in range(2)]
    bb = [mk.sbuf(f"x1b{s}", [128, D], BF16) for s in range(2)]
    fb = [mk.sbuf(f"x1f{s}", [128, D], BF16) for s in range(2)]
    x1t = [mk.sbuf(f"x1t{s}", [128, D], F32) for s in range(2)]
    junk = [mk.sbuf(f"x1junk{s}", [128, D], BF16) for s in range(2)]
    ob = [mk.sbuf(f"x1o{s}", [128, D], BF16) for s in range(2)]
    for i in range(TOK // 128):
        i0 = i * 128
        s = i % 2
        mk.dma("sync", x0t[s][:], T["x0_own"][i0:i0 + 128, :],
               reads=["x0_own"], writes=[f"x0t{s}"])
        for blk in range(4):
            if fused:
                mk.dma("sync", ab[s][:, blk * 512:(blk + 1) * 512],
                       T["ccA_out"][blk * 512:(blk + 1) * 512, i0:i0 + 128],
                       reads=["ccA_out"], writes=[f"x1a{s}.{blk}"],
                       transpose=True)
            else:
                mk.dma("sync", ab[s][:, blk * 512:(blk + 1) * 512],
                       T["a0s"][blk * 512:(blk + 1) * 512, i0:i0 + 128],
                       reads=["a0s"], writes=[f"x1a{s}.{blk}"],
                       transpose=True)
                mk.dma("sync", bb[s][:, blk * 512:(blk + 1) * 512],
                       T["a0p"][blk * 512:(blk + 1) * 512, i0:i0 + 128],
                       reads=["a0p"], writes=[f"x1b{s}.{blk}"],
                       transpose=True)
            mk.dma("sync", fb[s][:, blk * 512:(blk + 1) * 512],
                   T["f0T"][blk * 512:(blk + 1) * 512, i0:i0 + 128],
                   reads=[f"f0T.{blk * 4 + q}.{i // 4}" for q in range(4)],
                   writes=[f"x1f{s}.{blk}"], transpose=True)
        atoks = [f"x1a{s}.{b}" for b in range(4)]
        btoks = [f"x1b{s}.{b}" for b in range(4)]
        ftoks = [f"x1f{s}.{b}" for b in range(4)]
        if fused:
            mk.op("vector", (lambda e, s=s: e.tensor_add(
                      x1t[s][:], ab[s][:], fb[s][:])),
                  reads=atoks + ftoks, writes=[f"x1t{s}"])
        else:
            mk.op("vector", (lambda e, s=s: e.tensor_add(
                      x1t[s][:], ab[s][:], bb[s][:])),
                  reads=atoks + btoks, writes=[f"x1t{s}"])
            mk.op("vector", (lambda e, s=s: e.tensor_add(
                      x1t[s][:], x1t[s][:], fb[s][:])),
                  reads=[f"x1t{s}"] + ftoks, writes=[f"x1t{s}"])
        mk.op("vector", (lambda e, s=s: e.tensor_add(
                  x1t[s][:], x1t[s][:], x0t[s][:])),
              reads=[f"x1t{s}", f"x0t{s}"], writes=[f"x1t{s}"])
        mk.dma("sync", T["x1_tm"][i0:i0 + 128, :], x1t[s][:],
               reads=[f"x1t{s}"], writes=[f"x1_tm.{i}"])
        _ln_chain(mk, x1t[s], [f"x1t{s}"], D, junk[s], ob[s], f"x1o{s}",
                  sc[s])
        mk.dma("sync", T["xn1_tm"][i0:i0 + 128, :], ob[s][:],
               reads=[f"x1o{s}"], writes=[f"xn1_tm.{i}"])
    names = []
    for s in range(2):
        names += [f"x0t{s}", f"x1a{s}", f"x1b{s}", f"x1f{s}", f"x1t{s}",
                  f"x1junk{s}", f"x1o{s}"]
        names += [f"l1n{s}{n}" for n in ("sum", "s2", "nmean", "ex2",
                                         "nvar", "lnv", "rstd", "b", "eps")]
    mk.sfree(*names)



def _ph_l1_qkv(mk, T, sp_bf16=False):
    xnT = mk.sbuf("c_xnT", [128, 16 * 2048], BF16)
    for dc in range(16):
        for tb in range(4):
            mk.dma("sync", xnT[:, dc * 2048 + tb * 512:
                               dc * 2048 + (tb + 1) * 512],
                   T["xn1_tm"][tb * 512:(tb + 1) * 512,
                               dc * 128:(dc + 1) * 128],
                   reads=_tm_tok("xn1_tm", tb * 512, 512),
                   writes=[f"c_xnT.{dc}.{tb}"], transpose=True)
    wv = mk.sbuf("c_wv", [128, 16 * 2048], BF16)
    ev = [mk.sbuf(f"c_ev{i}", [128, 512], BF16) for i in range(2)]
    for dc in range(16):
        mk.dma("gpsimd", wv[:, dc * 2048:(dc + 1) * 2048],
               T["Wv1"][dc * 128:(dc + 1) * 128, :],
               reads=["Wv1"], writes=[f"c_wv.{dc}"])
    for dvc in range(16):
        for tb in range(4):
            bank = (dvc * 4 + tb) % 2
            for dc in range(16):
                mk.op("tensor", (lambda e, bank=bank, dc=dc, dvc=dvc, tb=tb:
                      e.matmul(mk.ps[bank][:],
                               wv[:, dc * 2048 + dvc * 128:
                                  dc * 2048 + (dvc + 1) * 128],
                               xnT[:, dc * 2048 + tb * 512:
                                   dc * 2048 + (tb + 1) * 512],
                               start=(dc == 0), stop=(dc == 15))),
                      reads=[f"c_wv.{dc}", f"c_xnT.{dc}.{tb}"],
                      writes=[f"ps.{bank}"])
            mk.op("vector", (lambda e, bank=bank: e.tensor_copy(
                      ev[bank][:], mk.ps[bank][:])),
                  reads=[f"ps.{bank}"], writes=[f"c_ev{bank}"])
            mk.dma("sync", T["vT1"][dvc * 128:(dvc + 1) * 128,
                                    tb * 512:(tb + 1) * 512],
                   ev[bank][:], reads=[f"c_ev{bank}"],
                   writes=[f"vT1.{dvc}.{tb}"])
    mk.sfree("c_wv")
    wp = mk.sbuf("c_wp", [128, 16 * 512], BF16)
    for wname, dst in (("Wq1", "q1"), ("Wk1", "k1")):
        for dg in range(4):
            for dc in range(16):
                mk.dma("gpsimd", wp[:, dc * 512:(dc + 1) * 512],
                       T[wname][dc * 128:(dc + 1) * 128,
                                dg * 512:(dg + 1) * 512],
                       reads=[wname], writes=[f"c_wp.{dc}"])
            for tc in range(16):
                bank = tc % 2
                for dc in range(16):
                    mk.op("tensor", (lambda e, bank=bank, dc=dc, tc=tc:
                          e.matmul(mk.ps[bank][:],
                                   xnT[:, dc * 2048 + tc * 128:
                                       dc * 2048 + (tc + 1) * 128],
                                   wp[:, dc * 512:(dc + 1) * 512],
                                   start=(dc == 0), stop=(dc == 15))),
                          reads=[f"c_wp.{dc}", f"c_xnT.{dc}.{tc // 4}"],
                          writes=[f"ps.{bank}"])
                mk.op("vector", (lambda e, bank=bank: e.tensor_copy(
                          ev[bank][:], mk.ps[bank][:])),
                      reads=[f"ps.{bank}"], writes=[f"c_ev{bank}"])
                mk.dma("sync", T[dst][tc * 128:(tc + 1) * 128,
                                      dg * 512:(dg + 1) * 512],
                       ev[bank][:], reads=[f"c_ev{bank}"],
                       writes=[f"{dst}.{tc}.{dg}"])
    mk.sfree("c_xnT", "c_wp")
    sp_dt = BF16 if sp_bf16 else F32
    qbuf = mk.sbuf("c_qb", [128, 16 * 2048], BF16)
    for sc_ in range(16):
        mk.dma("sync", qbuf[:, sc_ * 2048:(sc_ + 1) * 2048],
               T["q1"][sc_ * 128:(sc_ + 1) * 128, :],
               reads=[f"q1.{sc_}.{g}" for g in range(4)],
               writes=[f"c_qb.{sc_}"])
    kbuf = mk.sbuf("c_kb", [128, 16 * 512], BF16)
    evf = [mk.sbuf(f"c_evf{i}", [128, 512], sp_dt) for i in range(2)]
    for fb in range(4):
        for sc_ in range(16):
            mk.dma("sync", kbuf[:, sc_ * 512:(sc_ + 1) * 512],
                   T["k1"][sc_ * 128:(sc_ + 1) * 128,
                           fb * 512:(fb + 1) * 512],
                   reads=[f"k1.{sc_}.{fb}"], writes=[f"c_kb.{sc_}"])
        for ec in range(16):
            bank = ec % 2
            for sc_ in range(16):
                mk.op("tensor", (lambda e, bank=bank, sc_=sc_, ec=ec:
                      e.matmul(mk.ps[bank][:],
                               qbuf[:, sc_ * 2048 + ec * 128:
                                    sc_ * 2048 + (ec + 1) * 128],
                               kbuf[:, sc_ * 512:(sc_ + 1) * 512],
                               start=(sc_ == 0), stop=(sc_ == 15))),
                      reads=[f"c_qb.{sc_}", f"c_kb.{sc_}"],
                      writes=[f"ps.{bank}"])
            mk.op("vector", (lambda e, bank=bank: e.tensor_copy(
                      evf[bank][:], mk.ps[bank][:])),
                  reads=[f"ps.{bank}"], writes=[f"c_evf{bank}"])
            mk.dma("sync", T["sp"][ec * 128:(ec + 1) * 128,
                                   fb * 512:(fb + 1) * 512],
                   evf[bank][:], reads=[f"c_evf{bank}"],
                   writes=[f"sp.{ec}.{fb}"])
    mk.sfree("c_qb", "c_kb", "c_ev0", "c_ev1", "c_evf0", "c_evf1")
    wdt = [mk.sbuf(f"c_wd{i}", [128, 2048], BF16) for i in range(2)]
    for ic in range(64):
        sl = ic % 2
        mk.dma("gpsimd", wdt[sl][:], T["wd"][ic * 128:(ic + 1) * 128, :],
               reads=["wd"], writes=[f"c_wd{sl}"])
        mk.dma("sync", T["wd_bf"][ic * 128:(ic + 1) * 128, :], wdt[sl][:],
               reads=[f"c_wd{sl}"], writes=[f"wd_bf.{ic}"])
    mk.sfree("c_wd0", "c_wd1")



def _ph_l1_softmax(mk, T, fused=False):
    sc = {}
    for n in ("mx", "nmx", "rs", "rr"):
        sc[n] = mk.sbuf(f"sm{n}", [128, 1], F32)
    spa = mk.sbuf("smspa", [128, D], BF16 if fused else F32)
    spb = mk.sbuf("smspb", [128, D], F32)
    Sb = mk.sbuf("smS", [128, D], F32)
    Pb = mk.sbuf("smP", [128, D], F32)
    Wb = mk.sbuf("smW", [128, D], BF16)
    for ec in range(16):
        e0 = ec * 128
        if fused:
            mk.dma("sync", spa[:], T["ccS_out"][e0:e0 + 128, :],
                   reads=["ccS_out"], writes=["smspa"])
            mk.op("vector", lambda e: e.tensor_copy(Sb[:], spa[:]),
                  reads=["smspa"], writes=["smS"])
        else:
            mk.dma("sync", spa[:], T["sp_s"][e0:e0 + 128, :],
                   reads=["sp_s"], writes=["smspa"])
            mk.dma("sync", spb[:], T["sp_p"][e0:e0 + 128, :],
                   reads=["sp_p"], writes=["smspb"])
            mk.op("vector", lambda e: e.tensor_add(Sb[:], spa[:], spb[:]),
                  reads=["smspa", "smspb"], writes=["smS"])
        mk.op("vector", lambda e: e.reduce_max(sc["mx"][:], Sb[:], axis=AX.X),
              reads=["smS"], writes=["smmx"])
        mk.op("vector", lambda e: e.tensor_scalar_mul(sc["nmx"][:],
              sc["mx"][:], -0.125), reads=["smmx"], writes=["smnmx"])
        mk.op("scalar", lambda e: e.activation(out=Pb[:], in_=Sb[:],
              func=AF.Exp, bias=sc["nmx"][:, 0:1], scale=0.125,
              accum_out=sc["rs"][:, 0:1]),
              reads=["smS", "smnmx"], writes=["smP", "smrs"])
        mk.op("vector", lambda e: e.reciprocal(sc["rr"][:], sc["rs"][:]),
              reads=["smrs"], writes=["smrr"])
        mk.op("vector", lambda e: e.tensor_scalar_mul(Wb[:], Pb[:],
              sc["rr"][:, 0:1]), reads=["smP", "smrr"], writes=["smW"])
        mk.dma("sync", T["wbuf"][e0:e0 + 128, :], Wb[:],
               reads=["smW"], writes=[f"wbuf.{ec}"])
    mk.sfree("smmx", "smnmx", "smrs", "smrr", "smspa", "smspb", "smS",
             "smP", "smW")


def _ph_l1_av_wo(mk, T):
    wT = mk.sbuf("d_wT", [128, 16 * 2048], BF16)
    for fc in range(16):
        for eb in range(4):
            mk.dma("sync", wT[:, fc * 2048 + eb * 512:
                              fc * 2048 + (eb + 1) * 512],
                   T["wbuf"][eb * 512:(eb + 1) * 512,
                             fc * 128:(fc + 1) * 128],
                   reads=[f"wbuf.{eb * 4 + i}" for i in range(4)],
                   writes=[f"d_wT.{fc}.{eb}"], transpose=True)
    wo = mk.sbuf("d_wo", [128, 16 * 2048], BF16)
    for ec in range(16):
        mk.dma("gpsimd", wo[:, ec * 2048:(ec + 1) * 2048],
               T["Wo1"][ec * 128:(ec + 1) * 128, :],
               reads=["Wo1"], writes=[f"d_wo.{ec}"])
    vb = mk.sbuf("d_vb", [128, 16 * 512], BF16)
    ao = mk.sbuf("d_ao", [128, 16 * 512], BF16)
    ev = [mk.sbuf(f"d_ev{i}", [128, 512], BF16) for i in range(2)]
    for sb in range(4):
        s0 = sb * 512
        for fc in range(16):
            mk.dma("sync", vb[:, fc * 512:(fc + 1) * 512],
                   T["vT1"][fc * 128:(fc + 1) * 128, s0:s0 + 512],
                   reads=[f"vT1.{fc}.{sb}"], writes=[f"d_vb.{fc}"])
        for ec in range(16):
            bank = ec % 2
            for fc in range(16):
                mk.op("tensor", (lambda e, bank=bank, fc=fc, ec=ec:
                      e.matmul(mk.ps[bank][:],
                               wT[:, fc * 2048 + ec * 128:
                                  fc * 2048 + (ec + 1) * 128],
                               vb[:, fc * 512:(fc + 1) * 512],
                               start=(fc == 0), stop=(fc == 15))),
                      reads=[f"d_wT.{fc}.{ec // 4}", f"d_vb.{fc}"],
                      writes=[f"ps.{bank}"])
            mk.op("vector", (lambda e, bank=bank, ec=ec: e.tensor_copy(
                      ao[:, ec * 512:(ec + 1) * 512], mk.ps[bank][:])),
                  reads=[f"ps.{bank}"], writes=[f"d_ao.{ec}"])
        for do in range(16):
            bank = 2 + do % 2
            for ec in range(16):
                mk.op("tensor", (lambda e, bank=bank, ec=ec, do=do:
                      e.matmul(mk.ps[bank][:],
                               wo[:, ec * 2048 + do * 128:
                                  ec * 2048 + (do + 1) * 128],
                               ao[:, ec * 512:(ec + 1) * 512],
                               start=(ec == 0), stop=(ec == 15))),
                      reads=[f"d_wo.{ec}", f"d_ao.{ec}"],
                      writes=[f"ps.{bank}"])
            es = do % 2
            mk.op("vector", (lambda e, bank=bank, es=es: e.tensor_copy(
                      ev[es][:], mk.ps[bank][:])),
                  reads=[f"ps.{bank}"], writes=[f"d_ev{es}"])
            mk.dma("sync", T["a1T"][do * 128:(do + 1) * 128, s0:s0 + 512],
                   ev[es][:], reads=[f"d_ev{es}"],
                   writes=[f"a1T.{do}.{sb}"])
    mk.sfree("d_wT", "d_wo", "d_vb", "d_ao", "d_ev0", "d_ev1")


def _ph_l1_gateup(mk, T):
    xnT = mk.sbuf("g_xnT", [128, 16 * 2048], BF16)
    for dc in range(16):
        for tb in range(4):
            mk.dma("sync", xnT[:, dc * 2048 + tb * 512:
                               dc * 2048 + (tb + 1) * 512],
                   T["xn1_tm"][tb * 512:(tb + 1) * 512,
                               dc * 128:(dc + 1) * 128],
                   reads=_tm_tok("xn1_tm", tb * 512, 512), writes=[f"g_xnT.{dc}.{tb}"],
                   transpose=True)
    wg = mk.sbuf("g_wg", [128, 16 * 512], BF16)
    wu = mk.sbuf("g_wu", [128, 16 * 512], BF16)
    sg = mk.sbuf("g_sg", [128, 512], F32)
    hb = mk.sbuf("g_h", [128, 512], BF16)
    habs = mk.sbuf("g_habs", [128, 512], F32)
    q4 = [mk.sbuf(f"g_q{i}", [128, 128], F32) for i in range(5)]
    m4 = [mk.sbuf(f"g_m{i}", [128, 128], BF16) for i in range(4)]
    hs = [mk.sbuf(f"g_hs{i}", [128, 512], BF16) for i in range(2)]
    for ig in range(16):
        i0 = ig * 512
        for dc in range(16):
            mk.dma("gpsimd", wg[:, dc * 512:(dc + 1) * 512],
                   T["wg"][dc * 128:(dc + 1) * 128, i0:i0 + 512],
                   reads=["wg"], writes=[f"g_wg.{dc}"])
            mk.dma("gpsimd", wu[:, dc * 512:(dc + 1) * 512],
                   T["wu"][dc * 128:(dc + 1) * 128, i0:i0 + 512],
                   reads=["wu"], writes=[f"g_wu.{dc}"])
        for tc in range(16):
            pg, pu = 2 * (tc % 2), 2 * (tc % 2) + 1
            for dc in range(16):
                mk.op("tensor", (lambda e, dc=dc, tc=tc, pg=pg: e.matmul(
                          mk.ps[pg][:],
                          xnT[:, dc * 2048 + tc * 128:
                              dc * 2048 + (tc + 1) * 128],
                          wg[:, dc * 512:(dc + 1) * 512],
                          start=(dc == 0), stop=(dc == 15))),
                      reads=[f"g_xnT.{dc}.{tc // 4}", f"g_wg.{dc}"],
                      writes=[f"ps.{pg}"])
                mk.op("tensor", (lambda e, dc=dc, tc=tc, pu=pu: e.matmul(
                          mk.ps[pu][:],
                          xnT[:, dc * 2048 + tc * 128:
                              dc * 2048 + (tc + 1) * 128],
                          wu[:, dc * 512:(dc + 1) * 512],
                          start=(dc == 0), stop=(dc == 15))),
                      reads=[f"g_xnT.{dc}.{tc // 4}", f"g_wu.{dc}"],
                      writes=[f"ps.{pu}"])
            mk.op("scalar", (lambda e, pg=pg: e.activation(
                  out=sg[:], in_=mk.ps[pg][:],
                  func=AF.Silu)), reads=[f"ps.{pg}"], writes=["g_sg"])
            mk.op("vector", (lambda e, pu=pu: e.tensor_mul(hb[:], sg[:],
                  mk.ps[pu][:])), reads=["g_sg", f"ps.{pu}"], writes=["g_h"])
            mk.op("scalar", lambda e: e.activation(out=habs[:], in_=hb[:],
                  func=AF.Abs), reads=["g_h"], writes=["g_habs"])
            mk.op("vector", lambda e: e.tensor_max(q4[0][:], habs[:, 0::4],
                  habs[:, 1::4]), reads=["g_habs"], writes=["g_q0"])
            mk.op("vector", lambda e: e.tensor_tensor(q4[1][:],
                  habs[:, 0::4], habs[:, 1::4], ALU.min),
                  reads=["g_habs"], writes=["g_q1"])
            mk.op("vector", lambda e: e.tensor_max(q4[2][:], habs[:, 2::4],
                  habs[:, 3::4]), reads=["g_habs"], writes=["g_q2"])
            mk.op("vector", lambda e: e.tensor_tensor(q4[3][:],
                  habs[:, 2::4], habs[:, 3::4], ALU.min),
                  reads=["g_habs"], writes=["g_q3"])
            mk.op("vector", lambda e: e.tensor_tensor(q4[0][:], q4[0][:],
                  q4[2][:], ALU.min), reads=["g_q0", "g_q2"],
                  writes=["g_q0"])
            mk.op("vector", lambda e: e.tensor_max(q4[1][:], q4[1][:],
                  q4[3][:]), reads=["g_q1", "g_q3"], writes=["g_q1"])
            mk.op("vector", lambda e: e.tensor_max(q4[4][:], q4[0][:],
                  q4[1][:]), reads=["g_q0", "g_q1"], writes=["g_q4"])
            sl = tc % 2
            for jj in range(4):
                mk.op("vector", (lambda e, jj=jj: e.tensor_tensor(
                          m4[jj][:], habs[:, jj::4], q4[4][:], ALU.is_ge)),
                      reads=["g_habs", "g_q4"], writes=[f"g_m{jj}"])
            for jj in range(4):
                mk.op("gpsimd", (lambda e, jj=jj, sl=sl: e.tensor_mul(
                          hs[sl][:, jj::4], hb[:, jj::4], m4[jj][:])),
                      reads=["g_h", f"g_m{jj}"], writes=[f"g_hs{sl}"])
            mk.dma("sync", T["hsp"][tc * 128:(tc + 1) * 128, i0:i0 + 512],
                   hs[sl][:], reads=[f"g_hs{sl}"],
                   writes=[f"hsp.{tc}.{ig}"])
    mk.sfree("g_xnT", "g_wg", "g_wu", "g_sg", "g_h", "g_habs",
             "g_q0", "g_q1", "g_q2", "g_q3", "g_q4", "g_m0", "g_m1", "g_m2", "g_m3", "g_hs0", "g_hs1")


def _ph_l1_down_out(mk, T):
    for th in range(2):
        tt0 = th * 1024
        hT = mk.sbuf("o_hT", [128, 64 * 1024], BF16)
        for ic in range(64):
            for tq in range(2):
                mk.dma("sync", hT[:, ic * 1024 + tq * 512:
                                  ic * 1024 + (tq + 1) * 512],
                       T["hsp"][tt0 + tq * 512:tt0 + (tq + 1) * 512,
                                ic * 128:(ic + 1) * 128],
                       reads=[f"hsp.{(tt0 + tq * 512) // 128 + q}.{ic // 4}"
                              for q in range(4)],
                       writes=[f"o_hT.{ic}.{tq}"], transpose=True)
        wdc = mk.sbuf("o_wdc", [128, 64 * 128], BF16)
        a1b = [mk.sbuf(f"o_a1{i}", [128, 512], BF16) for i in range(2)]
        yev = [mk.sbuf(f"o_y{i}", [128, 512], BF16) for i in range(2)]
        for dc in range(16):
            for g8 in range(8):
                src = T["wd_bf"][g8 * 1024:(g8 + 1) * 1024,
                                 dc * 128:(dc + 1) * 128]
                src = src.rearrange("(g r) c -> r g c", g=8)
                mk.dma("sync", wdc[:, g8 * 1024:(g8 + 1) * 1024], src,
                       reads=[f"wd_bf.{g8 * 8 + i}" for i in range(8)],
                       writes=[f"o_wdc.{g8}"])
            for tq in range(2):
                bank = tq
                for ic in range(64):
                    mk.op("tensor", (lambda e, bank=bank, ic=ic, tq=tq:
                          e.matmul(mk.ps[bank][:],
                                   wdc[:, ic * 128:(ic + 1) * 128],
                                   hT[:, ic * 1024 + tq * 512:
                                      ic * 1024 + (tq + 1) * 512],
                                   start=(ic == 0), stop=(ic == 63))),
                          reads=[f"o_wdc.{ic // 8}", f"o_hT.{ic}.{tq}"],
                          writes=[f"ps.{bank}"])
                mk.dma("sync", a1b[tq][:],
                       T["a1T"][dc * 128:(dc + 1) * 128,
                                tt0 + tq * 512:tt0 + (tq + 1) * 512],
                       reads=[f"a1T.{dc}.{th * 2 + tq}"],
                       writes=[f"o_a1{tq}"])
                mk.op("vector", (lambda e, bank=bank, tq=tq: e.tensor_add(
                          yev[tq][:], mk.ps[bank][:], a1b[tq][:])),
                      reads=[f"ps.{bank}", f"o_a1{tq}"], writes=[f"o_y{tq}"])
                mk.dma("sync",
                       T["y1T"][dc * 128:(dc + 1) * 128,
                                tt0 + tq * 512:tt0 + (tq + 1) * 512],
                       yev[tq][:], reads=[f"o_y{tq}"],
                       writes=[f"y1T.{dc}.{th * 2 + tq}"])
        mk.sfree("o_hT", "o_wdc", "o_a10", "o_a11", "o_y0", "o_y1")
    x1t = mk.sbuf("o_x1", [128, D], F32)
    y1b = mk.sbuf("o_y1b", [128, D], BF16)
    outt = mk.sbuf("o_out", [128, D], F32)
    for i in range(TOK // 128):
        i0 = i * 128
        mk.dma("sync", x1t[:], T["x1_tm"][i0:i0 + 128, :],
               reads=[f"x1_tm.{i}"], writes=["o_x1"])
        for blk in range(4):
            mk.dma("sync", y1b[:, blk * 512:(blk + 1) * 512],
                   T["y1T"][blk * 512:(blk + 1) * 512, i0:i0 + 128],
                   reads=[f"y1T.{blk * 4 + q}.{i // 4}" for q in range(4)],
                   writes=[f"o_y1b.{blk}"], transpose=True)
        mk.op("vector", lambda e: e.tensor_add(outt[:], x1t[:], y1b[:]),
              reads=["o_x1"] + [f"o_y1b.{b}" for b in range(4)],
              writes=["o_out"])
        mk.dma("sync", T["OUT"][i0:i0 + 128, :], outt[:],
               reads=["o_out"], writes=[f"OUT.{i}"])
    mk.sfree("o_x1", "o_y1b", "o_out")



def _dram(nc, specs):
    T = {}
    for name, shape, dt, kind in specs:
        T[name] = nc.dram_tensor(name, shape, dt, kind=kind).ap()
    return T


def _build_A():
    nc = bass.Bass("TRN2", target_bir_lowering=False)
    T = _dram(nc, [
        ("x0", [S, D], F32, "ExternalInput"),
        ("Wq_h", [D, 1024], F32, "ExternalInput"),
        ("Wqr_h", [D, 512], F32, "ExternalInput"),
        ("Wkvd", [D, DKV], F32, "ExternalInput"),
        ("Wkvup_h", [DKV, 2048], F32, "ExternalInput"),
        ("Wkr", [D, RD], F32, "ExternalInput"),
        ("Wo_h", [1024, D], F32, "ExternalInput"),
        ("ropeC", [128, S], F32, "ExternalInput"),
        ("ropeS", [128, S], F32, "ExternalInput"),
        ("masks", [4, 128, 512], BF16, "ExternalInput"),
        ("xn0_tm", [S, D], BF16, "ExternalOutput"),
        ("a0pT", [D, S], BF16, "ExternalOutput"),
        ("qcT", [1024, S], BF16, "Internal"),
        ("qrT", [512, S], BF16, "Internal"),
        ("qrraw", [512, S], F32, "Internal"),
        ("krraw", [64, S], F32, "Internal"),
        ("kcT", [1024, S], BF16, "Internal"),
        ("v_tm", [S, 1024], BF16, "Internal"),
        ("onormT", [1024, S], BF16, "Internal"),
    ])
    mk = MK(nc)
    _ph_ln(mk, T, "x0", "xn0_tm", S, "ln0")
    _ph_l0_proj(mk, T)
    _ph_l0_attn(mk, T)
    _ph_l0_wo(mk, T)
    mk.finalize()
    return nc


def _build_B():
    nc = bass.Bass("TRN2", target_bir_lowering=False)
    T = _dram(nc, [
        ("x0_own", [TOK, D], F32, "ExternalInput"),
        ("xn0_own", [TOK, D], BF16, "ExternalInput"),
        ("w1", [DFF, D], F32, "ExternalInput"),
        ("w2", [D, DFF], F32, "ExternalInput"),
        ("b1t", [128, 64], F32, "ExternalInput"),
        ("b2t", [128, 16], F32, "ExternalInput"),
        ("a0s", [D, TOK], BF16, "ExternalInput"),
        ("a0p", [D, TOK], BF16, "ExternalInput"),
        ("x1_tm", [TOK, D], F32, "ExternalOutput"),
        ("xn1_tm", [TOK, D], BF16, "ExternalOutput"),
        ("w1q", [DFF, D], BF16, "Internal"),
        ("w2q", [D, DFF], BF16, "Internal"),
        ("f0T", [D, TOK], BF16, "Internal"),
    ])
    mk = MK(nc)
    _ph_quant(mk, T, "w1", "w1q", 64, 2048)
    _ph_quant(mk, T, "w2", "w2q", 16, 8192)
    _ph_l0_ffn(mk, T)
    _ph_x1_ln1(mk, T)
    mk.finalize()
    return nc


def _build_C():
    nc = bass.Bass("TRN2", target_bir_lowering=False)
    T = _dram(nc, [
        ("xn1_tm", [TOK, D], BF16, "ExternalInput"),
        ("Wq1", [D, D], F32, "ExternalInput"),
        ("Wk1", [D, D], F32, "ExternalInput"),
        ("Wv1", [D, D], F32, "ExternalInput"),
        ("wd", [DFF, D], F32, "ExternalInput"),
        ("sp", [D, D], F32, "ExternalOutput"),
        ("vT1", [D, TOK], BF16, "ExternalOutput"),
        ("wd_bf", [DFF, D], BF16, "ExternalOutput"),
        ("q1", [TOK, D], BF16, "Internal"),
        ("k1", [TOK, D], BF16, "Internal"),
    ])
    mk = MK(nc)
    _ph_l1_qkv(mk, T)
    mk.finalize()
    return nc


def _build_D():
    nc = bass.Bass("TRN2", target_bir_lowering=False)
    T = _dram(nc, [
        ("sp_s", [D, D], F32, "ExternalInput"),
        ("sp_p", [D, D], F32, "ExternalInput"),
        ("vT1", [D, TOK], BF16, "ExternalInput"),
        ("Wo1", [D, D], F32, "ExternalInput"),
        ("xn1_tm", [TOK, D], BF16, "ExternalInput"),
        ("wg", [D, DFF], F32, "ExternalInput"),
        ("wu", [D, DFF], F32, "ExternalInput"),
        ("wd_bf", [DFF, D], BF16, "ExternalInput"),
        ("x1_tm", [TOK, D], F32, "ExternalInput"),
        ("OUT", [TOK, D], F32, "ExternalOutput"),
        ("wbuf", [D, D], BF16, "Internal"),
        ("a1T", [D, TOK], BF16, "Internal"),
        ("hsp", [TOK, DFF], BF16, "Internal"),
        ("y1T", [D, TOK], BF16, "Internal"),
    ])
    mk = MK(nc)
    _ph_l1_softmax(mk, T)
    _ph_l1_av_wo(mk, T)
    _ph_l1_gateup(mk, T)
    _ph_l1_down_out(mk, T)
    mk.finalize()
    return nc




_RG = [[0, 1], [2, 3], [4, 5], [6, 7]]


def _build_fused():
    nc = bass.Bass("TRN2", target_bir_lowering=False)
    T = _dram(nc, [
        ("x0", [S, D], F32, "ExternalInput"),
        ("x0_own", [TOK, D], F32, "ExternalInput"),
        ("Wq_h", [D, 1024], F32, "ExternalInput"),
        ("Wqr_h", [D, 512], F32, "ExternalInput"),
        ("Wkvd", [D, DKV], F32, "ExternalInput"),
        ("Wkvup_h", [DKV, 2048], F32, "ExternalInput"),
        ("Wkr", [D, RD], F32, "ExternalInput"),
        ("Wo_h", [1024, D], F32, "ExternalInput"),
        ("ropeC", [128, S], F32, "ExternalInput"),
        ("ropeS", [128, S], F32, "ExternalInput"),
        ("masks", [4, 128, 512], BF16, "ExternalInput"),
        ("w1", [DFF, D], F32, "ExternalInput"),
        ("w2", [D, DFF], F32, "ExternalInput"),
        ("b1t", [128, 64], F32, "ExternalInput"),
        ("b2t", [128, 16], F32, "ExternalInput"),
        ("Wq1", [D, D], F32, "ExternalInput"),
        ("Wk1", [D, D], F32, "ExternalInput"),
        ("Wv1", [D, D], F32, "ExternalInput"),
        ("Wo1", [D, D], F32, "ExternalInput"),
        ("wg", [D, DFF], F32, "ExternalInput"),
        ("wu", [D, DFF], F32, "ExternalInput"),
        ("wd", [DFF, D], F32, "ExternalInput"),
        ("OUT", [TOK, D], F32, "ExternalOutput"),
        ("xn0_tm", [S, D], BF16, "Internal"),
        ("xn0_own", [TOK, D], BF16, "Internal"),
        ("qcT", [1024, S], BF16, "Internal"),
        ("qrT", [512, S], BF16, "Internal"),
        ("qrraw", [512, S], F32, "Internal"),
        ("krraw", [64, S], F32, "Internal"),
        ("kcT", [1024, S], BF16, "Internal"),
        ("v_tm", [S, 1024], BF16, "Internal"),
        ("onormT", [1024, S], BF16, "Internal"),
        ("ccA_in", [2 * D, TOK], BF16, "Internal"),
        ("ccA_out", [D, TOK], BF16, "Internal"),
        ("w1q", [DFF, D], BF16, "Internal"),
        ("w2q", [D, DFF], BF16, "Internal"),
        ("f0T", [D, TOK], BF16, "Internal"),
        ("x1_tm", [TOK, D], F32, "Internal"),
        ("xn1_tm", [TOK, D], BF16, "Internal"),
        ("q1", [TOK, D], BF16, "Internal"),
        ("k1", [TOK, D], BF16, "Internal"),
        ("vT1", [D, TOK], BF16, "Internal"),
        ("sp", [D, D], BF16, "Internal"),
        ("ccS_out", [D, D], BF16, "Internal"),
        ("wd_bf", [DFF, D], BF16, "Internal"),
        ("wbuf", [D, D], BF16, "Internal"),
        ("a1T", [D, TOK], BF16, "Internal"),
        ("hsp", [TOK, DFF], BF16, "Internal"),
        ("y1T", [D, TOK], BF16, "Internal"),
    ])
    mk = MK(nc)
    _ph_ln(mk, T, "x0", "xn0_tm", S, "ln0")
    _ph_ln(mk, T, "x0_own", "xn0_own", TOK, "lnb")
    _ph_l0_proj(mk, T)
    _ph_quant_p1(mk, T, "w1", 64, 2048)
    _ph_quant_p1(mk, T, "w2", 16, 8192, nb=1)
    work = [("w1", "w1q", i) for i in range(64)] + \
           [("w2", "w2q", i) for i in range(16)]

    def filler(slot):
        for _ in range(2 if slot >= 8 else 0):
            if work:
                wn, qn, i = work.pop(0)
                _quant_p2_tile(mk, T, wn, qn, i)

    _ph_l0_attn(mk, T, filler=filler)
    while work:
        wn, qn, i = work.pop(0)
        _quant_p2_tile(mk, T, wn, qn, i)
    _ph_quant_free(mk, T, "w1")
    _ph_quant_free(mk, T, "w2")
    _ph_l0_wo(mk, T, cc=True)
    mk.op("gpsimd", lambda e: e.collective_compute(
              "ReduceScatter", ALU.add, replica_groups=_RG,
              ins=[T["ccA_in"][:, :]], outs=[T["ccA_out"][:, :]]),
          reads=[f"ccA_in.{do}.{tb}" for do in range(16) for tb in range(8)],
          writes=["ccA_out"])
    _ph_l0_ffn(mk, T)
    _ph_x1_ln1(mk, T, fused=True)
    _ph_l1_qkv(mk, T, sp_bf16=True)
    mk.op("gpsimd", lambda e: e.collective_compute(
              "AllReduce", ALU.add, replica_groups=_RG,
              ins=[T["sp"][:, :]], outs=[T["ccS_out"][:, :]]),
          reads=[f"sp.{ec}.{fb}" for ec in range(16) for fb in range(4)],
          writes=["ccS_out"])
    _ph_l1_gateup(mk, T)
    _ph_l1_softmax(mk, T, fused=True)
    _ph_l1_av_wo(mk, T)
    _ph_l1_down_out(mk, T)
    mk.finalize()
    return nc



_FUSED = True

def kernel(**inp):
    if _FUSED:
        return _kernel_fused(**inp)
    return _kernel_unfused(**inp)


def _common_host(inp):
    f32 = np.float32

    def np32(name):
        return np.ascontiguousarray(np.asarray(inp[name], dtype=f32))

    W = {k: np32(k) for k in inp}
    b1 = W["L0_ffn_b1"]
    b2 = W["L0_ffn_b2"]
    W["b1t"] = np.ascontiguousarray(b1.reshape(64, 128).T)
    W["b2t"] = np.ascontiguousarray(b2.reshape(16, 128).T)
    return W


def _kernel_fused(**inp):
    ropeC, ropeS, masks = _host_tables()
    W = _common_host(inp)
    x0 = W["L0_attn_x"]
    cores = list(range(8))
    nc = _build_fused()
    in_maps = []
    for c in cores:
        b, j = c // 2, c % 2
        own = slice(j * TOK, (j + 1) * TOK)
        Wkvu = W["L0_attn_Wkv_up"]
        in_maps.append({
            "x0": x0[b],
            "x0_own": np.ascontiguousarray(x0[b][own]),
            "Wq_h": np.ascontiguousarray(
                W["L0_attn_Wq"][:, j * 1024:(j + 1) * 1024]),
            "Wqr_h": np.ascontiguousarray(
                W["L0_attn_Wq_rope"][:, j * 512:(j + 1) * 512]),
            "Wkvd": W["L0_attn_Wkv_down"],
            "Wkvup_h": np.ascontiguousarray(np.concatenate(
                [Wkvu[:, j * 1024:(j + 1) * 1024],
                 Wkvu[:, 2048 + j * 1024:2048 + (j + 1) * 1024]], axis=1)),
            "Wkr": W["L0_attn_Wk_rope"],
            "Wo_h": np.ascontiguousarray(
                W["L0_attn_Wo"][j * 1024:(j + 1) * 1024, :]),
            "ropeC": ropeC, "ropeS": ropeS, "masks": masks,
            "w1": W["L0_ffn_w1"], "w2": W["L0_ffn_w2"],
            "b1t": W["b1t"], "b2t": W["b2t"],
            "Wq1": W["L1_attn_Wq"], "Wk1": W["L1_attn_Wk"],
            "Wv1": W["L1_attn_Wv"], "Wo1": W["L1_attn_Wo"],
            "wg": W["L1_ffn_w_gate"], "wu": W["L1_ffn_w_up"],
            "wd": W["L1_ffn_w_down"],
        })
    res = run_bass_kernel_spmd(nc, in_maps, core_ids=cores).results
    out = np.empty((B, S, D), np.float32)
    for c in cores:
        b, j = c // 2, c % 2
        out[b, j * TOK:(j + 1) * TOK, :] = res[c]["OUT"]
    return out


def _kernel_unfused(**inp):
    f32 = np.float32
    ropeC, ropeS, masks = _host_tables()
    cores = list(range(8))

    def np32(name):
        return np.ascontiguousarray(np.asarray(inp[name], dtype=f32))

    x0 = np32("L0_attn_x")
    Wq, Wqr = np32("L0_attn_Wq"), np32("L0_attn_Wq_rope")
    Wkvd, Wkvu = np32("L0_attn_Wkv_down"), np32("L0_attn_Wkv_up")
    Wkr, Wo = np32("L0_attn_Wk_rope"), np32("L0_attn_Wo")
    w1, b1 = np32("L0_ffn_w1"), np32("L0_ffn_b1")
    w2, b2 = np32("L0_ffn_w2"), np32("L0_ffn_b2")
    Wq1, Wk1 = np32("L1_attn_Wq"), np32("L1_attn_Wk")
    Wv1, Wo1 = np32("L1_attn_Wv"), np32("L1_attn_Wo")
    wg, wu, wd = np32("L1_ffn_w_gate"), np32("L1_ffn_w_up"), \
        np32("L1_ffn_w_down")
    b1t = np.ascontiguousarray(b1.reshape(64, 128).T)
    b2t = np.ascontiguousarray(b2.reshape(16, 128).T)

    ncA = _build_A()
    inA = []
    for c in cores:
        b, j = c // 2, c % 2
        inA.append({
            "x0": x0[b],
            "Wq_h": np.ascontiguousarray(Wq[:, j * 1024:(j + 1) * 1024]),
            "Wqr_h": np.ascontiguousarray(Wqr[:, j * 512:(j + 1) * 512]),
            "Wkvd": Wkvd,
            "Wkvup_h": np.ascontiguousarray(np.concatenate(
                [Wkvu[:, j * 1024:(j + 1) * 1024],
                 Wkvu[:, 2048 + j * 1024:2048 + (j + 1) * 1024]], axis=1)),
            "Wkr": Wkr,
            "Wo_h": np.ascontiguousarray(Wo[j * 1024:(j + 1) * 1024, :]),
            "ropeC": ropeC, "ropeS": ropeS, "masks": masks,
        })
    rA = run_bass_kernel_spmd(ncA, inA, core_ids=cores).results

    ncB = _build_B()
    inB = []
    for c in cores:
        b, j = c // 2, c % 2
        own = slice(j * TOK, (j + 1) * TOK)
        inB.append({
            "x0_own": np.ascontiguousarray(x0[b][own]),
            "xn0_own": np.ascontiguousarray(rA[c]["xn0_tm"][own]),
            "w1": w1, "w2": w2, "b1t": b1t, "b2t": b2t,
            "a0s": np.ascontiguousarray(rA[c]["a0pT"][:, own]),
            "a0p": np.ascontiguousarray(rA[c ^ 1]["a0pT"][:, own]),
        })
    rB = run_bass_kernel_spmd(ncB, inB, core_ids=cores).results

    ncC = _build_C()
    inC = [{"xn1_tm": rB[c]["xn1_tm"], "Wq1": Wq1, "Wk1": Wk1, "Wv1": Wv1,
            "wd": wd} for c in cores]
    rC = run_bass_kernel_spmd(ncC, inC, core_ids=cores).results

    ncD = _build_D()
    inD = [{"sp_s": rC[c]["sp"], "sp_p": rC[c ^ 1]["sp"],
            "vT1": rC[c]["vT1"], "Wo1": Wo1, "xn1_tm": rB[c]["xn1_tm"],
            "wg": wg, "wu": wu, "wd_bf": rC[c]["wd_bf"],
            "x1_tm": rB[c]["x1_tm"]} for c in cores]
    rD = run_bass_kernel_spmd(ncD, inD, core_ids=cores).results

    out = np.empty((B, S, D), np.float32)
    for c in cores:
        b, j = c // 2, c % 2
        out[b, j * TOK:(j + 1) * TOK, :] = rD[c]["OUT"]
    return out


# version 06
# speedup vs baseline: 1.2436x; 1.0226x over previous
import math
import numpy as np
import ml_dtypes
import concourse.bass as bass
import concourse.mybir as mybir
from concourse.bass_utils import run_bass_kernel_spmd

F32 = mybir.dt.float32
BF16 = mybir.dt.bfloat16
AF = mybir.ActivationFunctionType
ALU = mybir.AluOpType
AX = mybir.AxisListType

B, S, D = 4, 4096, 2048
H, HD, RD = 16, 128, 64
DKV = 512
DFF = 8192
TOK = S // 2
SCALE0 = 1.0 / math.sqrt(HD + RD)

BF = ml_dtypes.bfloat16



class _Op:
    __slots__ = ("stream", "emit", "deps", "need_inc", "is_dma", "dma_use",
                 "dma_sem", "count", "tag")

    def __init__(self, stream, emit, is_dma, tag):
        self.stream = stream
        self.emit = emit
        self.deps = []
        self.need_inc = False
        self.is_dma = is_dma
        self.dma_use = 0
        self.dma_sem = None
        self.count = 0
        self.tag = tag


_ENGS = ("tensor", "vector", "scalar", "gpsimd", "sync")
_DMA_POOL = 16


class MK:
    def __init__(self, nc):
        self.nc = nc
        self.streams = {e: [] for e in _ENGS}
        self.comp_sem = {e: nc.alloc_semaphore(f"cs_{e}") for e in _ENGS}
        self.dma_sems = {}
        self.dma_ops = {}
        self.tok_w = {}
        self.tok_r = {}
        self.pending = []
        base = (nc.sbuf_base + 63) & ~63
        self.sb_free = [[base, nc.sbuf_top]]
        self.sb_allocs = {}
        self.ps = [nc.alloc_psum_tensor(f"ps{i}", [128, 512], F32).ap()
                   for i in range(8)]
        self._uid = 0

    def sbuf(self, name, shape, dtype):
        per_part = int(np.prod(shape[1:])) * (2 if dtype == BF16 else 4)
        size = (per_part + 63) & ~63
        for seg in self.sb_free:
            if seg[1] - seg[0] >= size:
                off = seg[0]
                seg[0] += size
                break
        else:
            raise MemoryError(f"sbuf full allocating {name} ({size}B/part)")
        self._uid += 1
        h = self.nc.alloc_sbuf_tensor_at(f"{name}_{self._uid}", shape, dtype,
                                         offset=off)
        ap = h.ap()
        self.sb_allocs[name] = [off, size, ap, list(self.pending)]
        return ap

    def sfree(self, *names):
        pend = list(self.pending)
        for name in names:
            off, size, _, _ = self.sb_allocs.pop(name)
            self.sb_free.append([off, off + size])
            pref = name + "."
            for t in list(self.tok_w):
                if t == name or t.startswith(pref):
                    pend.append(self.tok_w.pop(t))
            for t in list(self.tok_r):
                if t == name or t.startswith(pref):
                    pend.extend(self.tok_r.pop(t))
        self.sb_free.sort()
        merged = []
        for seg in self.sb_free:
            if merged and merged[-1][1] >= seg[0]:
                merged[-1][1] = max(merged[-1][1], seg[1])
            elif seg[1] > seg[0]:
                merged.append(seg)
        self.sb_free = merged
        self.pending = pend

    def barrier(self):
        self.pending = [s[-1] for s in self.streams.values() if s]

    def _record(self, op, reads, writes):
        deps = []
        for t in reads:
            w = self.tok_w.get(t)
            if w is not None:
                deps.append(w)
        for t in writes:
            w = self.tok_w.get(t)
            if w is not None:
                deps.append(w)
            deps.extend(self.tok_r.get(t, ()))
            al = self.sb_allocs.get(t.split(".", 1)[0])
            if al is not None and al[3]:
                deps.extend(al[3])
        for d in deps:
            if d is op:
                continue
            if (d.stream == op.stream and not d.is_dma and not op.is_dma
                    and op.stream == "tensor"):
                continue
            op.deps.append(d)
            d.need_inc = True
        for t in reads:
            self.tok_r.setdefault(t, []).append(op)
        for t in writes:
            self.tok_w[t] = op
            self.tok_r[t] = []
        self.streams[op.stream].append(op)
        return op

    def op(self, stream, emit, reads=(), writes=(), tag=""):
        return self._record(_Op(stream, emit, False, tag), reads, writes)

    def dma(self, queue, out_ap, in_ap, reads=(), writes=(), transpose=False,
            tag=""):
        if transpose:
            emit = lambda e: e.dma_start_transpose(out_ap, in_ap)
        else:
            emit = lambda e: e.dma_start(out=out_ap, in_=in_ap)
        op = _Op(queue, emit, True, tag)
        ops = self.dma_ops.setdefault(queue, [])
        j = len(ops)
        if queue not in self.dma_sems:
            self.dma_sems[queue] = [
                self.nc.alloc_semaphore(f"dq_{queue}{i}")
                for i in range(_DMA_POOL)]
        op.dma_sem = self.dma_sems[queue][j % _DMA_POOL]
        op.dma_use = j // _DMA_POOL
        op.need_inc = True
        if j >= _DMA_POOL:
            op.deps.append(ops[j - _DMA_POOL])
        ops.append(op)
        return self._record(op, reads, writes)

    def _completion(self, op):
        if op.is_dma:
            return op.dma_sem, 16 * (op.dma_use + 1)
        return self.comp_sem[op.stream], op.count

    def finalize(self):
        for e in _ENGS:
            c = 0
            for op in self.streams[e]:
                if not op.is_dma and op.need_inc:
                    c += 1
                    op.count = c
        known = {e: {} for e in _ENGS}

        def section(eng_name):
            def run(eng):
                kn = known[eng_name]
                for op in self.streams[eng_name]:
                    best = {}
                    for d in op.deps:
                        sem, val = self._completion(d)
                        if val > best.get(id(sem), (None, -1))[1]:
                            best[id(sem)] = (sem, val)
                    for sem, val in best.values():
                        if kn.get(id(sem), -1) >= val:
                            continue
                        eng.wait_ge(sem, val)
                        kn[id(sem)] = val
                    inst = op.emit(eng)
                    if op.need_inc:
                        if op.is_dma:
                            inst.then_inc(op.dma_sem, 16)
                        else:
                            inst.then_inc(self.comp_sem[op.stream], 1)
                if eng_name == "sync":
                    for q, ops in self.dma_ops.items():
                        fin = {}
                        for op in ops:
                            fin[id(op.dma_sem)] = (op.dma_sem,
                                                   16 * (op.dma_use + 1))
                        for sem, val in fin.values():
                            if kn.get(id(sem), -1) < val:
                                eng.wait_ge(sem, val)
                                kn[id(sem)] = val
                    for e2 in _ENGS:
                        if e2 == "sync":
                            continue
                        cnt = [o.count for o in self.streams[e2]
                               if not o.is_dma and o.need_inc]
                        if cnt and kn.get(id(self.comp_sem[e2]), -1) < cnt[-1]:
                            eng.wait_ge(self.comp_sem[e2], cnt[-1])
            return run

        with self.nc.Block() as block:
            if self.streams["tensor"]:
                block.tensor(section("tensor"))
            if self.streams["vector"]:
                block.vector(section("vector"))
            if self.streams["scalar"]:
                block.scalar(section("scalar"))
            if self.streams["gpsimd"]:
                block.gpsimd(section("gpsimd"))
            block.sync(section("sync"))



def _host_tables():
    inv = 1.0 / (10000.0 ** (np.arange(0, RD, 2, dtype=np.float64) / RD))
    pos = np.arange(S, dtype=np.float64)
    fr = inv[:, None] * pos[None, :]
    c64 = np.concatenate([np.cos(fr), np.cos(fr)], 0)
    s64 = np.concatenate([-np.sin(fr), np.sin(fr)], 0)
    ropeC = np.concatenate([c64, c64], 0).astype(np.float32)
    ropeS = np.concatenate([s64, s64], 0).astype(np.float32)
    qi = np.arange(512)[None, None, :]
    p = np.arange(128)[None, :, None]
    m = np.arange(4)[:, None, None]
    masks = (qi >= 128 * m + p).astype(np.float32).astype(BF)
    return ropeC, ropeS, masks



def _alloc_ln_scratch(mk, pref):
    sc = {}
    for n in ("sum", "s2", "nmean", "ex2", "nvar", "lnv", "rstd", "b"):
        sc[n] = mk.sbuf(f"{pref}{n}", [128, 1], F32)
    sc["eps"] = mk.sbuf(f"{pref}eps", [128, 1], F32)
    sc["_p"] = pref
    mk.op("vector", lambda e: e.memset(sc["eps"][:], 1e-5),
          writes=[f"{pref}eps"])
    return sc


def _ln_chain(mk, xt, xt_toks, n_elems, junk, out_bf, out_tok, sc):
    p = sc["_p"]
    inv = 1.0 / n_elems
    mk.op("vector", lambda e: e.reduce_sum(sc["sum"][:], xt[:], axis=AX.X),
          reads=xt_toks, writes=[f"{p}sum"])
    mk.op("scalar", lambda e: e.activation(out=junk[:], in_=xt[:],
          func=AF.Square, accum_out=sc["s2"][:, 0:1]),
          reads=xt_toks, writes=[f"{p}s2", f"{p}junk"])
    mk.op("vector", lambda e: e.tensor_scalar_mul(sc["nmean"][:], sc["sum"][:],
          -inv), reads=[f"{p}sum"], writes=[f"{p}nmean"])
    mk.op("vector", lambda e: e.tensor_scalar_mul(sc["ex2"][:], sc["s2"][:],
          inv), reads=[f"{p}s2"], writes=[f"{p}ex2"])
    mk.op("vector", lambda e: e.tensor_scalar(sc["nvar"][:], sc["nmean"][:],
          sc["nmean"][:, 0:1], sc["ex2"][:, 0:1], ALU.mult, ALU.subtract),
          reads=[f"{p}nmean", f"{p}ex2"], writes=[f"{p}nvar"])
    mk.op("scalar", lambda e: e.activation(out=sc["lnv"][:], in_=sc["nvar"][:],
          func=AF.Ln, bias=sc["eps"][:, 0:1], scale=-1.0),
          reads=[f"{p}nvar", f"{p}eps"], writes=[f"{p}lnv"])
    mk.op("scalar", lambda e: e.activation(out=sc["rstd"][:], in_=sc["lnv"][:],
          func=AF.Exp, scale=-0.5),
          reads=[f"{p}lnv"], writes=[f"{p}rstd"])
    mk.op("vector", lambda e: e.tensor_scalar_mul(sc["b"][:], sc["nmean"][:],
          sc["rstd"][:, 0:1]),
          reads=[f"{p}nmean", f"{p}rstd"], writes=[f"{p}b"])
    mk.op("vector", lambda e: e.tensor_scalar(out_bf[:], xt[:],
          sc["rstd"][:, 0:1], sc["b"][:, 0:1], ALU.mult, ALU.add),
          reads=list(xt_toks) + [f"{p}rstd", f"{p}b"], writes=[out_tok])


def _ph_ln(mk, T, src, dst, nrows, pref):
    sc = _alloc_ln_scratch(mk, pref)
    xt = [mk.sbuf(f"{pref}x{i}", [128, D], F32) for i in range(2)]
    junk = mk.sbuf(f"{pref}junk", [128, D], BF16)
    ob = [mk.sbuf(f"{pref}o{i}", [128, D], BF16) for i in range(2)]
    for i in range(nrows // 128):
        sl = i % 2
        mk.dma("sync", xt[sl][:], T[src][i * 128:(i + 1) * 128, :],
               reads=[src], writes=[f"{pref}x{sl}"])
        _ln_chain(mk, xt[sl], [f"{pref}x{sl}"], D, junk, ob[sl],
                  f"{pref}o{sl}", sc)
        mk.dma("sync", T[dst][i * 128:(i + 1) * 128, :], ob[sl][:],
               reads=[f"{pref}o{sl}"], writes=[f"{dst}.{i}"])
    mk.sfree(*[f"{pref}{n}" for n in
               ("sum", "s2", "nmean", "ex2", "nvar", "lnv", "rstd", "b",
                "eps", "junk", "x0", "x1", "o0", "o1")])


def _tm_tok(dst, i0, n):
    return [f"{dst}.{i}" for i in range(i0 // 128, (i0 + n + 127) // 128)]



def _ph_l0_proj(mk, T):
    nc = mk.nc
    wq = mk.sbuf("wq", [128, 16 * 1024], BF16)
    wqr = mk.sbuf("wqr", [128, 16 * 512], BF16)
    wkvd = mk.sbuf("wkvd", [128, 16 * 512], BF16)
    wkr = mk.sbuf("wkr", [128, 16 * 64], BF16)
    for dc in range(16):
        mk.dma("gpsimd", wq[:, dc * 1024:(dc + 1) * 1024],
               T["Wq_h"][dc * 128:(dc + 1) * 128, :],
               reads=["Wq_h"], writes=[f"wq.{dc}"])
        mk.dma("gpsimd", wqr[:, dc * 512:(dc + 1) * 512],
               T["Wqr_h"][dc * 128:(dc + 1) * 128, :],
               reads=["Wqr_h"], writes=[f"wqr.{dc}"])
        mk.dma("gpsimd", wkvd[:, dc * 512:(dc + 1) * 512],
               T["Wkvd"][dc * 128:(dc + 1) * 128, :],
               reads=["Wkvd"], writes=[f"wkvd.{dc}"])
        mk.dma("gpsimd", wkr[:, dc * 64:(dc + 1) * 64],
               T["Wkr"][dc * 128:(dc + 1) * 128, :],
               reads=["Wkr"], writes=[f"wkr.{dc}"])
    ropeC = mk.sbuf("ropeC", [128, S], F32)
    ropeS = mk.sbuf("ropeS", [128, S], F32)
    mk.dma("sync", ropeC[:], T["ropeC"][:, :], reads=["ropeC_in"],
           writes=["ropeC"])
    mk.dma("sync", ropeS[:], T["ropeS"][:, :], reads=["ropeS_in"],
           writes=["ropeS"])
    rx = [mk.sbuf(f"rx{i}", [128, 512], F32) for i in range(2)]
    rr = [mk.sbuf(f"rr{i}", [128, 512], F32) for i in range(2)]
    rt = [mk.sbuf(f"rt{i}", [128, 512], F32) for i in range(2)]
    ro = [mk.sbuf(f"ro{i}", [128, 512], BF16) for i in range(2)]
    krT = mk.sbuf("krT", [64, S], BF16)
    T["_krT"] = krT

    def rope_block(src_, stok, r0, nrow, t0, dst_ap, dtok, sl):
        xa, ra, ta, oa = rx[sl], rr[sl], rt[sl], ro[sl]
        rts = [f"rr{sl}.{g}{ab}" for g in range(nrow // 64) for ab in "ab"]
        mk.dma("sync", xa[0:nrow, :], src_[r0:r0 + nrow, t0:t0 + 512],
               reads=[stok], writes=[f"rx{sl}"])
        for g in range(nrow // 64):
            mk.dma("sync", ra[g * 64:g * 64 + 32, :],
                   src_[r0 + g * 64 + 32:r0 + g * 64 + 64, t0:t0 + 512],
                   reads=[stok], writes=[f"rr{sl}.{g}a"])
            mk.dma("sync", ra[g * 64 + 32:g * 64 + 64, :],
                   src_[r0 + g * 64:r0 + g * 64 + 32, t0:t0 + 512],
                   reads=[stok], writes=[f"rr{sl}.{g}b"])
        mk.op("vector", lambda e: e.tensor_mul(ta[0:nrow, :], xa[0:nrow, :],
              ropeC[0:nrow, t0:t0 + 512]),
              reads=[f"rx{sl}", "ropeC"], writes=[f"rt{sl}"])
        mk.op("vector", lambda e: e.tensor_mul(ra[0:nrow, :], ra[0:nrow, :],
              ropeS[0:nrow, t0:t0 + 512]),
              reads=rts + ["ropeS"], writes=rts)
        mk.op("vector", lambda e: e.tensor_add(oa[0:nrow, :], ta[0:nrow, :],
              ra[0:nrow, :]),
              reads=[f"rt{sl}"] + rts, writes=[f"ro{sl}"])
        mk.dma("sync", dst_ap, oa[0:nrow, :],
               reads=[f"ro{sl}"], writes=[dtok])

    ckvT = mk.sbuf("ckvT", [128, 4 * S], BF16)
    xnT = [mk.sbuf(f"pxnT{i}", [128, 16 * 512], BF16) for i in range(2)]
    ev = [mk.sbuf(f"pev{i}", [128, 512], BF16) for i in range(4)]
    evf = [mk.sbuf(f"pevf{i}", [128, 512], F32) for i in range(2)]
    pi = [0]

    def mm_chain(n_k, lhsT_of, rhs_of, rtoks, mrows=128):
        bank = pi[0] % 2
        pi[0] += 1
        ptok = f"ps.{bank}"
        for k in range(n_k):
            mk.op("tensor",
                  (lambda e, k=k, bank=bank: e.matmul(
                      mk.ps[bank][0:mrows, :], lhsT_of(k), rhs_of(k),
                      start=(k == 0), stop=(k == n_k - 1))),
                  reads=rtoks(k), writes=[ptok])
        return mk.ps[bank], ptok

    for tb in range(8):
        t0 = tb * 512
        sl = tb % 2
        for dc in range(16):
            mk.dma("sync", xnT[sl][:, dc * 512:(dc + 1) * 512],
                   T["xn0_tm"][t0:t0 + 512, dc * 128:(dc + 1) * 128],
                   reads=_tm_tok("xn0_tm", t0, 512),
                   writes=[f"pxnT{sl}.{dc}"], transpose=True)
        for oc in range(4):
            p, ptok = mm_chain(
                16,
                lambda k, oc=oc: wkvd[:, k * 512 + oc * 128:
                                      k * 512 + (oc + 1) * 128],
                lambda k, sl=sl: xnT[sl][:, k * 512:(k + 1) * 512],
                lambda k, sl=sl: [f"wkvd.{k}", f"pxnT{sl}.{k}"])
            mk.op("vector",
                  (lambda e, p=p, oc=oc, t0=t0: e.tensor_copy(
                      ckvT[:, oc * S + t0:oc * S + t0 + 512], p[:])),
                  reads=[ptok], writes=[f"ckvT.{oc}.{tb}"])
        p, ptok = mm_chain(
            16,
            lambda k: wkr[:, k * 64:(k + 1) * 64],
            lambda k, sl=sl: xnT[sl][:, k * 512:(k + 1) * 512],
            lambda k, sl=sl: [f"wkr.{k}", f"pxnT{sl}.{k}"], mrows=64)
        fs = tb % 2
        mk.op("vector",
              (lambda e, p=p, fs=fs: e.tensor_copy(
                  evf[fs][0:64, :], p[0:64, :])),
              reads=[ptok], writes=[f"pevf{fs}"])
        mk.dma("sync", T["krraw"][0:64, t0:t0 + 512], evf[fs][0:64, :],
               reads=[f"pevf{fs}"], writes=[f"krraw.{tb}"])
        for oc in range(8):
            p, ptok = mm_chain(
                16,
                lambda k, oc=oc: wq[:, k * 1024 + oc * 128:
                                    k * 1024 + (oc + 1) * 128],
                lambda k, sl=sl: xnT[sl][:, k * 512:(k + 1) * 512],
                lambda k, sl=sl: [f"wq.{k}", f"pxnT{sl}.{k}"])
            es = oc % 4
            mk.op("vector",
                  (lambda e, p=p, es=es: e.tensor_copy(ev[es][:], p[:])),
                  reads=[ptok], writes=[f"pev{es}"])
            mk.dma("sync", T["qcT"][oc * 128:(oc + 1) * 128, t0:t0 + 512],
                   ev[es][:], reads=[f"pev{es}"], writes=[f"qcT.{oc}.{tb}"])
        for oc in range(4):
            p, ptok = mm_chain(
                16,
                lambda k, oc=oc: wqr[:, k * 512 + oc * 128:
                                     k * 512 + (oc + 1) * 128],
                lambda k, sl=sl: xnT[sl][:, k * 512:(k + 1) * 512],
                lambda k, sl=sl: [f"wqr.{k}", f"pxnT{sl}.{k}"])
            fs = oc % 2
            mk.op("vector",
                  (lambda e, p=p, fs=fs: e.tensor_copy(evf[fs][:], p[:])),
                  reads=[ptok], writes=[f"pevf{fs}"])
            mk.dma("sync", T["qrraw"][oc * 128:(oc + 1) * 128, t0:t0 + 512],
                   evf[fs][:], reads=[f"pevf{fs}"],
                   writes=[f"qrraw.{oc}.{tb}"])
        for r in range(4):
            sl = (tb * 4 + r) % 2
            rope_block(T["qrraw"], f"qrraw.{r}.{tb}", r * 128, 128, t0,
                       T["qrT"][r * 128:(r + 1) * 128, t0:t0 + 512],
                       f"qrT.{r}.{tb}", sl)
        rope_block(T["krraw"], f"krraw.{tb}", 0, 64, t0,
                   krT[0:64, t0:t0 + 512], f"krT.{tb}", tb % 2)
    mk.sfree("wq", "wqr", "wkvd", "wkr", "ropeC", "ropeS", "rx0", "rx1",
             "rr0", "rr1", "rt0", "rt1", "ro0", "ro1")
    wkvu = mk.sbuf("wkvu", [128, 4 * 2048], BF16)
    for cc in range(4):
        mk.dma("gpsimd", wkvu[:, cc * 2048:(cc + 1) * 2048],
               T["Wkvup_h"][cc * 128:(cc + 1) * 128, :],
               reads=["Wkvup_h"], writes=[f"wkvu.{cc}"])
    for tb in range(8):
        t0 = tb * 512
        for oc in range(8):
            p, ptok = mm_chain(
                4,
                lambda k, oc=oc: wkvu[:, k * 2048 + oc * 128:
                                      k * 2048 + (oc + 1) * 128],
                lambda k, t0=t0: ckvT[:, k * S + t0:k * S + t0 + 512],
                lambda k, tb=tb: [f"wkvu.{k}", f"ckvT.{k}.{tb}"])
            es = oc % 4
            mk.op("vector",
                  (lambda e, p=p, es=es: e.tensor_copy(ev[es][:], p[:])),
                  reads=[ptok], writes=[f"pev{es}"])
            mk.dma("sync", T["kcT"][oc * 128:(oc + 1) * 128, t0:t0 + 512],
                   ev[es][:], reads=[f"pev{es}"], writes=[f"kcT.{oc}.{tb}"])
    for tc in range(S // 128):
        for dvb in range(2):
            p, ptok = mm_chain(
                4,
                lambda k, tc=tc: ckvT[:, k * S + tc * 128:k * S + (tc+1)*128],
                lambda k, dvb=dvb: wkvu[:, k * 2048 + 1024 + dvb * 512:
                                        k * 2048 + 1024 + (dvb + 1) * 512],
                lambda k, tc=tc: [f"wkvu.{k}", f"ckvT.{k}.{tc // 4}"])
            es = dvb % 4
            mk.op("vector",
                  (lambda e, p=p, es=es: e.tensor_copy(ev[es][:], p[:])),
                  reads=[ptok], writes=[f"pev{es}"])
            mk.dma("sync", T["v_tm"][tc * 128:(tc + 1) * 128,
                                     dvb * 512:(dvb + 1) * 512],
                   ev[es][:], reads=[f"pev{es}"], writes=[f"v_tm.{tc}.{dvb}"])
    mk.sfree("wkvu", "ckvT", "pxnT0", "pxnT1", "pev0", "pev1", "pev2",
             "pev3", "pevf0", "pevf1")
def _ph_l0_attn(mk, T, filler=None):
    krT = T["_krT"]
    mask = mk.sbuf("mask", [128, 4 * 512], BF16)
    for m in range(4):
        mk.dma("sync", mask[:, m * 512:(m + 1) * 512], T["masks"][m, :, :],
               reads=["masks_in"], writes=[f"mask.{m}"])
    sc192 = mk.sbuf("sc192", [128, 1], F32)
    mk.op("vector", lambda e: e.memset(sc192[:], SCALE0), writes=["sc192"])
    ones_b = mk.sbuf("ones_b", [128, 1], BF16)
    mk.op("vector", lambda e: e.memset(ones_b[:], 1.0), writes=["ones_b"])
    ones_f = mk.sbuf("ones_f", [1, 128], F32)
    mk.op("vector", lambda e: e.memset(ones_f[:], 1.0), writes=["ones_f"])

    qc = [mk.sbuf(f"aqc{i}", [128, 512], BF16) for i in range(3)]
    qr = [mk.sbuf(f"aqr{i}", [64, 512], BF16) for i in range(3)]
    kb = [mk.sbuf(f"akb{i}", [128, S], BF16) for i in range(2)]
    vb = [mk.sbuf(f"avb{i}", [128, S], BF16) for i in range(2)]
    pt = [mk.sbuf(f"apt{i}", [128, 512], BF16) for i in range(3)]
    dens = [mk.sbuf(f"adens{i}", [1, 512], F32) for i in range(2)]
    denr = [mk.sbuf(f"adenr{i}", [1, 512], F32) for i in range(2)]
    bcs = mk.sbuf("abcs", [128, 512], F32)
    onb = mk.sbuf("aonb", [128, 512], BF16)
    psS = [mk.ps[0], mk.ps[1], mk.ps[6]]
    psSt = ["ps.0", "ps.1", "ps.6"]
    psO = [mk.ps[2], mk.ps[3]]
    psD = [mk.ps[4][0:1, :], mk.ps[5][0:1, :]]
    psB = mk.ps[7]
    pend = []

    def epilogue():
        if not pend:
            return
        h, qb, par = pend.pop(0)
        q0 = qb * 512
        mk.op("vector", (lambda e, par=par: e.tensor_copy(
                  dens[par][:], psD[par][:])),
              reads=[f"ps.{4 + par}"], writes=[f"adens{par}"])
        mk.op("vector", (lambda e, par=par: e.reciprocal(
                  denr[par][:], dens[par][:])),
              reads=[f"adens{par}"], writes=[f"adenr{par}"])
        mk.op("tensor", (lambda e, par=par: e.matmul(
                  psB[:], ones_f[:], denr[par][:], start=True, stop=True)),
              reads=["ones_f", f"adenr{par}"], writes=["ps.7"])
        mk.op("vector", lambda e: e.tensor_copy(bcs[:], psB[:]),
              reads=["ps.7"], writes=["abcs"])
        mk.op("vector", (lambda e, par=par: e.tensor_mul(
                  onb[:], psO[par][:], bcs[:])),
              reads=[f"ps.{2 + par}", "abcs"], writes=["aonb"])
        mk.dma("sync", T["onormT"][h * 128:(h + 1) * 128, q0:q0 + 512],
               onb[:], reads=["aonb"], writes=[f"onormT.{h}.{qb}"])

    for h in range(8):
        sh = h % 2
        mk.dma("sync", kb[sh][:],
               T["kcT"][h * 128:(h + 1) * 128, :],
               reads=[f"kcT.{h}.{g}" for g in range(8)],
               writes=[f"akb{sh}"])
        for g4 in range(8):
            vsrc = T["v_tm"][g4 * 512:(g4 + 1) * 512,
                             h * 128:(h + 1) * 128]
            vsrc = vsrc.rearrange("(g r) c -> r g c", g=4)
            mk.dma("sync", vb[sh][:, g4 * 512:(g4 + 1) * 512], vsrc,
                   reads=[f"v_tm.{g4 * 4 + i}.{1}" for i in range(4)]
                   + [f"v_tm.{g4 * 4 + i}.{0}" for i in range(4)],
                   writes=[f"avb{sh}.{g4}"])
        for qb in range(8):
            q0 = qb * 512
            nk = 4 * qb + 4
            par = qb % 2
            sq = qb % 3
            mk.dma("scalar", qc[sq][:],
                   T["qcT"][h * 128:(h + 1) * 128, q0:q0 + 512],
                   reads=[f"qcT.{h}.{qb}"], writes=[f"aqc{sq}"])
            mk.dma("scalar", qr[sq][:],
                   T["qrT"][h * 64:(h + 1) * 64, q0:q0 + 512],
                   reads=[f"qrT.{h // 2}.{qb}"], writes=[f"aqr{sq}"])
            for k in range(nk):
                sl = k % 3
                mk.op("tensor", (lambda e, sl=sl, sh=sh, sq=sq, k=k:
                      e.matmul(
                          psS[sl][:], kb[sh][:, k * 128:(k + 1) * 128],
                          qc[sq][:], start=True, stop=False)),
                      reads=[f"akb{sh}", f"aqc{sq}"], writes=[psSt[sl]])
                mk.op("tensor", (lambda e, sl=sl, sq=sq, k=k: e.matmul(
                          psS[sl][:], krT[0:64, k * 128:(k + 1) * 128],
                          qr[sq][:], start=False, stop=True)),
                      reads=[f"krT.{k // 4}", f"aqr{sq}"], writes=[psSt[sl]])
                mk.op("scalar", (lambda e, sl=sl: e.activation(
                          out=pt[sl][:], in_=psS[sl][:], func=AF.Exp,
                          scale=sc192[:, 0:1])),
                      reads=[psSt[sl], "sc192"], writes=[f"apt{sl}"])
                m = k - 4 * qb
                if m >= 0:
                    mk.op("vector", (lambda e, sl=sl, m=m: e.tensor_mul(
                              pt[sl][:], pt[sl][:],
                              mask[:, m * 512:(m + 1) * 512])),
                          reads=[f"apt{sl}", f"mask.{m}"],
                          writes=[f"apt{sl}"])
                mk.op("tensor", (lambda e, sl=sl, sh=sh, k=k, nk=nk,
                      par=par: e.matmul(psO[par][:],
                               vb[sh][:, k * 128:(k + 1) * 128], pt[sl][:],
                               start=(k == 0), stop=(k == nk - 1))),
                      reads=[f"avb{sh}.{k // 4}", f"apt{sl}"],
                      writes=[f"ps.{2 + par}"])
                mk.op("tensor", (lambda e, sl=sl, k=k, nk=nk, par=par:
                      e.matmul(psD[par][:], ones_b[:], pt[sl][:],
                               start=(k == 0), stop=(k == nk - 1))),
                      reads=["ones_b", f"apt{sl}"], writes=[f"ps.{4 + par}"])
            pend.append((h, qb, par))
            if len(pend) > 1:
                epilogue()
            if filler is not None:
                filler(h * 8 + qb)
    while pend:
        epilogue()
    mk.sfree("mask", "sc192", "ones_b", "ones_f", "abcs", "aonb",
             "aqc0", "aqc1", "aqc2", "aqr0", "aqr1", "aqr2", "akb0", "akb1", "avb0", "avb1",
             "adens0", "adens1", "adenr0", "adenr1", "apt0", "apt1", "apt2",
             "krT")


def _ph_l0_wo(mk, T, cc=False):
    woh = mk.sbuf("woh", [128, 8 * 2048], BF16)
    for r in range(8):
        mk.dma("gpsimd", woh[:, r * 2048:(r + 1) * 2048],
               T["Wo_h"][r * 128:(r + 1) * 128, :],
               reads=["Wo_h"], writes=[f"woh.{r}"])
    onb = [mk.sbuf(f"won{i}", [128, 8 * 512], BF16) for i in range(2)]
    ev = [mk.sbuf(f"wev{i}", [128, 512], BF16) for i in range(2)]
    for tb in range(8):
        t0 = tb * 512
        sl = tb % 2
        for r in range(8):
            mk.dma("sync", onb[sl][:, r * 512:(r + 1) * 512],
                   T["onormT"][r * 128:(r + 1) * 128, t0:t0 + 512],
                   reads=[f"onormT.{r}.{tb}"], writes=[f"won{sl}.{r}"])
        for do in range(16):
            bank = do % 2
            for r in range(8):
                mk.op("tensor", (lambda e, bank=bank, r=r, do=do, sl=sl:
                      e.matmul(mk.ps[bank][:],
                               woh[:, r * 2048 + do * 128:
                                   r * 2048 + (do + 1) * 128],
                               onb[sl][:, r * 512:(r + 1) * 512],
                               start=(r == 0), stop=(r == 7))),
                      reads=[f"woh.{r}", f"won{sl}.{r}"],
                      writes=[f"ps.{bank}"])
            mk.op("vector", (lambda e, bank=bank: e.tensor_copy(
                      ev[bank][:], mk.ps[bank][:])),
                  reads=[f"ps.{bank}"], writes=[f"wev{bank}"])
            if cc:
                jblk, tq = tb // 4, tb % 4
                dst = T["ccA_in"][jblk * D + do * 128:jblk * D + (do + 1) * 128,
                                  tq * 512:(tq + 1) * 512]
                dtok = f"ccA_in.{do}.{tb}"
            else:
                dst = T["a0pT"][do * 128:(do + 1) * 128, t0:t0 + 512]
                dtok = f"a0pT.{do}.{tb}"
            mk.dma("sync", dst, ev[bank][:], reads=[f"wev{bank}"],
                   writes=[dtok])
    mk.sfree("woh", "won0", "won1", "wev0", "wev1")



def _ph_quant_p1(mk, T, wname, nt, ncols, nb=2):
    p = f"q{wname}"
    wt = [mk.sbuf(f"{p}w{i}", [128, ncols], F32) for i in range(nb)]
    acc = mk.sbuf(f"{p}acc", [128, 64], F32)
    accv = mk.sbuf(f"{p}accv", [128, 1], F32)
    ones128 = mk.sbuf(f"{p}o128", [128, 1], F32)
    ones1 = mk.sbuf(f"{p}o1", [1, 128], F32)
    cpos = mk.sbuf(f"{p}cp", [1, 1], F32)
    cneg = mk.sbuf(f"{p}cn", [1, 1], F32)
    trow = mk.sbuf(f"{p}tr", [1, 2], F32)
    tpos = mk.sbuf(f"{p}tp", [128, 1], F32)
    tneg = mk.sbuf(f"{p}tn", [128, 1], F32)
    numel = nt * 128 * ncols
    mk.op("vector", lambda e: e.memset(ones128[:], 1.0), writes=[f"{p}o128"])
    mk.op("vector", lambda e: e.memset(ones1[:], 1.0), writes=[f"{p}o1"])
    mk.op("vector", lambda e: e.memset(cpos[:], 0.7 / numel),
          writes=[f"{p}cp"])
    mk.op("vector", lambda e: e.memset(cneg[:], -0.7 / numel),
          writes=[f"{p}cn"])
    for i in range(nt):
        sl = i % nb
        mk.dma("gpsimd", wt[sl][:], T[wname][i * 128:(i + 1) * 128, :],
               reads=[wname], writes=[f"{p}w{sl}"])
        mk.op("vector", (lambda e, sl=sl, i=i: e.reduce_sum(
                  acc[:, i:i + 1], wt[sl][:], axis=AX.X,
                  apply_absolute_value=True)),
              reads=[f"{p}w{sl}"], writes=[f"{p}acc.{i}"])
    mk.op("vector", lambda e: e.reduce_sum(accv[:], acc[:, 0:nt], axis=AX.X),
          reads=[f"{p}acc.{i}" for i in range(nt)], writes=[f"{p}accv"])
    T[f"_{p}tp"], T[f"_{p}tn"] = tpos, tneg
    T[f"_{p}p1b"] = (accv, ones128, ones1, cpos, cneg, trow, tpos, tneg)
    s1 = [mk.sbuf(f"{p}s1{i}", [128, ncols], BF16) for i in range(nb)]
    s2 = [mk.sbuf(f"{p}s2{i}", [128, ncols], BF16) for i in range(nb)]
    wq = [mk.sbuf(f"{p}wq{i}", [128, ncols], BF16) for i in range(nb)]
    T[f"_{p}bufs"] = (wt, s1, s2, wq)


def _ph_quant_p1b(mk, T, wname):
    p = f"q{wname}"
    accv, ones128, ones1, cpos, cneg, trow, tpos, tneg = T[f"_{p}p1b"]
    psT = mk.ps[7][0:1, 0:1]
    mk.op("tensor", lambda e: e.matmul(psT, ones128[:], accv[:],
          start=True, stop=True),
          reads=[f"{p}o128", f"{p}accv"], writes=["ps.7"])
    mk.op("vector", lambda e: e.tensor_scalar_mul(trow[0:1, 0:1], psT,
          cpos[0:1, 0:1]),
          reads=["ps.7", f"{p}cp"], writes=[f"{p}tr"])
    mk.op("vector", lambda e: e.tensor_scalar_mul(trow[0:1, 1:2], psT,
          cneg[0:1, 0:1]),
          reads=["ps.7", f"{p}cn", f"{p}tr"], writes=[f"{p}tr"])
    psBC = mk.ps[7][:, 0:2]
    mk.op("tensor", lambda e: e.matmul(psBC, ones1[:], trow[0:1, :],
          start=True, stop=True),
          reads=[f"{p}o1", f"{p}tr"], writes=["ps.7"])
    mk.op("vector", lambda e: e.tensor_copy(tpos[:], mk.ps[7][:, 0:1]),
          reads=["ps.7"], writes=[f"{p}tp"])
    mk.op("vector", lambda e: e.tensor_copy(tneg[:], mk.ps[7][:, 1:2]),
          reads=["ps.7"], writes=[f"{p}tn"])
    mk.sfree(f"{p}acc", f"{p}accv", f"{p}o128", f"{p}o1", f"{p}cp",
             f"{p}cn", f"{p}tr")


def _quant_p2_tile(mk, T, wname, qname, i):
    p = f"q{wname}"
    wt, s1, s2, wq = T[f"_{p}bufs"]
    tpos, tneg = T[f"_{p}tp"], T[f"_{p}tn"]
    nb = len(wt)
    sl = i % nb
    mk.dma("gpsimd", wt[sl][:], T[wname][i * 128:(i + 1) * 128, :],
           reads=[wname], writes=[f"{p}w{sl}"])
    mk.op("scalar", (lambda e, sl=sl: e.activation(
              out=s1[sl][:], in_=wt[sl][:], func=AF.Sign,
              bias=tneg[:, 0:1])),
          reads=[f"{p}w{sl}", f"{p}tn"], writes=[f"{p}s1{sl}"])
    mk.op("scalar", (lambda e, sl=sl: e.activation(
              out=s2[sl][:], in_=wt[sl][:], func=AF.Sign,
              bias=tpos[:, 0:1])),
          reads=[f"{p}w{sl}", f"{p}tp"], writes=[f"{p}s2{sl}"])
    mk.op("vector", (lambda e, sl=sl: e.tensor_add(
              wq[sl][:], s1[sl][:], s2[sl][:])),
          reads=[f"{p}s1{sl}", f"{p}s2{sl}"], writes=[f"{p}wq{sl}"])
    mk.dma("gpsimd", T[qname][i * 128:(i + 1) * 128, :], wq[sl][:],
           reads=[f"{p}wq{sl}"], writes=[f"{qname}.{i}"])


def _ph_quant_free(mk, T, wname):
    p = f"q{wname}"
    nb = len(T[f"_{p}bufs"][0])
    names = []
    for i in range(nb):
        names += [f"{p}w{i}", f"{p}s1{i}", f"{p}s2{i}", f"{p}wq{i}"]
    mk.sfree(f"{p}tp", f"{p}tn", *names)


def _ph_quant(mk, T, wname, qname, nt, ncols):
    _ph_quant_p1(mk, T, wname, nt, ncols)
    _ph_quant_p1b(mk, T, wname)
    for i in range(nt):
        _quant_p2_tile(mk, T, wname, qname, i)
    _ph_quant_free(mk, T, wname)


def _ph_l0_ffn(mk, T):
    b1t = mk.sbuf("b1t", [128, 64], F32)
    b2t = mk.sbuf("b2t", [128, 16], F32)
    mk.dma("sync", b1t[:], T["b1t"][:, :], reads=["b1t_in"], writes=["b1t"])
    mk.dma("sync", b2t[:], T["b2t"][:, :], reads=["b2t_in"], writes=["b2t"])
    for th in range(2):
        tt0 = th * 1024
        xnT = mk.sbuf("fxnT", [128, 16 * 1024], BF16)
        hbuf = mk.sbuf("fh", [128, 64 * 1024], BF16)
        w1t = mk.sbuf("fw1t", [128, 16 * 512], BF16)
        w2t = [mk.sbuf(f"fw2t{i}", [128, 512], BF16) for i in range(2)]
        fev = [mk.sbuf(f"fev{i}", [128, 512], BF16) for i in range(2)]
        for dc in range(16):
            for tq in range(2):
                mk.dma("sync", xnT[:, dc * 1024 + tq * 512:
                                   dc * 1024 + (tq + 1) * 512],
                       T["xn0_own"][tt0 + tq * 512:tt0 + (tq + 1) * 512,
                                    dc * 128:(dc + 1) * 128],
                       reads=_tm_tok("xn0_own", tt0 + tq * 512, 512), writes=[f"fxnT.{dc}.{tq}"],
                       transpose=True)
        for fg in range(16):
            for dc in range(16):
                mk.dma("sync", w1t[:, dc * 512:(dc + 1) * 512],
                       T["w1q"][fg * 512:(fg + 1) * 512,
                                dc * 128:(dc + 1) * 128],
                       reads=[f"w1q.{fg * 4 + i}" for i in range(4)],
                       writes=[f"fw1t.{dc}"], transpose=True)
            for half in range(2):
                for dc in range(16):
                    for fl in range(2):
                        for tq in range(2):
                            pb = half * 4 + fl * 2 + tq
                            fc = fg * 4 + half * 2 + fl
                            mk.op("tensor",
                                  (lambda e, pb=pb, dc=dc, fl=fl, tq=tq,
                                   half=half: e.matmul(
                                      mk.ps[pb][:],
                                      w1t[:, dc * 512 + (half * 2 + fl) * 128:
                                          dc * 512 + (half * 2 + fl + 1) * 128],
                                      xnT[:, dc * 1024 + tq * 512:
                                          dc * 1024 + (tq + 1) * 512],
                                      start=(dc == 0), stop=(dc == 15))),
                                  reads=[f"fw1t.{dc}", f"fxnT.{dc}.{tq}"],
                                  writes=[f"ps.{pb}"])
                for fl in range(2):
                    for tq in range(2):
                        pb = half * 4 + fl * 2 + tq
                        fc = fg * 4 + half * 2 + fl
                        mk.op("scalar",
                              (lambda e, pb=pb, fc=fc, tq=tq: e.activation(
                                  out=hbuf[:, fc * 1024 + tq * 512:
                                           fc * 1024 + (tq + 1) * 512],
                                  in_=mk.ps[pb][:], func=AF.Gelu,
                                  bias=b1t[:, fc:fc + 1], scale=0.5)),
                              reads=[f"ps.{pb}", "b1t"],
                              writes=[f"fh.{fc}.{tq}"])
        for dg in range(4):
            for fc in range(64):
                sl = fc % 2
                mk.dma("sync", w2t[sl][:],
                       T["w2q"][dg * 512:(dg + 1) * 512,
                                fc * 128:(fc + 1) * 128],
                       reads=[f"w2q.{dg * 4 + i}" for i in range(4)],
                       writes=[f"fw2t{sl}"], transpose=True)
                for do4 in range(4):
                    for tq in range(2):
                        pb = do4 * 2 + tq
                        mk.op("tensor",
                              (lambda e, pb=pb, sl=sl, do4=do4, tq=tq, fc=fc:
                               e.matmul(
                                  mk.ps[pb][:],
                                  w2t[sl][:, do4 * 128:(do4 + 1) * 128],
                                  hbuf[:, fc * 1024 + tq * 512:
                                       fc * 1024 + (tq + 1) * 512],
                                  start=(fc == 0), stop=(fc == 63))),
                              reads=[f"fw2t{sl}", f"fh.{fc}.{tq}"],
                              writes=[f"ps.{pb}"])
            for do4 in range(4):
                for tq in range(2):
                    pb = do4 * 2 + tq
                    do = dg * 4 + do4
                    es = pb % 2
                    mk.op("vector",
                          (lambda e, pb=pb, do=do, es=es: e.tensor_scalar(
                              fev[es][:], mk.ps[pb][:], 0.5,
                              b2t[:, do:do + 1], ALU.mult, ALU.add)),
                          reads=[f"ps.{pb}", "b2t"], writes=[f"fev{es}"])
                    mk.dma("sync",
                           T["f0T"][do * 128:(do + 1) * 128,
                                    tt0 + tq * 512:tt0 + (tq + 1) * 512],
                           fev[es][:], reads=[f"fev{es}"],
                           writes=[f"f0T.{do}.{th * 2 + tq}"])
        mk.sfree("fxnT", "fh", "fw1t", "fw2t0", "fw2t1", "fev0", "fev1")
    mk.sfree("b1t", "b2t")


def _ph_x1_ln1(mk, T, fused=False):
    sc = [_alloc_ln_scratch(mk, f"l1n{s}") for s in range(2)]
    x0t = [mk.sbuf(f"x0t{s}", [128, D], F32) for s in range(2)]
    ab = [mk.sbuf(f"x1a{s}", [128, D], BF16) for s in range(2)]
    bb = [mk.sbuf(f"x1b{s}", [128, D], BF16) for s in range(2)]
    fb = [mk.sbuf(f"x1f{s}", [128, D], BF16) for s in range(2)]
    x1t = [mk.sbuf(f"x1t{s}", [128, D], F32) for s in range(2)]
    junk = [mk.sbuf(f"x1junk{s}", [128, D], BF16) for s in range(2)]
    ob = [mk.sbuf(f"x1o{s}", [128, D], BF16) for s in range(2)]
    for i in range(TOK // 128):
        i0 = i * 128
        s = i % 2
        mk.dma("sync", x0t[s][:], T["x0_own"][i0:i0 + 128, :],
               reads=["x0_own"], writes=[f"x0t{s}"])
        for blk in range(4):
            if fused:
                mk.dma("sync", ab[s][:, blk * 512:(blk + 1) * 512],
                       T["ccA_out"][blk * 512:(blk + 1) * 512, i0:i0 + 128],
                       reads=["ccA_out"], writes=[f"x1a{s}.{blk}"],
                       transpose=True)
            else:
                mk.dma("sync", ab[s][:, blk * 512:(blk + 1) * 512],
                       T["a0s"][blk * 512:(blk + 1) * 512, i0:i0 + 128],
                       reads=["a0s"], writes=[f"x1a{s}.{blk}"],
                       transpose=True)
                mk.dma("sync", bb[s][:, blk * 512:(blk + 1) * 512],
                       T["a0p"][blk * 512:(blk + 1) * 512, i0:i0 + 128],
                       reads=["a0p"], writes=[f"x1b{s}.{blk}"],
                       transpose=True)
            mk.dma("sync", fb[s][:, blk * 512:(blk + 1) * 512],
                   T["f0T"][blk * 512:(blk + 1) * 512, i0:i0 + 128],
                   reads=[f"f0T.{blk * 4 + q}.{i // 4}" for q in range(4)],
                   writes=[f"x1f{s}.{blk}"], transpose=True)
        atoks = [f"x1a{s}.{b}" for b in range(4)]
        btoks = [f"x1b{s}.{b}" for b in range(4)]
        ftoks = [f"x1f{s}.{b}" for b in range(4)]
        if fused:
            mk.op("vector", (lambda e, s=s: e.tensor_add(
                      x1t[s][:], ab[s][:], fb[s][:])),
                  reads=atoks + ftoks, writes=[f"x1t{s}"])
        else:
            mk.op("vector", (lambda e, s=s: e.tensor_add(
                      x1t[s][:], ab[s][:], bb[s][:])),
                  reads=atoks + btoks, writes=[f"x1t{s}"])
            mk.op("vector", (lambda e, s=s: e.tensor_add(
                      x1t[s][:], x1t[s][:], fb[s][:])),
                  reads=[f"x1t{s}"] + ftoks, writes=[f"x1t{s}"])
        mk.op("vector", (lambda e, s=s: e.tensor_add(
                  x1t[s][:], x1t[s][:], x0t[s][:])),
              reads=[f"x1t{s}", f"x0t{s}"], writes=[f"x1t{s}"])
        mk.dma("sync", T["x1_tm"][i0:i0 + 128, :], x1t[s][:],
               reads=[f"x1t{s}"], writes=[f"x1_tm.{i}"])
        _ln_chain(mk, x1t[s], [f"x1t{s}"], D, junk[s], ob[s], f"x1o{s}",
                  sc[s])
        mk.dma("sync", T["xn1_tm"][i0:i0 + 128, :], ob[s][:],
               reads=[f"x1o{s}"], writes=[f"xn1_tm.{i}"])
    names = []
    for s in range(2):
        names += [f"x0t{s}", f"x1a{s}", f"x1b{s}", f"x1f{s}", f"x1t{s}",
                  f"x1junk{s}", f"x1o{s}"]
        names += [f"l1n{s}{n}" for n in ("sum", "s2", "nmean", "ex2",
                                         "nvar", "lnv", "rstd", "b", "eps")]
    mk.sfree(*names)



def _ph_l1_qkv(mk, T, sp_bf16=False):
    xnT = mk.sbuf("c_xnT", [128, 16 * 2048], BF16)
    for dc in range(16):
        for tb in range(4):
            mk.dma("sync", xnT[:, dc * 2048 + tb * 512:
                               dc * 2048 + (tb + 1) * 512],
                   T["xn1_tm"][tb * 512:(tb + 1) * 512,
                               dc * 128:(dc + 1) * 128],
                   reads=_tm_tok("xn1_tm", tb * 512, 512),
                   writes=[f"c_xnT.{dc}.{tb}"], transpose=True)
    wv = mk.sbuf("c_wv", [128, 16 * 2048], BF16)
    ev = [mk.sbuf(f"c_ev{i}", [128, 512], BF16) for i in range(2)]
    for dc in range(16):
        mk.dma("gpsimd", wv[:, dc * 2048:(dc + 1) * 2048],
               T["Wv1"][dc * 128:(dc + 1) * 128, :],
               reads=["Wv1"], writes=[f"c_wv.{dc}"])
    for dvc in range(16):
        for tb in range(4):
            bank = (dvc * 4 + tb) % 2
            for dc in range(16):
                mk.op("tensor", (lambda e, bank=bank, dc=dc, dvc=dvc, tb=tb:
                      e.matmul(mk.ps[bank][:],
                               wv[:, dc * 2048 + dvc * 128:
                                  dc * 2048 + (dvc + 1) * 128],
                               xnT[:, dc * 2048 + tb * 512:
                                   dc * 2048 + (tb + 1) * 512],
                               start=(dc == 0), stop=(dc == 15))),
                      reads=[f"c_wv.{dc}", f"c_xnT.{dc}.{tb}"],
                      writes=[f"ps.{bank}"])
            mk.op("vector", (lambda e, bank=bank: e.tensor_copy(
                      ev[bank][:], mk.ps[bank][:])),
                  reads=[f"ps.{bank}"], writes=[f"c_ev{bank}"])
            mk.dma("sync", T["vT1"][dvc * 128:(dvc + 1) * 128,
                                    tb * 512:(tb + 1) * 512],
                   ev[bank][:], reads=[f"c_ev{bank}"],
                   writes=[f"vT1.{dvc}.{tb}"])
    mk.sfree("c_wv")
    wp = mk.sbuf("c_wp", [128, 16 * 512], BF16)
    for wname, dst in (("Wq1", "q1"), ("Wk1", "k1")):
        for dg in range(4):
            for dc in range(16):
                mk.dma("gpsimd", wp[:, dc * 512:(dc + 1) * 512],
                       T[wname][dc * 128:(dc + 1) * 128,
                                dg * 512:(dg + 1) * 512],
                       reads=[wname], writes=[f"c_wp.{dc}"])
            for tc in range(16):
                bank = tc % 2
                for dc in range(16):
                    mk.op("tensor", (lambda e, bank=bank, dc=dc, tc=tc:
                          e.matmul(mk.ps[bank][:],
                                   xnT[:, dc * 2048 + tc * 128:
                                       dc * 2048 + (tc + 1) * 128],
                                   wp[:, dc * 512:(dc + 1) * 512],
                                   start=(dc == 0), stop=(dc == 15))),
                          reads=[f"c_wp.{dc}", f"c_xnT.{dc}.{tc // 4}"],
                          writes=[f"ps.{bank}"])
                mk.op("vector", (lambda e, bank=bank: e.tensor_copy(
                          ev[bank][:], mk.ps[bank][:])),
                      reads=[f"ps.{bank}"], writes=[f"c_ev{bank}"])
                mk.dma("sync", T[dst][tc * 128:(tc + 1) * 128,
                                      dg * 512:(dg + 1) * 512],
                       ev[bank][:], reads=[f"c_ev{bank}"],
                       writes=[f"{dst}.{tc}.{dg}"])
    mk.sfree("c_xnT", "c_wp")
    sp_dt = BF16 if sp_bf16 else F32
    qbuf = mk.sbuf("c_qb", [128, 16 * 2048], BF16)
    for sc_ in range(16):
        mk.dma("sync", qbuf[:, sc_ * 2048:(sc_ + 1) * 2048],
               T["q1"][sc_ * 128:(sc_ + 1) * 128, :],
               reads=[f"q1.{sc_}.{g}" for g in range(4)],
               writes=[f"c_qb.{sc_}"])
    kbuf = mk.sbuf("c_kb", [128, 16 * 512], BF16)
    evf = [mk.sbuf(f"c_evf{i}", [128, 512], sp_dt) for i in range(2)]
    for fb in range(4):
        for sc_ in range(16):
            mk.dma("sync", kbuf[:, sc_ * 512:(sc_ + 1) * 512],
                   T["k1"][sc_ * 128:(sc_ + 1) * 128,
                           fb * 512:(fb + 1) * 512],
                   reads=[f"k1.{sc_}.{fb}"], writes=[f"c_kb.{sc_}"])
        for ec in range(16):
            bank = ec % 2
            for sc_ in range(16):
                mk.op("tensor", (lambda e, bank=bank, sc_=sc_, ec=ec:
                      e.matmul(mk.ps[bank][:],
                               qbuf[:, sc_ * 2048 + ec * 128:
                                    sc_ * 2048 + (ec + 1) * 128],
                               kbuf[:, sc_ * 512:(sc_ + 1) * 512],
                               start=(sc_ == 0), stop=(sc_ == 15))),
                      reads=[f"c_qb.{sc_}", f"c_kb.{sc_}"],
                      writes=[f"ps.{bank}"])
            mk.op("vector", (lambda e, bank=bank: e.tensor_copy(
                      evf[bank][:], mk.ps[bank][:])),
                  reads=[f"ps.{bank}"], writes=[f"c_evf{bank}"])
            mk.dma("sync", T["sp"][ec * 128:(ec + 1) * 128,
                                   fb * 512:(fb + 1) * 512],
                   evf[bank][:], reads=[f"c_evf{bank}"],
                   writes=[f"sp.{ec}.{fb}"])
    mk.sfree("c_qb", "c_kb", "c_ev0", "c_ev1", "c_evf0", "c_evf1")
    wdt = [mk.sbuf(f"c_wd{i}", [128, 2048], BF16) for i in range(2)]
    for ic in range(64):
        sl = ic % 2
        mk.dma("gpsimd", wdt[sl][:], T["wd"][ic * 128:(ic + 1) * 128, :],
               reads=["wd"], writes=[f"c_wd{sl}"])
        mk.dma("sync", T["wd_bf"][ic * 128:(ic + 1) * 128, :], wdt[sl][:],
               reads=[f"c_wd{sl}"], writes=[f"wd_bf.{ic}"])
    mk.sfree("c_wd0", "c_wd1")



def _ph_l1_softmax(mk, T, fused=False):
    sc = {}
    for n in ("mx", "nmx", "rs", "rr"):
        sc[n] = mk.sbuf(f"sm{n}", [128, 1], F32)
    spa = mk.sbuf("smspa", [128, D], BF16 if fused else F32)
    spb = mk.sbuf("smspb", [128, D], F32)
    Sb = mk.sbuf("smS", [128, D], F32)
    Pb = mk.sbuf("smP", [128, D], F32)
    Wb = mk.sbuf("smW", [128, D], BF16)
    for ec in range(16):
        e0 = ec * 128
        if fused:
            mk.dma("sync", spa[:], T["ccS_out"][e0:e0 + 128, :],
                   reads=["ccS_out"], writes=["smspa"])
            mk.op("vector", lambda e: e.tensor_copy(Sb[:], spa[:]),
                  reads=["smspa"], writes=["smS"])
        else:
            mk.dma("sync", spa[:], T["sp_s"][e0:e0 + 128, :],
                   reads=["sp_s"], writes=["smspa"])
            mk.dma("sync", spb[:], T["sp_p"][e0:e0 + 128, :],
                   reads=["sp_p"], writes=["smspb"])
            mk.op("vector", lambda e: e.tensor_add(Sb[:], spa[:], spb[:]),
                  reads=["smspa", "smspb"], writes=["smS"])
        mk.op("vector", lambda e: e.reduce_max(sc["mx"][:], Sb[:], axis=AX.X),
              reads=["smS"], writes=["smmx"])
        mk.op("vector", lambda e: e.tensor_scalar_mul(sc["nmx"][:],
              sc["mx"][:], -0.125), reads=["smmx"], writes=["smnmx"])
        mk.op("scalar", lambda e: e.activation(out=Pb[:], in_=Sb[:],
              func=AF.Exp, bias=sc["nmx"][:, 0:1], scale=0.125,
              accum_out=sc["rs"][:, 0:1]),
              reads=["smS", "smnmx"], writes=["smP", "smrs"])
        mk.op("vector", lambda e: e.reciprocal(sc["rr"][:], sc["rs"][:]),
              reads=["smrs"], writes=["smrr"])
        mk.op("vector", lambda e: e.tensor_scalar_mul(Wb[:], Pb[:],
              sc["rr"][:, 0:1]), reads=["smP", "smrr"], writes=["smW"])
        mk.dma("sync", T["wbuf"][e0:e0 + 128, :], Wb[:],
               reads=["smW"], writes=[f"wbuf.{ec}"])
    mk.sfree("smmx", "smnmx", "smrs", "smrr", "smspa", "smspb", "smS",
             "smP", "smW")


def _ph_l1_av_wo(mk, T):
    wT = mk.sbuf("d_wT", [128, 16 * 2048], BF16)
    for fc in range(16):
        for eb in range(4):
            mk.dma("sync", wT[:, fc * 2048 + eb * 512:
                              fc * 2048 + (eb + 1) * 512],
                   T["wbuf"][eb * 512:(eb + 1) * 512,
                             fc * 128:(fc + 1) * 128],
                   reads=[f"wbuf.{eb * 4 + i}" for i in range(4)],
                   writes=[f"d_wT.{fc}.{eb}"], transpose=True)
    wo = mk.sbuf("d_wo", [128, 16 * 2048], BF16)
    for ec in range(16):
        mk.dma("gpsimd", wo[:, ec * 2048:(ec + 1) * 2048],
               T["Wo1"][ec * 128:(ec + 1) * 128, :],
               reads=["Wo1"], writes=[f"d_wo.{ec}"])
    vb = mk.sbuf("d_vb", [128, 16 * 512], BF16)
    ao = mk.sbuf("d_ao", [128, 16 * 512], BF16)
    ev = [mk.sbuf(f"d_ev{i}", [128, 512], BF16) for i in range(2)]
    for sb in range(4):
        s0 = sb * 512
        for fc in range(16):
            mk.dma("sync", vb[:, fc * 512:(fc + 1) * 512],
                   T["vT1"][fc * 128:(fc + 1) * 128, s0:s0 + 512],
                   reads=[f"vT1.{fc}.{sb}"], writes=[f"d_vb.{fc}"])
        for ec in range(16):
            bank = ec % 2
            for fc in range(16):
                mk.op("tensor", (lambda e, bank=bank, fc=fc, ec=ec:
                      e.matmul(mk.ps[bank][:],
                               wT[:, fc * 2048 + ec * 128:
                                  fc * 2048 + (ec + 1) * 128],
                               vb[:, fc * 512:(fc + 1) * 512],
                               start=(fc == 0), stop=(fc == 15))),
                      reads=[f"d_wT.{fc}.{ec // 4}", f"d_vb.{fc}"],
                      writes=[f"ps.{bank}"])
            mk.op("vector", (lambda e, bank=bank, ec=ec: e.tensor_copy(
                      ao[:, ec * 512:(ec + 1) * 512], mk.ps[bank][:])),
                  reads=[f"ps.{bank}"], writes=[f"d_ao.{ec}"])
        for do in range(16):
            bank = 2 + do % 2
            for ec in range(16):
                mk.op("tensor", (lambda e, bank=bank, ec=ec, do=do:
                      e.matmul(mk.ps[bank][:],
                               wo[:, ec * 2048 + do * 128:
                                  ec * 2048 + (do + 1) * 128],
                               ao[:, ec * 512:(ec + 1) * 512],
                               start=(ec == 0), stop=(ec == 15))),
                      reads=[f"d_wo.{ec}", f"d_ao.{ec}"],
                      writes=[f"ps.{bank}"])
            es = do % 2
            mk.op("vector", (lambda e, bank=bank, es=es: e.tensor_copy(
                      ev[es][:], mk.ps[bank][:])),
                  reads=[f"ps.{bank}"], writes=[f"d_ev{es}"])
            mk.dma("sync", T["a1T"][do * 128:(do + 1) * 128, s0:s0 + 512],
                   ev[es][:], reads=[f"d_ev{es}"],
                   writes=[f"a1T.{do}.{sb}"])
    mk.sfree("d_wT", "d_wo", "d_vb", "d_ao", "d_ev0", "d_ev1")


def _ph_l1_gateup_prefetch(mk, T):
    wg = [mk.sbuf(f"g_wg{s}", [128, 16 * 512], BF16) for s in range(2)]
    wu = [mk.sbuf(f"g_wu{s}", [128, 16 * 512], BF16) for s in range(2)]
    T["_g_w"] = (wg, wu)

    def load_ig(ig):
        s = ig % 2
        i0 = ig * 512
        for dc in range(16):
            mk.dma("gpsimd", wg[s][:, dc * 512:(dc + 1) * 512],
                   T["wg"][dc * 128:(dc + 1) * 128, i0:i0 + 512],
                   reads=["wg"], writes=[f"g_wg{s}.{dc}"])
            mk.dma("gpsimd", wu[s][:, dc * 512:(dc + 1) * 512],
                   T["wu"][dc * 128:(dc + 1) * 128, i0:i0 + 512],
                   reads=["wu"], writes=[f"g_wu{s}.{dc}"])
    T["_g_load"] = load_ig
    load_ig(0)
    load_ig(1)


def _ph_l1_gateup(mk, T):
    xnT = mk.sbuf("g_xnT", [128, 16 * 2048], BF16)
    for dc in range(16):
        for tb in range(4):
            mk.dma("sync", xnT[:, dc * 2048 + tb * 512:
                               dc * 2048 + (tb + 1) * 512],
                   T["xn1_tm"][tb * 512:(tb + 1) * 512,
                               dc * 128:(dc + 1) * 128],
                   reads=_tm_tok("xn1_tm", tb * 512, 512),
                   writes=[f"g_xnT.{dc}.{tb}"],
                   transpose=True)
    wg, wu = T["_g_w"]
    load_ig = T["_g_load"]
    sg = [mk.sbuf(f"g_sg{s}", [128, 512], F32) for s in range(2)]
    hb = [mk.sbuf(f"g_h{s}", [128, 512], BF16) for s in range(2)]
    habs = [mk.sbuf(f"g_habs{s}", [128, 512], F32) for s in range(2)]
    q4 = [mk.sbuf(f"g_q{i}", [128, 128], F32) for i in range(5)]
    m4 = [mk.sbuf(f"g_m{i}", [128, 128], BF16) for i in range(4)]
    hs = [mk.sbuf(f"g_hs{i}", [128, 512], BF16) for i in range(2)]
    for ig in range(16):
        i0 = ig * 512
        s = ig % 2
        if ig + 2 < 16:
            load_ig(ig + 2)
        for tc in range(16):
            pg, pu = 2 * (tc % 2), 2 * (tc % 2) + 1
            for dc in range(16):
                mk.op("tensor", (lambda e, dc=dc, tc=tc, pg=pg, s=s:
                      e.matmul(
                          mk.ps[pg][:],
                          xnT[:, dc * 2048 + tc * 128:
                              dc * 2048 + (tc + 1) * 128],
                          wg[s][:, dc * 512:(dc + 1) * 512],
                          start=(dc == 0), stop=(dc == 15))),
                      reads=[f"g_xnT.{dc}.{tc // 4}", f"g_wg{s}.{dc}"],
                      writes=[f"ps.{pg}"])
                mk.op("tensor", (lambda e, dc=dc, tc=tc, pu=pu, s=s:
                      e.matmul(
                          mk.ps[pu][:],
                          xnT[:, dc * 2048 + tc * 128:
                              dc * 2048 + (tc + 1) * 128],
                          wu[s][:, dc * 512:(dc + 1) * 512],
                          start=(dc == 0), stop=(dc == 15))),
                      reads=[f"g_xnT.{dc}.{tc // 4}", f"g_wu{s}.{dc}"],
                      writes=[f"ps.{pu}"])
            st = tc % 2
            mk.op("scalar", (lambda e, pg=pg, st=st: e.activation(
                  out=sg[st][:], in_=mk.ps[pg][:],
                  func=AF.Silu)), reads=[f"ps.{pg}"], writes=[f"g_sg{st}"])
            mk.op("vector", (lambda e, pu=pu, st=st: e.tensor_mul(
                  hb[st][:], sg[st][:],
                  mk.ps[pu][:])), reads=[f"g_sg{st}", f"ps.{pu}"],
                  writes=[f"g_h{st}"])
            mk.op("scalar", (lambda e, st=st: e.activation(
                  out=habs[st][:], in_=hb[st][:],
                  func=AF.Abs)), reads=[f"g_h{st}"], writes=[f"g_habs{st}"])
            ha = habs[st]
            mk.op("vector", (lambda e, ha=ha: e.tensor_max(
                  q4[0][:], ha[:, 0::4],
                  ha[:, 1::4])), reads=[f"g_habs{st}"], writes=["g_q0"])
            mk.op("vector", (lambda e, ha=ha: e.tensor_tensor(q4[1][:],
                  ha[:, 0::4], ha[:, 1::4], ALU.min)),
                  reads=[f"g_habs{st}"], writes=["g_q1"])
            mk.op("vector", (lambda e, ha=ha: e.tensor_max(
                  q4[2][:], ha[:, 2::4],
                  ha[:, 3::4])), reads=[f"g_habs{st}"], writes=["g_q2"])
            mk.op("vector", (lambda e, ha=ha: e.tensor_tensor(q4[3][:],
                  ha[:, 2::4], ha[:, 3::4], ALU.min)),
                  reads=[f"g_habs{st}"], writes=["g_q3"])
            mk.op("vector", lambda e: e.tensor_tensor(q4[0][:], q4[0][:],
                  q4[2][:], ALU.min), reads=["g_q0", "g_q2"],
                  writes=["g_q0"])
            mk.op("vector", lambda e: e.tensor_max(q4[1][:], q4[1][:],
                  q4[3][:]), reads=["g_q1", "g_q3"], writes=["g_q1"])
            mk.op("vector", lambda e: e.tensor_max(q4[4][:], q4[0][:],
                  q4[1][:]), reads=["g_q0", "g_q1"], writes=["g_q4"])
            for jj in range(4):
                mk.op("vector", (lambda e, jj=jj, ha=ha: e.tensor_tensor(
                          m4[jj][:], ha[:, jj::4], q4[4][:], ALU.is_ge)),
                      reads=[f"g_habs{st}", "g_q4"], writes=[f"g_m{jj}"])
            for jj in range(4):
                mk.op("gpsimd", (lambda e, jj=jj, st=st: e.tensor_mul(
                          hs[st][:, jj::4], hb[st][:, jj::4], m4[jj][:])),
                      reads=[f"g_h{st}", f"g_m{jj}"], writes=[f"g_hs{st}"])
            mk.dma("sync", T["hsp"][tc * 128:(tc + 1) * 128, i0:i0 + 512],
                   hs[st][:], reads=[f"g_hs{st}"],
                   writes=[f"hsp.{tc}.{ig}"])
    mk.sfree("g_xnT", "g_wg0", "g_wg1", "g_wu0", "g_wu1", "g_sg0", "g_sg1",
             "g_h0", "g_h1", "g_habs0", "g_habs1",
             "g_q0", "g_q1", "g_q2", "g_q3", "g_q4", "g_m0", "g_m1",
             "g_m2", "g_m3", "g_hs0", "g_hs1")


def _ph_l1_down_out(mk, T):
    x1t = [mk.sbuf(f"o_x1{s}", [128, D], F32) for s in range(2)]
    y1b = [mk.sbuf(f"o_y1b{s}", [128, D], BF16) for s in range(2)]
    outt = [mk.sbuf(f"o_out{s}", [128, D], F32) for s in range(2)]

    def out_tile(i):
        i0 = i * 128
        s = i % 2
        mk.dma("scalar", x1t[s][:], T["x1_tm"][i0:i0 + 128, :],
               reads=[f"x1_tm.{i}"], writes=[f"o_x1{s}"])
        for blk in range(4):
            mk.dma("sync", y1b[s][:, blk * 512:(blk + 1) * 512],
                   T["y1T"][blk * 512:(blk + 1) * 512, i0:i0 + 128],
                   reads=[f"y1T.{blk * 4 + q}.{i // 4}" for q in range(4)],
                   writes=[f"o_y1b{s}.{blk}"], transpose=True)
        mk.op("vector", (lambda e, s=s: e.tensor_add(
                  outt[s][:], x1t[s][:], y1b[s][:])),
              reads=[f"o_x1{s}"] + [f"o_y1b{s}.{b}" for b in range(4)],
              writes=[f"o_out{s}"])
        mk.dma("sync", T["OUT"][i0:i0 + 128, :], outt[s][:],
               reads=[f"o_out{s}"], writes=[f"OUT.{i}"])

    for th in range(2):
        tt0 = th * 1024
        hT = mk.sbuf("o_hT", [128, 64 * 1024], BF16)
        for ic in range(64):
            for tq in range(2):
                mk.dma("sync", hT[:, ic * 1024 + tq * 512:
                                  ic * 1024 + (tq + 1) * 512],
                       T["hsp"][tt0 + tq * 512:tt0 + (tq + 1) * 512,
                                ic * 128:(ic + 1) * 128],
                       reads=[f"hsp.{(tt0 + tq * 512) // 128 + q}.{ic // 4}"
                              for q in range(4)],
                       writes=[f"o_hT.{ic}.{tq}"], transpose=True)
        wdc = mk.sbuf("o_wdc", [128, 64 * 128], BF16)
        a1b = [mk.sbuf(f"o_a1{i}", [128, 512], BF16) for i in range(2)]
        yev = [mk.sbuf(f"o_y{i}", [128, 512], BF16) for i in range(2)]
        for dc in range(16):
            for g8 in range(8):
                src = T["wd_bf"][g8 * 1024:(g8 + 1) * 1024,
                                 dc * 128:(dc + 1) * 128]
                src = src.rearrange("(g r) c -> r g c", g=8)
                mk.dma("sync", wdc[:, g8 * 1024:(g8 + 1) * 1024], src,
                       reads=[f"wd_bf.{g8 * 8 + i}" for i in range(8)],
                       writes=[f"o_wdc.{g8}"])
            for tq in range(2):
                bank = tq
                for ic in range(64):
                    mk.op("tensor", (lambda e, bank=bank, ic=ic, tq=tq:
                          e.matmul(mk.ps[bank][:],
                                   wdc[:, ic * 128:(ic + 1) * 128],
                                   hT[:, ic * 1024 + tq * 512:
                                      ic * 1024 + (tq + 1) * 512],
                                   start=(ic == 0), stop=(ic == 63))),
                          reads=[f"o_wdc.{ic // 8}", f"o_hT.{ic}.{tq}"],
                          writes=[f"ps.{bank}"])
                mk.dma("scalar", a1b[tq][:],
                       T["a1T"][dc * 128:(dc + 1) * 128,
                                tt0 + tq * 512:tt0 + (tq + 1) * 512],
                       reads=[f"a1T.{dc}.{th * 2 + tq}"],
                       writes=[f"o_a1{tq}"])
                mk.op("vector", (lambda e, bank=bank, tq=tq: e.tensor_add(
                          yev[tq][:], mk.ps[bank][:], a1b[tq][:])),
                      reads=[f"ps.{bank}", f"o_a1{tq}"], writes=[f"o_y{tq}"])
                mk.dma("sync",
                       T["y1T"][dc * 128:(dc + 1) * 128,
                                tt0 + tq * 512:tt0 + (tq + 1) * 512],
                       yev[tq][:], reads=[f"o_y{tq}"],
                       writes=[f"y1T.{dc}.{th * 2 + tq}"])
        mk.sfree("o_hT", "o_wdc", "o_a10", "o_a11", "o_y0", "o_y1")
        for i in range(th * 8, th * 8 + 8):
            out_tile(i)
    mk.sfree("o_x10", "o_x11", "o_y1b0", "o_y1b1", "o_out0", "o_out1")



def _dram(nc, specs):
    T = {}
    for name, shape, dt, kind in specs:
        T[name] = nc.dram_tensor(name, shape, dt, kind=kind).ap()
    return T


def _build_A():
    nc = bass.Bass("TRN2", target_bir_lowering=False)
    T = _dram(nc, [
        ("x0", [S, D], F32, "ExternalInput"),
        ("Wq_h", [D, 1024], F32, "ExternalInput"),
        ("Wqr_h", [D, 512], F32, "ExternalInput"),
        ("Wkvd", [D, DKV], F32, "ExternalInput"),
        ("Wkvup_h", [DKV, 2048], F32, "ExternalInput"),
        ("Wkr", [D, RD], F32, "ExternalInput"),
        ("Wo_h", [1024, D], F32, "ExternalInput"),
        ("ropeC", [128, S], F32, "ExternalInput"),
        ("ropeS", [128, S], F32, "ExternalInput"),
        ("masks", [4, 128, 512], BF16, "ExternalInput"),
        ("xn0_tm", [S, D], BF16, "ExternalOutput"),
        ("a0pT", [D, S], BF16, "ExternalOutput"),
        ("qcT", [1024, S], BF16, "Internal"),
        ("qrT", [512, S], BF16, "Internal"),
        ("qrraw", [512, S], F32, "Internal"),
        ("krraw", [64, S], F32, "Internal"),
        ("kcT", [1024, S], BF16, "Internal"),
        ("v_tm", [S, 1024], BF16, "Internal"),
        ("onormT", [1024, S], BF16, "Internal"),
    ])
    mk = MK(nc)
    _ph_ln(mk, T, "x0", "xn0_tm", S, "ln0")
    _ph_l0_proj(mk, T)
    _ph_l0_attn(mk, T)
    _ph_l0_wo(mk, T)
    mk.finalize()
    return nc


def _build_B():
    nc = bass.Bass("TRN2", target_bir_lowering=False)
    T = _dram(nc, [
        ("x0_own", [TOK, D], F32, "ExternalInput"),
        ("xn0_own", [TOK, D], BF16, "ExternalInput"),
        ("w1", [DFF, D], F32, "ExternalInput"),
        ("w2", [D, DFF], F32, "ExternalInput"),
        ("b1t", [128, 64], F32, "ExternalInput"),
        ("b2t", [128, 16], F32, "ExternalInput"),
        ("a0s", [D, TOK], BF16, "ExternalInput"),
        ("a0p", [D, TOK], BF16, "ExternalInput"),
        ("x1_tm", [TOK, D], F32, "ExternalOutput"),
        ("xn1_tm", [TOK, D], BF16, "ExternalOutput"),
        ("w1q", [DFF, D], BF16, "Internal"),
        ("w2q", [D, DFF], BF16, "Internal"),
        ("f0T", [D, TOK], BF16, "Internal"),
    ])
    mk = MK(nc)
    _ph_quant(mk, T, "w1", "w1q", 64, 2048)
    _ph_quant(mk, T, "w2", "w2q", 16, 8192)
    _ph_l0_ffn(mk, T)
    _ph_x1_ln1(mk, T)
    mk.finalize()
    return nc


def _build_C():
    nc = bass.Bass("TRN2", target_bir_lowering=False)
    T = _dram(nc, [
        ("xn1_tm", [TOK, D], BF16, "ExternalInput"),
        ("Wq1", [D, D], F32, "ExternalInput"),
        ("Wk1", [D, D], F32, "ExternalInput"),
        ("Wv1", [D, D], F32, "ExternalInput"),
        ("wd", [DFF, D], F32, "ExternalInput"),
        ("sp", [D, D], F32, "ExternalOutput"),
        ("vT1", [D, TOK], BF16, "ExternalOutput"),
        ("wd_bf", [DFF, D], BF16, "ExternalOutput"),
        ("q1", [TOK, D], BF16, "Internal"),
        ("k1", [TOK, D], BF16, "Internal"),
    ])
    mk = MK(nc)
    _ph_l1_qkv(mk, T)
    mk.finalize()
    return nc


def _build_D():
    nc = bass.Bass("TRN2", target_bir_lowering=False)
    T = _dram(nc, [
        ("sp_s", [D, D], F32, "ExternalInput"),
        ("sp_p", [D, D], F32, "ExternalInput"),
        ("vT1", [D, TOK], BF16, "ExternalInput"),
        ("Wo1", [D, D], F32, "ExternalInput"),
        ("xn1_tm", [TOK, D], BF16, "ExternalInput"),
        ("wg", [D, DFF], F32, "ExternalInput"),
        ("wu", [D, DFF], F32, "ExternalInput"),
        ("wd_bf", [DFF, D], BF16, "ExternalInput"),
        ("x1_tm", [TOK, D], F32, "ExternalInput"),
        ("OUT", [TOK, D], F32, "ExternalOutput"),
        ("wbuf", [D, D], BF16, "Internal"),
        ("a1T", [D, TOK], BF16, "Internal"),
        ("hsp", [TOK, DFF], BF16, "Internal"),
        ("y1T", [D, TOK], BF16, "Internal"),
    ])
    mk = MK(nc)
    _ph_l1_softmax(mk, T)
    _ph_l1_av_wo(mk, T)
    _ph_l1_gateup(mk, T)
    _ph_l1_down_out(mk, T)
    mk.finalize()
    return nc




_RG = [[0, 1], [2, 3], [4, 5], [6, 7]]


def _build_fused():
    nc = bass.Bass("TRN2", target_bir_lowering=False)
    T = _dram(nc, [
        ("x0", [S, D], F32, "ExternalInput"),
        ("x0_own", [TOK, D], F32, "ExternalInput"),
        ("Wq_h", [D, 1024], F32, "ExternalInput"),
        ("Wqr_h", [D, 512], F32, "ExternalInput"),
        ("Wkvd", [D, DKV], F32, "ExternalInput"),
        ("Wkvup_h", [DKV, 2048], F32, "ExternalInput"),
        ("Wkr", [D, RD], F32, "ExternalInput"),
        ("Wo_h", [1024, D], F32, "ExternalInput"),
        ("ropeC", [128, S], F32, "ExternalInput"),
        ("ropeS", [128, S], F32, "ExternalInput"),
        ("masks", [4, 128, 512], BF16, "ExternalInput"),
        ("w1", [DFF, D], F32, "ExternalInput"),
        ("w2", [D, DFF], F32, "ExternalInput"),
        ("b1t", [128, 64], F32, "ExternalInput"),
        ("b2t", [128, 16], F32, "ExternalInput"),
        ("Wq1", [D, D], F32, "ExternalInput"),
        ("Wk1", [D, D], F32, "ExternalInput"),
        ("Wv1", [D, D], F32, "ExternalInput"),
        ("Wo1", [D, D], F32, "ExternalInput"),
        ("wg", [D, DFF], F32, "ExternalInput"),
        ("wu", [D, DFF], F32, "ExternalInput"),
        ("wd", [DFF, D], F32, "ExternalInput"),
        ("OUT", [TOK, D], F32, "ExternalOutput"),
        ("xn0_tm", [S, D], BF16, "Internal"),
        ("xn0_own", [TOK, D], BF16, "Internal"),
        ("qcT", [1024, S], BF16, "Internal"),
        ("qrT", [512, S], BF16, "Internal"),
        ("qrraw", [512, S], F32, "Internal"),
        ("krraw", [64, S], F32, "Internal"),
        ("kcT", [1024, S], BF16, "Internal"),
        ("v_tm", [S, 1024], BF16, "Internal"),
        ("onormT", [1024, S], BF16, "Internal"),
        ("ccA_in", [2 * D, TOK], BF16, "Internal"),
        ("ccA_out", [D, TOK], BF16, "Internal"),
        ("w1q", [DFF, D], BF16, "Internal"),
        ("w2q", [D, DFF], BF16, "Internal"),
        ("f0T", [D, TOK], BF16, "Internal"),
        ("x1_tm", [TOK, D], F32, "Internal"),
        ("xn1_tm", [TOK, D], BF16, "Internal"),
        ("q1", [TOK, D], BF16, "Internal"),
        ("k1", [TOK, D], BF16, "Internal"),
        ("vT1", [D, TOK], BF16, "Internal"),
        ("sp", [D, D], BF16, "Internal"),
        ("ccS_out", [D, D], BF16, "Internal"),
        ("wd_bf", [DFF, D], BF16, "Internal"),
        ("wbuf", [D, D], BF16, "Internal"),
        ("a1T", [D, TOK], BF16, "Internal"),
        ("hsp", [TOK, DFF], BF16, "Internal"),
        ("y1T", [D, TOK], BF16, "Internal"),
    ])
    mk = MK(nc)
    _ph_ln(mk, T, "x0", "xn0_tm", S, "ln0")
    _ph_ln(mk, T, "x0_own", "xn0_own", TOK, "lnb")
    _ph_l0_proj(mk, T)
    _ph_quant_p1(mk, T, "w1", 64, 2048)
    _ph_quant_p1(mk, T, "w2", 16, 8192, nb=1)
    work = [("w1", "w1q", i) for i in range(64)] + \
           [("w2", "w2q", i) for i in range(16)]

    def filler(slot):
        if slot == 7:
            _ph_quant_p1b(mk, T, "w1")
            _ph_quant_p1b(mk, T, "w2")
        if slot >= 24:
            for _ in range(2):
                if work:
                    wn, qn, i = work.pop(0)
                    _quant_p2_tile(mk, T, wn, qn, i)

    _ph_l0_attn(mk, T, filler=filler)
    while work:
        wn, qn, i = work.pop(0)
        _quant_p2_tile(mk, T, wn, qn, i)
    _ph_quant_free(mk, T, "w1")
    _ph_quant_free(mk, T, "w2")
    _ph_l0_wo(mk, T, cc=True)
    mk.op("gpsimd", lambda e: e.collective_compute(
              "ReduceScatter", ALU.add, replica_groups=_RG,
              ins=[T["ccA_in"][:, :]], outs=[T["ccA_out"][:, :]]),
          reads=[f"ccA_in.{do}.{tb}" for do in range(16) for tb in range(8)],
          writes=["ccA_out"])
    _ph_l0_ffn(mk, T)
    _ph_x1_ln1(mk, T, fused=True)
    _ph_l1_qkv(mk, T, sp_bf16=True)
    _ph_l1_gateup_prefetch(mk, T)
    mk.op("gpsimd", lambda e: e.collective_compute(
              "AllReduce", ALU.add, replica_groups=_RG,
              ins=[T["sp"][:, :]], outs=[T["ccS_out"][:, :]]),
          reads=[f"sp.{ec}.{fb}" for ec in range(16) for fb in range(4)],
          writes=["ccS_out"])
    _ph_l1_gateup(mk, T)
    _ph_l1_softmax(mk, T, fused=True)
    _ph_l1_av_wo(mk, T)
    _ph_l1_down_out(mk, T)
    mk.finalize()
    return nc



_FUSED = True

def kernel(**inp):
    if _FUSED:
        return _kernel_fused(**inp)
    return _kernel_unfused(**inp)


def _common_host(inp):
    f32 = np.float32

    def np32(name):
        return np.ascontiguousarray(np.asarray(inp[name], dtype=f32))

    W = {k: np32(k) for k in inp}
    b1 = W["L0_ffn_b1"]
    b2 = W["L0_ffn_b2"]
    W["b1t"] = np.ascontiguousarray(b1.reshape(64, 128).T)
    W["b2t"] = np.ascontiguousarray(b2.reshape(16, 128).T)
    return W


def _kernel_fused(**inp):
    ropeC, ropeS, masks = _host_tables()
    W = _common_host(inp)
    x0 = W["L0_attn_x"]
    cores = list(range(8))
    nc = _build_fused()
    in_maps = []
    for c in cores:
        b, j = c // 2, c % 2
        own = slice(j * TOK, (j + 1) * TOK)
        Wkvu = W["L0_attn_Wkv_up"]
        in_maps.append({
            "x0": x0[b],
            "x0_own": np.ascontiguousarray(x0[b][own]),
            "Wq_h": np.ascontiguousarray(
                W["L0_attn_Wq"][:, j * 1024:(j + 1) * 1024]),
            "Wqr_h": np.ascontiguousarray(
                W["L0_attn_Wq_rope"][:, j * 512:(j + 1) * 512]),
            "Wkvd": W["L0_attn_Wkv_down"],
            "Wkvup_h": np.ascontiguousarray(np.concatenate(
                [Wkvu[:, j * 1024:(j + 1) * 1024],
                 Wkvu[:, 2048 + j * 1024:2048 + (j + 1) * 1024]], axis=1)),
            "Wkr": W["L0_attn_Wk_rope"],
            "Wo_h": np.ascontiguousarray(
                W["L0_attn_Wo"][j * 1024:(j + 1) * 1024, :]),
            "ropeC": ropeC, "ropeS": ropeS, "masks": masks,
            "w1": W["L0_ffn_w1"], "w2": W["L0_ffn_w2"],
            "b1t": W["b1t"], "b2t": W["b2t"],
            "Wq1": W["L1_attn_Wq"], "Wk1": W["L1_attn_Wk"],
            "Wv1": W["L1_attn_Wv"], "Wo1": W["L1_attn_Wo"],
            "wg": W["L1_ffn_w_gate"], "wu": W["L1_ffn_w_up"],
            "wd": W["L1_ffn_w_down"],
        })
    res = run_bass_kernel_spmd(nc, in_maps, core_ids=cores).results
    out = np.empty((B, S, D), np.float32)
    for c in cores:
        b, j = c // 2, c % 2
        out[b, j * TOK:(j + 1) * TOK, :] = res[c]["OUT"]
    return out


def _kernel_unfused(**inp):
    f32 = np.float32
    ropeC, ropeS, masks = _host_tables()
    cores = list(range(8))

    def np32(name):
        return np.ascontiguousarray(np.asarray(inp[name], dtype=f32))

    x0 = np32("L0_attn_x")
    Wq, Wqr = np32("L0_attn_Wq"), np32("L0_attn_Wq_rope")
    Wkvd, Wkvu = np32("L0_attn_Wkv_down"), np32("L0_attn_Wkv_up")
    Wkr, Wo = np32("L0_attn_Wk_rope"), np32("L0_attn_Wo")
    w1, b1 = np32("L0_ffn_w1"), np32("L0_ffn_b1")
    w2, b2 = np32("L0_ffn_w2"), np32("L0_ffn_b2")
    Wq1, Wk1 = np32("L1_attn_Wq"), np32("L1_attn_Wk")
    Wv1, Wo1 = np32("L1_attn_Wv"), np32("L1_attn_Wo")
    wg, wu, wd = np32("L1_ffn_w_gate"), np32("L1_ffn_w_up"), \
        np32("L1_ffn_w_down")
    b1t = np.ascontiguousarray(b1.reshape(64, 128).T)
    b2t = np.ascontiguousarray(b2.reshape(16, 128).T)

    ncA = _build_A()
    inA = []
    for c in cores:
        b, j = c // 2, c % 2
        inA.append({
            "x0": x0[b],
            "Wq_h": np.ascontiguousarray(Wq[:, j * 1024:(j + 1) * 1024]),
            "Wqr_h": np.ascontiguousarray(Wqr[:, j * 512:(j + 1) * 512]),
            "Wkvd": Wkvd,
            "Wkvup_h": np.ascontiguousarray(np.concatenate(
                [Wkvu[:, j * 1024:(j + 1) * 1024],
                 Wkvu[:, 2048 + j * 1024:2048 + (j + 1) * 1024]], axis=1)),
            "Wkr": Wkr,
            "Wo_h": np.ascontiguousarray(Wo[j * 1024:(j + 1) * 1024, :]),
            "ropeC": ropeC, "ropeS": ropeS, "masks": masks,
        })
    rA = run_bass_kernel_spmd(ncA, inA, core_ids=cores).results

    ncB = _build_B()
    inB = []
    for c in cores:
        b, j = c // 2, c % 2
        own = slice(j * TOK, (j + 1) * TOK)
        inB.append({
            "x0_own": np.ascontiguousarray(x0[b][own]),
            "xn0_own": np.ascontiguousarray(rA[c]["xn0_tm"][own]),
            "w1": w1, "w2": w2, "b1t": b1t, "b2t": b2t,
            "a0s": np.ascontiguousarray(rA[c]["a0pT"][:, own]),
            "a0p": np.ascontiguousarray(rA[c ^ 1]["a0pT"][:, own]),
        })
    rB = run_bass_kernel_spmd(ncB, inB, core_ids=cores).results

    ncC = _build_C()
    inC = [{"xn1_tm": rB[c]["xn1_tm"], "Wq1": Wq1, "Wk1": Wk1, "Wv1": Wv1,
            "wd": wd} for c in cores]
    rC = run_bass_kernel_spmd(ncC, inC, core_ids=cores).results

    ncD = _build_D()
    inD = [{"sp_s": rC[c]["sp"], "sp_p": rC[c ^ 1]["sp"],
            "vT1": rC[c]["vT1"], "Wo1": Wo1, "xn1_tm": rB[c]["xn1_tm"],
            "wg": wg, "wu": wu, "wd_bf": rC[c]["wd_bf"],
            "x1_tm": rB[c]["x1_tm"]} for c in cores]
    rD = run_bass_kernel_spmd(ncD, inD, core_ids=cores).results

    out = np.empty((B, S, D), np.float32)
    for c in cores:
        b, j = c // 2, c % 2
        out[b, j * TOK:(j + 1) * TOK, :] = rD[c]["OUT"]
    return out


# version 07
# speedup vs baseline: 1.3641x; 1.0969x over previous
import math
import numpy as np
import ml_dtypes
import concourse.bass as bass
import concourse.mybir as mybir
from concourse.bass_utils import run_bass_kernel_spmd

F32 = mybir.dt.float32
BF16 = mybir.dt.bfloat16
AF = mybir.ActivationFunctionType
ALU = mybir.AluOpType
AX = mybir.AxisListType

B, S, D = 4, 4096, 2048
H, HD, RD = 16, 128, 64
DKV = 512
DFF = 8192
TOK = S // 2
SCALE0 = 1.0 / math.sqrt(HD + RD)

BF = ml_dtypes.bfloat16



class _Op:
    __slots__ = ("stream", "emit", "deps", "need_inc", "is_dma", "dma_use",
                 "dma_sem", "count", "tag")

    def __init__(self, stream, emit, is_dma, tag):
        self.stream = stream
        self.emit = emit
        self.deps = []
        self.need_inc = False
        self.is_dma = is_dma
        self.dma_use = 0
        self.dma_sem = None
        self.count = 0
        self.tag = tag


_ENGS = ("tensor", "vector", "scalar", "gpsimd", "sync")
_DMA_POOL = 16


class MK:
    def __init__(self, nc):
        self.nc = nc
        self.streams = {e: [] for e in _ENGS}
        self.comp_sem = {e: nc.alloc_semaphore(f"cs_{e}") for e in _ENGS}
        self.dma_sems = {}
        self.dma_ops = {}
        self.tok_w = {}
        self.tok_r = {}
        self.pending = []
        base = (nc.sbuf_base + 63) & ~63
        self.sb_free = [[base, nc.sbuf_top]]
        self.sb_allocs = {}
        self.ps = [nc.alloc_psum_tensor(f"ps{i}", [128, 512], F32).ap()
                   for i in range(8)]
        self._uid = 0

    def sbuf(self, name, shape, dtype):
        per_part = int(np.prod(shape[1:])) * (2 if dtype == BF16 else 4)
        size = (per_part + 63) & ~63
        for seg in self.sb_free:
            if seg[1] - seg[0] >= size:
                off = seg[0]
                seg[0] += size
                break
        else:
            raise MemoryError(f"sbuf full allocating {name} ({size}B/part)")
        self._uid += 1
        h = self.nc.alloc_sbuf_tensor_at(f"{name}_{self._uid}", shape, dtype,
                                         offset=off)
        ap = h.ap()
        self.sb_allocs[name] = [off, size, ap, list(self.pending)]
        return ap

    def sfree(self, *names):
        pend = list(self.pending)
        for name in names:
            off, size, _, _ = self.sb_allocs.pop(name)
            self.sb_free.append([off, off + size])
            pref = name + "."
            for t in list(self.tok_w):
                if t == name or t.startswith(pref):
                    pend.append(self.tok_w.pop(t))
            for t in list(self.tok_r):
                if t == name or t.startswith(pref):
                    pend.extend(self.tok_r.pop(t))
        self.sb_free.sort()
        merged = []
        for seg in self.sb_free:
            if merged and merged[-1][1] >= seg[0]:
                merged[-1][1] = max(merged[-1][1], seg[1])
            elif seg[1] > seg[0]:
                merged.append(seg)
        self.sb_free = merged
        self.pending = pend

    def barrier(self):
        self.pending = [s[-1] for s in self.streams.values() if s]

    def _record(self, op, reads, writes):
        deps = []
        for t in reads:
            w = self.tok_w.get(t)
            if w is not None:
                deps.append(w)
        for t in writes:
            w = self.tok_w.get(t)
            if w is not None:
                deps.append(w)
            deps.extend(self.tok_r.get(t, ()))
            al = self.sb_allocs.get(t.split(".", 1)[0])
            if al is not None and al[3]:
                deps.extend(al[3])
        for d in deps:
            if d is op:
                continue
            if (d.stream == op.stream and not d.is_dma and not op.is_dma
                    and op.stream == "tensor"):
                continue
            op.deps.append(d)
            d.need_inc = True
        for t in reads:
            self.tok_r.setdefault(t, []).append(op)
        for t in writes:
            self.tok_w[t] = op
            self.tok_r[t] = []
        self.streams[op.stream].append(op)
        return op

    def op(self, stream, emit, reads=(), writes=(), tag=""):
        return self._record(_Op(stream, emit, False, tag), reads, writes)

    def dma(self, queue, out_ap, in_ap, reads=(), writes=(), transpose=False,
            tag=""):
        if transpose:
            emit = lambda e: e.dma_start_transpose(out_ap, in_ap)
        else:
            emit = lambda e: e.dma_start(out=out_ap, in_=in_ap)
        op = _Op(queue, emit, True, tag)
        ops = self.dma_ops.setdefault(queue, [])
        j = len(ops)
        if queue not in self.dma_sems:
            self.dma_sems[queue] = [
                self.nc.alloc_semaphore(f"dq_{queue}{i}")
                for i in range(_DMA_POOL)]
        op.dma_sem = self.dma_sems[queue][j % _DMA_POOL]
        op.dma_use = j // _DMA_POOL
        op.need_inc = True
        if j >= _DMA_POOL:
            op.deps.append(ops[j - _DMA_POOL])
        ops.append(op)
        return self._record(op, reads, writes)

    def _completion(self, op):
        if op.is_dma:
            return op.dma_sem, 16 * (op.dma_use + 1)
        return self.comp_sem[op.stream], op.count

    def finalize(self):
        for e in _ENGS:
            c = 0
            for op in self.streams[e]:
                if not op.is_dma and op.need_inc:
                    c += 1
                    op.count = c
        known = {e: {} for e in _ENGS}

        def section(eng_name):
            def run(eng):
                kn = known[eng_name]
                for op in self.streams[eng_name]:
                    best = {}
                    for d in op.deps:
                        sem, val = self._completion(d)
                        if val > best.get(id(sem), (None, -1))[1]:
                            best[id(sem)] = (sem, val)
                    for sem, val in best.values():
                        if kn.get(id(sem), -1) >= val:
                            continue
                        eng.wait_ge(sem, val)
                        kn[id(sem)] = val
                    inst = op.emit(eng)
                    if op.need_inc:
                        if op.is_dma:
                            inst.then_inc(op.dma_sem, 16)
                        else:
                            inst.then_inc(self.comp_sem[op.stream], 1)
                if eng_name == "sync":
                    for q, ops in self.dma_ops.items():
                        fin = {}
                        for op in ops:
                            fin[id(op.dma_sem)] = (op.dma_sem,
                                                   16 * (op.dma_use + 1))
                        for sem, val in fin.values():
                            if kn.get(id(sem), -1) < val:
                                eng.wait_ge(sem, val)
                                kn[id(sem)] = val
                    for e2 in _ENGS:
                        if e2 == "sync":
                            continue
                        cnt = [o.count for o in self.streams[e2]
                               if not o.is_dma and o.need_inc]
                        if cnt and kn.get(id(self.comp_sem[e2]), -1) < cnt[-1]:
                            eng.wait_ge(self.comp_sem[e2], cnt[-1])
            return run

        with self.nc.Block() as block:
            if self.streams["tensor"]:
                block.tensor(section("tensor"))
            if self.streams["vector"]:
                block.vector(section("vector"))
            if self.streams["scalar"]:
                block.scalar(section("scalar"))
            if self.streams["gpsimd"]:
                block.gpsimd(section("gpsimd"))
            block.sync(section("sync"))



def _host_tables():
    inv = 1.0 / (10000.0 ** (np.arange(0, RD, 2, dtype=np.float64) / RD))
    pos = np.arange(S, dtype=np.float64)
    fr = inv[:, None] * pos[None, :]
    c64 = np.concatenate([np.cos(fr), np.cos(fr)], 0)
    s64 = np.concatenate([-np.sin(fr), np.sin(fr)], 0)
    ropeC = np.concatenate([c64, c64], 0).astype(np.float32)
    ropeS = np.concatenate([s64, s64], 0).astype(np.float32)
    qi = np.arange(512)[None, None, :]
    p = np.arange(128)[None, :, None]
    m = np.arange(4)[:, None, None]
    masks = (qi >= 128 * m + p).astype(np.float32).astype(BF)
    return ropeC, ropeS, masks



def _alloc_ln_scratch(mk, pref):
    sc = {}
    for n in ("sum", "s2", "nmean", "ex2", "nvar", "lnv", "rstd", "b"):
        sc[n] = mk.sbuf(f"{pref}{n}", [128, 1], F32)
    sc["eps"] = mk.sbuf(f"{pref}eps", [128, 1], F32)
    sc["_p"] = pref
    mk.op("vector", lambda e: e.memset(sc["eps"][:], 1e-5),
          writes=[f"{pref}eps"])
    return sc


def _ln_chain(mk, xt, xt_toks, n_elems, junk, out_bf, out_tok, sc):
    p = sc["_p"]
    inv = 1.0 / n_elems
    mk.op("vector", lambda e: e.reduce_sum(sc["sum"][:], xt[:], axis=AX.X),
          reads=xt_toks, writes=[f"{p}sum"])
    mk.op("scalar", lambda e: e.activation(out=junk[:], in_=xt[:],
          func=AF.Square, accum_out=sc["s2"][:, 0:1]),
          reads=xt_toks, writes=[f"{p}s2", f"{p}junk"])
    mk.op("vector", lambda e: e.tensor_scalar_mul(sc["nmean"][:], sc["sum"][:],
          -inv), reads=[f"{p}sum"], writes=[f"{p}nmean"])
    mk.op("vector", lambda e: e.tensor_scalar_mul(sc["ex2"][:], sc["s2"][:],
          inv), reads=[f"{p}s2"], writes=[f"{p}ex2"])
    mk.op("vector", lambda e: e.tensor_scalar(sc["nvar"][:], sc["nmean"][:],
          sc["nmean"][:, 0:1], sc["ex2"][:, 0:1], ALU.mult, ALU.subtract),
          reads=[f"{p}nmean", f"{p}ex2"], writes=[f"{p}nvar"])
    mk.op("scalar", lambda e: e.activation(out=sc["lnv"][:], in_=sc["nvar"][:],
          func=AF.Ln, bias=sc["eps"][:, 0:1], scale=-1.0),
          reads=[f"{p}nvar", f"{p}eps"], writes=[f"{p}lnv"])
    mk.op("scalar", lambda e: e.activation(out=sc["rstd"][:], in_=sc["lnv"][:],
          func=AF.Exp, scale=-0.5),
          reads=[f"{p}lnv"], writes=[f"{p}rstd"])
    mk.op("vector", lambda e: e.tensor_scalar_mul(sc["b"][:], sc["nmean"][:],
          sc["rstd"][:, 0:1]),
          reads=[f"{p}nmean", f"{p}rstd"], writes=[f"{p}b"])
    mk.op("vector", lambda e: e.tensor_scalar(out_bf[:], xt[:],
          sc["rstd"][:, 0:1], sc["b"][:, 0:1], ALU.mult, ALU.add),
          reads=list(xt_toks) + [f"{p}rstd", f"{p}b"], writes=[out_tok])


def _ph_ln(mk, T, src, dst, nrows, pref):
    sc = _alloc_ln_scratch(mk, pref)
    xt = [mk.sbuf(f"{pref}x{i}", [128, D], F32) for i in range(2)]
    junk = mk.sbuf(f"{pref}junk", [128, D], BF16)
    ob = [mk.sbuf(f"{pref}o{i}", [128, D], BF16) for i in range(2)]
    for i in range(nrows // 128):
        sl = i % 2
        mk.dma("sync", xt[sl][:], T[src][i * 128:(i + 1) * 128, :],
               reads=[src], writes=[f"{pref}x{sl}"])
        _ln_chain(mk, xt[sl], [f"{pref}x{sl}"], D, junk, ob[sl],
                  f"{pref}o{sl}", sc)
        mk.dma("sync", T[dst][i * 128:(i + 1) * 128, :], ob[sl][:],
               reads=[f"{pref}o{sl}"], writes=[f"{dst}.{i}"])
    mk.sfree(*[f"{pref}{n}" for n in
               ("sum", "s2", "nmean", "ex2", "nvar", "lnv", "rstd", "b",
                "eps", "junk", "x0", "x1", "o0", "o1")])


def _tm_tok(dst, i0, n):
    return [f"{dst}.{i}" for i in range(i0 // 128, (i0 + n + 127) // 128)]



def _ph_l0_proj(mk, T):
    nc = mk.nc
    wq = mk.sbuf("wq", [128, 16 * 1024], BF16)
    wqr = mk.sbuf("wqr", [128, 16 * 512], BF16)
    wkvd = mk.sbuf("wkvd", [128, 16 * 512], BF16)
    wkr = mk.sbuf("wkr", [128, 16 * 64], BF16)
    for dc in range(16):
        mk.dma("gpsimd", wq[:, dc * 1024:(dc + 1) * 1024],
               T["Wq_h"][dc * 128:(dc + 1) * 128, :],
               reads=["Wq_h"], writes=[f"wq.{dc}"])
        mk.dma("gpsimd", wqr[:, dc * 512:(dc + 1) * 512],
               T["Wqr_h"][dc * 128:(dc + 1) * 128, :],
               reads=["Wqr_h"], writes=[f"wqr.{dc}"])
        mk.dma("gpsimd", wkvd[:, dc * 512:(dc + 1) * 512],
               T["Wkvd"][dc * 128:(dc + 1) * 128, :],
               reads=["Wkvd"], writes=[f"wkvd.{dc}"])
        mk.dma("gpsimd", wkr[:, dc * 64:(dc + 1) * 64],
               T["Wkr"][dc * 128:(dc + 1) * 128, :],
               reads=["Wkr"], writes=[f"wkr.{dc}"])
    ropeC = mk.sbuf("ropeC", [128, S], F32)
    ropeS = mk.sbuf("ropeS", [128, S], F32)
    mk.dma("sync", ropeC[:], T["ropeC"][:, :], reads=["ropeC_in"],
           writes=["ropeC"])
    mk.dma("sync", ropeS[:], T["ropeS"][:, :], reads=["ropeS_in"],
           writes=["ropeS"])
    rx = [mk.sbuf(f"rx{i}", [128, 512], F32) for i in range(2)]
    rr = [mk.sbuf(f"rr{i}", [128, 512], F32) for i in range(2)]
    rt = [mk.sbuf(f"rt{i}", [128, 512], F32) for i in range(2)]
    ro = [mk.sbuf(f"ro{i}", [128, 512], BF16) for i in range(2)]
    krT = mk.sbuf("krT", [64, S], BF16)
    T["_krT"] = krT

    def rope_block(src_, stok, r0, nrow, t0, dst_ap, dtok, sl):
        xa, ra, ta, oa = rx[sl], rr[sl], rt[sl], ro[sl]
        rts = [f"rr{sl}.{g}{ab}" for g in range(nrow // 64) for ab in "ab"]
        mk.dma("sync", xa[0:nrow, :], src_[r0:r0 + nrow, t0:t0 + 512],
               reads=[stok], writes=[f"rx{sl}"])
        for g in range(nrow // 64):
            mk.dma("sync", ra[g * 64:g * 64 + 32, :],
                   src_[r0 + g * 64 + 32:r0 + g * 64 + 64, t0:t0 + 512],
                   reads=[stok], writes=[f"rr{sl}.{g}a"])
            mk.dma("sync", ra[g * 64 + 32:g * 64 + 64, :],
                   src_[r0 + g * 64:r0 + g * 64 + 32, t0:t0 + 512],
                   reads=[stok], writes=[f"rr{sl}.{g}b"])
        mk.op("vector", lambda e: e.tensor_mul(ta[0:nrow, :], xa[0:nrow, :],
              ropeC[0:nrow, t0:t0 + 512]),
              reads=[f"rx{sl}", "ropeC"], writes=[f"rt{sl}"])
        mk.op("vector", lambda e: e.tensor_mul(ra[0:nrow, :], ra[0:nrow, :],
              ropeS[0:nrow, t0:t0 + 512]),
              reads=rts + ["ropeS"], writes=rts)
        mk.op("vector", lambda e: e.tensor_add(oa[0:nrow, :], ta[0:nrow, :],
              ra[0:nrow, :]),
              reads=[f"rt{sl}"] + rts, writes=[f"ro{sl}"])
        mk.dma("sync", dst_ap, oa[0:nrow, :],
               reads=[f"ro{sl}"], writes=[dtok])

    ckvT = mk.sbuf("ckvT", [128, 4 * S], BF16)
    xnT = [mk.sbuf(f"pxnT{i}", [128, 16 * 512], BF16) for i in range(2)]
    ev = [mk.sbuf(f"pev{i}", [128, 512], BF16) for i in range(4)]
    evf = [mk.sbuf(f"pevf{i}", [128, 512], F32) for i in range(2)]
    pi = [0]

    def mm_chain(n_k, lhsT_of, rhs_of, rtoks, mrows=128):
        bank = pi[0] % 2
        pi[0] += 1
        ptok = f"ps.{bank}"
        for k in range(n_k):
            mk.op("tensor",
                  (lambda e, k=k, bank=bank: e.matmul(
                      mk.ps[bank][0:mrows, :], lhsT_of(k), rhs_of(k),
                      start=(k == 0), stop=(k == n_k - 1))),
                  reads=rtoks(k), writes=[ptok])
        return mk.ps[bank], ptok

    for tb in range(8):
        t0 = tb * 512
        sl = tb % 2
        for dc in range(16):
            mk.dma("sync", xnT[sl][:, dc * 512:(dc + 1) * 512],
                   T["xn0_tm"][t0:t0 + 512, dc * 128:(dc + 1) * 128],
                   reads=_tm_tok("xn0_tm", t0, 512),
                   writes=[f"pxnT{sl}.{dc}"], transpose=True)
        for oc in range(4):
            p, ptok = mm_chain(
                16,
                lambda k, oc=oc: wkvd[:, k * 512 + oc * 128:
                                      k * 512 + (oc + 1) * 128],
                lambda k, sl=sl: xnT[sl][:, k * 512:(k + 1) * 512],
                lambda k, sl=sl: [f"wkvd.{k}", f"pxnT{sl}.{k}"])
            mk.op("vector",
                  (lambda e, p=p, oc=oc, t0=t0: e.tensor_copy(
                      ckvT[:, oc * S + t0:oc * S + t0 + 512], p[:])),
                  reads=[ptok], writes=[f"ckvT.{oc}.{tb}"])
        p, ptok = mm_chain(
            16,
            lambda k: wkr[:, k * 64:(k + 1) * 64],
            lambda k, sl=sl: xnT[sl][:, k * 512:(k + 1) * 512],
            lambda k, sl=sl: [f"wkr.{k}", f"pxnT{sl}.{k}"], mrows=64)
        fs = tb % 2
        mk.op("vector",
              (lambda e, p=p, fs=fs: e.tensor_copy(
                  evf[fs][0:64, :], p[0:64, :])),
              reads=[ptok], writes=[f"pevf{fs}"])
        mk.dma("sync", T["krraw"][0:64, t0:t0 + 512], evf[fs][0:64, :],
               reads=[f"pevf{fs}"], writes=[f"krraw.{tb}"])
        for oc in range(8):
            p, ptok = mm_chain(
                16,
                lambda k, oc=oc: wq[:, k * 1024 + oc * 128:
                                    k * 1024 + (oc + 1) * 128],
                lambda k, sl=sl: xnT[sl][:, k * 512:(k + 1) * 512],
                lambda k, sl=sl: [f"wq.{k}", f"pxnT{sl}.{k}"])
            es = oc % 4
            mk.op("vector",
                  (lambda e, p=p, es=es: e.tensor_copy(ev[es][:], p[:])),
                  reads=[ptok], writes=[f"pev{es}"])
            mk.dma("sync", T["qcT"][oc * 128:(oc + 1) * 128, t0:t0 + 512],
                   ev[es][:], reads=[f"pev{es}"], writes=[f"qcT.{oc}.{tb}"])
        for oc in range(4):
            p, ptok = mm_chain(
                16,
                lambda k, oc=oc: wqr[:, k * 512 + oc * 128:
                                     k * 512 + (oc + 1) * 128],
                lambda k, sl=sl: xnT[sl][:, k * 512:(k + 1) * 512],
                lambda k, sl=sl: [f"wqr.{k}", f"pxnT{sl}.{k}"])
            fs = oc % 2
            mk.op("vector",
                  (lambda e, p=p, fs=fs: e.tensor_copy(evf[fs][:], p[:])),
                  reads=[ptok], writes=[f"pevf{fs}"])
            mk.dma("sync", T["qrraw"][oc * 128:(oc + 1) * 128, t0:t0 + 512],
                   evf[fs][:], reads=[f"pevf{fs}"],
                   writes=[f"qrraw.{oc}.{tb}"])
        for r in range(4):
            sl = (tb * 4 + r) % 2
            rope_block(T["qrraw"], f"qrraw.{r}.{tb}", r * 128, 128, t0,
                       T["qrT"][r * 128:(r + 1) * 128, t0:t0 + 512],
                       f"qrT.{r}.{tb}", sl)
        rope_block(T["krraw"], f"krraw.{tb}", 0, 64, t0,
                   krT[0:64, t0:t0 + 512], f"krT.{tb}", tb % 2)
    mk.sfree("wq", "wqr", "wkvd", "wkr", "ropeC", "ropeS", "rx0", "rx1",
             "rr0", "rr1", "rt0", "rt1", "ro0", "ro1")
    wkvu = mk.sbuf("wkvu", [128, 4 * 2048], BF16)
    for cc in range(4):
        mk.dma("gpsimd", wkvu[:, cc * 2048:(cc + 1) * 2048],
               T["Wkvup_h"][cc * 128:(cc + 1) * 128, :],
               reads=["Wkvup_h"], writes=[f"wkvu.{cc}"])
    for tb in range(8):
        t0 = tb * 512
        for oc in range(8):
            p, ptok = mm_chain(
                4,
                lambda k, oc=oc: wkvu[:, k * 2048 + oc * 128:
                                      k * 2048 + (oc + 1) * 128],
                lambda k, t0=t0: ckvT[:, k * S + t0:k * S + t0 + 512],
                lambda k, tb=tb: [f"wkvu.{k}", f"ckvT.{k}.{tb}"])
            es = oc % 4
            mk.op("vector",
                  (lambda e, p=p, es=es: e.tensor_copy(ev[es][:], p[:])),
                  reads=[ptok], writes=[f"pev{es}"])
            mk.dma("sync", T["kcT"][oc * 128:(oc + 1) * 128, t0:t0 + 512],
                   ev[es][:], reads=[f"pev{es}"], writes=[f"kcT.{oc}.{tb}"])
    for tc in range(S // 128):
        for dvb in range(2):
            p, ptok = mm_chain(
                4,
                lambda k, tc=tc: ckvT[:, k * S + tc * 128:k * S + (tc+1)*128],
                lambda k, dvb=dvb: wkvu[:, k * 2048 + 1024 + dvb * 512:
                                        k * 2048 + 1024 + (dvb + 1) * 512],
                lambda k, tc=tc: [f"wkvu.{k}", f"ckvT.{k}.{tc // 4}"])
            es = dvb % 4
            mk.op("vector",
                  (lambda e, p=p, es=es: e.tensor_copy(ev[es][:], p[:])),
                  reads=[ptok], writes=[f"pev{es}"])
            mk.dma("sync", T["v_tm"][tc * 128:(tc + 1) * 128,
                                     dvb * 512:(dvb + 1) * 512],
                   ev[es][:], reads=[f"pev{es}"], writes=[f"v_tm.{tc}.{dvb}"])
    mk.sfree("wkvu", "ckvT", "pxnT0", "pxnT1", "pev0", "pev1", "pev2",
             "pev3", "pevf0", "pevf1")
def _ph_l0_attn(mk, T, filler=None):
    krT = T["_krT"]
    mask = mk.sbuf("mask", [128, 4 * 512], BF16)
    for m in range(4):
        mk.dma("sync", mask[:, m * 512:(m + 1) * 512], T["masks"][m, :, :],
               reads=["masks_in"], writes=[f"mask.{m}"])
    sc192 = mk.sbuf("sc192", [128, 1], F32)
    mk.op("vector", lambda e: e.memset(sc192[:], SCALE0), writes=["sc192"])
    ones_b = mk.sbuf("ones_b", [128, 1], BF16)
    mk.op("vector", lambda e: e.memset(ones_b[:], 1.0), writes=["ones_b"])
    ones_f = mk.sbuf("ones_f", [1, 128], F32)
    mk.op("vector", lambda e: e.memset(ones_f[:], 1.0), writes=["ones_f"])

    qc = [mk.sbuf(f"aqc{i}", [128, 512], BF16) for i in range(3)]
    qr = [mk.sbuf(f"aqr{i}", [64, 512], BF16) for i in range(3)]
    kb = [mk.sbuf(f"akb{i}", [128, S], BF16) for i in range(2)]
    vb = [mk.sbuf(f"avb{i}", [128, S], BF16) for i in range(2)]
    pt = [mk.sbuf(f"apt{i}", [128, 512], BF16) for i in range(3)]
    dens = [mk.sbuf(f"adens{i}", [1, 512], F32) for i in range(2)]
    denr = [mk.sbuf(f"adenr{i}", [1, 512], F32) for i in range(2)]
    bcs = mk.sbuf("abcs", [128, 512], F32)
    onb = mk.sbuf("aonb", [128, 512], BF16)
    psS = [mk.ps[0], mk.ps[1], mk.ps[6]]
    psSt = ["ps.0", "ps.1", "ps.6"]
    psO = [mk.ps[2], mk.ps[3]]
    psD = [mk.ps[4][0:1, :], mk.ps[5][0:1, :]]
    psB = mk.ps[7]
    pend = []

    def epilogue():
        if not pend:
            return
        h, qb, par = pend.pop(0)
        q0 = qb * 512
        mk.op("vector", (lambda e, par=par: e.tensor_copy(
                  dens[par][:], psD[par][:])),
              reads=[f"ps.{4 + par}"], writes=[f"adens{par}"])
        mk.op("vector", (lambda e, par=par: e.reciprocal(
                  denr[par][:], dens[par][:])),
              reads=[f"adens{par}"], writes=[f"adenr{par}"])
        mk.op("tensor", (lambda e, par=par: e.matmul(
                  psB[:], ones_f[:], denr[par][:], start=True, stop=True)),
              reads=["ones_f", f"adenr{par}"], writes=["ps.7"])
        mk.op("vector", lambda e: e.tensor_copy(bcs[:], psB[:]),
              reads=["ps.7"], writes=["abcs"])
        mk.op("vector", (lambda e, par=par: e.tensor_mul(
                  onb[:], psO[par][:], bcs[:])),
              reads=[f"ps.{2 + par}", "abcs"], writes=["aonb"])
        mk.dma("sync", T["onormT"][h * 128:(h + 1) * 128, q0:q0 + 512],
               onb[:], reads=["aonb"], writes=[f"onormT.{h}.{qb}"])

    for h in range(8):
        sh = h % 2
        mk.dma("sync", kb[sh][:],
               T["kcT"][h * 128:(h + 1) * 128, :],
               reads=[f"kcT.{h}.{g}" for g in range(8)],
               writes=[f"akb{sh}"])
        for g4 in range(8):
            vsrc = T["v_tm"][g4 * 512:(g4 + 1) * 512,
                             h * 128:(h + 1) * 128]
            vsrc = vsrc.rearrange("(g r) c -> r g c", g=4)
            mk.dma("sync", vb[sh][:, g4 * 512:(g4 + 1) * 512], vsrc,
                   reads=[f"v_tm.{g4 * 4 + i}.{1}" for i in range(4)]
                   + [f"v_tm.{g4 * 4 + i}.{0}" for i in range(4)],
                   writes=[f"avb{sh}.{g4}"])
        for qb in range(8):
            q0 = qb * 512
            nk = 4 * qb + 4
            par = qb % 2
            sq = qb % 3
            mk.dma("scalar", qc[sq][:],
                   T["qcT"][h * 128:(h + 1) * 128, q0:q0 + 512],
                   reads=[f"qcT.{h}.{qb}"], writes=[f"aqc{sq}"])
            mk.dma("scalar", qr[sq][:],
                   T["qrT"][h * 64:(h + 1) * 64, q0:q0 + 512],
                   reads=[f"qrT.{h // 2}.{qb}"], writes=[f"aqr{sq}"])
            def emit_S(k):
                sl = k % 3
                mk.op("tensor", (lambda e, sl=sl, sh=sh, sq=sq, k=k:
                      e.matmul(
                          psS[sl][:], kb[sh][:, k * 128:(k + 1) * 128],
                          qc[sq][:], start=True, stop=False)),
                      reads=[f"akb{sh}", f"aqc{sq}"], writes=[psSt[sl]])
                mk.op("tensor", (lambda e, sl=sl, sq=sq, k=k: e.matmul(
                          psS[sl][:], krT[0:64, k * 128:(k + 1) * 128],
                          qr[sq][:], start=False, stop=True)),
                      reads=[f"krT.{k // 4}", f"aqr{sq}"], writes=[psSt[sl]])
                mk.op("scalar", (lambda e, sl=sl: e.activation(
                          out=pt[sl][:], in_=psS[sl][:], func=AF.Exp,
                          scale=sc192[:, 0:1])),
                      reads=[psSt[sl], "sc192"], writes=[f"apt{sl}"])
                m = k - 4 * qb
                if m >= 0:
                    mk.op("vector", (lambda e, sl=sl, m=m: e.tensor_mul(
                              pt[sl][:], pt[sl][:],
                              mask[:, m * 512:(m + 1) * 512])),
                          reads=[f"apt{sl}", f"mask.{m}"],
                          writes=[f"apt{sl}"])

            def emit_AV(k):
                sl = k % 3
                mk.op("tensor", (lambda e, sl=sl, sh=sh, k=k, nk=nk,
                      par=par: e.matmul(psO[par][:],
                               vb[sh][:, k * 128:(k + 1) * 128], pt[sl][:],
                               start=(k == 0), stop=(k == nk - 1))),
                      reads=[f"avb{sh}.{k // 4}", f"apt{sl}"],
                      writes=[f"ps.{2 + par}"])
                mk.op("tensor", (lambda e, sl=sl, k=k, nk=nk, par=par:
                      e.matmul(psD[par][:], ones_b[:], pt[sl][:],
                               start=(k == 0), stop=(k == nk - 1))),
                      reads=["ones_b", f"apt{sl}"], writes=[f"ps.{4 + par}"])

            for k in range(nk):
                emit_S(k)
                if k >= 2:
                    emit_AV(k - 2)
            emit_AV(nk - 2)
            emit_AV(nk - 1)
            pend.append((h, qb, par))
            if len(pend) > 1:
                epilogue()
            if filler is not None:
                filler(h * 8 + qb)
    while pend:
        epilogue()
    mk.sfree("mask", "sc192", "ones_b", "ones_f", "abcs", "aonb",
             "aqc0", "aqc1", "aqc2", "aqr0", "aqr1", "aqr2", "akb0", "akb1", "avb0", "avb1",
             "adens0", "adens1", "adenr0", "adenr1", "apt0", "apt1", "apt2",
             "krT")


def _ph_l0_wo(mk, T, cc=False):
    woh = mk.sbuf("woh", [128, 8 * 2048], BF16)
    for r in range(8):
        mk.dma("gpsimd", woh[:, r * 2048:(r + 1) * 2048],
               T["Wo_h"][r * 128:(r + 1) * 128, :],
               reads=["Wo_h"], writes=[f"woh.{r}"])
    onb = [mk.sbuf(f"won{i}", [128, 8 * 512], BF16) for i in range(2)]
    ev = [mk.sbuf(f"wev{i}", [128, 512], BF16) for i in range(2)]
    for tb in range(8):
        t0 = tb * 512
        sl = tb % 2
        for r in range(8):
            mk.dma("sync", onb[sl][:, r * 512:(r + 1) * 512],
                   T["onormT"][r * 128:(r + 1) * 128, t0:t0 + 512],
                   reads=[f"onormT.{r}.{tb}"], writes=[f"won{sl}.{r}"])
        for do in range(16):
            bank = do % 2
            for r in range(8):
                mk.op("tensor", (lambda e, bank=bank, r=r, do=do, sl=sl:
                      e.matmul(mk.ps[bank][:],
                               woh[:, r * 2048 + do * 128:
                                   r * 2048 + (do + 1) * 128],
                               onb[sl][:, r * 512:(r + 1) * 512],
                               start=(r == 0), stop=(r == 7))),
                      reads=[f"woh.{r}", f"won{sl}.{r}"],
                      writes=[f"ps.{bank}"])
            mk.op("vector", (lambda e, bank=bank: e.tensor_copy(
                      ev[bank][:], mk.ps[bank][:])),
                  reads=[f"ps.{bank}"], writes=[f"wev{bank}"])
            if cc:
                jblk, tq = tb // 4, tb % 4
                dst = T["ccA_in"][jblk * D + do * 128:jblk * D + (do + 1) * 128,
                                  tq * 512:(tq + 1) * 512]
                dtok = f"ccA_in.{do}.{tb}"
            else:
                dst = T["a0pT"][do * 128:(do + 1) * 128, t0:t0 + 512]
                dtok = f"a0pT.{do}.{tb}"
            mk.dma("sync", dst, ev[bank][:], reads=[f"wev{bank}"],
                   writes=[dtok])
    mk.sfree("woh", "won0", "won1", "wev0", "wev1")



def _ph_quant_p1(mk, T, wname, nt, ncols, nb=2):
    p = f"q{wname}"
    wt = [mk.sbuf(f"{p}w{i}", [128, ncols], F32) for i in range(nb)]
    acc = mk.sbuf(f"{p}acc", [128, 64], F32)
    accv = mk.sbuf(f"{p}accv", [128, 1], F32)
    ones128 = mk.sbuf(f"{p}o128", [128, 1], F32)
    ones1 = mk.sbuf(f"{p}o1", [1, 128], F32)
    cpos = mk.sbuf(f"{p}cp", [1, 1], F32)
    cneg = mk.sbuf(f"{p}cn", [1, 1], F32)
    trow = mk.sbuf(f"{p}tr", [1, 2], F32)
    tpos = mk.sbuf(f"{p}tp", [128, 1], F32)
    tneg = mk.sbuf(f"{p}tn", [128, 1], F32)
    numel = nt * 128 * ncols
    mk.op("vector", lambda e: e.memset(ones128[:], 1.0), writes=[f"{p}o128"])
    mk.op("vector", lambda e: e.memset(ones1[:], 1.0), writes=[f"{p}o1"])
    mk.op("vector", lambda e: e.memset(cpos[:], 0.7 / numel),
          writes=[f"{p}cp"])
    mk.op("vector", lambda e: e.memset(cneg[:], -0.7 / numel),
          writes=[f"{p}cn"])
    for i in range(nt):
        sl = i % nb
        mk.dma("gpsimd", wt[sl][:], T[wname][i * 128:(i + 1) * 128, :],
               reads=[wname], writes=[f"{p}w{sl}"])
        mk.op("vector", (lambda e, sl=sl, i=i: e.reduce_sum(
                  acc[:, i:i + 1], wt[sl][:], axis=AX.X,
                  apply_absolute_value=True)),
              reads=[f"{p}w{sl}"], writes=[f"{p}acc.{i}"])
    mk.op("vector", lambda e: e.reduce_sum(accv[:], acc[:, 0:nt], axis=AX.X),
          reads=[f"{p}acc.{i}" for i in range(nt)], writes=[f"{p}accv"])
    T[f"_{p}tp"], T[f"_{p}tn"] = tpos, tneg
    T[f"_{p}p1b"] = (accv, ones128, ones1, cpos, cneg, trow, tpos, tneg)
    s1 = [mk.sbuf(f"{p}s1{i}", [128, ncols], BF16) for i in range(nb)]
    s2 = [mk.sbuf(f"{p}s2{i}", [128, ncols], BF16) for i in range(nb)]
    wq = [mk.sbuf(f"{p}wq{i}", [128, ncols], BF16) for i in range(nb)]
    T[f"_{p}bufs"] = (wt, s1, s2, wq)


def _ph_quant_p1b(mk, T, wname):
    p = f"q{wname}"
    accv, ones128, ones1, cpos, cneg, trow, tpos, tneg = T[f"_{p}p1b"]
    psT = mk.ps[7][0:1, 0:1]
    mk.op("tensor", lambda e: e.matmul(psT, ones128[:], accv[:],
          start=True, stop=True),
          reads=[f"{p}o128", f"{p}accv"], writes=["ps.7"])
    mk.op("vector", lambda e: e.tensor_scalar_mul(trow[0:1, 0:1], psT,
          cpos[0:1, 0:1]),
          reads=["ps.7", f"{p}cp"], writes=[f"{p}tr"])
    mk.op("vector", lambda e: e.tensor_scalar_mul(trow[0:1, 1:2], psT,
          cneg[0:1, 0:1]),
          reads=["ps.7", f"{p}cn", f"{p}tr"], writes=[f"{p}tr"])
    psBC = mk.ps[7][:, 0:2]
    mk.op("tensor", lambda e: e.matmul(psBC, ones1[:], trow[0:1, :],
          start=True, stop=True),
          reads=[f"{p}o1", f"{p}tr"], writes=["ps.7"])
    mk.op("vector", lambda e: e.tensor_copy(tpos[:], mk.ps[7][:, 0:1]),
          reads=["ps.7"], writes=[f"{p}tp"])
    mk.op("vector", lambda e: e.tensor_copy(tneg[:], mk.ps[7][:, 1:2]),
          reads=["ps.7"], writes=[f"{p}tn"])
    mk.sfree(f"{p}acc", f"{p}accv", f"{p}o128", f"{p}o1", f"{p}cp",
             f"{p}cn", f"{p}tr")


def _quant_p2_tile(mk, T, wname, qname, i):
    p = f"q{wname}"
    wt, s1, s2, wq = T[f"_{p}bufs"]
    tpos, tneg = T[f"_{p}tp"], T[f"_{p}tn"]
    nb = len(wt)
    sl = i % nb
    mk.dma("gpsimd", wt[sl][:], T[wname][i * 128:(i + 1) * 128, :],
           reads=[wname], writes=[f"{p}w{sl}"])
    mk.op("scalar", (lambda e, sl=sl: e.activation(
              out=s1[sl][:], in_=wt[sl][:], func=AF.Sign,
              bias=tneg[:, 0:1])),
          reads=[f"{p}w{sl}", f"{p}tn"], writes=[f"{p}s1{sl}"])
    mk.op("scalar", (lambda e, sl=sl: e.activation(
              out=s2[sl][:], in_=wt[sl][:], func=AF.Sign,
              bias=tpos[:, 0:1])),
          reads=[f"{p}w{sl}", f"{p}tp"], writes=[f"{p}s2{sl}"])
    mk.op("vector", (lambda e, sl=sl: e.tensor_add(
              wq[sl][:], s1[sl][:], s2[sl][:])),
          reads=[f"{p}s1{sl}", f"{p}s2{sl}"], writes=[f"{p}wq{sl}"])
    mk.dma("gpsimd", T[qname][i * 128:(i + 1) * 128, :], wq[sl][:],
           reads=[f"{p}wq{sl}"], writes=[f"{qname}.{i}"])


def _ph_quant_free(mk, T, wname):
    p = f"q{wname}"
    nb = len(T[f"_{p}bufs"][0])
    names = []
    for i in range(nb):
        names += [f"{p}w{i}", f"{p}s1{i}", f"{p}s2{i}", f"{p}wq{i}"]
    mk.sfree(f"{p}tp", f"{p}tn", *names)


def _ph_quant(mk, T, wname, qname, nt, ncols):
    _ph_quant_p1(mk, T, wname, nt, ncols)
    _ph_quant_p1b(mk, T, wname)
    for i in range(nt):
        _quant_p2_tile(mk, T, wname, qname, i)
    _ph_quant_free(mk, T, wname)


def _ph_l0_ffn(mk, T):
    b1t = mk.sbuf("b1t", [128, 64], F32)
    b2t = mk.sbuf("b2t", [128, 16], F32)
    mk.dma("sync", b1t[:], T["b1t"][:, :], reads=["b1t_in"], writes=["b1t"])
    mk.dma("sync", b2t[:], T["b2t"][:, :], reads=["b2t_in"], writes=["b2t"])
    for th in range(2):
        tt0 = th * 1024
        xnT = mk.sbuf("fxnT", [128, 16 * 1024], BF16)
        hbuf = mk.sbuf("fh", [128, 64 * 1024], BF16)
        w1t = mk.sbuf("fw1t", [128, 16 * 512], BF16)
        w2t = [mk.sbuf(f"fw2t{i}", [128, 512], BF16) for i in range(2)]
        fev = [mk.sbuf(f"fev{i}", [128, 512], BF16) for i in range(2)]
        for dc in range(16):
            for tq in range(2):
                mk.dma("sync", xnT[:, dc * 1024 + tq * 512:
                                   dc * 1024 + (tq + 1) * 512],
                       T["xn0_own"][tt0 + tq * 512:tt0 + (tq + 1) * 512,
                                    dc * 128:(dc + 1) * 128],
                       reads=_tm_tok("xn0_own", tt0 + tq * 512, 512), writes=[f"fxnT.{dc}.{tq}"],
                       transpose=True)
        for fg in range(16):
            for dc in range(16):
                mk.dma("sync", w1t[:, dc * 512:(dc + 1) * 512],
                       T["w1q"][fg * 512:(fg + 1) * 512,
                                dc * 128:(dc + 1) * 128],
                       reads=[f"w1q.{fg * 4 + i}" for i in range(4)],
                       writes=[f"fw1t.{dc}"], transpose=True)
            for half in range(2):
                for dc in range(16):
                    for fl in range(2):
                        for tq in range(2):
                            pb = half * 4 + fl * 2 + tq
                            fc = fg * 4 + half * 2 + fl
                            mk.op("tensor",
                                  (lambda e, pb=pb, dc=dc, fl=fl, tq=tq,
                                   half=half: e.matmul(
                                      mk.ps[pb][:],
                                      w1t[:, dc * 512 + (half * 2 + fl) * 128:
                                          dc * 512 + (half * 2 + fl + 1) * 128],
                                      xnT[:, dc * 1024 + tq * 512:
                                          dc * 1024 + (tq + 1) * 512],
                                      start=(dc == 0), stop=(dc == 15))),
                                  reads=[f"fw1t.{dc}", f"fxnT.{dc}.{tq}"],
                                  writes=[f"ps.{pb}"])
                for fl in range(2):
                    for tq in range(2):
                        pb = half * 4 + fl * 2 + tq
                        fc = fg * 4 + half * 2 + fl
                        mk.op("scalar",
                              (lambda e, pb=pb, fc=fc, tq=tq: e.activation(
                                  out=hbuf[:, fc * 1024 + tq * 512:
                                           fc * 1024 + (tq + 1) * 512],
                                  in_=mk.ps[pb][:], func=AF.Gelu,
                                  bias=b1t[:, fc:fc + 1], scale=0.5)),
                              reads=[f"ps.{pb}", "b1t"],
                              writes=[f"fh.{fc}.{tq}"])
        for dg in range(4):
            for fc in range(64):
                sl = fc % 2
                mk.dma("sync", w2t[sl][:],
                       T["w2q"][dg * 512:(dg + 1) * 512,
                                fc * 128:(fc + 1) * 128],
                       reads=[f"w2q.{dg * 4 + i}" for i in range(4)],
                       writes=[f"fw2t{sl}"], transpose=True)
                for do4 in range(4):
                    for tq in range(2):
                        pb = do4 * 2 + tq
                        mk.op("tensor",
                              (lambda e, pb=pb, sl=sl, do4=do4, tq=tq, fc=fc:
                               e.matmul(
                                  mk.ps[pb][:],
                                  w2t[sl][:, do4 * 128:(do4 + 1) * 128],
                                  hbuf[:, fc * 1024 + tq * 512:
                                       fc * 1024 + (tq + 1) * 512],
                                  start=(fc == 0), stop=(fc == 63))),
                              reads=[f"fw2t{sl}", f"fh.{fc}.{tq}"],
                              writes=[f"ps.{pb}"])
            for do4 in range(4):
                for tq in range(2):
                    pb = do4 * 2 + tq
                    do = dg * 4 + do4
                    es = pb % 2
                    mk.op("vector",
                          (lambda e, pb=pb, do=do, es=es: e.tensor_scalar(
                              fev[es][:], mk.ps[pb][:], 0.5,
                              b2t[:, do:do + 1], ALU.mult, ALU.add)),
                          reads=[f"ps.{pb}", "b2t"], writes=[f"fev{es}"])
                    mk.dma("sync",
                           T["f0T"][do * 128:(do + 1) * 128,
                                    tt0 + tq * 512:tt0 + (tq + 1) * 512],
                           fev[es][:], reads=[f"fev{es}"],
                           writes=[f"f0T.{do}.{th * 2 + tq}"])
        mk.sfree("fxnT", "fh", "fw1t", "fw2t0", "fw2t1", "fev0", "fev1")
    mk.sfree("b1t", "b2t")


def _ph_x1_ln1(mk, T, fused=False):
    sc = [_alloc_ln_scratch(mk, f"l1n{s}") for s in range(2)]
    x0t = [mk.sbuf(f"x0t{s}", [128, D], F32) for s in range(2)]
    ab = [mk.sbuf(f"x1a{s}", [128, D], BF16) for s in range(2)]
    bb = [mk.sbuf(f"x1b{s}", [128, D], BF16) for s in range(2)]
    fb = [mk.sbuf(f"x1f{s}", [128, D], BF16) for s in range(2)]
    x1t = [mk.sbuf(f"x1t{s}", [128, D], F32) for s in range(2)]
    junk = [mk.sbuf(f"x1junk{s}", [128, D], BF16) for s in range(2)]
    ob = [mk.sbuf(f"x1o{s}", [128, D], BF16) for s in range(2)]
    for i in range(TOK // 128):
        i0 = i * 128
        s = i % 2
        mk.dma("sync", x0t[s][:], T["x0_own"][i0:i0 + 128, :],
               reads=["x0_own"], writes=[f"x0t{s}"])
        for blk in range(4):
            if fused:
                mk.dma("sync", ab[s][:, blk * 512:(blk + 1) * 512],
                       T["ccA_out"][blk * 512:(blk + 1) * 512, i0:i0 + 128],
                       reads=["ccA_out"], writes=[f"x1a{s}.{blk}"],
                       transpose=True)
            else:
                mk.dma("sync", ab[s][:, blk * 512:(blk + 1) * 512],
                       T["a0s"][blk * 512:(blk + 1) * 512, i0:i0 + 128],
                       reads=["a0s"], writes=[f"x1a{s}.{blk}"],
                       transpose=True)
                mk.dma("sync", bb[s][:, blk * 512:(blk + 1) * 512],
                       T["a0p"][blk * 512:(blk + 1) * 512, i0:i0 + 128],
                       reads=["a0p"], writes=[f"x1b{s}.{blk}"],
                       transpose=True)
            mk.dma("sync", fb[s][:, blk * 512:(blk + 1) * 512],
                   T["f0T"][blk * 512:(blk + 1) * 512, i0:i0 + 128],
                   reads=[f"f0T.{blk * 4 + q}.{i // 4}" for q in range(4)],
                   writes=[f"x1f{s}.{blk}"], transpose=True)
        atoks = [f"x1a{s}.{b}" for b in range(4)]
        btoks = [f"x1b{s}.{b}" for b in range(4)]
        ftoks = [f"x1f{s}.{b}" for b in range(4)]
        if fused:
            mk.op("vector", (lambda e, s=s: e.tensor_add(
                      x1t[s][:], ab[s][:], fb[s][:])),
                  reads=atoks + ftoks, writes=[f"x1t{s}"])
        else:
            mk.op("vector", (lambda e, s=s: e.tensor_add(
                      x1t[s][:], ab[s][:], bb[s][:])),
                  reads=atoks + btoks, writes=[f"x1t{s}"])
            mk.op("vector", (lambda e, s=s: e.tensor_add(
                      x1t[s][:], x1t[s][:], fb[s][:])),
                  reads=[f"x1t{s}"] + ftoks, writes=[f"x1t{s}"])
        mk.op("vector", (lambda e, s=s: e.tensor_add(
                  x1t[s][:], x1t[s][:], x0t[s][:])),
              reads=[f"x1t{s}", f"x0t{s}"], writes=[f"x1t{s}"])
        mk.dma("sync", T["x1_tm"][i0:i0 + 128, :], x1t[s][:],
               reads=[f"x1t{s}"], writes=[f"x1_tm.{i}"])
        _ln_chain(mk, x1t[s], [f"x1t{s}"], D, junk[s], ob[s], f"x1o{s}",
                  sc[s])
        mk.dma("sync", T["xn1_tm"][i0:i0 + 128, :], ob[s][:],
               reads=[f"x1o{s}"], writes=[f"xn1_tm.{i}"])
    names = []
    for s in range(2):
        names += [f"x0t{s}", f"x1a{s}", f"x1b{s}", f"x1f{s}", f"x1t{s}",
                  f"x1junk{s}", f"x1o{s}"]
        names += [f"l1n{s}{n}" for n in ("sum", "s2", "nmean", "ex2",
                                         "nvar", "lnv", "rstd", "b", "eps")]
    mk.sfree(*names)



def _ph_l1_qkv(mk, T, sp_bf16=False):
    xnT = mk.sbuf("c_xnT", [128, 16 * 2048], BF16)
    for dc in range(16):
        for tb in range(4):
            mk.dma("sync", xnT[:, dc * 2048 + tb * 512:
                               dc * 2048 + (tb + 1) * 512],
                   T["xn1_tm"][tb * 512:(tb + 1) * 512,
                               dc * 128:(dc + 1) * 128],
                   reads=_tm_tok("xn1_tm", tb * 512, 512),
                   writes=[f"c_xnT.{dc}.{tb}"], transpose=True)
    wv = mk.sbuf("c_wv", [128, 16 * 2048], BF16)
    ev = [mk.sbuf(f"c_ev{i}", [128, 512], BF16) for i in range(2)]
    for dc in range(16):
        mk.dma("gpsimd", wv[:, dc * 2048:(dc + 1) * 2048],
               T["Wv1"][dc * 128:(dc + 1) * 128, :],
               reads=["Wv1"], writes=[f"c_wv.{dc}"])
    for dvc in range(16):
        for tb in range(4):
            bank = (dvc * 4 + tb) % 2
            for dc in range(16):
                mk.op("tensor", (lambda e, bank=bank, dc=dc, dvc=dvc, tb=tb:
                      e.matmul(mk.ps[bank][:],
                               wv[:, dc * 2048 + dvc * 128:
                                  dc * 2048 + (dvc + 1) * 128],
                               xnT[:, dc * 2048 + tb * 512:
                                   dc * 2048 + (tb + 1) * 512],
                               start=(dc == 0), stop=(dc == 15))),
                      reads=[f"c_wv.{dc}", f"c_xnT.{dc}.{tb}"],
                      writes=[f"ps.{bank}"])
            mk.op("vector", (lambda e, bank=bank: e.tensor_copy(
                      ev[bank][:], mk.ps[bank][:])),
                  reads=[f"ps.{bank}"], writes=[f"c_ev{bank}"])
            mk.dma("sync", T["vT1"][dvc * 128:(dvc + 1) * 128,
                                    tb * 512:(tb + 1) * 512],
                   ev[bank][:], reads=[f"c_ev{bank}"],
                   writes=[f"vT1.{dvc}.{tb}"])
    mk.sfree("c_wv")
    wp = mk.sbuf("c_wp", [128, 16 * 512], BF16)
    for wname, dst in (("Wq1", "q1"), ("Wk1", "k1")):
        for dg in range(4):
            for dc in range(16):
                mk.dma("gpsimd", wp[:, dc * 512:(dc + 1) * 512],
                       T[wname][dc * 128:(dc + 1) * 128,
                                dg * 512:(dg + 1) * 512],
                       reads=[wname], writes=[f"c_wp.{dc}"])
            for tc in range(16):
                bank = tc % 2
                for dc in range(16):
                    mk.op("tensor", (lambda e, bank=bank, dc=dc, tc=tc:
                          e.matmul(mk.ps[bank][:],
                                   xnT[:, dc * 2048 + tc * 128:
                                       dc * 2048 + (tc + 1) * 128],
                                   wp[:, dc * 512:(dc + 1) * 512],
                                   start=(dc == 0), stop=(dc == 15))),
                          reads=[f"c_wp.{dc}", f"c_xnT.{dc}.{tc // 4}"],
                          writes=[f"ps.{bank}"])
                mk.op("vector", (lambda e, bank=bank: e.tensor_copy(
                          ev[bank][:], mk.ps[bank][:])),
                      reads=[f"ps.{bank}"], writes=[f"c_ev{bank}"])
                mk.dma("sync", T[dst][tc * 128:(tc + 1) * 128,
                                      dg * 512:(dg + 1) * 512],
                       ev[bank][:], reads=[f"c_ev{bank}"],
                       writes=[f"{dst}.{tc}.{dg}"])
    mk.sfree("c_xnT", "c_wp")
    sp_dt = BF16 if sp_bf16 else F32
    qbuf = mk.sbuf("c_qb", [128, 16 * 2048], BF16)
    for sc_ in range(16):
        mk.dma("sync", qbuf[:, sc_ * 2048:(sc_ + 1) * 2048],
               T["q1"][sc_ * 128:(sc_ + 1) * 128, :],
               reads=[f"q1.{sc_}.{g}" for g in range(4)],
               writes=[f"c_qb.{sc_}"])
    kbuf = mk.sbuf("c_kb", [128, 16 * 512], BF16)
    evf = [mk.sbuf(f"c_evf{i}", [128, 512], sp_dt) for i in range(2)]
    for fb in range(4):
        for sc_ in range(16):
            mk.dma("sync", kbuf[:, sc_ * 512:(sc_ + 1) * 512],
                   T["k1"][sc_ * 128:(sc_ + 1) * 128,
                           fb * 512:(fb + 1) * 512],
                   reads=[f"k1.{sc_}.{fb}"], writes=[f"c_kb.{sc_}"])
        for ec in range(16):
            bank = ec % 2
            for sc_ in range(16):
                mk.op("tensor", (lambda e, bank=bank, sc_=sc_, ec=ec:
                      e.matmul(mk.ps[bank][:],
                               qbuf[:, sc_ * 2048 + ec * 128:
                                    sc_ * 2048 + (ec + 1) * 128],
                               kbuf[:, sc_ * 512:(sc_ + 1) * 512],
                               start=(sc_ == 0), stop=(sc_ == 15))),
                      reads=[f"c_qb.{sc_}", f"c_kb.{sc_}"],
                      writes=[f"ps.{bank}"])
            mk.op("vector", (lambda e, bank=bank: e.tensor_copy(
                      evf[bank][:], mk.ps[bank][:])),
                  reads=[f"ps.{bank}"], writes=[f"c_evf{bank}"])
            mk.dma("sync", T["sp"][ec * 128:(ec + 1) * 128,
                                   fb * 512:(fb + 1) * 512],
                   evf[bank][:], reads=[f"c_evf{bank}"],
                   writes=[f"sp.{ec}.{fb}"])
    mk.sfree("c_qb", "c_kb", "c_ev0", "c_ev1", "c_evf0", "c_evf1")



def _softmax_row_setup(mk, T, fused):
    sc = {}
    for n in ("mx", "nmx", "rs", "rr"):
        sc[n] = mk.sbuf(f"sm{n}", [128, 1], F32)
    sc["spa"] = mk.sbuf("smspa", [128, D], BF16 if fused else F32)
    sc["spb"] = mk.sbuf("smspb", [128, D], F32)
    sc["S"] = mk.sbuf("smS", [128, D], F32)
    sc["P"] = mk.sbuf("smP", [128, D], F32)
    sc["W"] = mk.sbuf("smW", [128, D], BF16)
    return sc

def _softmax_row(mk, T, sc, ec, fused):
    spa, spb, Sb, Pb, Wb = sc["spa"], sc["spb"], sc["S"], sc["P"], sc["W"]
    e0 = ec * 128
    if fused:
        mk.dma("sync", spa[:], T["ccS_out"][e0:e0 + 128, :],
               reads=["ccS_out"], writes=["smspa"])
        mk.op("vector", lambda e: e.tensor_copy(Sb[:], spa[:]),
              reads=["smspa"], writes=["smS"])
    else:
        mk.dma("sync", spa[:], T["sp_s"][e0:e0 + 128, :],
               reads=["sp_s"], writes=["smspa"])
        mk.dma("sync", spb[:], T["sp_p"][e0:e0 + 128, :],
               reads=["sp_p"], writes=["smspb"])
        mk.op("vector", lambda e: e.tensor_add(Sb[:], spa[:], spb[:]),
              reads=["smspa", "smspb"], writes=["smS"])
    mk.op("vector", lambda e: e.reduce_max(sc["mx"][:], Sb[:], axis=AX.X),
          reads=["smS"], writes=["smmx"])
    mk.op("vector", lambda e: e.tensor_scalar_mul(sc["nmx"][:],
          sc["mx"][:], -0.125), reads=["smmx"], writes=["smnmx"])
    mk.op("scalar", lambda e: e.activation(out=Pb[:], in_=Sb[:],
          func=AF.Exp, bias=sc["nmx"][:, 0:1], scale=0.125,
          accum_out=sc["rs"][:, 0:1]),
          reads=["smS", "smnmx"], writes=["smP", "smrs"])
    mk.op("vector", lambda e: e.reciprocal(sc["rr"][:], sc["rs"][:]),
          reads=["smrs"], writes=["smrr"])
    mk.op("vector", lambda e: e.tensor_scalar_mul(Wb[:], Pb[:],
          sc["rr"][:, 0:1]), reads=["smP", "smrr"], writes=["smW"])
    mk.dma("sync", T["wbuf"][e0:e0 + 128, :], Wb[:],
           reads=["smW"], writes=[f"wbuf.{ec}"])

def _softmax_free(mk):
    mk.sfree("smmx", "smnmx", "smrs", "smrr", "smspa", "smspb", "smS",
             "smP", "smW")

def _softmax_filler(mk, T):
    sc = _softmax_row_setup(mk, T, fused=True)
    state = {"ec": 0}

    def filler(ig):
        if ig < 2:
            return
        while state["ec"] < min(16, (ig - 1) * 2):
            _softmax_row(mk, T, sc, state["ec"], fused=True)
            state["ec"] += 1
    return filler

def _ph_l1_softmax(mk, T, fused=False):
    sc = _softmax_row_setup(mk, T, fused)
    for ec in range(16):
        _softmax_row(mk, T, sc, ec, fused)
    _softmax_free(mk)

def _ph_l1_av_wo(mk, T):
    wT = mk.sbuf("d_wT", [128, 16 * 2048], BF16)
    for fc in range(16):
        for eb in range(4):
            mk.dma("sync", wT[:, fc * 2048 + eb * 512:
                              fc * 2048 + (eb + 1) * 512],
                   T["wbuf"][eb * 512:(eb + 1) * 512,
                             fc * 128:(fc + 1) * 128],
                   reads=[f"wbuf.{eb * 4 + i}" for i in range(4)],
                   writes=[f"d_wT.{fc}.{eb}"], transpose=True)
    wo = mk.sbuf("d_wo", [128, 16 * 2048], BF16)
    for ec in range(16):
        mk.dma("gpsimd", wo[:, ec * 2048:(ec + 1) * 2048],
               T["Wo1"][ec * 128:(ec + 1) * 128, :],
               reads=["Wo1"], writes=[f"d_wo.{ec}"])
    vb = mk.sbuf("d_vb", [128, 16 * 512], BF16)
    ao = mk.sbuf("d_ao", [128, 16 * 512], BF16)
    ev = [mk.sbuf(f"d_ev{i}", [128, 512], BF16) for i in range(2)]
    for sb in range(4):
        s0 = sb * 512
        for fc in range(16):
            mk.dma("sync", vb[:, fc * 512:(fc + 1) * 512],
                   T["vT1"][fc * 128:(fc + 1) * 128, s0:s0 + 512],
                   reads=[f"vT1.{fc}.{sb}"], writes=[f"d_vb.{fc}"])
        for ec in range(16):
            bank = ec % 2
            for fc in range(16):
                mk.op("tensor", (lambda e, bank=bank, fc=fc, ec=ec:
                      e.matmul(mk.ps[bank][:],
                               wT[:, fc * 2048 + ec * 128:
                                  fc * 2048 + (ec + 1) * 128],
                               vb[:, fc * 512:(fc + 1) * 512],
                               start=(fc == 0), stop=(fc == 15))),
                      reads=[f"d_wT.{fc}.{ec // 4}", f"d_vb.{fc}"],
                      writes=[f"ps.{bank}"])
            mk.op("vector", (lambda e, bank=bank, ec=ec: e.tensor_copy(
                      ao[:, ec * 512:(ec + 1) * 512], mk.ps[bank][:])),
                  reads=[f"ps.{bank}"], writes=[f"d_ao.{ec}"])
        for do in range(16):
            bank = 2 + do % 2
            for ec in range(16):
                mk.op("tensor", (lambda e, bank=bank, ec=ec, do=do:
                      e.matmul(mk.ps[bank][:],
                               wo[:, ec * 2048 + do * 128:
                                  ec * 2048 + (do + 1) * 128],
                               ao[:, ec * 512:(ec + 1) * 512],
                               start=(ec == 0), stop=(ec == 15))),
                      reads=[f"d_wo.{ec}", f"d_ao.{ec}"],
                      writes=[f"ps.{bank}"])
            es = do % 2
            mk.op("vector", (lambda e, bank=bank, es=es: e.tensor_copy(
                      ev[es][:], mk.ps[bank][:])),
                  reads=[f"ps.{bank}"], writes=[f"d_ev{es}"])
            mk.dma("sync", T["a1T"][do * 128:(do + 1) * 128, s0:s0 + 512],
                   ev[es][:], reads=[f"d_ev{es}"],
                   writes=[f"a1T.{do}.{sb}"])
    mk.sfree("d_wT", "d_wo", "d_vb", "d_ao", "d_ev0", "d_ev1")


def _ph_l1_gateup_prefetch(mk, T):
    wg = [mk.sbuf(f"g_wg{s}", [128, 16 * 512], BF16) for s in range(2)]
    wu = [mk.sbuf(f"g_wu{s}", [128, 16 * 512], BF16) for s in range(2)]
    T["_g_w"] = (wg, wu)

    def load_ig(ig):
        s = ig % 2
        i0 = ig * 512
        for dc in range(16):
            mk.dma("gpsimd", wg[s][:, dc * 512:(dc + 1) * 512],
                   T["wg"][dc * 128:(dc + 1) * 128, i0:i0 + 512],
                   reads=["wg"], writes=[f"g_wg{s}.{dc}"])
            mk.dma("gpsimd", wu[s][:, dc * 512:(dc + 1) * 512],
                   T["wu"][dc * 128:(dc + 1) * 128, i0:i0 + 512],
                   reads=["wu"], writes=[f"g_wu{s}.{dc}"])
    T["_g_load"] = load_ig
    load_ig(0)
    load_ig(1)


def _ph_l1_gateup(mk, T, filler=None):
    xnT = mk.sbuf("g_xnT", [128, 16 * 2048], BF16)
    for dc in range(16):
        for tb in range(4):
            mk.dma("sync", xnT[:, dc * 2048 + tb * 512:
                               dc * 2048 + (tb + 1) * 512],
                   T["xn1_tm"][tb * 512:(tb + 1) * 512,
                               dc * 128:(dc + 1) * 128],
                   reads=_tm_tok("xn1_tm", tb * 512, 512),
                   writes=[f"g_xnT.{dc}.{tb}"],
                   transpose=True)
    wg, wu = T["_g_w"]
    load_ig = T["_g_load"]
    sg = [mk.sbuf(f"g_sg{s}", [128, 512], F32) for s in range(2)]
    hb = [mk.sbuf(f"g_h{s}", [128, 512], BF16) for s in range(2)]
    habs = [mk.sbuf(f"g_habs{s}", [128, 512], F32) for s in range(2)]
    q4 = [mk.sbuf(f"g_q{i}", [128, 128], F32) for i in range(5)]
    m4 = [mk.sbuf(f"g_m{i}", [128, 128], BF16) for i in range(4)]
    hs = [mk.sbuf(f"g_hs{i}", [128, 512], BF16) for i in range(2)]
    for ig in range(16):
        i0 = ig * 512
        s = ig % 2
        if ig + 2 < 16:
            load_ig(ig + 2)
        if filler is not None:
            filler(ig)
        for tc in range(16):
            pg, pu = 2 * (tc % 2), 2 * (tc % 2) + 1
            for dc in range(16):
                mk.op("tensor", (lambda e, dc=dc, tc=tc, pg=pg, s=s:
                      e.matmul(
                          mk.ps[pg][:],
                          xnT[:, dc * 2048 + tc * 128:
                              dc * 2048 + (tc + 1) * 128],
                          wg[s][:, dc * 512:(dc + 1) * 512],
                          start=(dc == 0), stop=(dc == 15))),
                      reads=[f"g_xnT.{dc}.{tc // 4}", f"g_wg{s}.{dc}"],
                      writes=[f"ps.{pg}"])
                mk.op("tensor", (lambda e, dc=dc, tc=tc, pu=pu, s=s:
                      e.matmul(
                          mk.ps[pu][:],
                          xnT[:, dc * 2048 + tc * 128:
                              dc * 2048 + (tc + 1) * 128],
                          wu[s][:, dc * 512:(dc + 1) * 512],
                          start=(dc == 0), stop=(dc == 15))),
                      reads=[f"g_xnT.{dc}.{tc // 4}", f"g_wu{s}.{dc}"],
                      writes=[f"ps.{pu}"])
            st = tc % 2
            mk.op("scalar", (lambda e, pg=pg, st=st: e.activation(
                  out=sg[st][:], in_=mk.ps[pg][:],
                  func=AF.Silu)), reads=[f"ps.{pg}"], writes=[f"g_sg{st}"])
            mk.op("vector", (lambda e, pu=pu, st=st: e.tensor_mul(
                  hb[st][:], sg[st][:],
                  mk.ps[pu][:])), reads=[f"g_sg{st}", f"ps.{pu}"],
                  writes=[f"g_h{st}"])
            mk.op("scalar", (lambda e, st=st: e.activation(
                  out=habs[st][:], in_=hb[st][:],
                  func=AF.Abs)), reads=[f"g_h{st}"], writes=[f"g_habs{st}"])
            ha = habs[st]
            mk.op("vector", (lambda e, ha=ha: e.tensor_max(
                  q4[0][:], ha[:, 0::4],
                  ha[:, 1::4])), reads=[f"g_habs{st}"], writes=["g_q0"])
            mk.op("vector", (lambda e, ha=ha: e.tensor_tensor(q4[1][:],
                  ha[:, 0::4], ha[:, 1::4], ALU.min)),
                  reads=[f"g_habs{st}"], writes=["g_q1"])
            mk.op("vector", (lambda e, ha=ha: e.tensor_max(
                  q4[2][:], ha[:, 2::4],
                  ha[:, 3::4])), reads=[f"g_habs{st}"], writes=["g_q2"])
            mk.op("vector", (lambda e, ha=ha: e.tensor_tensor(q4[3][:],
                  ha[:, 2::4], ha[:, 3::4], ALU.min)),
                  reads=[f"g_habs{st}"], writes=["g_q3"])
            mk.op("vector", lambda e: e.tensor_tensor(q4[0][:], q4[0][:],
                  q4[2][:], ALU.min), reads=["g_q0", "g_q2"],
                  writes=["g_q0"])
            mk.op("vector", lambda e: e.tensor_max(q4[1][:], q4[1][:],
                  q4[3][:]), reads=["g_q1", "g_q3"], writes=["g_q1"])
            mk.op("vector", lambda e: e.tensor_max(q4[4][:], q4[0][:],
                  q4[1][:]), reads=["g_q0", "g_q1"], writes=["g_q4"])
            for jj in range(4):
                mk.op("vector", (lambda e, jj=jj, ha=ha: e.tensor_tensor(
                          m4[jj][:], ha[:, jj::4], q4[4][:], ALU.is_ge)),
                      reads=[f"g_habs{st}", "g_q4"], writes=[f"g_m{jj}"])
            for jj in range(4):
                mk.op("gpsimd", (lambda e, jj=jj, st=st: e.tensor_mul(
                          hs[st][:, jj::4], hb[st][:, jj::4], m4[jj][:])),
                      reads=[f"g_h{st}", f"g_m{jj}"], writes=[f"g_hs{st}"])
            mk.dma("sync", T["hsp"][tc * 128:(tc + 1) * 128, i0:i0 + 512],
                   hs[st][:], reads=[f"g_hs{st}"],
                   writes=[f"hsp.{tc}.{ig}"])
    if filler is not None:
        filler(17)
        _softmax_free(mk)
    mk.sfree("g_xnT", "g_wg0", "g_wg1", "g_wu0", "g_wu1", "g_sg0", "g_sg1",
             "g_h0", "g_h1", "g_habs0", "g_habs1",
             "g_q0", "g_q1", "g_q2", "g_q3", "g_q4", "g_m0", "g_m1",
             "g_m2", "g_m3", "g_hs0", "g_hs1")


def _ph_wd_cast(mk, T):
    wdt = [mk.sbuf(f"c_wd{i}", [128, 2048], BF16) for i in range(2)]
    for ic in range(64):
        sl = ic % 2
        mk.dma("gpsimd", wdt[sl][:], T["wd"][ic * 128:(ic + 1) * 128, :],
               reads=["wd"], writes=[f"c_wd{sl}"])
        mk.dma("sync", T["wd_bf"][ic * 128:(ic + 1) * 128, :], wdt[sl][:],
               reads=[f"c_wd{sl}"], writes=[f"wd_bf.{ic}"])
    mk.sfree("c_wd0", "c_wd1")


def _ph_l1_down_out(mk, T):
    x1t = [mk.sbuf(f"o_x1{s}", [128, D], F32) for s in range(2)]
    y1b = [mk.sbuf(f"o_y1b{s}", [128, D], BF16) for s in range(2)]
    outt = [mk.sbuf(f"o_out{s}", [128, D], F32) for s in range(2)]

    def out_tile(i):
        i0 = i * 128
        s = i % 2
        mk.dma("scalar", x1t[s][:], T["x1_tm"][i0:i0 + 128, :],
               reads=[f"x1_tm.{i}"], writes=[f"o_x1{s}"])
        for blk in range(4):
            mk.dma("sync", y1b[s][:, blk * 512:(blk + 1) * 512],
                   T["y1T"][blk * 512:(blk + 1) * 512, i0:i0 + 128],
                   reads=[f"y1T.{blk * 4 + q}.{i // 4}" for q in range(4)],
                   writes=[f"o_y1b{s}.{blk}"], transpose=True)
        mk.op("vector", (lambda e, s=s: e.tensor_add(
                  outt[s][:], x1t[s][:], y1b[s][:])),
              reads=[f"o_x1{s}"] + [f"o_y1b{s}.{b}" for b in range(4)],
              writes=[f"o_out{s}"])
        mk.dma("sync", T["OUT"][i0:i0 + 128, :], outt[s][:],
               reads=[f"o_out{s}"], writes=[f"OUT.{i}"])

    for th in range(2):
        tt0 = th * 1024
        hT = mk.sbuf("o_hT", [128, 64 * 1024], BF16)
        for ic in range(64):
            for tq in range(2):
                mk.dma("sync", hT[:, ic * 1024 + tq * 512:
                                  ic * 1024 + (tq + 1) * 512],
                       T["hsp"][tt0 + tq * 512:tt0 + (tq + 1) * 512,
                                ic * 128:(ic + 1) * 128],
                       reads=[f"hsp.{(tt0 + tq * 512) // 128 + q}.{ic // 4}"
                              for q in range(4)],
                       writes=[f"o_hT.{ic}.{tq}"], transpose=True)
        wdc = mk.sbuf("o_wdc", [128, 64 * 128], BF16)
        a1b = [mk.sbuf(f"o_a1{i}", [128, 512], BF16) for i in range(2)]
        yev = [mk.sbuf(f"o_y{i}", [128, 512], BF16) for i in range(2)]
        for dc in range(16):
            for g8 in range(8):
                src = T["wd_bf"][g8 * 1024:(g8 + 1) * 1024,
                                 dc * 128:(dc + 1) * 128]
                src = src.rearrange("(g r) c -> r g c", g=8)
                mk.dma("sync", wdc[:, g8 * 1024:(g8 + 1) * 1024], src,
                       reads=[f"wd_bf.{g8 * 8 + i}" for i in range(8)],
                       writes=[f"o_wdc.{g8}"])
            for tq in range(2):
                bank = tq
                for ic in range(64):
                    mk.op("tensor", (lambda e, bank=bank, ic=ic, tq=tq:
                          e.matmul(mk.ps[bank][:],
                                   wdc[:, ic * 128:(ic + 1) * 128],
                                   hT[:, ic * 1024 + tq * 512:
                                      ic * 1024 + (tq + 1) * 512],
                                   start=(ic == 0), stop=(ic == 63))),
                          reads=[f"o_wdc.{ic // 8}", f"o_hT.{ic}.{tq}"],
                          writes=[f"ps.{bank}"])
                mk.dma("scalar", a1b[tq][:],
                       T["a1T"][dc * 128:(dc + 1) * 128,
                                tt0 + tq * 512:tt0 + (tq + 1) * 512],
                       reads=[f"a1T.{dc}.{th * 2 + tq}"],
                       writes=[f"o_a1{tq}"])
                mk.op("vector", (lambda e, bank=bank, tq=tq: e.tensor_add(
                          yev[tq][:], mk.ps[bank][:], a1b[tq][:])),
                      reads=[f"ps.{bank}", f"o_a1{tq}"], writes=[f"o_y{tq}"])
                mk.dma("sync",
                       T["y1T"][dc * 128:(dc + 1) * 128,
                                tt0 + tq * 512:tt0 + (tq + 1) * 512],
                       yev[tq][:], reads=[f"o_y{tq}"],
                       writes=[f"y1T.{dc}.{th * 2 + tq}"])
        mk.sfree("o_hT", "o_wdc", "o_a10", "o_a11", "o_y0", "o_y1")
        for i in range(th * 8, th * 8 + 8):
            out_tile(i)
    mk.sfree("o_x10", "o_x11", "o_y1b0", "o_y1b1", "o_out0", "o_out1")



def _dram(nc, specs):
    T = {}
    for name, shape, dt, kind in specs:
        T[name] = nc.dram_tensor(name, shape, dt, kind=kind).ap()
    return T


def _build_A():
    nc = bass.Bass("TRN2", target_bir_lowering=False)
    T = _dram(nc, [
        ("x0", [S, D], F32, "ExternalInput"),
        ("Wq_h", [D, 1024], F32, "ExternalInput"),
        ("Wqr_h", [D, 512], F32, "ExternalInput"),
        ("Wkvd", [D, DKV], F32, "ExternalInput"),
        ("Wkvup_h", [DKV, 2048], F32, "ExternalInput"),
        ("Wkr", [D, RD], F32, "ExternalInput"),
        ("Wo_h", [1024, D], F32, "ExternalInput"),
        ("ropeC", [128, S], F32, "ExternalInput"),
        ("ropeS", [128, S], F32, "ExternalInput"),
        ("masks", [4, 128, 512], BF16, "ExternalInput"),
        ("xn0_tm", [S, D], BF16, "ExternalOutput"),
        ("a0pT", [D, S], BF16, "ExternalOutput"),
        ("qcT", [1024, S], BF16, "Internal"),
        ("qrT", [512, S], BF16, "Internal"),
        ("qrraw", [512, S], F32, "Internal"),
        ("krraw", [64, S], F32, "Internal"),
        ("kcT", [1024, S], BF16, "Internal"),
        ("v_tm", [S, 1024], BF16, "Internal"),
        ("onormT", [1024, S], BF16, "Internal"),
    ])
    mk = MK(nc)
    _ph_ln(mk, T, "x0", "xn0_tm", S, "ln0")
    _ph_l0_proj(mk, T)
    _ph_l0_attn(mk, T)
    _ph_l0_wo(mk, T)
    mk.finalize()
    return nc


def _build_B():
    nc = bass.Bass("TRN2", target_bir_lowering=False)
    T = _dram(nc, [
        ("x0_own", [TOK, D], F32, "ExternalInput"),
        ("xn0_own", [TOK, D], BF16, "ExternalInput"),
        ("w1", [DFF, D], F32, "ExternalInput"),
        ("w2", [D, DFF], F32, "ExternalInput"),
        ("b1t", [128, 64], F32, "ExternalInput"),
        ("b2t", [128, 16], F32, "ExternalInput"),
        ("a0s", [D, TOK], BF16, "ExternalInput"),
        ("a0p", [D, TOK], BF16, "ExternalInput"),
        ("x1_tm", [TOK, D], F32, "ExternalOutput"),
        ("xn1_tm", [TOK, D], BF16, "ExternalOutput"),
        ("w1q", [DFF, D], BF16, "Internal"),
        ("w2q", [D, DFF], BF16, "Internal"),
        ("f0T", [D, TOK], BF16, "Internal"),
    ])
    mk = MK(nc)
    _ph_quant(mk, T, "w1", "w1q", 64, 2048)
    _ph_quant(mk, T, "w2", "w2q", 16, 8192)
    _ph_l0_ffn(mk, T)
    _ph_x1_ln1(mk, T)
    mk.finalize()
    return nc


def _build_C():
    nc = bass.Bass("TRN2", target_bir_lowering=False)
    T = _dram(nc, [
        ("xn1_tm", [TOK, D], BF16, "ExternalInput"),
        ("Wq1", [D, D], F32, "ExternalInput"),
        ("Wk1", [D, D], F32, "ExternalInput"),
        ("Wv1", [D, D], F32, "ExternalInput"),
        ("wd", [DFF, D], F32, "ExternalInput"),
        ("sp", [D, D], F32, "ExternalOutput"),
        ("vT1", [D, TOK], BF16, "ExternalOutput"),
        ("wd_bf", [DFF, D], BF16, "ExternalOutput"),
        ("q1", [TOK, D], BF16, "Internal"),
        ("k1", [TOK, D], BF16, "Internal"),
    ])
    mk = MK(nc)
    _ph_l1_qkv(mk, T)
    _ph_wd_cast(mk, T)
    mk.finalize()
    return nc


def _build_D():
    nc = bass.Bass("TRN2", target_bir_lowering=False)
    T = _dram(nc, [
        ("sp_s", [D, D], F32, "ExternalInput"),
        ("sp_p", [D, D], F32, "ExternalInput"),
        ("vT1", [D, TOK], BF16, "ExternalInput"),
        ("Wo1", [D, D], F32, "ExternalInput"),
        ("xn1_tm", [TOK, D], BF16, "ExternalInput"),
        ("wg", [D, DFF], F32, "ExternalInput"),
        ("wu", [D, DFF], F32, "ExternalInput"),
        ("wd_bf", [DFF, D], BF16, "ExternalInput"),
        ("x1_tm", [TOK, D], F32, "ExternalInput"),
        ("OUT", [TOK, D], F32, "ExternalOutput"),
        ("wbuf", [D, D], BF16, "Internal"),
        ("a1T", [D, TOK], BF16, "Internal"),
        ("hsp", [TOK, DFF], BF16, "Internal"),
        ("y1T", [D, TOK], BF16, "Internal"),
    ])
    mk = MK(nc)
    _ph_l1_softmax(mk, T)
    _ph_l1_av_wo(mk, T)
    _ph_l1_gateup(mk, T)
    _ph_l1_down_out(mk, T)
    mk.finalize()
    return nc




_RG = [[0, 1], [2, 3], [4, 5], [6, 7]]


def _build_fused():
    nc = bass.Bass("TRN2", target_bir_lowering=False)
    T = _dram(nc, [
        ("x0", [S, D], F32, "ExternalInput"),
        ("x0_own", [TOK, D], F32, "ExternalInput"),
        ("Wq_h", [D, 1024], F32, "ExternalInput"),
        ("Wqr_h", [D, 512], F32, "ExternalInput"),
        ("Wkvd", [D, DKV], F32, "ExternalInput"),
        ("Wkvup_h", [DKV, 2048], F32, "ExternalInput"),
        ("Wkr", [D, RD], F32, "ExternalInput"),
        ("Wo_h", [1024, D], F32, "ExternalInput"),
        ("ropeC", [128, S], F32, "ExternalInput"),
        ("ropeS", [128, S], F32, "ExternalInput"),
        ("masks", [4, 128, 512], BF16, "ExternalInput"),
        ("w1", [DFF, D], F32, "ExternalInput"),
        ("w2", [D, DFF], F32, "ExternalInput"),
        ("b1t", [128, 64], F32, "ExternalInput"),
        ("b2t", [128, 16], F32, "ExternalInput"),
        ("Wq1", [D, D], F32, "ExternalInput"),
        ("Wk1", [D, D], F32, "ExternalInput"),
        ("Wv1", [D, D], F32, "ExternalInput"),
        ("Wo1", [D, D], F32, "ExternalInput"),
        ("wg", [D, DFF], F32, "ExternalInput"),
        ("wu", [D, DFF], F32, "ExternalInput"),
        ("wd", [DFF, D], F32, "ExternalInput"),
        ("OUT", [TOK, D], F32, "ExternalOutput"),
        ("xn0_tm", [S, D], BF16, "Internal"),
        ("xn0_own", [TOK, D], BF16, "Internal"),
        ("qcT", [1024, S], BF16, "Internal"),
        ("qrT", [512, S], BF16, "Internal"),
        ("qrraw", [512, S], F32, "Internal"),
        ("krraw", [64, S], F32, "Internal"),
        ("kcT", [1024, S], BF16, "Internal"),
        ("v_tm", [S, 1024], BF16, "Internal"),
        ("onormT", [1024, S], BF16, "Internal"),
        ("ccA_in", [2 * D, TOK], BF16, "Internal"),
        ("ccA_out", [D, TOK], BF16, "Internal"),
        ("w1q", [DFF, D], BF16, "Internal"),
        ("w2q", [D, DFF], BF16, "Internal"),
        ("f0T", [D, TOK], BF16, "Internal"),
        ("x1_tm", [TOK, D], F32, "Internal"),
        ("xn1_tm", [TOK, D], BF16, "Internal"),
        ("q1", [TOK, D], BF16, "Internal"),
        ("k1", [TOK, D], BF16, "Internal"),
        ("vT1", [D, TOK], BF16, "Internal"),
        ("sp", [D, D], BF16, "Internal"),
        ("ccS_out", [D, D], BF16, "Internal"),
        ("wd_bf", [DFF, D], BF16, "Internal"),
        ("wbuf", [D, D], BF16, "Internal"),
        ("a1T", [D, TOK], BF16, "Internal"),
        ("hsp", [TOK, DFF], BF16, "Internal"),
        ("y1T", [D, TOK], BF16, "Internal"),
    ])
    mk = MK(nc)
    _ph_ln(mk, T, "x0", "xn0_tm", S, "ln0")
    _ph_l0_proj(mk, T)
    _ph_quant_p1(mk, T, "w1", 64, 2048)
    _ph_quant_p1(mk, T, "w2", 16, 8192, nb=1)
    work = [("w1", "w1q", i) for i in range(64)] + \
           [("w2", "w2q", i) for i in range(16)]

    def filler(slot):
        if slot == 7:
            _ph_quant_p1b(mk, T, "w1")
            _ph_quant_p1b(mk, T, "w2")
        if slot >= 40:
            if work:
                wn, qn, i = work.pop(0)
                _quant_p2_tile(mk, T, wn, qn, i)

    _ph_l0_attn(mk, T, filler=filler)
    _ph_ln(mk, T, "x0_own", "xn0_own", TOK, "lnb")
    while work:
        wn, qn, i = work.pop(0)
        _quant_p2_tile(mk, T, wn, qn, i)
    _ph_quant_free(mk, T, "w1")
    _ph_quant_free(mk, T, "w2")
    _ph_l0_wo(mk, T, cc=True)
    mk.op("gpsimd", lambda e: e.collective_compute(
              "ReduceScatter", ALU.add, replica_groups=_RG,
              ins=[T["ccA_in"][:, :]], outs=[T["ccA_out"][:, :]]),
          reads=[f"ccA_in.{do}.{tb}" for do in range(16) for tb in range(8)],
          writes=["ccA_out"])
    _ph_l0_ffn(mk, T)
    _ph_x1_ln1(mk, T, fused=True)
    _ph_l1_qkv(mk, T, sp_bf16=True)
    _ph_l1_gateup_prefetch(mk, T)
    mk.op("gpsimd", lambda e: e.collective_compute(
              "AllReduce", ALU.add, replica_groups=_RG,
              ins=[T["sp"][:, :]], outs=[T["ccS_out"][:, :]]),
          reads=[f"sp.{ec}.{fb}" for ec in range(16) for fb in range(4)],
          writes=["ccS_out"])
    _ph_l1_gateup(mk, T, filler=_softmax_filler(mk, T))
    _ph_wd_cast(mk, T)
    _ph_l1_av_wo(mk, T)
    _ph_l1_down_out(mk, T)
    mk.finalize()
    return nc



_FUSED = True

def kernel(**inp):
    if _FUSED:
        return _kernel_fused(**inp)
    return _kernel_unfused(**inp)


def _common_host(inp):
    f32 = np.float32

    def np32(name):
        return np.ascontiguousarray(np.asarray(inp[name], dtype=f32))

    W = {k: np32(k) for k in inp}
    b1 = W["L0_ffn_b1"]
    b2 = W["L0_ffn_b2"]
    W["b1t"] = np.ascontiguousarray(b1.reshape(64, 128).T)
    W["b2t"] = np.ascontiguousarray(b2.reshape(16, 128).T)
    return W


def _kernel_fused(**inp):
    ropeC, ropeS, masks = _host_tables()
    W = _common_host(inp)
    x0 = W["L0_attn_x"]
    cores = list(range(8))
    nc = _build_fused()
    in_maps = []
    for c in cores:
        b, j = c // 2, c % 2
        own = slice(j * TOK, (j + 1) * TOK)
        Wkvu = W["L0_attn_Wkv_up"]
        in_maps.append({
            "x0": x0[b],
            "x0_own": np.ascontiguousarray(x0[b][own]),
            "Wq_h": np.ascontiguousarray(
                W["L0_attn_Wq"][:, j * 1024:(j + 1) * 1024]),
            "Wqr_h": np.ascontiguousarray(
                W["L0_attn_Wq_rope"][:, j * 512:(j + 1) * 512]),
            "Wkvd": W["L0_attn_Wkv_down"],
            "Wkvup_h": np.ascontiguousarray(np.concatenate(
                [Wkvu[:, j * 1024:(j + 1) * 1024],
                 Wkvu[:, 2048 + j * 1024:2048 + (j + 1) * 1024]], axis=1)),
            "Wkr": W["L0_attn_Wk_rope"],
            "Wo_h": np.ascontiguousarray(
                W["L0_attn_Wo"][j * 1024:(j + 1) * 1024, :]),
            "ropeC": ropeC, "ropeS": ropeS, "masks": masks,
            "w1": W["L0_ffn_w1"], "w2": W["L0_ffn_w2"],
            "b1t": W["b1t"], "b2t": W["b2t"],
            "Wq1": W["L1_attn_Wq"], "Wk1": W["L1_attn_Wk"],
            "Wv1": W["L1_attn_Wv"], "Wo1": W["L1_attn_Wo"],
            "wg": W["L1_ffn_w_gate"], "wu": W["L1_ffn_w_up"],
            "wd": W["L1_ffn_w_down"],
        })
    res = run_bass_kernel_spmd(nc, in_maps, core_ids=cores).results
    out = np.empty((B, S, D), np.float32)
    for c in cores:
        b, j = c // 2, c % 2
        out[b, j * TOK:(j + 1) * TOK, :] = res[c]["OUT"]
    return out


def _kernel_unfused(**inp):
    f32 = np.float32
    ropeC, ropeS, masks = _host_tables()
    cores = list(range(8))

    def np32(name):
        return np.ascontiguousarray(np.asarray(inp[name], dtype=f32))

    x0 = np32("L0_attn_x")
    Wq, Wqr = np32("L0_attn_Wq"), np32("L0_attn_Wq_rope")
    Wkvd, Wkvu = np32("L0_attn_Wkv_down"), np32("L0_attn_Wkv_up")
    Wkr, Wo = np32("L0_attn_Wk_rope"), np32("L0_attn_Wo")
    w1, b1 = np32("L0_ffn_w1"), np32("L0_ffn_b1")
    w2, b2 = np32("L0_ffn_w2"), np32("L0_ffn_b2")
    Wq1, Wk1 = np32("L1_attn_Wq"), np32("L1_attn_Wk")
    Wv1, Wo1 = np32("L1_attn_Wv"), np32("L1_attn_Wo")
    wg, wu, wd = np32("L1_ffn_w_gate"), np32("L1_ffn_w_up"), \
        np32("L1_ffn_w_down")
    b1t = np.ascontiguousarray(b1.reshape(64, 128).T)
    b2t = np.ascontiguousarray(b2.reshape(16, 128).T)

    ncA = _build_A()
    inA = []
    for c in cores:
        b, j = c // 2, c % 2
        inA.append({
            "x0": x0[b],
            "Wq_h": np.ascontiguousarray(Wq[:, j * 1024:(j + 1) * 1024]),
            "Wqr_h": np.ascontiguousarray(Wqr[:, j * 512:(j + 1) * 512]),
            "Wkvd": Wkvd,
            "Wkvup_h": np.ascontiguousarray(np.concatenate(
                [Wkvu[:, j * 1024:(j + 1) * 1024],
                 Wkvu[:, 2048 + j * 1024:2048 + (j + 1) * 1024]], axis=1)),
            "Wkr": Wkr,
            "Wo_h": np.ascontiguousarray(Wo[j * 1024:(j + 1) * 1024, :]),
            "ropeC": ropeC, "ropeS": ropeS, "masks": masks,
        })
    rA = run_bass_kernel_spmd(ncA, inA, core_ids=cores).results

    ncB = _build_B()
    inB = []
    for c in cores:
        b, j = c // 2, c % 2
        own = slice(j * TOK, (j + 1) * TOK)
        inB.append({
            "x0_own": np.ascontiguousarray(x0[b][own]),
            "xn0_own": np.ascontiguousarray(rA[c]["xn0_tm"][own]),
            "w1": w1, "w2": w2, "b1t": b1t, "b2t": b2t,
            "a0s": np.ascontiguousarray(rA[c]["a0pT"][:, own]),
            "a0p": np.ascontiguousarray(rA[c ^ 1]["a0pT"][:, own]),
        })
    rB = run_bass_kernel_spmd(ncB, inB, core_ids=cores).results

    ncC = _build_C()
    inC = [{"xn1_tm": rB[c]["xn1_tm"], "Wq1": Wq1, "Wk1": Wk1, "Wv1": Wv1,
            "wd": wd} for c in cores]
    rC = run_bass_kernel_spmd(ncC, inC, core_ids=cores).results

    ncD = _build_D()
    inD = [{"sp_s": rC[c]["sp"], "sp_p": rC[c ^ 1]["sp"],
            "vT1": rC[c]["vT1"], "Wo1": Wo1, "xn1_tm": rB[c]["xn1_tm"],
            "wg": wg, "wu": wu, "wd_bf": rC[c]["wd_bf"],
            "x1_tm": rB[c]["x1_tm"]} for c in cores]
    rD = run_bass_kernel_spmd(ncD, inD, core_ids=cores).results

    out = np.empty((B, S, D), np.float32)
    for c in cores:
        b, j = c // 2, c % 2
        out[b, j * TOK:(j + 1) * TOK, :] = rD[c]["OUT"]
    return out


# version 08 (final)
# speedup vs baseline: 1.4301x; 1.0484x over previous
import math
import numpy as np
import ml_dtypes
import concourse.bass as bass
import concourse.mybir as mybir
from concourse.bass_utils import run_bass_kernel_spmd

F32 = mybir.dt.float32
BF16 = mybir.dt.bfloat16
AF = mybir.ActivationFunctionType
ALU = mybir.AluOpType
AX = mybir.AxisListType

B, S, D = 4, 4096, 2048
H, HD, RD = 16, 128, 64
DKV = 512
DFF = 8192
TOK = S // 2
SCALE0 = 1.0 / math.sqrt(HD + RD)

BF = ml_dtypes.bfloat16



class _Op:
    __slots__ = ("stream", "emit", "deps", "need_inc", "is_dma", "dma_use",
                 "dma_sem", "count", "tag")

    def __init__(self, stream, emit, is_dma, tag):
        self.stream = stream
        self.emit = emit
        self.deps = []
        self.need_inc = False
        self.is_dma = is_dma
        self.dma_use = 0
        self.dma_sem = None
        self.count = 0
        self.tag = tag


_ENGS = ("tensor", "vector", "scalar", "gpsimd", "sync")
_DMA_POOL = 16


class MK:
    def __init__(self, nc):
        self.nc = nc
        self.streams = {e: [] for e in _ENGS}
        self.comp_sem = {e: nc.alloc_semaphore(f"cs_{e}") for e in _ENGS}
        self.dma_sems = {}
        self.dma_ops = {}
        self.tok_w = {}
        self.tok_r = {}
        self.pending = []
        base = (nc.sbuf_base + 63) & ~63
        self.sb_free = [[base, nc.sbuf_top]]
        self.sb_allocs = {}
        self.ps = [nc.alloc_psum_tensor(f"ps{i}", [128, 512], F32).ap()
                   for i in range(8)]
        self._uid = 0

    def sbuf(self, name, shape, dtype):
        per_part = int(np.prod(shape[1:])) * (2 if dtype == BF16 else 4)
        size = (per_part + 63) & ~63
        for seg in self.sb_free:
            if seg[1] - seg[0] >= size:
                off = seg[0]
                seg[0] += size
                break
        else:
            raise MemoryError(f"sbuf full allocating {name} ({size}B/part)")
        self._uid += 1
        h = self.nc.alloc_sbuf_tensor_at(f"{name}_{self._uid}", shape, dtype,
                                         offset=off)
        ap = h.ap()
        self.sb_allocs[name] = [off, size, ap, list(self.pending)]
        return ap

    def sfree(self, *names):
        pend = list(self.pending)
        for name in names:
            off, size, _, _ = self.sb_allocs.pop(name)
            self.sb_free.append([off, off + size])
            pref = name + "."
            for t in list(self.tok_w):
                if t == name or t.startswith(pref):
                    pend.append(self.tok_w.pop(t))
            for t in list(self.tok_r):
                if t == name or t.startswith(pref):
                    pend.extend(self.tok_r.pop(t))
        self.sb_free.sort()
        merged = []
        for seg in self.sb_free:
            if merged and merged[-1][1] >= seg[0]:
                merged[-1][1] = max(merged[-1][1], seg[1])
            elif seg[1] > seg[0]:
                merged.append(seg)
        self.sb_free = merged
        self.pending = pend

    def barrier(self):
        self.pending = [s[-1] for s in self.streams.values() if s]

    def _record(self, op, reads, writes):
        deps = []
        for t in reads:
            w = self.tok_w.get(t)
            if w is not None:
                deps.append(w)
        for t in writes:
            w = self.tok_w.get(t)
            if w is not None:
                deps.append(w)
            deps.extend(self.tok_r.get(t, ()))
            al = self.sb_allocs.get(t.split(".", 1)[0])
            if al is not None and al[3]:
                deps.extend(al[3])
        for d in deps:
            if d is op:
                continue
            if (d.stream == op.stream and not d.is_dma and not op.is_dma
                    and op.stream == "tensor"):
                continue
            op.deps.append(d)
            d.need_inc = True
        for t in reads:
            self.tok_r.setdefault(t, []).append(op)
        for t in writes:
            self.tok_w[t] = op
            self.tok_r[t] = []
        self.streams[op.stream].append(op)
        return op

    def op(self, stream, emit, reads=(), writes=(), tag=""):
        return self._record(_Op(stream, emit, False, tag), reads, writes)

    def dma(self, queue, out_ap, in_ap, reads=(), writes=(), transpose=False,
            tag=""):
        if transpose:
            emit = lambda e: e.dma_start_transpose(out_ap, in_ap)
        else:
            emit = lambda e: e.dma_start(out=out_ap, in_=in_ap)
        op = _Op(queue, emit, True, tag)
        ops = self.dma_ops.setdefault(queue, [])
        j = len(ops)
        if queue not in self.dma_sems:
            self.dma_sems[queue] = [
                self.nc.alloc_semaphore(f"dq_{queue}{i}")
                for i in range(_DMA_POOL)]
        op.dma_sem = self.dma_sems[queue][j % _DMA_POOL]
        op.dma_use = j // _DMA_POOL
        op.need_inc = True
        if j >= _DMA_POOL:
            op.deps.append(ops[j - _DMA_POOL])
        ops.append(op)
        return self._record(op, reads, writes)

    def _completion(self, op):
        if op.is_dma:
            return op.dma_sem, 16 * (op.dma_use + 1)
        return self.comp_sem[op.stream], op.count

    def finalize(self):
        for e in _ENGS:
            c = 0
            for op in self.streams[e]:
                if not op.is_dma and op.need_inc:
                    c += 1
                    op.count = c
        known = {e: {} for e in _ENGS}

        def section(eng_name):
            def run(eng):
                kn = known[eng_name]
                for op in self.streams[eng_name]:
                    best = {}
                    for d in op.deps:
                        sem, val = self._completion(d)
                        if val > best.get(id(sem), (None, -1))[1]:
                            best[id(sem)] = (sem, val)
                    for sem, val in best.values():
                        if kn.get(id(sem), -1) >= val:
                            continue
                        eng.wait_ge(sem, val)
                        kn[id(sem)] = val
                    inst = op.emit(eng)
                    if op.need_inc:
                        if op.is_dma:
                            inst.then_inc(op.dma_sem, 16)
                        else:
                            inst.then_inc(self.comp_sem[op.stream], 1)
                if eng_name == "sync":
                    for q, ops in self.dma_ops.items():
                        fin = {}
                        for op in ops:
                            fin[id(op.dma_sem)] = (op.dma_sem,
                                                   16 * (op.dma_use + 1))
                        for sem, val in fin.values():
                            if kn.get(id(sem), -1) < val:
                                eng.wait_ge(sem, val)
                                kn[id(sem)] = val
                    for e2 in _ENGS:
                        if e2 == "sync":
                            continue
                        cnt = [o.count for o in self.streams[e2]
                               if not o.is_dma and o.need_inc]
                        if cnt and kn.get(id(self.comp_sem[e2]), -1) < cnt[-1]:
                            eng.wait_ge(self.comp_sem[e2], cnt[-1])
            return run

        with self.nc.Block() as block:
            if self.streams["tensor"]:
                block.tensor(section("tensor"))
            if self.streams["vector"]:
                block.vector(section("vector"))
            if self.streams["scalar"]:
                block.scalar(section("scalar"))
            if self.streams["gpsimd"]:
                block.gpsimd(section("gpsimd"))
            block.sync(section("sync"))



def _host_tables():
    inv = 1.0 / (10000.0 ** (np.arange(0, RD, 2, dtype=np.float64) / RD))
    pos = np.arange(S, dtype=np.float64)
    fr = inv[:, None] * pos[None, :]
    c64 = np.concatenate([np.cos(fr), np.cos(fr)], 0)
    s64 = np.concatenate([-np.sin(fr), np.sin(fr)], 0)
    ropeC = np.concatenate([c64, c64], 0).astype(np.float32)
    ropeS = np.concatenate([s64, s64], 0).astype(np.float32)
    qi = np.arange(512)[None, None, :]
    p = np.arange(128)[None, :, None]
    m = np.arange(4)[:, None, None]
    masks = (qi >= 128 * m + p).astype(np.float32).astype(BF)
    return ropeC, ropeS, masks



def _alloc_ln_scratch(mk, pref):
    sc = {}
    for n in ("sum", "s2", "nmean", "ex2", "nvar", "lnv", "rstd", "b"):
        sc[n] = mk.sbuf(f"{pref}{n}", [128, 1], F32)
    sc["eps"] = mk.sbuf(f"{pref}eps", [128, 1], F32)
    sc["_p"] = pref
    mk.op("vector", lambda e: e.memset(sc["eps"][:], 1e-5),
          writes=[f"{pref}eps"])
    return sc


def _ln_chain(mk, xt, xt_toks, n_elems, junk, out_bf, out_tok, sc):
    p = sc["_p"]
    inv = 1.0 / n_elems
    mk.op("vector", lambda e: e.reduce_sum(sc["sum"][:], xt[:], axis=AX.X),
          reads=xt_toks, writes=[f"{p}sum"])
    mk.op("scalar", lambda e: e.activation(out=junk[:], in_=xt[:],
          func=AF.Square, accum_out=sc["s2"][:, 0:1]),
          reads=xt_toks, writes=[f"{p}s2", f"{p}junk"])
    mk.op("vector", lambda e: e.tensor_scalar_mul(sc["nmean"][:], sc["sum"][:],
          -inv), reads=[f"{p}sum"], writes=[f"{p}nmean"])
    mk.op("vector", lambda e: e.tensor_scalar_mul(sc["ex2"][:], sc["s2"][:],
          inv), reads=[f"{p}s2"], writes=[f"{p}ex2"])
    mk.op("vector", lambda e: e.tensor_scalar(sc["nvar"][:], sc["nmean"][:],
          sc["nmean"][:, 0:1], sc["ex2"][:, 0:1], ALU.mult, ALU.subtract),
          reads=[f"{p}nmean", f"{p}ex2"], writes=[f"{p}nvar"])
    mk.op("scalar", lambda e: e.activation(out=sc["lnv"][:], in_=sc["nvar"][:],
          func=AF.Ln, bias=sc["eps"][:, 0:1], scale=-1.0),
          reads=[f"{p}nvar", f"{p}eps"], writes=[f"{p}lnv"])
    mk.op("scalar", lambda e: e.activation(out=sc["rstd"][:], in_=sc["lnv"][:],
          func=AF.Exp, scale=-0.5),
          reads=[f"{p}lnv"], writes=[f"{p}rstd"])
    mk.op("vector", lambda e: e.tensor_scalar_mul(sc["b"][:], sc["nmean"][:],
          sc["rstd"][:, 0:1]),
          reads=[f"{p}nmean", f"{p}rstd"], writes=[f"{p}b"])
    mk.op("vector", lambda e: e.tensor_scalar(out_bf[:], xt[:],
          sc["rstd"][:, 0:1], sc["b"][:, 0:1], ALU.mult, ALU.add),
          reads=list(xt_toks) + [f"{p}rstd", f"{p}b"], writes=[out_tok])


def _ph_ln(mk, T, src, dst, nrows, pref):
    sc = [_alloc_ln_scratch(mk, f"{pref}{s}_") for s in range(2)]
    xt = [mk.sbuf(f"{pref}x{i}", [128, D], F32) for i in range(2)]
    junk = [mk.sbuf(f"{pref}junk{i}", [128, D], BF16) for i in range(2)]
    ob = [mk.sbuf(f"{pref}o{i}", [128, D], BF16) for i in range(2)]
    for i in range(nrows // 128):
        sl = i % 2
        mk.dma("sync", xt[sl][:], T[src][i * 128:(i + 1) * 128, :],
               reads=[src], writes=[f"{pref}x{sl}"])
        _ln_chain(mk, xt[sl], [f"{pref}x{sl}"], D, junk[sl], ob[sl],
                  f"{pref}o{sl}", sc[sl])
        mk.dma("sync", T[dst][i * 128:(i + 1) * 128, :], ob[sl][:],
               reads=[f"{pref}o{sl}"], writes=[f"{dst}.{i}"])
    names = [f"{pref}x0", f"{pref}x1", f"{pref}junk0", f"{pref}junk1",
             f"{pref}o0", f"{pref}o1"]
    for s in range(2):
        names += [f"{pref}{s}_{n}" for n in
                  ("sum", "s2", "nmean", "ex2", "nvar", "lnv", "rstd", "b",
                   "eps")]
    mk.sfree(*names)


def _tm_tok(dst, i0, n):
    return [f"{dst}.{i}" for i in range(i0 // 128, (i0 + n + 127) // 128)]



def _ph_l0_proj(mk, T):
    nc = mk.nc
    wq = mk.sbuf("wq", [128, 16 * 1024], BF16)
    wqr = mk.sbuf("wqr", [128, 16 * 512], BF16)
    wkvd = mk.sbuf("wkvd", [128, 16 * 512], BF16)
    wkr = mk.sbuf("wkr", [128, 16 * 64], BF16)
    for dc in range(16):
        mk.dma("gpsimd", wq[:, dc * 1024:(dc + 1) * 1024],
               T["Wq_h"][dc * 128:(dc + 1) * 128, :],
               reads=["Wq_h"], writes=[f"wq.{dc}"])
        mk.dma("gpsimd", wqr[:, dc * 512:(dc + 1) * 512],
               T["Wqr_h"][dc * 128:(dc + 1) * 128, :],
               reads=["Wqr_h"], writes=[f"wqr.{dc}"])
        mk.dma("gpsimd", wkvd[:, dc * 512:(dc + 1) * 512],
               T["Wkvd"][dc * 128:(dc + 1) * 128, :],
               reads=["Wkvd"], writes=[f"wkvd.{dc}"])
        mk.dma("gpsimd", wkr[:, dc * 64:(dc + 1) * 64],
               T["Wkr"][dc * 128:(dc + 1) * 128, :],
               reads=["Wkr"], writes=[f"wkr.{dc}"])
    ropeC = mk.sbuf("ropeC", [128, S], F32)
    ropeS = mk.sbuf("ropeS", [128, S], F32)
    mk.dma("sync", ropeC[:], T["ropeC"][:, :], reads=["ropeC_in"],
           writes=["ropeC"])
    mk.dma("sync", ropeS[:], T["ropeS"][:, :], reads=["ropeS_in"],
           writes=["ropeS"])
    rx = [mk.sbuf(f"rx{i}", [128, 512], F32) for i in range(2)]
    rr = [mk.sbuf(f"rr{i}", [128, 512], F32) for i in range(2)]
    rt = [mk.sbuf(f"rt{i}", [128, 512], F32) for i in range(2)]
    ro = [mk.sbuf(f"ro{i}", [128, 512], BF16) for i in range(2)]
    krT = mk.sbuf("krT", [64, S], BF16)
    T["_krT"] = krT

    def rope_block(src_, stok, r0, nrow, t0, dst_ap, dtok, sl):
        xa, ra, ta, oa = rx[sl], rr[sl], rt[sl], ro[sl]
        rts = [f"rr{sl}.{g}{ab}" for g in range(nrow // 64) for ab in "ab"]
        mk.dma("sync", xa[0:nrow, :], src_[r0:r0 + nrow, t0:t0 + 512],
               reads=[stok], writes=[f"rx{sl}"])
        for g in range(nrow // 64):
            mk.dma("sync", ra[g * 64:g * 64 + 32, :],
                   src_[r0 + g * 64 + 32:r0 + g * 64 + 64, t0:t0 + 512],
                   reads=[stok], writes=[f"rr{sl}.{g}a"])
            mk.dma("sync", ra[g * 64 + 32:g * 64 + 64, :],
                   src_[r0 + g * 64:r0 + g * 64 + 32, t0:t0 + 512],
                   reads=[stok], writes=[f"rr{sl}.{g}b"])
        mk.op("vector", lambda e: e.tensor_mul(ta[0:nrow, :], xa[0:nrow, :],
              ropeC[0:nrow, t0:t0 + 512]),
              reads=[f"rx{sl}", "ropeC"], writes=[f"rt{sl}"])
        mk.op("vector", lambda e: e.tensor_mul(ra[0:nrow, :], ra[0:nrow, :],
              ropeS[0:nrow, t0:t0 + 512]),
              reads=rts + ["ropeS"], writes=rts)
        mk.op("vector", lambda e: e.tensor_add(oa[0:nrow, :], ta[0:nrow, :],
              ra[0:nrow, :]),
              reads=[f"rt{sl}"] + rts, writes=[f"ro{sl}"])
        mk.dma("sync", dst_ap, oa[0:nrow, :],
               reads=[f"ro{sl}"], writes=[dtok])

    ckvT = mk.sbuf("ckvT", [128, 4 * S], BF16)
    xnT = [mk.sbuf(f"pxnT{i}", [128, 16 * 512], BF16) for i in range(2)]
    ev = [mk.sbuf(f"pev{i}", [128, 512], BF16) for i in range(4)]
    evf = [mk.sbuf(f"pevf{i}", [128, 512], F32) for i in range(2)]
    pi = [0]

    def mm_chain(n_k, lhsT_of, rhs_of, rtoks, mrows=128):
        bank = pi[0] % 2
        pi[0] += 1
        ptok = f"ps.{bank}"
        for k in range(n_k):
            mk.op("tensor",
                  (lambda e, k=k, bank=bank: e.matmul(
                      mk.ps[bank][0:mrows, :], lhsT_of(k), rhs_of(k),
                      start=(k == 0), stop=(k == n_k - 1))),
                  reads=rtoks(k), writes=[ptok])
        return mk.ps[bank], ptok

    for tb in range(8):
        t0 = tb * 512
        sl = tb % 2
        for dc in range(16):
            mk.dma("sync", xnT[sl][:, dc * 512:(dc + 1) * 512],
                   T["xn0_tm"][t0:t0 + 512, dc * 128:(dc + 1) * 128],
                   reads=_tm_tok("xn0_tm", t0, 512),
                   writes=[f"pxnT{sl}.{dc}"], transpose=True)
        for oc in range(4):
            p, ptok = mm_chain(
                16,
                lambda k, oc=oc: wkvd[:, k * 512 + oc * 128:
                                      k * 512 + (oc + 1) * 128],
                lambda k, sl=sl: xnT[sl][:, k * 512:(k + 1) * 512],
                lambda k, sl=sl: [f"wkvd.{k}", f"pxnT{sl}.{k}"])
            mk.op("vector",
                  (lambda e, p=p, oc=oc, t0=t0: e.tensor_copy(
                      ckvT[:, oc * S + t0:oc * S + t0 + 512], p[:])),
                  reads=[ptok], writes=[f"ckvT.{oc}.{tb}"])
        p, ptok = mm_chain(
            16,
            lambda k: wkr[:, k * 64:(k + 1) * 64],
            lambda k, sl=sl: xnT[sl][:, k * 512:(k + 1) * 512],
            lambda k, sl=sl: [f"wkr.{k}", f"pxnT{sl}.{k}"], mrows=64)
        fs = tb % 2
        mk.op("vector",
              (lambda e, p=p, fs=fs: e.tensor_copy(
                  evf[fs][0:64, :], p[0:64, :])),
              reads=[ptok], writes=[f"pevf{fs}"])
        mk.dma("sync", T["krraw"][0:64, t0:t0 + 512], evf[fs][0:64, :],
               reads=[f"pevf{fs}"], writes=[f"krraw.{tb}"])
        for oc in range(8):
            p, ptok = mm_chain(
                16,
                lambda k, oc=oc: wq[:, k * 1024 + oc * 128:
                                    k * 1024 + (oc + 1) * 128],
                lambda k, sl=sl: xnT[sl][:, k * 512:(k + 1) * 512],
                lambda k, sl=sl: [f"wq.{k}", f"pxnT{sl}.{k}"])
            es = oc % 4
            mk.op("vector",
                  (lambda e, p=p, es=es: e.tensor_copy(ev[es][:], p[:])),
                  reads=[ptok], writes=[f"pev{es}"])
            mk.dma("sync", T["qcT"][oc * 128:(oc + 1) * 128, t0:t0 + 512],
                   ev[es][:], reads=[f"pev{es}"], writes=[f"qcT.{oc}.{tb}"])
        for oc in range(4):
            p, ptok = mm_chain(
                16,
                lambda k, oc=oc: wqr[:, k * 512 + oc * 128:
                                     k * 512 + (oc + 1) * 128],
                lambda k, sl=sl: xnT[sl][:, k * 512:(k + 1) * 512],
                lambda k, sl=sl: [f"wqr.{k}", f"pxnT{sl}.{k}"])
            fs = oc % 2
            mk.op("vector",
                  (lambda e, p=p, fs=fs: e.tensor_copy(evf[fs][:], p[:])),
                  reads=[ptok], writes=[f"pevf{fs}"])
            mk.dma("sync", T["qrraw"][oc * 128:(oc + 1) * 128, t0:t0 + 512],
                   evf[fs][:], reads=[f"pevf{fs}"],
                   writes=[f"qrraw.{oc}.{tb}"])
        for r in range(4):
            sl = (tb * 4 + r) % 2
            rope_block(T["qrraw"], f"qrraw.{r}.{tb}", r * 128, 128, t0,
                       T["qrT"][r * 128:(r + 1) * 128, t0:t0 + 512],
                       f"qrT.{r}.{tb}", sl)
        rope_block(T["krraw"], f"krraw.{tb}", 0, 64, t0,
                   krT[0:64, t0:t0 + 512], f"krT.{tb}", tb % 2)
    mk.sfree("wq", "wqr", "wkvd", "wkr", "ropeC", "ropeS", "rx0", "rx1",
             "rr0", "rr1", "rt0", "rt1", "ro0", "ro1")
    wkvu = mk.sbuf("wkvu", [128, 4 * 2048], BF16)
    for cc in range(4):
        mk.dma("gpsimd", wkvu[:, cc * 2048:(cc + 1) * 2048],
               T["Wkvup_h"][cc * 128:(cc + 1) * 128, :],
               reads=["Wkvup_h"], writes=[f"wkvu.{cc}"])
    for tb in range(8):
        t0 = tb * 512
        for oc in range(8):
            p, ptok = mm_chain(
                4,
                lambda k, oc=oc: wkvu[:, k * 2048 + oc * 128:
                                      k * 2048 + (oc + 1) * 128],
                lambda k, t0=t0: ckvT[:, k * S + t0:k * S + t0 + 512],
                lambda k, tb=tb: [f"wkvu.{k}", f"ckvT.{k}.{tb}"])
            es = oc % 4
            mk.op("vector",
                  (lambda e, p=p, es=es: e.tensor_copy(ev[es][:], p[:])),
                  reads=[ptok], writes=[f"pev{es}"])
            mk.dma("sync", T["kcT"][oc * 128:(oc + 1) * 128, t0:t0 + 512],
                   ev[es][:], reads=[f"pev{es}"], writes=[f"kcT.{oc}.{tb}"])
    for tc in range(S // 128):
        for dvb in range(2):
            p, ptok = mm_chain(
                4,
                lambda k, tc=tc: ckvT[:, k * S + tc * 128:k * S + (tc+1)*128],
                lambda k, dvb=dvb: wkvu[:, k * 2048 + 1024 + dvb * 512:
                                        k * 2048 + 1024 + (dvb + 1) * 512],
                lambda k, tc=tc: [f"wkvu.{k}", f"ckvT.{k}.{tc // 4}"])
            es = dvb % 4
            mk.op("vector",
                  (lambda e, p=p, es=es: e.tensor_copy(ev[es][:], p[:])),
                  reads=[ptok], writes=[f"pev{es}"])
            mk.dma("sync", T["v_tm"][tc * 128:(tc + 1) * 128,
                                     dvb * 512:(dvb + 1) * 512],
                   ev[es][:], reads=[f"pev{es}"], writes=[f"v_tm.{tc}.{dvb}"])
    mk.sfree("wkvu", "ckvT", "pxnT0", "pxnT1", "pev0", "pev1", "pev2",
             "pev3", "pevf0", "pevf1")
def _ph_l0_attn(mk, T, filler=None):
    krT = T["_krT"]
    mask = mk.sbuf("mask", [128, 4 * 512], BF16)
    for m in range(4):
        mk.dma("sync", mask[:, m * 512:(m + 1) * 512], T["masks"][m, :, :],
               reads=["masks_in"], writes=[f"mask.{m}"])
    sc192 = mk.sbuf("sc192", [128, 1], F32)
    mk.op("vector", lambda e: e.memset(sc192[:], SCALE0), writes=["sc192"])
    ones_b = mk.sbuf("ones_b", [128, 1], BF16)
    mk.op("vector", lambda e: e.memset(ones_b[:], 1.0), writes=["ones_b"])
    ones_f = mk.sbuf("ones_f", [1, 128], F32)
    mk.op("vector", lambda e: e.memset(ones_f[:], 1.0), writes=["ones_f"])

    qc = [mk.sbuf(f"aqc{i}", [128, 512], BF16) for i in range(3)]
    qr = [mk.sbuf(f"aqr{i}", [64, 512], BF16) for i in range(3)]
    kb = [mk.sbuf(f"akb{i}", [128, S], BF16) for i in range(2)]
    vb = [mk.sbuf(f"avb{i}", [128, S], BF16) for i in range(2)]
    pt = [mk.sbuf(f"apt{i}", [128, 512], BF16) for i in range(3)]
    dens = [mk.sbuf(f"adens{i}", [1, 512], F32) for i in range(2)]
    denr = [mk.sbuf(f"adenr{i}", [1, 512], F32) for i in range(2)]
    bcs = mk.sbuf("abcs", [128, 512], F32)
    onb = mk.sbuf("aonb", [128, 512], BF16)
    psS = [mk.ps[0], mk.ps[1], mk.ps[6]]
    psSt = ["ps.0", "ps.1", "ps.6"]
    psO = [mk.ps[2], mk.ps[3]]
    psD = [mk.ps[4][0:1, :], mk.ps[5][0:1, :]]
    psB = mk.ps[7]
    pend = []

    def epilogue():
        if not pend:
            return
        h, qb, par = pend.pop(0)
        q0 = qb * 512
        mk.op("vector", (lambda e, par=par: e.tensor_copy(
                  dens[par][:], psD[par][:])),
              reads=[f"ps.{4 + par}"], writes=[f"adens{par}"])
        mk.op("vector", (lambda e, par=par: e.reciprocal(
                  denr[par][:], dens[par][:])),
              reads=[f"adens{par}"], writes=[f"adenr{par}"])
        mk.op("tensor", (lambda e, par=par: e.matmul(
                  psB[:], ones_f[:], denr[par][:], start=True, stop=True)),
              reads=["ones_f", f"adenr{par}"], writes=["ps.7"])
        mk.op("vector", lambda e: e.tensor_copy(bcs[:], psB[:]),
              reads=["ps.7"], writes=["abcs"])
        mk.op("vector", (lambda e, par=par: e.tensor_mul(
                  onb[:], psO[par][:], bcs[:])),
              reads=[f"ps.{2 + par}", "abcs"], writes=["aonb"])
        mk.dma("sync", T["onormT"][h * 128:(h + 1) * 128, q0:q0 + 512],
               onb[:], reads=["aonb"], writes=[f"onormT.{h}.{qb}"])

    for h in range(8):
        sh = h % 2
        mk.dma("sync", kb[sh][:],
               T["kcT"][h * 128:(h + 1) * 128, :],
               reads=[f"kcT.{h}.{g}" for g in range(8)],
               writes=[f"akb{sh}"])
        for g4 in range(8):
            vsrc = T["v_tm"][g4 * 512:(g4 + 1) * 512,
                             h * 128:(h + 1) * 128]
            vsrc = vsrc.rearrange("(g r) c -> r g c", g=4)
            mk.dma("sync", vb[sh][:, g4 * 512:(g4 + 1) * 512], vsrc,
                   reads=[f"v_tm.{g4 * 4 + i}.{1}" for i in range(4)]
                   + [f"v_tm.{g4 * 4 + i}.{0}" for i in range(4)],
                   writes=[f"avb{sh}.{g4}"])
        for qb in range(8):
            q0 = qb * 512
            nk = 4 * qb + 4
            par = qb % 2
            sq = qb % 3
            mk.dma("scalar", qc[sq][:],
                   T["qcT"][h * 128:(h + 1) * 128, q0:q0 + 512],
                   reads=[f"qcT.{h}.{qb}"], writes=[f"aqc{sq}"])
            mk.dma("scalar", qr[sq][:],
                   T["qrT"][h * 64:(h + 1) * 64, q0:q0 + 512],
                   reads=[f"qrT.{h // 2}.{qb}"], writes=[f"aqr{sq}"])
            def emit_S(k):
                sl = k % 3
                mk.op("tensor", (lambda e, sl=sl, sh=sh, sq=sq, k=k:
                      e.matmul(
                          psS[sl][:], kb[sh][:, k * 128:(k + 1) * 128],
                          qc[sq][:], start=True, stop=False)),
                      reads=[f"akb{sh}", f"aqc{sq}"], writes=[psSt[sl]])
                mk.op("tensor", (lambda e, sl=sl, sq=sq, k=k: e.matmul(
                          psS[sl][:], krT[0:64, k * 128:(k + 1) * 128],
                          qr[sq][:], start=False, stop=True)),
                      reads=[f"krT.{k // 4}", f"aqr{sq}"], writes=[psSt[sl]])
                mk.op("scalar", (lambda e, sl=sl: e.activation(
                          out=pt[sl][:], in_=psS[sl][:], func=AF.Exp,
                          scale=sc192[:, 0:1])),
                      reads=[psSt[sl], "sc192"], writes=[f"apt{sl}"])
                m = k - 4 * qb
                if m >= 0:
                    mk.op("vector", (lambda e, sl=sl, m=m: e.tensor_mul(
                              pt[sl][:], pt[sl][:],
                              mask[:, m * 512:(m + 1) * 512])),
                          reads=[f"apt{sl}", f"mask.{m}"],
                          writes=[f"apt{sl}"])

            def emit_AV(k):
                sl = k % 3
                mk.op("tensor", (lambda e, sl=sl, sh=sh, k=k, nk=nk,
                      par=par: e.matmul(psO[par][:],
                               vb[sh][:, k * 128:(k + 1) * 128], pt[sl][:],
                               start=(k == 0), stop=(k == nk - 1))),
                      reads=[f"avb{sh}.{k // 4}", f"apt{sl}"],
                      writes=[f"ps.{2 + par}"])
                mk.op("tensor", (lambda e, sl=sl, k=k, nk=nk, par=par:
                      e.matmul(psD[par][:], ones_b[:], pt[sl][:],
                               start=(k == 0), stop=(k == nk - 1))),
                      reads=["ones_b", f"apt{sl}"], writes=[f"ps.{4 + par}"])

            for k in range(nk):
                emit_S(k)
                if k >= 2:
                    emit_AV(k - 2)
            emit_AV(nk - 2)
            emit_AV(nk - 1)
            pend.append((h, qb, par))
            if len(pend) > 1:
                epilogue()
            if filler is not None:
                filler(h * 8 + qb)
    while pend:
        epilogue()
    mk.sfree("mask", "sc192", "ones_b", "ones_f", "abcs", "aonb",
             "aqc0", "aqc1", "aqc2", "aqr0", "aqr1", "aqr2", "akb0", "akb1", "avb0", "avb1",
             "adens0", "adens1", "adenr0", "adenr1", "apt0", "apt1", "apt2",
             "krT")


def _ph_l0_wo(mk, T, cc=False):
    woh = mk.sbuf("woh", [128, 8 * 2048], BF16)
    for r in range(8):
        mk.dma("gpsimd", woh[:, r * 2048:(r + 1) * 2048],
               T["Wo_h"][r * 128:(r + 1) * 128, :],
               reads=["Wo_h"], writes=[f"woh.{r}"])
    onb = [mk.sbuf(f"won{i}", [128, 8 * 512], BF16) for i in range(2)]
    ev = [mk.sbuf(f"wev{i}", [128, 512], BF16) for i in range(2)]
    for tb in range(8):
        t0 = tb * 512
        sl = tb % 2
        for r in range(8):
            mk.dma("sync", onb[sl][:, r * 512:(r + 1) * 512],
                   T["onormT"][r * 128:(r + 1) * 128, t0:t0 + 512],
                   reads=[f"onormT.{r}.{tb}"], writes=[f"won{sl}.{r}"])
        for do in range(16):
            bank = do % 2
            for r in range(8):
                mk.op("tensor", (lambda e, bank=bank, r=r, do=do, sl=sl:
                      e.matmul(mk.ps[bank][:],
                               woh[:, r * 2048 + do * 128:
                                   r * 2048 + (do + 1) * 128],
                               onb[sl][:, r * 512:(r + 1) * 512],
                               start=(r == 0), stop=(r == 7))),
                      reads=[f"woh.{r}", f"won{sl}.{r}"],
                      writes=[f"ps.{bank}"])
            mk.op("vector", (lambda e, bank=bank: e.tensor_copy(
                      ev[bank][:], mk.ps[bank][:])),
                  reads=[f"ps.{bank}"], writes=[f"wev{bank}"])
            if cc:
                jblk, tq = tb // 4, tb % 4
                dst = T["ccA_in"][jblk * D + do * 128:jblk * D + (do + 1) * 128,
                                  tq * 512:(tq + 1) * 512]
                dtok = f"ccA_in.{do}.{tb}"
            else:
                dst = T["a0pT"][do * 128:(do + 1) * 128, t0:t0 + 512]
                dtok = f"a0pT.{do}.{tb}"
            mk.dma("sync", dst, ev[bank][:], reads=[f"wev{bank}"],
                   writes=[dtok])
    mk.sfree("woh", "won0", "won1", "wev0", "wev1")



def _ph_quant_p1(mk, T, wname, nt, ncols, nb=2):
    p = f"q{wname}"
    wt = [mk.sbuf(f"{p}w{i}", [128, ncols], F32) for i in range(nb)]
    acc = mk.sbuf(f"{p}acc", [128, 64], F32)
    accv = mk.sbuf(f"{p}accv", [128, 1], F32)
    ones128 = mk.sbuf(f"{p}o128", [128, 1], F32)
    ones1 = mk.sbuf(f"{p}o1", [1, 128], F32)
    cpos = mk.sbuf(f"{p}cp", [1, 1], F32)
    cneg = mk.sbuf(f"{p}cn", [1, 1], F32)
    trow = mk.sbuf(f"{p}tr", [1, 2], F32)
    tpos = mk.sbuf(f"{p}tp", [128, 1], F32)
    tneg = mk.sbuf(f"{p}tn", [128, 1], F32)
    numel = nt * 128 * ncols
    mk.op("vector", lambda e: e.memset(ones128[:], 1.0), writes=[f"{p}o128"])
    mk.op("vector", lambda e: e.memset(ones1[:], 1.0), writes=[f"{p}o1"])
    mk.op("vector", lambda e: e.memset(cpos[:], 0.7 / numel),
          writes=[f"{p}cp"])
    mk.op("vector", lambda e: e.memset(cneg[:], -0.7 / numel),
          writes=[f"{p}cn"])
    for i in range(nt):
        sl = i % nb
        mk.dma("gpsimd", wt[sl][:], T[wname][i * 128:(i + 1) * 128, :],
               reads=[wname], writes=[f"{p}w{sl}"])
        mk.op("vector", (lambda e, sl=sl, i=i: e.reduce_sum(
                  acc[:, i:i + 1], wt[sl][:], axis=AX.X,
                  apply_absolute_value=True)),
              reads=[f"{p}w{sl}"], writes=[f"{p}acc.{i}"])
    mk.op("vector", lambda e: e.reduce_sum(accv[:], acc[:, 0:nt], axis=AX.X),
          reads=[f"{p}acc.{i}" for i in range(nt)], writes=[f"{p}accv"])
    T[f"_{p}tp"], T[f"_{p}tn"] = tpos, tneg
    T[f"_{p}p1b"] = (accv, ones128, ones1, cpos, cneg, trow, tpos, tneg)
    s1 = [mk.sbuf(f"{p}s1{i}", [128, ncols], BF16) for i in range(nb)]
    s2 = [mk.sbuf(f"{p}s2{i}", [128, ncols], BF16) for i in range(nb)]
    wq = [mk.sbuf(f"{p}wq{i}", [128, ncols], BF16) for i in range(nb)]
    T[f"_{p}bufs"] = (wt, s1, s2, wq)


def _ph_quant_p1b(mk, T, wname):
    p = f"q{wname}"
    accv, ones128, ones1, cpos, cneg, trow, tpos, tneg = T[f"_{p}p1b"]
    psT = mk.ps[7][0:1, 0:1]
    mk.op("tensor", lambda e: e.matmul(psT, ones128[:], accv[:],
          start=True, stop=True),
          reads=[f"{p}o128", f"{p}accv"], writes=["ps.7"])
    mk.op("vector", lambda e: e.tensor_scalar_mul(trow[0:1, 0:1], psT,
          cpos[0:1, 0:1]),
          reads=["ps.7", f"{p}cp"], writes=[f"{p}tr"])
    mk.op("vector", lambda e: e.tensor_scalar_mul(trow[0:1, 1:2], psT,
          cneg[0:1, 0:1]),
          reads=["ps.7", f"{p}cn", f"{p}tr"], writes=[f"{p}tr"])
    psBC = mk.ps[7][:, 0:2]
    mk.op("tensor", lambda e: e.matmul(psBC, ones1[:], trow[0:1, :],
          start=True, stop=True),
          reads=[f"{p}o1", f"{p}tr"], writes=["ps.7"])
    mk.op("vector", lambda e: e.tensor_copy(tpos[:], mk.ps[7][:, 0:1]),
          reads=["ps.7"], writes=[f"{p}tp"])
    mk.op("vector", lambda e: e.tensor_copy(tneg[:], mk.ps[7][:, 1:2]),
          reads=["ps.7"], writes=[f"{p}tn"])
    mk.sfree(f"{p}acc", f"{p}accv", f"{p}o128", f"{p}o1", f"{p}cp",
             f"{p}cn", f"{p}tr")


def _quant_p2_tile(mk, T, wname, qname, i):
    p = f"q{wname}"
    wt, s1, s2, wq = T[f"_{p}bufs"]
    tpos, tneg = T[f"_{p}tp"], T[f"_{p}tn"]
    nb = len(wt)
    sl = i % nb
    mk.dma("gpsimd", wt[sl][:], T[wname][i * 128:(i + 1) * 128, :],
           reads=[wname], writes=[f"{p}w{sl}"])
    mk.op("scalar", (lambda e, sl=sl: e.activation(
              out=s1[sl][:], in_=wt[sl][:], func=AF.Sign,
              bias=tneg[:, 0:1])),
          reads=[f"{p}w{sl}", f"{p}tn"], writes=[f"{p}s1{sl}"])
    mk.op("scalar", (lambda e, sl=sl: e.activation(
              out=s2[sl][:], in_=wt[sl][:], func=AF.Sign,
              bias=tpos[:, 0:1])),
          reads=[f"{p}w{sl}", f"{p}tp"], writes=[f"{p}s2{sl}"])
    mk.op("vector", (lambda e, sl=sl: e.tensor_add(
              wq[sl][:], s1[sl][:], s2[sl][:])),
          reads=[f"{p}s1{sl}", f"{p}s2{sl}"], writes=[f"{p}wq{sl}"])
    mk.dma("gpsimd", T[qname][i * 128:(i + 1) * 128, :], wq[sl][:],
           reads=[f"{p}wq{sl}"], writes=[f"{qname}.{i}"])


def _ph_quant_free(mk, T, wname):
    p = f"q{wname}"
    nb = len(T[f"_{p}bufs"][0])
    names = []
    for i in range(nb):
        names += [f"{p}w{i}", f"{p}s1{i}", f"{p}s2{i}", f"{p}wq{i}"]
    mk.sfree(f"{p}tp", f"{p}tn", *names)


def _ph_quant(mk, T, wname, qname, nt, ncols):
    _ph_quant_p1(mk, T, wname, nt, ncols)
    _ph_quant_p1b(mk, T, wname)
    for i in range(nt):
        _quant_p2_tile(mk, T, wname, qname, i)
    _ph_quant_free(mk, T, wname)


def _ph_l0_ffn(mk, T):
    b1t = mk.sbuf("b1t", [128, 64], F32)
    b2t = mk.sbuf("b2t", [128, 16], F32)
    mk.dma("sync", b1t[:], T["b1t"][:, :], reads=["b1t_in"], writes=["b1t"])
    mk.dma("sync", b2t[:], T["b2t"][:, :], reads=["b2t_in"], writes=["b2t"])
    for th in range(2):
        tt0 = th * 1024
        xnT = mk.sbuf("fxnT", [128, 16 * 1024], BF16)
        hbuf = mk.sbuf("fh", [128, 64 * 1024], BF16)
        w1t = mk.sbuf("fw1t", [128, 16 * 512], BF16)
        w2t = [mk.sbuf(f"fw2t{i}", [128, 512], BF16) for i in range(2)]
        fev = [mk.sbuf(f"fev{i}", [128, 512], BF16) for i in range(2)]
        for dc in range(16):
            for tq in range(2):
                mk.dma("sync", xnT[:, dc * 1024 + tq * 512:
                                   dc * 1024 + (tq + 1) * 512],
                       T["xn0_own"][tt0 + tq * 512:tt0 + (tq + 1) * 512,
                                    dc * 128:(dc + 1) * 128],
                       reads=_tm_tok("xn0_own", tt0 + tq * 512, 512), writes=[f"fxnT.{dc}.{tq}"],
                       transpose=True)
        for fg in range(16):
            for dc in range(16):
                mk.dma("sync", w1t[:, dc * 512:(dc + 1) * 512],
                       T["w1q"][fg * 512:(fg + 1) * 512,
                                dc * 128:(dc + 1) * 128],
                       reads=[f"w1q.{fg * 4 + i}" for i in range(4)],
                       writes=[f"fw1t.{dc}"], transpose=True)
            for half in range(2):
                for dc in range(16):
                    for fl in range(2):
                        for tq in range(2):
                            pb = half * 4 + fl * 2 + tq
                            fc = fg * 4 + half * 2 + fl
                            mk.op("tensor",
                                  (lambda e, pb=pb, dc=dc, fl=fl, tq=tq,
                                   half=half: e.matmul(
                                      mk.ps[pb][:],
                                      w1t[:, dc * 512 + (half * 2 + fl) * 128:
                                          dc * 512 + (half * 2 + fl + 1) * 128],
                                      xnT[:, dc * 1024 + tq * 512:
                                          dc * 1024 + (tq + 1) * 512],
                                      start=(dc == 0), stop=(dc == 15))),
                                  reads=[f"fw1t.{dc}", f"fxnT.{dc}.{tq}"],
                                  writes=[f"ps.{pb}"])
                for fl in range(2):
                    for tq in range(2):
                        pb = half * 4 + fl * 2 + tq
                        fc = fg * 4 + half * 2 + fl
                        mk.op("scalar",
                              (lambda e, pb=pb, fc=fc, tq=tq: e.activation(
                                  out=hbuf[:, fc * 1024 + tq * 512:
                                           fc * 1024 + (tq + 1) * 512],
                                  in_=mk.ps[pb][:], func=AF.Gelu,
                                  bias=b1t[:, fc:fc + 1], scale=0.5)),
                              reads=[f"ps.{pb}", "b1t"],
                              writes=[f"fh.{fc}.{tq}"])
        for dg in range(4):
            for fc in range(64):
                sl = fc % 2
                mk.dma("sync", w2t[sl][:],
                       T["w2q"][dg * 512:(dg + 1) * 512,
                                fc * 128:(fc + 1) * 128],
                       reads=[f"w2q.{dg * 4 + i}" for i in range(4)],
                       writes=[f"fw2t{sl}"], transpose=True)
                for do4 in range(4):
                    for tq in range(2):
                        pb = do4 * 2 + tq
                        mk.op("tensor",
                              (lambda e, pb=pb, sl=sl, do4=do4, tq=tq, fc=fc:
                               e.matmul(
                                  mk.ps[pb][:],
                                  w2t[sl][:, do4 * 128:(do4 + 1) * 128],
                                  hbuf[:, fc * 1024 + tq * 512:
                                       fc * 1024 + (tq + 1) * 512],
                                  start=(fc == 0), stop=(fc == 63))),
                              reads=[f"fw2t{sl}", f"fh.{fc}.{tq}"],
                              writes=[f"ps.{pb}"])
            for do4 in range(4):
                for tq in range(2):
                    pb = do4 * 2 + tq
                    do = dg * 4 + do4
                    es = pb % 2
                    mk.op("vector",
                          (lambda e, pb=pb, do=do, es=es: e.tensor_scalar(
                              fev[es][:], mk.ps[pb][:], 0.5,
                              b2t[:, do:do + 1], ALU.mult, ALU.add)),
                          reads=[f"ps.{pb}", "b2t"], writes=[f"fev{es}"])
                    mk.dma("sync",
                           T["f0T"][do * 128:(do + 1) * 128,
                                    tt0 + tq * 512:tt0 + (tq + 1) * 512],
                           fev[es][:], reads=[f"fev{es}"],
                           writes=[f"f0T.{do}.{th * 2 + tq}"])
        mk.sfree("fxnT", "fh", "fw1t", "fw2t0", "fw2t1", "fev0", "fev1")
    mk.sfree("b1t", "b2t")


def _ph_x1_ln1(mk, T, fused=False, after_quarter=None):
    sc = [_alloc_ln_scratch(mk, f"l1n{s}") for s in range(2)]
    x0t = [mk.sbuf(f"x0t{s}", [128, D], F32) for s in range(2)]
    ab = [mk.sbuf(f"x1a{s}", [128, D], BF16) for s in range(2)]
    bb = [mk.sbuf(f"x1b{s}", [128, D], BF16) for s in range(2)]
    fb = [mk.sbuf(f"x1f{s}", [128, D], BF16) for s in range(2)]
    x1t = [mk.sbuf(f"x1t{s}", [128, D], F32) for s in range(2)]
    junk = [mk.sbuf(f"x1junk{s}", [128, D], BF16) for s in range(2)]
    ob = [mk.sbuf(f"x1o{s}", [128, D], BF16) for s in range(2)]
    for i in range(TOK // 128):
        i0 = i * 128
        s = i % 2
        mk.dma("sync", x0t[s][:], T["x0_own"][i0:i0 + 128, :],
               reads=["x0_own"], writes=[f"x0t{s}"])
        for blk in range(4):
            if fused:
                mk.dma("sync", ab[s][:, blk * 512:(blk + 1) * 512],
                       T["ccA_out"][blk * 512:(blk + 1) * 512, i0:i0 + 128],
                       reads=["ccA_out"], writes=[f"x1a{s}.{blk}"],
                       transpose=True)
            else:
                mk.dma("sync", ab[s][:, blk * 512:(blk + 1) * 512],
                       T["a0s"][blk * 512:(blk + 1) * 512, i0:i0 + 128],
                       reads=["a0s"], writes=[f"x1a{s}.{blk}"],
                       transpose=True)
                mk.dma("sync", bb[s][:, blk * 512:(blk + 1) * 512],
                       T["a0p"][blk * 512:(blk + 1) * 512, i0:i0 + 128],
                       reads=["a0p"], writes=[f"x1b{s}.{blk}"],
                       transpose=True)
            mk.dma("sync", fb[s][:, blk * 512:(blk + 1) * 512],
                   T["f0T"][blk * 512:(blk + 1) * 512, i0:i0 + 128],
                   reads=[f"f0T.{blk * 4 + q}.{i // 4}" for q in range(4)],
                   writes=[f"x1f{s}.{blk}"], transpose=True)
        atoks = [f"x1a{s}.{b}" for b in range(4)]
        btoks = [f"x1b{s}.{b}" for b in range(4)]
        ftoks = [f"x1f{s}.{b}" for b in range(4)]
        if fused:
            mk.op("vector", (lambda e, s=s: e.tensor_add(
                      x1t[s][:], ab[s][:], fb[s][:])),
                  reads=atoks + ftoks, writes=[f"x1t{s}"])
        else:
            mk.op("vector", (lambda e, s=s: e.tensor_add(
                      x1t[s][:], ab[s][:], bb[s][:])),
                  reads=atoks + btoks, writes=[f"x1t{s}"])
            mk.op("vector", (lambda e, s=s: e.tensor_add(
                      x1t[s][:], x1t[s][:], fb[s][:])),
                  reads=[f"x1t{s}"] + ftoks, writes=[f"x1t{s}"])
        mk.op("vector", (lambda e, s=s: e.tensor_add(
                  x1t[s][:], x1t[s][:], x0t[s][:])),
              reads=[f"x1t{s}", f"x0t{s}"], writes=[f"x1t{s}"])
        mk.dma("sync", T["x1_tm"][i0:i0 + 128, :], x1t[s][:],
               reads=[f"x1t{s}"], writes=[f"x1_tm.{i}"])
        _ln_chain(mk, x1t[s], [f"x1t{s}"], D, junk[s], ob[s], f"x1o{s}",
                  sc[s])
        mk.dma("sync", T["xn1_tm"][i0:i0 + 128, :], ob[s][:],
               reads=[f"x1o{s}"], writes=[f"xn1_tm.{i}"])
        if after_quarter is not None and i % 4 == 3:
            after_quarter(i // 4)
    names = []
    for s in range(2):
        names += [f"x0t{s}", f"x1a{s}", f"x1b{s}", f"x1f{s}", f"x1t{s}",
                  f"x1junk{s}", f"x1o{s}"]
        names += [f"l1n{s}{n}" for n in ("sum", "s2", "nmean", "ex2",
                                         "nvar", "lnv", "rstd", "b", "eps")]
    mk.sfree(*names)



def _qkv_xnT_setup(mk, T):
    xnT = mk.sbuf("c_xnT", [128, 16 * 2048], BF16)
    T["_c_xnT"] = xnT

    def load_tb(tb):
        for dc in range(16):
            mk.dma("sync", xnT[:, dc * 2048 + tb * 512:
                               dc * 2048 + (tb + 1) * 512],
                   T["xn1_tm"][tb * 512:(tb + 1) * 512,
                               dc * 128:(dc + 1) * 128],
                   reads=_tm_tok("xn1_tm", tb * 512, 512),
                   writes=[f"c_xnT.{dc}.{tb}"], transpose=True)
    return load_tb


def _ph_l1_qkv(mk, T, sp_bf16=False, preloaded=False):
    if not preloaded:
        load_tb = _qkv_xnT_setup(mk, T)
        for tb in range(4):
            load_tb(tb)
    xnT = T["_c_xnT"]
    wv = mk.sbuf("c_wv", [128, 16 * 2048], BF16)
    ev = [mk.sbuf(f"c_ev{i}", [128, 512], BF16) for i in range(2)]
    for dc in range(16):
        mk.dma("gpsimd", wv[:, dc * 2048:(dc + 1) * 2048],
               T["Wv1"][dc * 128:(dc + 1) * 128, :],
               reads=["Wv1"], writes=[f"c_wv.{dc}"])
    for dvc in range(16):
        for tb in range(4):
            bank = (dvc * 4 + tb) % 2
            for dc in range(16):
                mk.op("tensor", (lambda e, bank=bank, dc=dc, dvc=dvc, tb=tb:
                      e.matmul(mk.ps[bank][:],
                               wv[:, dc * 2048 + dvc * 128:
                                  dc * 2048 + (dvc + 1) * 128],
                               xnT[:, dc * 2048 + tb * 512:
                                   dc * 2048 + (tb + 1) * 512],
                               start=(dc == 0), stop=(dc == 15))),
                      reads=[f"c_wv.{dc}", f"c_xnT.{dc}.{tb}"],
                      writes=[f"ps.{bank}"])
            mk.op("vector", (lambda e, bank=bank: e.tensor_copy(
                      ev[bank][:], mk.ps[bank][:])),
                  reads=[f"ps.{bank}"], writes=[f"c_ev{bank}"])
            mk.dma("sync", T["vT1"][dvc * 128:(dvc + 1) * 128,
                                    tb * 512:(tb + 1) * 512],
                   ev[bank][:], reads=[f"c_ev{bank}"],
                   writes=[f"vT1.{dvc}.{tb}"])
    mk.sfree("c_wv")
    wp = mk.sbuf("c_wp", [128, 16 * 512], BF16)
    for wname, dst in (("Wq1", "q1"), ("Wk1", "k1")):
        for dg in range(4):
            for dc in range(16):
                mk.dma("gpsimd", wp[:, dc * 512:(dc + 1) * 512],
                       T[wname][dc * 128:(dc + 1) * 128,
                                dg * 512:(dg + 1) * 512],
                       reads=[wname], writes=[f"c_wp.{dc}"])
            for tc in range(16):
                bank = tc % 2
                for dc in range(16):
                    mk.op("tensor", (lambda e, bank=bank, dc=dc, tc=tc:
                          e.matmul(mk.ps[bank][:],
                                   xnT[:, dc * 2048 + tc * 128:
                                       dc * 2048 + (tc + 1) * 128],
                                   wp[:, dc * 512:(dc + 1) * 512],
                                   start=(dc == 0), stop=(dc == 15))),
                          reads=[f"c_wp.{dc}", f"c_xnT.{dc}.{tc // 4}"],
                          writes=[f"ps.{bank}"])
                mk.op("vector", (lambda e, bank=bank: e.tensor_copy(
                          ev[bank][:], mk.ps[bank][:])),
                      reads=[f"ps.{bank}"], writes=[f"c_ev{bank}"])
                mk.dma("sync", T[dst][tc * 128:(tc + 1) * 128,
                                      dg * 512:(dg + 1) * 512],
                       ev[bank][:], reads=[f"c_ev{bank}"],
                       writes=[f"{dst}.{tc}.{dg}"])
    mk.sfree("c_xnT", "c_wp")
    sp_dt = BF16 if sp_bf16 else F32
    qbuf = mk.sbuf("c_qb", [128, 16 * 2048], BF16)
    for sc_ in range(16):
        mk.dma("sync", qbuf[:, sc_ * 2048:(sc_ + 1) * 2048],
               T["q1"][sc_ * 128:(sc_ + 1) * 128, :],
               reads=[f"q1.{sc_}.{g}" for g in range(4)],
               writes=[f"c_qb.{sc_}"])
    kbuf = mk.sbuf("c_kb", [128, 16 * 512], BF16)
    evf = [mk.sbuf(f"c_evf{i}", [128, 512], sp_dt) for i in range(2)]
    for fb in range(4):
        for sc_ in range(16):
            mk.dma("sync", kbuf[:, sc_ * 512:(sc_ + 1) * 512],
                   T["k1"][sc_ * 128:(sc_ + 1) * 128,
                           fb * 512:(fb + 1) * 512],
                   reads=[f"k1.{sc_}.{fb}"], writes=[f"c_kb.{sc_}"])
        for ec in range(16):
            bank = ec % 2
            for sc_ in range(16):
                mk.op("tensor", (lambda e, bank=bank, sc_=sc_, ec=ec:
                      e.matmul(mk.ps[bank][:],
                               qbuf[:, sc_ * 2048 + ec * 128:
                                    sc_ * 2048 + (ec + 1) * 128],
                               kbuf[:, sc_ * 512:(sc_ + 1) * 512],
                               start=(sc_ == 0), stop=(sc_ == 15))),
                      reads=[f"c_qb.{sc_}", f"c_kb.{sc_}"],
                      writes=[f"ps.{bank}"])
            mk.op("vector", (lambda e, bank=bank: e.tensor_copy(
                      evf[bank][:], mk.ps[bank][:])),
                  reads=[f"ps.{bank}"], writes=[f"c_evf{bank}"])
            mk.dma("sync", T["sp"][ec * 128:(ec + 1) * 128,
                                   fb * 512:(fb + 1) * 512],
                   evf[bank][:], reads=[f"c_evf{bank}"],
                   writes=[f"sp.{ec}.{fb}"])
    mk.sfree("c_qb", "c_kb", "c_ev0", "c_ev1", "c_evf0", "c_evf1")



def _softmax_row_setup(mk, T, fused):
    sc = {}
    for n in ("mx", "nmx", "rs", "rr"):
        sc[n] = mk.sbuf(f"sm{n}", [128, 1], F32)
    sc["spa"] = mk.sbuf("smspa", [128, D], BF16 if fused else F32)
    sc["spb"] = mk.sbuf("smspb", [128, D], F32)
    sc["S"] = mk.sbuf("smS", [128, D], F32)
    sc["P"] = mk.sbuf("smP", [128, D], F32)
    sc["W"] = mk.sbuf("smW", [128, D], BF16)
    return sc

def _softmax_row(mk, T, sc, ec, fused):
    spa, spb, Sb, Pb, Wb = sc["spa"], sc["spb"], sc["S"], sc["P"], sc["W"]
    e0 = ec * 128
    if fused:
        mk.dma("sync", spa[:], T["ccS_out"][e0:e0 + 128, :],
               reads=["ccS_out"], writes=["smspa"])
        mk.op("vector", lambda e: e.tensor_copy(Sb[:], spa[:]),
              reads=["smspa"], writes=["smS"])
    else:
        mk.dma("sync", spa[:], T["sp_s"][e0:e0 + 128, :],
               reads=["sp_s"], writes=["smspa"])
        mk.dma("sync", spb[:], T["sp_p"][e0:e0 + 128, :],
               reads=["sp_p"], writes=["smspb"])
        mk.op("vector", lambda e: e.tensor_add(Sb[:], spa[:], spb[:]),
              reads=["smspa", "smspb"], writes=["smS"])
    mk.op("vector", lambda e: e.reduce_max(sc["mx"][:], Sb[:], axis=AX.X),
          reads=["smS"], writes=["smmx"])
    mk.op("vector", lambda e: e.tensor_scalar_mul(sc["nmx"][:],
          sc["mx"][:], -0.125), reads=["smmx"], writes=["smnmx"])
    mk.op("scalar", lambda e: e.activation(out=Pb[:], in_=Sb[:],
          func=AF.Exp, bias=sc["nmx"][:, 0:1], scale=0.125,
          accum_out=sc["rs"][:, 0:1]),
          reads=["smS", "smnmx"], writes=["smP", "smrs"])
    mk.op("vector", lambda e: e.reciprocal(sc["rr"][:], sc["rs"][:]),
          reads=["smrs"], writes=["smrr"])
    mk.op("vector", lambda e: e.tensor_scalar_mul(Wb[:], Pb[:],
          sc["rr"][:, 0:1]), reads=["smP", "smrr"], writes=["smW"])
    mk.dma("sync", T["wbuf"][e0:e0 + 128, :], Wb[:],
           reads=["smW"], writes=[f"wbuf.{ec}"])

def _softmax_free(mk):
    mk.sfree("smmx", "smnmx", "smrs", "smrr", "smspa", "smspb", "smS",
             "smP", "smW")

def _softmax_filler(mk, T):
    sc = _softmax_row_setup(mk, T, fused=True)
    state = {"ec": 0}

    def filler(ig):
        if ig < 2:
            return
        while state["ec"] < min(16, (ig - 1) * 2):
            _softmax_row(mk, T, sc, state["ec"], fused=True)
            state["ec"] += 1
    return filler

def _ph_l1_softmax(mk, T, fused=False):
    sc = _softmax_row_setup(mk, T, fused)
    for ec in range(16):
        _softmax_row(mk, T, sc, ec, fused)
    _softmax_free(mk)

def _ph_l1_av_wo(mk, T):
    wT = mk.sbuf("d_wT", [128, 16 * 2048], BF16)
    for fc in range(16):
        for eb in range(4):
            mk.dma("sync", wT[:, fc * 2048 + eb * 512:
                              fc * 2048 + (eb + 1) * 512],
                   T["wbuf"][eb * 512:(eb + 1) * 512,
                             fc * 128:(fc + 1) * 128],
                   reads=[f"wbuf.{eb * 4 + i}" for i in range(4)],
                   writes=[f"d_wT.{fc}.{eb}"], transpose=True)
    wo = mk.sbuf("d_wo", [128, 16 * 2048], BF16)
    for ec in range(16):
        mk.dma("gpsimd", wo[:, ec * 2048:(ec + 1) * 2048],
               T["Wo1"][ec * 128:(ec + 1) * 128, :],
               reads=["Wo1"], writes=[f"d_wo.{ec}"])
    vb = mk.sbuf("d_vb", [128, 16 * 512], BF16)
    ao = mk.sbuf("d_ao", [128, 16 * 512], BF16)
    ev = [mk.sbuf(f"d_ev{i}", [128, 512], BF16) for i in range(2)]
    for sb in range(4):
        s0 = sb * 512
        for fc in range(16):
            mk.dma("sync", vb[:, fc * 512:(fc + 1) * 512],
                   T["vT1"][fc * 128:(fc + 1) * 128, s0:s0 + 512],
                   reads=[f"vT1.{fc}.{sb}"], writes=[f"d_vb.{fc}"])
        for ec in range(16):
            bank = ec % 2
            for fc in range(16):
                mk.op("tensor", (lambda e, bank=bank, fc=fc, ec=ec:
                      e.matmul(mk.ps[bank][:],
                               wT[:, fc * 2048 + ec * 128:
                                  fc * 2048 + (ec + 1) * 128],
                               vb[:, fc * 512:(fc + 1) * 512],
                               start=(fc == 0), stop=(fc == 15))),
                      reads=[f"d_wT.{fc}.{ec // 4}", f"d_vb.{fc}"],
                      writes=[f"ps.{bank}"])
            mk.op("vector", (lambda e, bank=bank, ec=ec: e.tensor_copy(
                      ao[:, ec * 512:(ec + 1) * 512], mk.ps[bank][:])),
                  reads=[f"ps.{bank}"], writes=[f"d_ao.{ec}"])
        for do in range(16):
            bank = 2 + do % 2
            for ec in range(16):
                mk.op("tensor", (lambda e, bank=bank, ec=ec, do=do:
                      e.matmul(mk.ps[bank][:],
                               wo[:, ec * 2048 + do * 128:
                                  ec * 2048 + (do + 1) * 128],
                               ao[:, ec * 512:(ec + 1) * 512],
                               start=(ec == 0), stop=(ec == 15))),
                      reads=[f"d_wo.{ec}", f"d_ao.{ec}"],
                      writes=[f"ps.{bank}"])
            es = do % 2
            mk.op("vector", (lambda e, bank=bank, es=es: e.tensor_copy(
                      ev[es][:], mk.ps[bank][:])),
                  reads=[f"ps.{bank}"], writes=[f"d_ev{es}"])
            mk.dma("sync", T["a1T"][do * 128:(do + 1) * 128, s0:s0 + 512],
                   ev[es][:], reads=[f"d_ev{es}"],
                   writes=[f"a1T.{do}.{sb}"])
    mk.sfree("d_wT", "d_wo", "d_vb", "d_ao", "d_ev0", "d_ev1")


def _ph_l1_gateup_prefetch(mk, T):
    wg = [mk.sbuf(f"g_wg{s}", [128, 16 * 512], BF16) for s in range(2)]
    wu = [mk.sbuf(f"g_wu{s}", [128, 16 * 512], BF16) for s in range(2)]
    T["_g_w"] = (wg, wu)

    def load_ig(ig):
        s = ig % 2
        i0 = ig * 512
        for dc in range(16):
            mk.dma("gpsimd", wg[s][:, dc * 512:(dc + 1) * 512],
                   T["wg"][dc * 128:(dc + 1) * 128, i0:i0 + 512],
                   reads=["wg"], writes=[f"g_wg{s}.{dc}"])
            mk.dma("gpsimd", wu[s][:, dc * 512:(dc + 1) * 512],
                   T["wu"][dc * 128:(dc + 1) * 128, i0:i0 + 512],
                   reads=["wu"], writes=[f"g_wu{s}.{dc}"])
    T["_g_load"] = load_ig
    load_ig(0)
    load_ig(1)


def _ph_l1_gateup(mk, T, filler=None):
    xnT = mk.sbuf("g_xnT", [128, 16 * 2048], BF16)
    for dc in range(16):
        for tb in range(4):
            mk.dma("sync", xnT[:, dc * 2048 + tb * 512:
                               dc * 2048 + (tb + 1) * 512],
                   T["xn1_tm"][tb * 512:(tb + 1) * 512,
                               dc * 128:(dc + 1) * 128],
                   reads=_tm_tok("xn1_tm", tb * 512, 512),
                   writes=[f"g_xnT.{dc}.{tb}"],
                   transpose=True)
    wg, wu = T["_g_w"]
    load_ig = T["_g_load"]
    sg = [mk.sbuf(f"g_sg{s}", [128, 512], F32) for s in range(2)]
    hb = [mk.sbuf(f"g_h{s}", [128, 512], BF16) for s in range(2)]
    habs = [mk.sbuf(f"g_habs{s}", [128, 512], F32) for s in range(2)]
    q4 = [mk.sbuf(f"g_q{i}", [128, 128], F32) for i in range(5)]
    m4 = [mk.sbuf(f"g_m{i}", [128, 128], BF16) for i in range(4)]
    hs = [mk.sbuf(f"g_hs{i}", [128, 512], BF16) for i in range(2)]
    for ig in range(16):
        i0 = ig * 512
        s = ig % 2
        if ig + 2 < 16:
            load_ig(ig + 2)
        if filler is not None:
            filler(ig)
        for tc in range(16):
            pg, pu = 2 * (tc % 2), 2 * (tc % 2) + 1
            for dc in range(16):
                mk.op("tensor", (lambda e, dc=dc, tc=tc, pg=pg, s=s:
                      e.matmul(
                          mk.ps[pg][:],
                          xnT[:, dc * 2048 + tc * 128:
                              dc * 2048 + (tc + 1) * 128],
                          wg[s][:, dc * 512:(dc + 1) * 512],
                          start=(dc == 0), stop=(dc == 15))),
                      reads=[f"g_xnT.{dc}.{tc // 4}", f"g_wg{s}.{dc}"],
                      writes=[f"ps.{pg}"])
                mk.op("tensor", (lambda e, dc=dc, tc=tc, pu=pu, s=s:
                      e.matmul(
                          mk.ps[pu][:],
                          xnT[:, dc * 2048 + tc * 128:
                              dc * 2048 + (tc + 1) * 128],
                          wu[s][:, dc * 512:(dc + 1) * 512],
                          start=(dc == 0), stop=(dc == 15))),
                      reads=[f"g_xnT.{dc}.{tc // 4}", f"g_wu{s}.{dc}"],
                      writes=[f"ps.{pu}"])
            st = tc % 2
            mk.op("scalar", (lambda e, pg=pg, st=st: e.activation(
                  out=sg[st][:], in_=mk.ps[pg][:],
                  func=AF.Silu)), reads=[f"ps.{pg}"], writes=[f"g_sg{st}"])
            mk.op("vector", (lambda e, pu=pu, st=st: e.tensor_mul(
                  hb[st][:], sg[st][:],
                  mk.ps[pu][:])), reads=[f"g_sg{st}", f"ps.{pu}"],
                  writes=[f"g_h{st}"])
            mk.op("scalar", (lambda e, st=st: e.activation(
                  out=habs[st][:], in_=hb[st][:],
                  func=AF.Abs)), reads=[f"g_h{st}"], writes=[f"g_habs{st}"])
            ha = habs[st]
            mk.op("vector", (lambda e, ha=ha: e.tensor_max(
                  q4[0][:], ha[:, 0::4],
                  ha[:, 1::4])), reads=[f"g_habs{st}"], writes=["g_q0"])
            mk.op("vector", (lambda e, ha=ha: e.tensor_tensor(q4[1][:],
                  ha[:, 0::4], ha[:, 1::4], ALU.min)),
                  reads=[f"g_habs{st}"], writes=["g_q1"])
            mk.op("vector", (lambda e, ha=ha: e.tensor_max(
                  q4[2][:], ha[:, 2::4],
                  ha[:, 3::4])), reads=[f"g_habs{st}"], writes=["g_q2"])
            mk.op("vector", (lambda e, ha=ha: e.tensor_tensor(q4[3][:],
                  ha[:, 2::4], ha[:, 3::4], ALU.min)),
                  reads=[f"g_habs{st}"], writes=["g_q3"])
            mk.op("vector", lambda e: e.tensor_tensor(q4[0][:], q4[0][:],
                  q4[2][:], ALU.min), reads=["g_q0", "g_q2"],
                  writes=["g_q0"])
            mk.op("vector", lambda e: e.tensor_max(q4[1][:], q4[1][:],
                  q4[3][:]), reads=["g_q1", "g_q3"], writes=["g_q1"])
            mk.op("vector", lambda e: e.tensor_max(q4[4][:], q4[0][:],
                  q4[1][:]), reads=["g_q0", "g_q1"], writes=["g_q4"])
            for jj in range(4):
                mk.op("vector", (lambda e, jj=jj, ha=ha: e.tensor_tensor(
                          m4[jj][:], ha[:, jj::4], q4[4][:], ALU.is_ge)),
                      reads=[f"g_habs{st}", "g_q4"], writes=[f"g_m{jj}"])
            for jj in range(4):
                mk.op("gpsimd", (lambda e, jj=jj, st=st: e.tensor_mul(
                          hs[st][:, jj::4], hb[st][:, jj::4], m4[jj][:])),
                      reads=[f"g_h{st}", f"g_m{jj}"], writes=[f"g_hs{st}"])
            mk.dma("sync", T["hsp"][tc * 128:(tc + 1) * 128, i0:i0 + 512],
                   hs[st][:], reads=[f"g_hs{st}"],
                   writes=[f"hsp.{tc}.{ig}"])
    if filler is not None:
        filler(17)
        _softmax_free(mk)
    mk.sfree("g_xnT", "g_wg0", "g_wg1", "g_wu0", "g_wu1", "g_sg0", "g_sg1",
             "g_h0", "g_h1", "g_habs0", "g_habs1",
             "g_q0", "g_q1", "g_q2", "g_q3", "g_q4", "g_m0", "g_m1",
             "g_m2", "g_m3", "g_hs0", "g_hs1")


def _ph_wd_cast(mk, T):
    wdt = [mk.sbuf(f"c_wd{i}", [128, 2048], BF16) for i in range(2)]
    for ic in range(64):
        sl = ic % 2
        mk.dma("gpsimd", wdt[sl][:], T["wd"][ic * 128:(ic + 1) * 128, :],
               reads=["wd"], writes=[f"c_wd{sl}"])
        mk.dma("gpsimd", T["wd_bf"][ic * 128:(ic + 1) * 128, :], wdt[sl][:],
               reads=[f"c_wd{sl}"], writes=[f"wd_bf.{ic}"])
    mk.sfree("c_wd0", "c_wd1")


def _ph_l1_down_out(mk, T):
    x1t = [mk.sbuf(f"o_x1{s}", [128, D], F32) for s in range(2)]
    y1b = [mk.sbuf(f"o_y1b{s}", [128, D], BF16) for s in range(2)]
    outt = [mk.sbuf(f"o_out{s}", [128, D], F32) for s in range(2)]

    def out_tile(i):
        i0 = i * 128
        s = i % 2
        mk.dma("scalar", x1t[s][:], T["x1_tm"][i0:i0 + 128, :],
               reads=[f"x1_tm.{i}"], writes=[f"o_x1{s}"])
        for blk in range(4):
            mk.dma("sync", y1b[s][:, blk * 512:(blk + 1) * 512],
                   T["y1T"][blk * 512:(blk + 1) * 512, i0:i0 + 128],
                   reads=[f"y1T.{blk * 4 + q}.{i // 4}" for q in range(4)],
                   writes=[f"o_y1b{s}.{blk}"], transpose=True)
        mk.op("vector", (lambda e, s=s: e.tensor_add(
                  outt[s][:], x1t[s][:], y1b[s][:])),
              reads=[f"o_x1{s}"] + [f"o_y1b{s}.{b}" for b in range(4)],
              writes=[f"o_out{s}"])
        mk.dma("sync", T["OUT"][i0:i0 + 128, :], outt[s][:],
               reads=[f"o_out{s}"], writes=[f"OUT.{i}"])

    for th in range(2):
        tt0 = th * 1024
        hT = mk.sbuf("o_hT", [128, 64 * 1024], BF16)
        for ic in range(64):
            for tq in range(2):
                mk.dma("sync", hT[:, ic * 1024 + tq * 512:
                                  ic * 1024 + (tq + 1) * 512],
                       T["hsp"][tt0 + tq * 512:tt0 + (tq + 1) * 512,
                                ic * 128:(ic + 1) * 128],
                       reads=[f"hsp.{(tt0 + tq * 512) // 128 + q}.{ic // 4}"
                              for q in range(4)],
                       writes=[f"o_hT.{ic}.{tq}"], transpose=True)
        wdc = mk.sbuf("o_wdc", [128, 64 * 128], BF16)
        a1b = [mk.sbuf(f"o_a1{i}", [128, 512], BF16) for i in range(2)]
        yev = [mk.sbuf(f"o_y{i}", [128, 512], BF16) for i in range(2)]
        for dc in range(16):
            for g8 in range(8):
                src = T["wd_bf"][g8 * 1024:(g8 + 1) * 1024,
                                 dc * 128:(dc + 1) * 128]
                src = src.rearrange("(g r) c -> r g c", g=8)
                mk.dma("sync", wdc[:, g8 * 1024:(g8 + 1) * 1024], src,
                       reads=[f"wd_bf.{g8 * 8 + i}" for i in range(8)],
                       writes=[f"o_wdc.{g8}"])
            for tq in range(2):
                bank = tq
                for ic in range(64):
                    mk.op("tensor", (lambda e, bank=bank, ic=ic, tq=tq:
                          e.matmul(mk.ps[bank][:],
                                   wdc[:, ic * 128:(ic + 1) * 128],
                                   hT[:, ic * 1024 + tq * 512:
                                      ic * 1024 + (tq + 1) * 512],
                                   start=(ic == 0), stop=(ic == 63))),
                          reads=[f"o_wdc.{ic // 8}", f"o_hT.{ic}.{tq}"],
                          writes=[f"ps.{bank}"])
                mk.dma("scalar", a1b[tq][:],
                       T["a1T"][dc * 128:(dc + 1) * 128,
                                tt0 + tq * 512:tt0 + (tq + 1) * 512],
                       reads=[f"a1T.{dc}.{th * 2 + tq}"],
                       writes=[f"o_a1{tq}"])
                mk.op("vector", (lambda e, bank=bank, tq=tq: e.tensor_add(
                          yev[tq][:], mk.ps[bank][:], a1b[tq][:])),
                      reads=[f"ps.{bank}", f"o_a1{tq}"], writes=[f"o_y{tq}"])
                mk.dma("sync",
                       T["y1T"][dc * 128:(dc + 1) * 128,
                                tt0 + tq * 512:tt0 + (tq + 1) * 512],
                       yev[tq][:], reads=[f"o_y{tq}"],
                       writes=[f"y1T.{dc}.{th * 2 + tq}"])
        mk.sfree("o_hT", "o_wdc", "o_a10", "o_a11", "o_y0", "o_y1")
        for i in range(th * 8, th * 8 + 8):
            out_tile(i)
    mk.sfree("o_x10", "o_x11", "o_y1b0", "o_y1b1", "o_out0", "o_out1")



def _dram(nc, specs):
    T = {}
    for name, shape, dt, kind in specs:
        T[name] = nc.dram_tensor(name, shape, dt, kind=kind).ap()
    return T


def _build_A():
    nc = bass.Bass("TRN2", target_bir_lowering=False)
    T = _dram(nc, [
        ("x0", [S, D], F32, "ExternalInput"),
        ("Wq_h", [D, 1024], F32, "ExternalInput"),
        ("Wqr_h", [D, 512], F32, "ExternalInput"),
        ("Wkvd", [D, DKV], F32, "ExternalInput"),
        ("Wkvup_h", [DKV, 2048], F32, "ExternalInput"),
        ("Wkr", [D, RD], F32, "ExternalInput"),
        ("Wo_h", [1024, D], F32, "ExternalInput"),
        ("ropeC", [128, S], F32, "ExternalInput"),
        ("ropeS", [128, S], F32, "ExternalInput"),
        ("masks", [4, 128, 512], BF16, "ExternalInput"),
        ("xn0_tm", [S, D], BF16, "ExternalOutput"),
        ("a0pT", [D, S], BF16, "ExternalOutput"),
        ("qcT", [1024, S], BF16, "Internal"),
        ("qrT", [512, S], BF16, "Internal"),
        ("qrraw", [512, S], F32, "Internal"),
        ("krraw", [64, S], F32, "Internal"),
        ("kcT", [1024, S], BF16, "Internal"),
        ("v_tm", [S, 1024], BF16, "Internal"),
        ("onormT", [1024, S], BF16, "Internal"),
    ])
    mk = MK(nc)
    _ph_ln(mk, T, "x0", "xn0_tm", S, "ln0")
    _ph_l0_proj(mk, T)
    _ph_l0_attn(mk, T)
    _ph_l0_wo(mk, T)
    mk.finalize()
    return nc


def _build_B():
    nc = bass.Bass("TRN2", target_bir_lowering=False)
    T = _dram(nc, [
        ("x0_own", [TOK, D], F32, "ExternalInput"),
        ("xn0_own", [TOK, D], BF16, "ExternalInput"),
        ("w1", [DFF, D], F32, "ExternalInput"),
        ("w2", [D, DFF], F32, "ExternalInput"),
        ("b1t", [128, 64], F32, "ExternalInput"),
        ("b2t", [128, 16], F32, "ExternalInput"),
        ("a0s", [D, TOK], BF16, "ExternalInput"),
        ("a0p", [D, TOK], BF16, "ExternalInput"),
        ("x1_tm", [TOK, D], F32, "ExternalOutput"),
        ("xn1_tm", [TOK, D], BF16, "ExternalOutput"),
        ("w1q", [DFF, D], BF16, "Internal"),
        ("w2q", [D, DFF], BF16, "Internal"),
        ("f0T", [D, TOK], BF16, "Internal"),
    ])
    mk = MK(nc)
    _ph_quant(mk, T, "w1", "w1q", 64, 2048)
    _ph_quant(mk, T, "w2", "w2q", 16, 8192)
    _ph_l0_ffn(mk, T)
    _ph_x1_ln1(mk, T)
    mk.finalize()
    return nc


def _build_C():
    nc = bass.Bass("TRN2", target_bir_lowering=False)
    T = _dram(nc, [
        ("xn1_tm", [TOK, D], BF16, "ExternalInput"),
        ("Wq1", [D, D], F32, "ExternalInput"),
        ("Wk1", [D, D], F32, "ExternalInput"),
        ("Wv1", [D, D], F32, "ExternalInput"),
        ("wd", [DFF, D], F32, "ExternalInput"),
        ("sp", [D, D], F32, "ExternalOutput"),
        ("vT1", [D, TOK], BF16, "ExternalOutput"),
        ("wd_bf", [DFF, D], BF16, "ExternalOutput"),
        ("q1", [TOK, D], BF16, "Internal"),
        ("k1", [TOK, D], BF16, "Internal"),
    ])
    mk = MK(nc)
    _ph_l1_qkv(mk, T)
    _ph_wd_cast(mk, T)
    mk.finalize()
    return nc


def _build_D():
    nc = bass.Bass("TRN2", target_bir_lowering=False)
    T = _dram(nc, [
        ("sp_s", [D, D], F32, "ExternalInput"),
        ("sp_p", [D, D], F32, "ExternalInput"),
        ("vT1", [D, TOK], BF16, "ExternalInput"),
        ("Wo1", [D, D], F32, "ExternalInput"),
        ("xn1_tm", [TOK, D], BF16, "ExternalInput"),
        ("wg", [D, DFF], F32, "ExternalInput"),
        ("wu", [D, DFF], F32, "ExternalInput"),
        ("wd_bf", [DFF, D], BF16, "ExternalInput"),
        ("x1_tm", [TOK, D], F32, "ExternalInput"),
        ("OUT", [TOK, D], F32, "ExternalOutput"),
        ("wbuf", [D, D], BF16, "Internal"),
        ("a1T", [D, TOK], BF16, "Internal"),
        ("hsp", [TOK, DFF], BF16, "Internal"),
        ("y1T", [D, TOK], BF16, "Internal"),
    ])
    mk = MK(nc)
    _ph_l1_softmax(mk, T)
    _ph_l1_av_wo(mk, T)
    _ph_l1_gateup(mk, T)
    _ph_l1_down_out(mk, T)
    mk.finalize()
    return nc




_RG = [[0, 1], [2, 3], [4, 5], [6, 7]]


def _build_fused():
    nc = bass.Bass("TRN2", target_bir_lowering=False)
    T = _dram(nc, [
        ("x0", [S, D], F32, "ExternalInput"),
        ("x0_own", [TOK, D], F32, "ExternalInput"),
        ("Wq_h", [D, 1024], F32, "ExternalInput"),
        ("Wqr_h", [D, 512], F32, "ExternalInput"),
        ("Wkvd", [D, DKV], F32, "ExternalInput"),
        ("Wkvup_h", [DKV, 2048], F32, "ExternalInput"),
        ("Wkr", [D, RD], F32, "ExternalInput"),
        ("Wo_h", [1024, D], F32, "ExternalInput"),
        ("ropeC", [128, S], F32, "ExternalInput"),
        ("ropeS", [128, S], F32, "ExternalInput"),
        ("masks", [4, 128, 512], BF16, "ExternalInput"),
        ("w1", [DFF, D], F32, "ExternalInput"),
        ("w2", [D, DFF], F32, "ExternalInput"),
        ("b1t", [128, 64], F32, "ExternalInput"),
        ("b2t", [128, 16], F32, "ExternalInput"),
        ("Wq1", [D, D], F32, "ExternalInput"),
        ("Wk1", [D, D], F32, "ExternalInput"),
        ("Wv1", [D, D], F32, "ExternalInput"),
        ("Wo1", [D, D], F32, "ExternalInput"),
        ("wg", [D, DFF], F32, "ExternalInput"),
        ("wu", [D, DFF], F32, "ExternalInput"),
        ("wd", [DFF, D], F32, "ExternalInput"),
        ("OUT", [TOK, D], F32, "ExternalOutput"),
        ("xn0_tm", [S, D], BF16, "Internal"),
        ("xn0_own", [TOK, D], BF16, "Internal"),
        ("qcT", [1024, S], BF16, "Internal"),
        ("qrT", [512, S], BF16, "Internal"),
        ("qrraw", [512, S], F32, "Internal"),
        ("krraw", [64, S], F32, "Internal"),
        ("kcT", [1024, S], BF16, "Internal"),
        ("v_tm", [S, 1024], BF16, "Internal"),
        ("onormT", [1024, S], BF16, "Internal"),
        ("ccA_in", [2 * D, TOK], BF16, "Internal"),
        ("ccA_out", [D, TOK], BF16, "Internal"),
        ("w1q", [DFF, D], BF16, "Internal"),
        ("w2q", [D, DFF], BF16, "Internal"),
        ("f0T", [D, TOK], BF16, "Internal"),
        ("x1_tm", [TOK, D], F32, "Internal"),
        ("xn1_tm", [TOK, D], BF16, "Internal"),
        ("q1", [TOK, D], BF16, "Internal"),
        ("k1", [TOK, D], BF16, "Internal"),
        ("vT1", [D, TOK], BF16, "Internal"),
        ("sp", [D, D], BF16, "Internal"),
        ("ccS_out", [D, D], BF16, "Internal"),
        ("wd_bf", [DFF, D], BF16, "Internal"),
        ("wbuf", [D, D], BF16, "Internal"),
        ("a1T", [D, TOK], BF16, "Internal"),
        ("hsp", [TOK, DFF], BF16, "Internal"),
        ("y1T", [D, TOK], BF16, "Internal"),
    ])
    mk = MK(nc)
    _ph_ln(mk, T, "x0", "xn0_tm", S, "ln0")
    _ph_l0_proj(mk, T)
    _ph_quant_p1(mk, T, "w1", 64, 2048)
    _ph_quant_p1(mk, T, "w2", 16, 8192, nb=1)
    work = [("w1", "w1q", i) for i in range(64)] + \
           [("w2", "w2q", i) for i in range(16)]

    def filler(slot):
        if slot == 7:
            _ph_quant_p1b(mk, T, "w1")
            _ph_quant_p1b(mk, T, "w2")
        if slot >= 24:
            for _ in range(2):
                if work:
                    wn, qn, i = work.pop(0)
                    _quant_p2_tile(mk, T, wn, qn, i)

    _ph_l0_attn(mk, T, filler=filler)
    _ph_ln(mk, T, "x0_own", "xn0_own", TOK, "lnb")
    while work:
        wn, qn, i = work.pop(0)
        _quant_p2_tile(mk, T, wn, qn, i)
    _ph_quant_free(mk, T, "w1")
    _ph_quant_free(mk, T, "w2")
    _ph_l0_wo(mk, T, cc=True)
    mk.op("gpsimd", lambda e: e.collective_compute(
              "ReduceScatter", ALU.add, replica_groups=_RG,
              ins=[T["ccA_in"][:, :]], outs=[T["ccA_out"][:, :]]),
          reads=[f"ccA_in.{do}.{tb}" for do in range(16) for tb in range(8)],
          writes=["ccA_out"])
    _ph_l0_ffn(mk, T)
    qkv_load_tb = _qkv_xnT_setup(mk, T)
    _ph_x1_ln1(mk, T, fused=True, after_quarter=qkv_load_tb)
    _ph_l1_qkv(mk, T, sp_bf16=True, preloaded=True)
    _ph_l1_gateup_prefetch(mk, T)
    mk.op("gpsimd", lambda e: e.collective_compute(
              "AllReduce", ALU.add, replica_groups=_RG,
              ins=[T["sp"][:, :]], outs=[T["ccS_out"][:, :]]),
          reads=[f"sp.{ec}.{fb}" for ec in range(16) for fb in range(4)],
          writes=["ccS_out"])
    _ph_l1_gateup(mk, T, filler=_softmax_filler(mk, T))
    _ph_l1_av_wo(mk, T)
    _ph_wd_cast(mk, T)
    _ph_l1_down_out(mk, T)
    mk.finalize()
    return nc



_FUSED = True

def kernel(**inp):
    if _FUSED:
        return _kernel_fused(**inp)
    return _kernel_unfused(**inp)


def _common_host(inp):
    f32 = np.float32

    def np32(name):
        return np.ascontiguousarray(np.asarray(inp[name], dtype=f32))

    W = {k: np32(k) for k in inp}
    b1 = W["L0_ffn_b1"]
    b2 = W["L0_ffn_b2"]
    W["b1t"] = np.ascontiguousarray(b1.reshape(64, 128).T)
    W["b2t"] = np.ascontiguousarray(b2.reshape(16, 128).T)
    return W


def _kernel_fused(**inp):
    ropeC, ropeS, masks = _host_tables()
    W = _common_host(inp)
    x0 = W["L0_attn_x"]
    cores = list(range(8))
    nc = _build_fused()
    in_maps = []
    for c in cores:
        b, j = c // 2, c % 2
        own = slice(j * TOK, (j + 1) * TOK)
        Wkvu = W["L0_attn_Wkv_up"]
        in_maps.append({
            "x0": x0[b],
            "x0_own": np.ascontiguousarray(x0[b][own]),
            "Wq_h": np.ascontiguousarray(
                W["L0_attn_Wq"][:, j * 1024:(j + 1) * 1024]),
            "Wqr_h": np.ascontiguousarray(
                W["L0_attn_Wq_rope"][:, j * 512:(j + 1) * 512]),
            "Wkvd": W["L0_attn_Wkv_down"],
            "Wkvup_h": np.ascontiguousarray(np.concatenate(
                [Wkvu[:, j * 1024:(j + 1) * 1024],
                 Wkvu[:, 2048 + j * 1024:2048 + (j + 1) * 1024]], axis=1)),
            "Wkr": W["L0_attn_Wk_rope"],
            "Wo_h": np.ascontiguousarray(
                W["L0_attn_Wo"][j * 1024:(j + 1) * 1024, :]),
            "ropeC": ropeC, "ropeS": ropeS, "masks": masks,
            "w1": W["L0_ffn_w1"], "w2": W["L0_ffn_w2"],
            "b1t": W["b1t"], "b2t": W["b2t"],
            "Wq1": W["L1_attn_Wq"], "Wk1": W["L1_attn_Wk"],
            "Wv1": W["L1_attn_Wv"], "Wo1": W["L1_attn_Wo"],
            "wg": W["L1_ffn_w_gate"], "wu": W["L1_ffn_w_up"],
            "wd": W["L1_ffn_w_down"],
        })
    res = run_bass_kernel_spmd(nc, in_maps, core_ids=cores).results
    out = np.empty((B, S, D), np.float32)
    for c in cores:
        b, j = c // 2, c % 2
        out[b, j * TOK:(j + 1) * TOK, :] = res[c]["OUT"]
    return out


def _kernel_unfused(**inp):
    f32 = np.float32
    ropeC, ropeS, masks = _host_tables()
    cores = list(range(8))

    def np32(name):
        return np.ascontiguousarray(np.asarray(inp[name], dtype=f32))

    x0 = np32("L0_attn_x")
    Wq, Wqr = np32("L0_attn_Wq"), np32("L0_attn_Wq_rope")
    Wkvd, Wkvu = np32("L0_attn_Wkv_down"), np32("L0_attn_Wkv_up")
    Wkr, Wo = np32("L0_attn_Wk_rope"), np32("L0_attn_Wo")
    w1, b1 = np32("L0_ffn_w1"), np32("L0_ffn_b1")
    w2, b2 = np32("L0_ffn_w2"), np32("L0_ffn_b2")
    Wq1, Wk1 = np32("L1_attn_Wq"), np32("L1_attn_Wk")
    Wv1, Wo1 = np32("L1_attn_Wv"), np32("L1_attn_Wo")
    wg, wu, wd = np32("L1_ffn_w_gate"), np32("L1_ffn_w_up"), \
        np32("L1_ffn_w_down")
    b1t = np.ascontiguousarray(b1.reshape(64, 128).T)
    b2t = np.ascontiguousarray(b2.reshape(16, 128).T)

    ncA = _build_A()
    inA = []
    for c in cores:
        b, j = c // 2, c % 2
        inA.append({
            "x0": x0[b],
            "Wq_h": np.ascontiguousarray(Wq[:, j * 1024:(j + 1) * 1024]),
            "Wqr_h": np.ascontiguousarray(Wqr[:, j * 512:(j + 1) * 512]),
            "Wkvd": Wkvd,
            "Wkvup_h": np.ascontiguousarray(np.concatenate(
                [Wkvu[:, j * 1024:(j + 1) * 1024],
                 Wkvu[:, 2048 + j * 1024:2048 + (j + 1) * 1024]], axis=1)),
            "Wkr": Wkr,
            "Wo_h": np.ascontiguousarray(Wo[j * 1024:(j + 1) * 1024, :]),
            "ropeC": ropeC, "ropeS": ropeS, "masks": masks,
        })
    rA = run_bass_kernel_spmd(ncA, inA, core_ids=cores).results

    ncB = _build_B()
    inB = []
    for c in cores:
        b, j = c // 2, c % 2
        own = slice(j * TOK, (j + 1) * TOK)
        inB.append({
            "x0_own": np.ascontiguousarray(x0[b][own]),
            "xn0_own": np.ascontiguousarray(rA[c]["xn0_tm"][own]),
            "w1": w1, "w2": w2, "b1t": b1t, "b2t": b2t,
            "a0s": np.ascontiguousarray(rA[c]["a0pT"][:, own]),
            "a0p": np.ascontiguousarray(rA[c ^ 1]["a0pT"][:, own]),
        })
    rB = run_bass_kernel_spmd(ncB, inB, core_ids=cores).results

    ncC = _build_C()
    inC = [{"xn1_tm": rB[c]["xn1_tm"], "Wq1": Wq1, "Wk1": Wk1, "Wv1": Wv1,
            "wd": wd} for c in cores]
    rC = run_bass_kernel_spmd(ncC, inC, core_ids=cores).results

    ncD = _build_D()
    inD = [{"sp_s": rC[c]["sp"], "sp_p": rC[c ^ 1]["sp"],
            "vT1": rC[c]["vT1"], "Wo1": Wo1, "xn1_tm": rB[c]["xn1_tm"],
            "wg": wg, "wu": wu, "wd_bf": rC[c]["wd_bf"],
            "x1_tm": rB[c]["x1_tm"]} for c in cores]
    rD = run_bass_kernel_spmd(ncD, inD, core_ids=cores).results

    out = np.empty((B, S, D), np.float32)
    for c in cores:
        b, j = c // 2, c % 2
        out[b, j * TOK:(j + 1) * TOK, :] = rD[c]["OUT"]
    return out
